# version 07
# speedup vs baseline: 3.8667x; 1.0632x over previous
import math
import os

import numpy as np
import ml_dtypes

os.environ.setdefault("BASS_INTERP_NUM_WORKERS", "8")

import concourse.bass as bass
import concourse.bass_types as bt
import concourse.mybir as mybir
from concourse.bass_utils import run_bass_kernel_spmd
from concourse.aot_env import allow_ici_all_to_all
from concourse.bass_isa import ReduceOp
from concourse.library_config import attn as LIB_ATTN, standard as LIB_STD

F32 = mybir.dt.float32
BF16 = mybir.dt.bfloat16
I32 = mybir.dt.int32
ACT = mybir.ActivationFunctionType
ALU = mybir.AluOpType

L = 16384
D = 1024
N = 2 * L
M = N // 2
NC = 8
TPC = L // NC
HALO = 128
TT = TPC + HALO
WIDTH = 3 * D

bf16 = ml_dtypes.bfloat16
KREV = 127 - np.arange(128)
KWRAP = (128 - np.arange(128)) % 128


def _f32(a):
    return np.ascontiguousarray(a, dtype=np.float32)


def _b16(a):
    return np.ascontiguousarray(np.asarray(a, dtype=np.float32).astype(bf16))


def _host_tables():
    t = {}
    wN = np.exp(-2j * np.pi * np.arange(N) / N)
    a_f = (1 - 1j * wN) / 2
    b_f = (1 + 1j * wN) / 2
    p_f = (1 + 1j * np.conj(wN[:M])) / 2
    q_f = (1 - 1j * np.conj(wN[:M])) / 2
    A1 = p_f * np.conj(a_f[:M]) * a_f[:M] + q_f * np.conj(a_f[M:]) * a_f[M:]
    A2 = p_f * np.conj(a_f[:M]) * b_f[:M] + q_f * np.conj(a_f[M:]) * b_f[M:]
    B1 = p_f * np.conj(b_f[:M]) * a_f[:M] + q_f * np.conj(b_f[M:]) * a_f[M:]
    B2 = p_f * np.conj(b_f[:M]) * b_f[:M] + q_f * np.conj(b_f[M:]) * b_f[M:]
    k2g = np.arange(128)[:, None]
    sg = np.arange(128)[None, :]
    f_nat = k2g * 128 + sg
    f_rev = (127 - k2g) * 128 + sg
    for nm, arr in (("A1", A1), ("A2", A2), ("B1", B1), ("B2", B2)):
        t[nm + "n"] = arr[f_nat]
        t[nm + "r"] = arr[f_rev]
    w128 = np.exp(-2j * np.pi * np.outer(np.arange(128), np.arange(128)) / 128)
    F1 = w128[:64]
    t["Ws_re"] = np.concatenate([F1.real, -F1.imag], 0)
    t["Ws_im"] = np.concatenate([F1.imag, F1.real], 0)
    t["F1f_re"] = np.ascontiguousarray(w128.real)
    t["F1f_im"] = np.ascontiguousarray(w128.imag)
    m2 = np.arange(128)
    vlist = [(a, "nat") for a in range(0, 65)] + [(b, "rev") for b in range(65, 128)]
    vlist += [(64, "rev"), (0, "wrap")]
    t["vlist"] = vlist
    Bv = np.empty((len(vlist), 3, 128, 128), dtype=np.float32)
    for vid, (k1, vm) in enumerate(vlist):
        B = np.exp(-2j * np.pi * (m2[:, None] * k1) / M) * w128
        if vm == "rev":
            B = B[:, KREV]
        elif vm == "wrap":
            B = B[:, KWRAP]
        Bv[vid, 0] = B.real
        Bv[vid, 1] = B.imag
        Bv[vid, 2] = -B.imag
    t["Bv"] = Bv
    tw = np.exp(-2j * np.pi * np.outer(m2, np.arange(128)) / M)
    t["Htw"] = np.stack([tw.real, tw.imag], 0)
    t["H2st"] = np.stack([
        np.stack([w128.real, w128.imag, -w128.imag], 0),
        np.stack([w128[:, KREV].real, w128[:, KREV].imag, -w128[:, KREV].imag], 0),
        np.stack([w128[:, KWRAP].real, w128[:, KWRAP].imag, -w128[:, KWRAP].imag], 0)], 0)
    iW = np.exp(+2j * np.pi * np.outer(np.arange(128), np.arange(128)) / 128)
    iWr = np.exp(+2j * np.pi * np.outer(127 - np.arange(128), np.arange(128)) / 128)
    t["iWn"] = np.stack([iW.real, iW.imag, -iW.imag], 0)
    t["iWr"] = np.stack([iWr.real, iWr.imag, -iWr.imag], 0)
    Fo = np.empty((128, 3, 128, 64), dtype=np.float32)
    m1p = np.arange(64)
    for m2p in range(128):
        m2v = 127 - m2p
        Ft = np.exp(+2j * np.pi * (np.outer(np.arange(128), np.arange(64)) / 128
                                   + (np.arange(128) * m2v / M)[:, None])) / M
        Fo[m2p, 0] = Ft.imag[:, 63 - m1p]
        Fo[m2p, 1] = Ft.real[:, 63 - m1p]
        Fo[m2p, 2] = -Ft.imag[:, 63 - m1p]
    t["Fo"] = Fo
    tg = np.arange(L, dtype=np.float64) / (L - 1)
    ang = 2.0 * math.pi * np.arange(L, dtype=np.float64) / L
    ph = 0.0001 * ang
    t["peT"] = np.stack([tg, np.cos(ph), -np.sin(ph)], 0).astype(np.float32)
    dmax, dmin = (math.log(0.01) / 0.3, math.log(0.01) / 1.5)
    t["deltas"] = np.abs(np.linspace(dmax, dmin, D)).astype(np.float32)
    t["iden"] = np.eye(128, dtype=np.float32)
    t["Atabs"] = np.stack(
        [x for nm in ("A1", "A2", "B1", "B2") for x in (t[nm + "n"].real, t[nm + "n"].imag)]
        + [x for nm in ("A1", "A2", "B1", "B2") for x in (t[nm + "r"].real, t[nm + "r"].imag)], 0)
    tp = 2 * math.pi
    c1 = np.float32(tp); c2 = np.float32(tp - float(c1)); c3 = np.float32(tp - float(c1) - float(c2))
    t["cw"] = (float(c1), float(c2), float(c3))
    return t


_TAB = None


def _tables():
    global _TAB
    if _TAB is None:
        _TAB = _host_tables()
    return _TAB


class Prog:
    ENGINES = ("sync", "gpsimd", "tensor", "vector", "scalar")

    def __init__(self, nc):
        self.nc = nc
        self.st = {e: [] for e in self.ENGINES}
        self.sems = {}
        self.cnt = {}

    def sem(self, name):
        if name not in self.sems:
            self.sems[name] = self.nc.alloc_semaphore(name)
            self.cnt[name] = 0
        return self.sems[name]

    def val(self, name):
        self.sem(name)
        return self.cnt[name]

    def wait(self, eng, name, val=None):
        s = self.sem(name)
        v = self.cnt[name] if val is None else val
        self.st[eng].append(lambda e, s=s, v=v: e.wait_ge(s, v))

    def run(self, eng, fn, inc=None, inc_amt=1, direct=None):
        if inc is None:
            self.st[eng].append(fn)
            return
        s = self.sem(inc)
        self.cnt[inc] += inc_amt
        if direct is None:
            direct = eng in ("sync", "tensor") or inc_amt >= 16 or inc == "cc"
        if direct:
            self.st[eng].append(lambda e, fn=fn, s=s, a=inc_amt: fn(e).then_inc(s, a))
        else:
            def _go(e, fn=fn, s=s, a=inc_amt):
                fn(e)
                e.drain().then_inc(s, a)
            self.st[eng].append(_go)

    def build(self):
        nc = self.nc

        class Shim:
            PASS = ("wait_ge", "engine_nop", "dma_start", "dma_start_transpose",
                    "collective_compute", "sem_inc", "drain", "nop", "wait_op")

            def __init__(self, eng, sem, state):
                self._eng = eng
                self._sem = sem
                self._state = state

            def __getattr__(self, nm):
                f = getattr(self._eng, nm)
                if nm in Shim.PASS or not callable(f):
                    return f

                def wrapped(*a, **k):
                    if self._state[0] > 0:
                        self._eng.wait_ge(self._sem, self._state[0])
                    inst = f(*a, **k)
                    inst.then_inc(self._sem, 1)
                    self._state[0] += 1
                    return inst
                return wrapped

        selfsems = {e: (self.sem("self_" + e), [0]) for e in ("vector", "scalar", "gpsimd")}
        with nc.Block() as block:
            @block.sync
            def _(e):
                for fn in self.st["sync"]:
                    fn(e)

            @block.gpsimd
            def _(e):
                sh = Shim(e, *selfsems["gpsimd"])
                for fn in self.st["gpsimd"]:
                    fn(sh)

            @block.tensor
            def _(e):
                for fn in self.st["tensor"]:
                    fn(e)

            @block.vector
            def _(e):
                sh = Shim(e, *selfsems["vector"])
                for fn in self.st["vector"]:
                    fn(sh)

            @block.scalar
            def _(e):
                sh = Shim(e, *selfsems["scalar"])
                for fn in self.st["scalar"]:
                    fn(sh)


def build_nc(dbg=(), stop_after=None):
    tab = _tables()
    dbg = set(dbg)
    nc = bass.Bass("TRN2", target_bir_lowering=False, num_devices=NC)

    def din(name, shape, dt=F32):
        return nc.dram_tensor(name, list(shape), dt, kind="ExternalInput").ap()

    h_xs = din("xs", (TT, D))
    h_W_in = din("W_in", (D, WIDTH))
    h_b_in = din("b_in", (128, 24))
    h_short_w = din("short_w", (128, 24, 3))
    h_short_b = din("short_b", (128, 24))
    h_W1 = din("W1", (3, 64))
    h_b1f1 = din("b1f1", (64, 2))
    h_W2s = din("W2s", (64, 8))
    h_b2f2 = din("b2f2", (8, 2))
    h_W3p = din("W3p", (80, 128))
    h_fb = din("fb", (1, D))
    h_fbt = din("fbt", (128, 8))
    h_W_out = din("W_out", (D, D))
    h_b_out = din("b_out", (1, D))
    h_rhs = din("rhs", (D, D))
    h_cvec = din("cvec", (128, 8))
    h_wms = din("wms", (6, 8, 128, 128))
    h_hmask = din("halo_mask", (128, 1))
    h_toff = din("toff", (128, 1))
    h_peT = din("peT", (3, L))
    h_deltas = din("deltas", (128, 8))
    h_Ws = din("Ws", (2, 128, 128), BF16)
    h_F1f = din("F1f", (2, 128, 128), BF16)
    h_Bv = din("Bv", (130, 128, 3, 128), BF16)
    h_Htw = din("Htw", (2, 128, 128), BF16)
    h_H2st = din("H2st", (3, 3, 128, 128), BF16)
    h_iWn = din("iWn", (3, 128, 128), BF16)
    h_iWr = din("iWr", (3, 128, 128), BF16)
    h_Fo = din("Fo", (32, 128, 4, 3, 64), BF16)
    h_Atabs = din("Atabs", (128, 16, 128), BF16)
    h_idenF = din("idenF", (128, 128))
    h_idenB = din("idenB", (128, 128), BF16)

    out = nc.dram_tensor("out", [TPC, D], F32, kind="ExternalOutput").ap()

    vb_in = nc.dram_tensor("vb_in", [D, TPC], BF16).ap()
    vb_out = nc.dram_tensor("vb_out", [D, TPC], BF16).ap()
    gb_in = nc.dram_tensor("gb_in", [120, 4 * 11 * 128], BF16).ap()
    gb_out = nc.dram_tensor("gb_out", [960, 4 * 11 * 128], BF16).ap()
    rb_in = nc.dram_tensor("rb_in", [D, TPC], BF16).ap()
    rb_out = nc.dram_tensor("rb_out", [D, TPC], BF16).ap()
    zscr = nc.dram_tensor("zscr", [128, 4096], BF16).ap()
    x0sp = nc.dram_tensor("x0sp", [8, 128, TPC], BF16).ap()
    mspill = nc.dram_tensor("mspill", [128, 48], F32).ap()
    arsp = nc.dram_tensor("arsp", [128, 16], F32).ap()
    msin = nc.dram_tensor("msin", [128, 6], F32).ap()
    msout = nc.dram_tensor("msout", [1024, 6], F32).ap()
    h_oz = din("ozrow", (2, 4096), BF16)

    dbg_t = {}

    def dout(name, shape, dt=F32):
        if name in dbg:
            dbg_t[name] = nc.dram_tensor("dbg_" + name, list(shape), dt,
                                         kind="ExternalOutput").ap()
        return dbg_t.get(name)

    P = Prog(nc)
    VL = _tables()["vlist"]

    def _th(x):
        return x.tensor if hasattr(x, "tensor") else x
    from contextlib import ExitStack
    es = ExitStack()

    def sb(name, shape, dt):
        return es.enter_context(nc.sbuf_tensor(name, list(shape), dt))

    def ps(name, cols=512):
        return es.enter_context(nc.psum_tensor(name, [128, cols], F32))

    with es:
        small = sb("small", (128, 256), F32)
        modv = sb("modv", (128, 48), F32)
        silb = sb("silb", (128, 8), F32)
        wmb = sb("wmb", (128, 2, 128), F32)
        bsw = sb("bsw", (128, 24, 3), F32)
        bsb = sb("bsb", (128, 24), F32)
        binb = sb("binb", (128, 24), F32)
        idenF = sb("idenFs", (128, 128), F32)
        hmsk = sb("hmsk", (128, 1), F32)
        toffs = sb("toffs", (128, 1), F32)
        dts = sb("dts", (128, 8), F32)
        x0bb = sb("x0bb", (128, 2, TPC), BF16)
        w1b = sb("w1b", (128, 64), F32)
        b1f1b = sb("b1f1b", (128, 2), F32)
        w2b = sb("w2b", (128, 8), F32)
        b2f2b = sb("b2f2b", (128, 2), F32)
        w3b = sb("w3b", (128, 128), BF16)
        Wsb = sb("Wsb", (128, 2, 128), BF16)
        Htwb = sb("Htwb", (128, 2, 128), BF16)
        H2stb = sb("H2stb", (128, 3, 3, 128), BF16)
        iWnb = sb("iWnb", (128, 3, 128), BF16)
        iWrb = sb("iWrb", (128, 3, 128), BF16)
        Atabsb = sb("Atabsb", (128, 16, 128), BF16)
        idenB = sb("idenBs", (128, 128), BF16)
        Bvb = sb("Bvb", (128, 2, 3, 128), BF16)
        Fob2 = sb("Fob2", (128, 2, 4, 3, 64), BF16)
        Gstb = sb("Gstb", (128, 2, 4, 128), BF16)
        zqb = sb("zqb", (128, 2, 4, 128), BF16)
        rwb = sb("rwb", (128, 2, D), F32)
        fbtb = sb("fbtb", (128, 8), F32)
        rowb = sb("rowb", (1, D), F32)
        gborow = sb("gborow", (1, D), BF16)
        onesr = sb("onesr", (1, 128), BF16)
        onesF = sb("onesF", (1, 128), F32)
        modsh = sb("modsh", (128, 6), F32)

        SB_TOP = 229344
        AR_SZ = 137728
        AR0 = SB_TOP - AR_SZ
        assert nc.sbuf_bytes_remaining >= AR_SZ + 2048, nc.sbuf_bytes_remaining

        def at(name, shape, dt, off):
            nbytes = int(np.prod(shape[1:])) * (2 if dt == BF16 else 4)
            assert off + nbytes <= AR_SZ, (name, off, nbytes)
            flat = int(np.prod(shape[1:]))
            ap = nc.alloc_sbuf_tensor_at(name, [shape[0], flat], dt, offset=AR0 + off).ap()
            if len(shape) == 2:
                return ap
            dims = " ".join(f"d{i}" for i in range(1, len(shape)))
            kw = {f"d{i}": shape[i] for i in range(1, len(shape))}
            return ap.rearrange(f"p ({dims}) -> p {dims}", **kw)

        winb = at("winb", (128, 8, WIDTH), BF16, 0)
        hFb = at("hF", (128, 8, TT), BF16, 49152)
        ub = at("ub", (128, 2, TT), BF16, 83968)
        ucb = at("ucb", (128, 3, TPC), BF16, 92672)
        wvb = at("wvb", (128, 2, TPC), BF16, 104960)
        ramp = at("ramp", (128, TPC), F32, 113152)
        Etile = at("Etile", (128, TPC), BF16, 121344)
        xt2 = at("xt2", (128, 2, D), F32, 125440)
        gsc = at("gsc", (128, TPC), BF16, 125440)
        xhp = at("xhp", (128, D), F32, 133632)
        Abuf = at("Abuf", (128, 2, 128, 128), BF16, 0)
        XQ = at("XQ", (128, 2, 132, 128), BF16, 67584)
        x2a = at("x2a", (128, 3, D), F32, 0)
        da = at("da", (128, 2, D), F32, 12288)
        qFa = at("qFa", (128, 2, 8, 128), BF16, 20480)
        rhq = at("rhq", (128, 8, D), BF16, 24576)
        h2a = at("h2a", (128, 2, D), F32, 40960)
        qlb = at("qlb", (128, 2, 1024), BF16, 49152)
        xt3 = at("xt3", (128, 2, D), F32, 53248)
        cvF = at("cvF", (128, 8, TPC), BF16, 61440)
        cbT = at("cbT", (128, D), F32, 94208)
        sbT = at("sbT", (128, D), F32, 98304)
        x0t = at("x0t", (128, 2, 8, 128), BF16, 102400)
        ygb = at("ygb", (128, 2, 8, 128), BF16, 106496)
        woutb = at("woutb", (128, 8, D), BF16, 110592)
        arinvT = at("arinvT", (128, D), F32, 126976)
        gbarT = at("gbarT", (128, D), F32, 131072)
        ervb = at("ervb", (128, 8, 128), BF16, 135168)
        itile = at("itile", (128, 128), F32, 137216)
        HB = 83968
        peb = at("peb", (128, 2, 512), F32, HB)
        argb = at("argb", (128, 512), F32, HB + 4096)
        rrf = at("rrf", (128, 512), F32, HB + 6144)
        h1cb = at("h1cb", (128, 512), F32, HB + 8192)
        H2buf = at("H2buf", (128, 4096), BF16, HB + 10240)
        zHb = at("zHb", (128, 10, 128), BF16, HB + 18432)
        AHb = at("AHb", (128, 2, 10, 128), BF16, HB + 20992)
        AHtwb = at("AHtwb", (128, 2, 10, 128), BF16, HB + 26112)
        ZHb = at("ZHb", (128, 2, 10, 130), BF16, HB + 31232)
        Gsh = at("Gsh", (128, 4, 128, 10), BF16, HB + 36448)
        Gtsh = at("Gtsh", (128, 4, 11, 128), BF16, HB)
        csc = at("csc", (128, 2, 64, 10), F32, HB + 20992)
        cscg = at("cscg", (128, 2, 64, 10), F32, HB + 26112)

        pm = [ps(f"pm{i}") for i in range(7)]
        pmB = es.enter_context(nc.psum_tensor("pmB", [128, 1024], BF16))
        pmB = pmB.ap() if hasattr(pmB, "ap") else pmB

        SEM_LD = "ld"
        SEM_GD = "gd"
        SEM_T = "t"
        SEM_EV = "ev"
        SEM_VV = "vv"
        SEM_CC = "cc"

        def ld(dst, src, inc=SEM_LD, eng="sync"):
            P.run(eng, lambda e, d=dst, s=src: e.dma_start(out=d, in_=s), inc=inc, inc_amt=16)

        ld(idenF[:, :], h_idenF[:, :])
        ld(hmsk[:, :], h_hmask[:, :])
        ld(toffs[:, :], h_toff[:, :])
        ld(dts[:, :], h_deltas[:, :])
        ld(bsw[:, :, :], h_short_w[:, :, :])
        ld(bsb[:, :], h_short_b[:, :])
        ld(binb[:, :], h_b_in[:, :])
        ld(silb[:, :], h_cvec[:, :])
        P.run("gpsimd", lambda e: e.dma_start(out=winb[:, :, :],
                                              in_=h_W_in.rearrange("(g p) w -> p g w", p=128)),
              inc="winb", inc_amt=16)
        n_ld0 = P.val(SEM_LD)

        P.wait("scalar", SEM_LD, n_ld0)
        P.run("scalar", lambda e: e.activation(out=silb[:, :], in_=silb[:, :], func=ACT.Silu),
              inc=SEM_EV)
        ev_sil = P.val(SEM_EV)
        t_mod_base = P.val(SEM_T)
        k = 0
        for oc in range(6):
            for ci in range(8):
                k += 1
                if k > 2:
                    P.wait("gpsimd", SEM_T, t_mod_base + k - 2)
                P.run("gpsimd",
                      lambda e, b=(k - 1) % 2, o=oc, c=ci: e.dma_start(
                          out=wmb[:, b, :], in_=h_wms[o, c, :, :]),
                      inc=f"wm{(k - 1) % 2}", inc_amt=16)
        P.wait("tensor", SEM_EV, ev_sil)
        k = 0
        for oc in range(6):
            for ci in range(8):
                k += 1
                P.wait("tensor", f"wm{(k - 1) % 2}", 16 * ((k - 1) // 2 + 1))
                P.run("tensor",
                      lambda e, b=(k - 1) % 2, o=oc, c=ci: e.matmul(
                          pm[0][:, o:o + 1], wmb[:, b, :], silb[:, c:c + 1],
                          start=(c == 0), stop=(c == 7)),
                      inc=SEM_T)
        t_mod = P.val(SEM_T)
        P.wait("vector", SEM_T, t_mod)
        P.run("vector", lambda e: e.tensor_copy(out=modsh[:, :], in_=pm[0][:, 0:6]),
              inc=SEM_VV)
        P.wait("gpsimd", SEM_VV, P.val(SEM_VV))
        P.run("gpsimd", lambda e: e.dma_start(out=msin[:, :], in_=modsh[:, :]),
              inc="msi", inc_amt=16)
        P.wait("gpsimd", "msi", 16)
        def cc_mod_fn(e):
            with allow_ici_all_to_all():
                return e.collective_compute(
                    "AllGather", ALU.bypass, replica_groups=[list(range(NC))],
                    ins=[msin.opt()], outs=[msout.opt()])
        P.run("gpsimd", cc_mod_fn, inc=SEM_CC)
        cc_mod = P.val(SEM_CC)
        P.wait("gpsimd", SEM_CC, cc_mod)
        def mgd(e):
            d = bt.AP(tensor=_th(modv), offset=0, ap=[[48, 128], [6, 8], [1, 6]])
            s = bt.AP(tensor=msout.tensor, offset=0, ap=[[6, 128], [768, 8], [1, 6]])
            return e.dma_start(out=d, in_=s, allow_slow_non_contiguous=True)
        P.run("gpsimd", mgd, inc="mgd", inc_amt=16)
        P.wait("vector", "mgd", 16)
        P.wait("scalar", "mgd", 16)
        P.wait("scalar", SEM_VV, P.val(SEM_VV))
        def modsgn(e):
            e.sign(small[:, 48:56], modv[:, 16:24])
            e.sign(small[:, 64:72], modv[:, 40:48])
            e.activation(out=small[:, 56:64], in_=modv[:, 16:24], func=ACT.Abs)
            return e.activation(out=small[:, 72:80], in_=modv[:, 40:48], func=ACT.Abs)
        P.run("scalar", modsgn, inc=SEM_EV)
        P.wait("vector", SEM_EV, P.val(SEM_EV))
        def modpost(e):
            e.tensor_copy(out=small[:, 0:8], in_=modv[:, 0:8])
            e.tensor_scalar_add(small[:, 8:16], modv[:, 8:16], 1.0)
            e.tensor_scalar_add(small[:, 32:40], modv[:, 32:40], 1.0)
            e.tensor_copy(out=small[:, 24:32], in_=modv[:, 24:32])
            for sg, ab, dst in ((48, 56, 16), (64, 72, 40)):
                e.tensor_mul(small[:, 80:88], small[:, sg:sg + 8], small[:, sg:sg + 8])
                e.tensor_sub(small[:, 88:96], small[:, sg:sg + 8], small[:, 80:88])
                e.tensor_scalar_add(small[:, 88:96], small[:, 88:96], 1.0)
                e.tensor_scalar_max(small[:, 80:88], small[:, ab:ab + 8], 0.25)
                e.tensor_mul(small[:, dst:dst + 8], small[:, 88:96], small[:, 80:88])
            return e.engine_nop()
        P.run("vector", modpost, inc=SEM_VV)
        vv_mod = P.val(SEM_VV)
        SA, CA1, GAC, SB_, CB1, GBC = 0, 8, 16, 24, 32, 40

        if stop_after == "mod":
            P.wait("gpsimd", SEM_VV, vv_mod)
            for i in range(TPC // 128):
                P.run("gpsimd", lambda e, i=i: e.dma_start(
                    out=out[i * 128:(i + 1) * 128, 0:64], in_=small[:, 0:64]),
                      inc=SEM_GD, inc_amt=16)
            P.wait("gpsimd", SEM_GD, P.val(SEM_GD))
            P.wait("sync", SEM_GD, P.val(SEM_GD))
            P.build()
            return nc, dbg_t

        ld(w1b[0:3, :], h_W1[:, :], inc="hld")
        ld(b1f1b[0:64, :], h_b1f1[:, :], inc="hld")
        ld(w2b[0:64, 0:8], h_W2s[:, :], inc="hld")
        ld(b2f2b[0:8, :], h_b2f2[:, :], inc="hld")
        for i in range(2):
            ld(Wsb[:, i, :], h_Ws[i, :, :], inc="hld")
            ld(Htwb[:, i, :], h_Htw[i, :, :], inc="hld")
        for c in range(3):
            for i in range(3):
                ld(H2stb[:, c, i, :], h_H2st[c, i, :, :], inc="hld")
        for i in range(3):
            ld(iWnb[:, i, :], h_iWn[i, :, :], inc="hld")
            ld(iWrb[:, i, :], h_iWr[i, :, :], inc="hld")
        ld(Atabsb[:, :, :], h_Atabs[:, :, :], inc="hld")
        ld(idenB[:, :], h_idenB[:, :], inc="hld")
        P.run("gpsimd", lambda e: e.dma_start(out=w3b[0:80, :], in_=h_W3p[:, :]),
              inc="w3l", inc_amt=16)
        n_hld = P.val("hld")
        CW1, CW2, CW3 = tab["cw"]

        P.wait("vector", "hld", n_hld)
        def hpre(e):
            e.tensor_mul(small[0:64, 96:97], b1f1b[0:64, 0:1], b1f1b[0:64, 1:2])
            return e.tensor_mul(small[0:8, 97:98], b2f2b[0:8, 0:1], b2f2b[0:8, 1:2])
        P.run("vector", hpre, inc=SEM_VV)
        vv_hpre = P.val(SEM_VV)
        P.wait("scalar", SEM_VV, vv_hpre)
        P.wait("tensor", "hld", n_hld)

        def rr_sin(rows, dst_ap, ev_arg):
            P.wait("vector", SEM_EV, ev_arg)
            MAGIC = 12582912.0
            def rrv(e):
                e.tensor_scalar_mul(rrf[0:rows, :], argb[0:rows, :], 1.0 / (2 * math.pi))
                e.tensor_scalar(rrf[0:rows, :], rrf[0:rows, :], MAGIC, MAGIC, ALU.add, ALU.subtract)
                e.scalar_tensor_tensor(argb[0:rows, :], rrf[0:rows, :], -CW1,
                                       argb[0:rows, :], ALU.mult, ALU.add)
                e.scalar_tensor_tensor(argb[0:rows, :], rrf[0:rows, :], -CW2,
                                       argb[0:rows, :], ALU.mult, ALU.add)
                e.scalar_tensor_tensor(argb[0:rows, :], rrf[0:rows, :], -CW3,
                                       argb[0:rows, :], ALU.mult, ALU.add)
                return e.tensor_scalar(argb[0:rows, :], argb[0:rows, :],
                                       -math.pi, math.pi, ALU.max, ALU.min)
            P.run("vector", rrv, inc=SEM_VV)
            vvc = P.val(SEM_VV)
            P.wait("scalar", SEM_VV, vvc)
            P.run("scalar", lambda e: e.activation(out=dst_ap, in_=argb[0:rows, :], func=ACT.Sin),
                  inc=SEM_EV)
            return P.val(SEM_EV), vvc

        ev_h1psum = {}
        vv_cody_last = 0
        t_h2mm_last = 0
        for fc in range(32):
            b = fc % 2
            band = fc % 4
            cg = fc // 4
            if fc >= 2:
                P.wait("sync", SEM_T, ev_h1psum[fc - 2])
            ld(peb[0:3, b, :], h_peT[:, fc * 512:(fc + 1) * 512], inc=f"pe{b}")
            P.wait("tensor", f"pe{b}", 16 * (fc // 2 + 1))
            if fc >= 1:
                P.wait("tensor", SEM_EV, ev_arg1[fc - 1])
            P.run("tensor", lambda e, b=b: e.matmul(pm[6][0:64, 0:512], w1b[0:3, 0:64],
                                                    peb[0:3, b, :], start=True, stop=True),
                  inc=SEM_T)
            ev_h1psum[fc] = P.val(SEM_T)
            P.wait("scalar", SEM_T, P.val(SEM_T))
            P.wait("scalar", SEM_VV, vv_cody_last)
            P.run("scalar", lambda e: e.activation(out=argb[0:64, :], in_=pm[6][0:64, 0:512],
                                                   func=ACT.Identity, scale=b1f1b[0:64, 1:2],
                                                   bias=small[0:64, 96:97]), inc=SEM_EV)
            if fc == 0:
                ev_arg1 = {}
            ev_arg1[fc] = P.val(SEM_EV)
            P.wait("scalar", SEM_T, t_h2mm_last)
            ev, vv_cody_last = rr_sin(64, h1cb[0:64, :], ev_arg1[fc])
            P.wait("tensor", SEM_EV, ev)
            P.run("tensor", lambda e: e.matmul(pm[1][0:8, 0:512], w2b[0:64, 0:8],
                                               h1cb[0:64, 0:512], start=True, stop=True),
                  inc=SEM_T)
            t_h2mm_last = P.val(SEM_T)
            P.wait("scalar", SEM_T, P.val(SEM_T))
            P.run("scalar", lambda e: e.activation(out=argb[0:8, :], in_=pm[1][0:8, 0:512],
                                                   func=ACT.Identity, scale=b2f2b[0:8, 1:2],
                                                   bias=small[0:8, 97:98]), inc=SEM_EV)
            h2dst = bt.AP(tensor=_th(H2buf), offset=(band * 32) * 4096 + cg * 512,
                          ap=[[4096, 8], [1, 256], [256, 2]])
            ev, vv_cody_last = rr_sin(8, h2dst, P.val(SEM_EV))
        if stop_after == "hmlp":
            P.wait("gpsimd", SEM_EV, P.val(SEM_EV))
            P.wait("gpsimd", SEM_VV, P.val(SEM_VV))
            for i in range(TPC // 128):
                P.run("gpsimd", lambda e, i=i: e.dma_start(
                    out=out[i * 128:(i + 1) * 128, 0:64], in_=small[:, 0:64]),
                      inc=SEM_GD, inc_amt=16)
            P.wait("gpsimd", SEM_GD, P.val(SEM_GD))
            P.wait("sync", SEM_GD, P.val(SEM_GD))
            P.build()
            return nc, dbg_t

        P.wait("sync", SEM_EV, P.val(SEM_EV))
        P.wait("sync", SEM_VV, P.val(SEM_VV))
        for w_ in range(4):
            P.run("sync", lambda e, w_=w_: e.dma_start(out=zscr[32 * w_:32 * w_ + 8, :],
                                                       in_=H2buf[32 * w_:32 * w_ + 8, :]),
                  inc="zhs", inc_amt=16)
        for w_ in range(4):
            P.run("sync", lambda e, w_=w_: e.dma_start(out=zscr[32 * w_ + 8:32 * w_ + 10, :],
                                                       in_=h_oz[:, :]), inc="zhs", inc_amt=16)
        P.wait("sync", "zhs", P.val("zhs"))
        for p_ in range(2):
            for w_ in range(4):
                for r_ in range(2):
                    for u_ in range(8):
                        def zhdma(e, p_=p_, w_=w_, r_=r_, u_=u_):
                            row = 10 * 128
                            dst = bt.AP(tensor=zHb.tensor,
                                        offset=(p_ * 64 + 8 * u_ + 2 * w_ + r_) * row,
                                        ap=[[row, 1], [128, 10], [1, 128]])
                            v = bt.AP(tensor=zscr.tensor,
                                      offset=(32 * w_) * 4096 + 512 * u_ + 256 * p_ + 128 * r_,
                                      ap=[[4096, 10], [1, 128]])
                            return e.dma_start(out=dst, in_=v)
                        P.run("sync", zhdma, inc="zh", inc_amt=16)
        if stop_after == "zh":
            P.wait("gpsimd", "zh", P.val("zh"))
            for i in range(TPC // 128):
                P.run("gpsimd", lambda e, i=i: e.dma_start(
                    out=out[i * 128:(i + 1) * 128, 0:64], in_=small[:, 0:64]),
                      inc=SEM_GD, inc_amt=16)
            P.wait("gpsimd", SEM_GD, P.val(SEM_GD))
            P.wait("sync", SEM_GD, P.val(SEM_GD))
            P.build()
            return nc, dbg_t

        P.wait("tensor", "zh", P.val("zh"))
        for j in range(10):
            bank = 6 if (j % 2 == 0) else 1
            if j >= 2:
                P.wait("tensor", SEM_EV, ev_ah[j - 2])
            def s1h(e, j=j, bank=bank):
                e.matmul(pm[bank][:, 0:128], zHb[:, j, :], Wsb[:, 0, :], start=True, stop=True)
                return e.matmul(pm[bank][:, 128:256], zHb[:, j, :], Wsb[:, 1, :], start=True, stop=True)
            P.run("tensor", s1h, inc=SEM_T)
            P.wait("scalar", SEM_T, P.val(SEM_T))
            def ahev(e, j=j, bank=bank):
                e.copy(out=AHb[:, 0, j, :], in_=pm[bank][:, 0:128])
                return e.copy(out=AHb[:, 1, j, :], in_=pm[bank][:, 128:256])
            P.run("scalar", ahev, inc=SEM_EV)
            if j == 0:
                ev_ah = {}
            ev_ah[j] = P.val(SEM_EV)
        ev_ah_all = P.val(SEM_EV)
        P.wait("vector", SEM_EV, ev_ah_all)
        def htw(e):
            twr = Htwb[:, 0, :].rearrange("k (o m) -> k o m", o=1).broadcast_to((128, 10, 128))
            twi = Htwb[:, 1, :].rearrange("k (o m) -> k o m", o=1).broadcast_to((128, 10, 128))
            e.tensor_mul(AHtwb[:, 0, :, :], AHb[:, 0, :, :], twr)
            e.tensor_mul(zHb[:, :, :], AHb[:, 1, :, :], twi)
            e.tensor_sub(AHtwb[:, 0, :, :], AHtwb[:, 0, :, :], zHb[:, :, :])
            e.tensor_mul(AHtwb[:, 1, :, :], AHb[:, 0, :, :], twi)
            e.tensor_mul(zHb[:, :, :], AHb[:, 1, :, :], twr)
            return e.tensor_add(AHtwb[:, 1, :, :], AHtwb[:, 1, :, :], zHb[:, :, :])
        P.run("vector", htw, inc=SEM_VV)
        vv_htw = P.val(SEM_VV)
        P.wait("tensor", SEM_VV, vv_htw)
        CLS = ((0, 0, 65, 0), (1, 64, 128, 65), (2, 0, 1, 129))
        hz_k = 0
        hz_banks = {}
        for cls, k1lo, k1hi, slot0 in CLS:
            nk = k1hi - k1lo
            for c0 in range(0, nk, 51):
                cn = min(51, nk - c0)
                bre, bim = ((2, 3), (4, 5))[hz_k % 2]
                hz_k += 1
                if bre in hz_banks:
                    P.wait("tensor", SEM_EV, hz_banks[bre])
                def s2h(e, cls=cls, k1lo=k1lo, c0=c0, cn=cn, bre=bre, bim=bim):
                    mvr = AHtwb[:, 0, :, k1lo + c0:k1lo + c0 + cn]
                    mvi = AHtwb[:, 1, :, k1lo + c0:k1lo + c0 + cn]
                    F = 10 * cn
                    pr = pm[bre][:, 0:F].rearrange("k (j c) -> k j c", j=10)
                    pi = pm[bim][:, 0:F].rearrange("k (j c) -> k j c", j=10)
                    e.matmul(pr, H2stb[:, cls, 0, :], mvr, start=True, stop=False)
                    e.matmul(pr, H2stb[:, cls, 2, :], mvi, start=False, stop=True)
                    e.matmul(pi, H2stb[:, cls, 1, :], mvr, start=True, stop=False)
                    return e.matmul(pi, H2stb[:, cls, 0, :], mvi, start=False, stop=True)
                P.run("tensor", s2h, inc=SEM_T)
                P.wait("scalar", SEM_T, P.val(SEM_T))
                def hzev(e, c0=c0, cn=cn, slot0=slot0, bre=bre, bim=bim):
                    F = 10 * cn
                    e.copy(out=ZHb[:, 0, :, slot0 + c0:slot0 + c0 + cn],
                           in_=pm[bre][:, 0:F].rearrange("k (j c) -> k j c", j=10))
                    return e.copy(out=ZHb[:, 1, :, slot0 + c0:slot0 + c0 + cn],
                                  in_=pm[bim][:, 0:F].rearrange("k (j c) -> k j c", j=10))
                P.run("scalar", hzev, inc=SEM_EV)
                hz_banks[bre] = P.val(SEM_EV)
                ev_hz = P.val(SEM_EV)
        if "zhb" in dbg:
            t = dout("zhb", (128, 2 * 10 * 130), BF16)
            P.wait("gpsimd", SEM_EV, ev_hz)
            P.run("gpsimd", lambda e, t=t: e.dma_start(out=t[:, :], in_=ZHb[:, :, :, :].rearrange("k a b c -> k (a b c)")),
                  inc="dbg1", inc_amt=16)

        if stop_after == "s2h":
            P.wait("gpsimd", SEM_EV, ev_hz)
            for i in range(TPC // 128):
                P.run("gpsimd", lambda e, i=i: e.dma_start(
                    out=out[i * 128:(i + 1) * 128, 0:64], in_=small[:, 0:64]),
                      inc=SEM_GD, inc_amt=16)
            P.wait("gpsimd", SEM_GD, P.val(SEM_GD))
            P.wait("sync", SEM_GD, P.val(SEM_GD))
            P.build()
            return nc, dbg_t

        P.wait("vector", SEM_EV, ev_hz)
        P.wait("gpsimd", SEM_EV, ev_hz)

        def zh_ap(pl, sl0, sl1, step=1):
            base = ZHb[:, pl, :, :]
            if step == 1:
                v = base[:, :, sl0:sl1]
            else:
                v = base[:, :, sl1:sl0:-1] if False else base[:, :, sl0:sl1:step]
            return v.rearrange("k j c -> k c j")

        def tab_ap(t, c0, c1):
            n = c1 - c0
            return Atabsb[:, t, c0:c1].rearrange("k (c o) -> k c o", o=1).broadcast_to((128, n, 10))

        def emit_combine(eng, gplane, outsl, n1ap, pxap, tre, tim, pre, pim, csc0, csc1):
            g, part = gplane
            out = Gsh[:, g, outsl[0]:outsl[1], :]
            Hnr, Hni = n1ap
            Hpr, Hpi = pxap
            if part == "re":
                def f(e):
                    e.tensor_mul(csc0, tre, Hnr)
                    e.tensor_mul(csc1, tim, Hni)
                    e.tensor_sub(csc0, csc0, csc1)
                    e.tensor_mul(csc1, pre, Hpr)
                    e.tensor_add(csc0, csc0, csc1)
                    e.tensor_mul(csc1, pim, Hpi)
                    return e.tensor_add(out, csc0, csc1)
            else:
                def f(e):
                    e.tensor_mul(csc0, tre, Hni)
                    e.tensor_mul(csc1, tim, Hnr)
                    e.tensor_add(csc0, csc0, csc1)
                    e.tensor_mul(csc1, pre, Hpi)
                    e.tensor_sub(csc0, csc0, csc1)
                    e.tensor_mul(csc1, pim, Hpr)
                    return e.tensor_add(out, csc0, csc1)
            P.run(eng, f, inc=(SEM_VV if eng == "vector" else "gg"))

        csc = at("csc", (128, 2, 64, 10), BF16, HB + 11264)
        cscg = at("cscg", (128, 2, 64, 10), BF16, HB + 13824)
        for grp, (c0, c1, nsl, psl, tset) in enumerate((
                (1, 65, (1, 65), (128, 64, -1), 0),
                (65, 128, (66, 129), (63, 0, -1), 8),
                (0, 1, (0, 1), (129, 130), 0))):
            n = c1 - c0
            nr = ZHb[:, 0, :, nsl[0]:nsl[1]].rearrange("k j c -> k c j")
            ni = ZHb[:, 1, :, nsl[0]:nsl[1]].rearrange("k j c -> k c j")
            if len(psl) == 3:
                pr = ZHb[:, 0, :, psl[0]:psl[1]:psl[2]].rearrange("k j c -> k c j")
                pi = ZHb[:, 1, :, psl[0]:psl[1]:psl[2]].rearrange("k j c -> k c j")
            else:
                pr = ZHb[:, 0, :, psl[0]:psl[1]].rearrange("k j c -> k c j")
                pi = ZHb[:, 1, :, psl[0]:psl[1]].rearrange("k j c -> k c j")
            cs0 = csc[:, 0, 0:n, :]
            cs1 = csc[:, 1, 0:n, :]
            for gi, (t1, t2) in enumerate(((0, 2), (4, 6))):
                a1r = tab_ap(tset + t1, c0, c1); a1i = tab_ap(tset + t1 + 1, c0, c1)
                a2r = tab_ap(tset + t2, c0, c1); a2i = tab_ap(tset + t2 + 1, c0, c1)
                emit_combine("vector", (2 * gi, "re"), (c0, c1), (nr, ni), (pr, pi),
                             a1r, a1i, a2r, a2i, cs0, cs1)
                emit_combine("gpsimd", (2 * gi + 1, "im"), (c0, c1), (nr, ni), (pr, pi),
                             a1r, a1i, a2r, a2i, cscg[:, 0, 0:n, :], cscg[:, 1, 0:n, :])
        vv_comb = P.val(SEM_VV)
        gg_comb = P.val("gg")

        if stop_after == "gcomb":
            P.wait("gpsimd", SEM_VV, vv_comb)
            P.wait("gpsimd", "gg", gg_comb)
            for i in range(TPC // 128):
                P.run("gpsimd", lambda e, i=i: e.dma_start(
                    out=out[i * 128:(i + 1) * 128, 0:64], in_=small[:, 0:64]),
                      inc=SEM_GD, inc_amt=16)
            P.wait("gpsimd", SEM_GD, P.val(SEM_GD))
            P.wait("sync", SEM_GD, P.val(SEM_GD))
            P.build()
            return nc, dbg_t

        P.wait("tensor", SEM_VV, vv_comb)
        P.wait("tensor", "gg", gg_comb)
        gt_ev = {}
        for g in range(4):
            for ci in range(11):
                col0 = 12 * ci if ci < 10 else 116
                ncol = 12
                key = (g * 11 + ci) % 2
                if key in gt_ev:
                    P.wait("tensor", "gte", gt_ev[key])
                def gtr(e, g=g, col0=col0, ncol=ncol, key=key):
                    return e.matmul(pm[2 + key][0:ncol * 10, 0:128],
                                    Gsh[:, g, col0:col0 + ncol, :], idenB[:, :],
                                    start=True, stop=True)
                P.run("tensor", gtr, inc=SEM_T)
                P.wait("scalar", SEM_T, P.val(SEM_T))
                def gte(e, g=g, ci=ci, ncol=ncol, key=key):
                    return e.copy(out=Gtsh[0:ncol * 10, g, ci, :],
                                  in_=pm[2 + key][0:ncol * 10, 0:128])
                P.run("scalar", gte, inc="gte")
                gt_ev[key] = P.val("gte")
        P.wait("gpsimd", "gte", P.val("gte"))

        if stop_after == "gtrT":
            for i in range(TPC // 128):
                P.run("gpsimd", lambda e, i=i: e.dma_start(
                    out=out[i * 128:(i + 1) * 128, 0:64], in_=small[:, 0:64]),
                      inc=SEM_GD, inc_amt=16)
            P.wait("gpsimd", SEM_GD, P.val(SEM_GD))
            P.wait("sync", SEM_GD, P.val(SEM_GD))
            P.build()
            return nc, dbg_t

        P.run("gpsimd", lambda e: e.dma_start(out=gb_in[:, :],
                                              in_=Gtsh[0:120, :, :, :].rearrange("p a b c -> p (a b c)")),
              inc="gbi", inc_amt=16)
        P.wait("gpsimd", "gbi", 16)

        if stop_after == "gtr":
            for i in range(TPC // 128):
                P.run("gpsimd", lambda e, i=i: e.dma_start(
                    out=out[i * 128:(i + 1) * 128, 0:64], in_=small[:, 0:64]),
                      inc=SEM_GD, inc_amt=16)
            P.wait("gpsimd", SEM_GD, P.val(SEM_GD))
            P.wait("sync", SEM_GD, P.val(SEM_GD))
            P.build()
            return nc, dbg_t

        def cc_g_fn(e):
            with allow_ici_all_to_all():
                return e.collective_compute(
                    "AllGather", ALU.bypass, replica_groups=[list(range(NC))],
                    ins=[gb_in.opt()], outs=[gb_out.opt()])
        P.run("gpsimd", cc_g_fn, inc=SEM_CC)
        cc_g = P.val(SEM_CC)
        if "zhb" in dbg:
            P.wait("gpsimd", "dbg1", 16)
        P.run("gpsimd", lambda e: e.engine_nop(), inc="hph")
        for engx in ("sync", "scalar", "vector", "tensor"):
            P.wait(engx, "hph", 1)

        if stop_after == "hpipe":
            P.wait("gpsimd", SEM_CC, cc_g)
            for i in range(TPC // 128):
                P.run("gpsimd", lambda e, i=i: e.dma_start(
                    out=out[i * 128:(i + 1) * 128, 0:64], in_=small[:, 0:64]),
                      inc=SEM_GD, inc_amt=16)
            P.wait("gpsimd", SEM_GD, P.val(SEM_GD))
            P.wait("sync", SEM_GD, P.val(SEM_GD))
            P.build()
            return nc, dbg_t

        ev_need_tp = []
        for i in range(TT // 128):
            b = i % 2
            if i >= 2:
                P.wait("sync", SEM_EV, ev_tile[i - 2])
            ld(xt2[:, b, :], h_xs[i * 128:(i + 1) * 128, :], inc=f"xa{b}")
            P.wait("vector", f"xa{b}", 16 * (i // 2 + 1))
            def stats(e, b=b):
                e.reduce_sum(small[:, 64:65], xt2[:, b, :], axis=mybir.AxisListType.X)
                e.tensor_scalar_mul(small[:, 64:65], small[:, 64:65], 1.0 / D)
                return e.engine_nop()
            P.run("vector", stats, inc=SEM_VV)
            P.wait("scalar", SEM_VV, P.val(SEM_VV))
            P.run("scalar", lambda e, b=b: e.square(out=xhp[:, :], in_=xt2[:, b, :]), inc=SEM_EV)
            P.wait("vector", SEM_EV, P.val(SEM_EV))
            def stats2(e, b=b):
                e.reduce_sum(small[:, 65:66], xhp[:, :], axis=mybir.AxisListType.X)
                e.tensor_scalar_mul(small[:, 65:66], small[:, 65:66], 1.0 / D)
                e.tensor_mul(small[:, 66:67], small[:, 64:65], small[:, 64:65])
                e.tensor_sub(small[:, 65:66], small[:, 65:66], small[:, 66:67])
                e.tensor_scalar_add(small[:, 65:66], small[:, 65:66], 1e-5)
                return e.engine_nop()
            P.run("vector", stats2, inc=SEM_VV)
            P.wait("scalar", SEM_VV, P.val(SEM_VV))
            P.run("scalar", lambda e: e.activation(out=small[:, 65:66], in_=small[:, 65:66],
                                                   func=ACT.Sqrt), inc=SEM_EV)
            P.wait("vector", SEM_EV, P.val(SEM_EV))
            def nmr(e):
                e.reciprocal(small[:, 65:66], small[:, 65:66])
                e.tensor_mul(small[:, 66:67], small[:, 64:65], small[:, 65:66])
                e.tensor_scalar_mul(small[:, 66:67], small[:, 66:67], -1.0)
                return e.engine_nop()
            P.run("vector", nmr, inc=SEM_VV)
            P.wait("scalar", SEM_VV, P.val(SEM_VV))
            P.run("scalar", lambda e, b=b: e.activation(out=xhp[:, :], in_=xt2[:, b, :],
                                                        func=ACT.Identity, scale=small[:, 65:66],
                                                        bias=small[:, 66:67]),
                  inc=SEM_EV)
            ev_norm = P.val(SEM_EV)
            if i == 0:
                ev_tile = {}
            ev_tile[i] = ev_norm
            P.wait("tensor", SEM_EV, ev_norm)
            P.wait("tensor", SEM_VV, vv_mod)
            for dc in range(8):
                bank = 2 + (dc % 2)
                if i > 0 or dc >= 2:
                    P.wait("tensor", SEM_EV, ev_tp[(i, dc - 2) if dc >= 2 else (i - 1, 6 + dc)])
                P.run("tensor", lambda e, bank=bank, dc=dc: e.transpose(
                    pm[bank][:, 0:128], xhp[:, dc * 128:(dc + 1) * 128], idenF[:, :]),
                      inc=SEM_T)
                P.wait("scalar", SEM_T, P.val(SEM_T))
                P.run("scalar", lambda e, bank=bank, dc=dc, i=i: e.activation(
                    out=hFb[:, dc, i * 128:(i + 1) * 128], in_=pm[bank][:, 0:128],
                    func=ACT.Identity, scale=small[:, CA1 + dc:CA1 + dc + 1],
                    bias=small[:, SA + dc:SA + dc + 1]), inc=SEM_EV)
                if i == 0 and dc == 0:
                    ev_tp = {}
                ev_tp[(i, dc)] = P.val(SEM_EV)
        ev_hF_done = P.val(SEM_EV)

        FCH = [(0, 512), (512, 1024), (1024, 1536), (1536, 2048), (2048, 2176)]
        P.wait("tensor", "winb", 16)
        P.wait("tensor", SEM_EV, ev_hF_done)

        bank_ev = {}
        ub_last_vv = [0, 0]

        def do_win_chunk(oc, ubuf):
            for fi, (f0, f1) in enumerate(FCH):
                bank = 4 + (fi % 2)
                if bank in bank_ev:
                    P.wait("tensor", SEM_EV, bank_ev[bank])
                for ci in range(8):
                    P.run("tensor", lambda e, bank=bank, ci=ci, oc=oc, f0=f0, f1=f1: e.matmul(
                        pm[bank][:, 0:f1 - f0], winb[:, ci, oc * 128:(oc + 1) * 128],
                        hFb[:, ci, f0:f1], start=(ci == 0), stop=(ci == 7)),
                          inc=(SEM_T if ci == 7 else None))
                P.wait("scalar", SEM_T, P.val(SEM_T))
                if fi == 0:
                    P.wait("scalar", SEM_VV, ub_last_vv[ubuf])
                P.run("scalar", lambda e, bank=bank, oc=oc, f0=f0, f1=f1, ubuf=ubuf: e.activation(
                    out=ub[:, ubuf, f0:f1], in_=pm[bank][:, 0:f1 - f0],
                    func=ACT.Identity, bias=binb[:, oc:oc + 1]), inc=SEM_EV)
                bank_ev[bank] = P.val(SEM_EV)
            return P.val(SEM_EV)

        def do_shortconv(oc, ubuf, dst_ap):
            def sc(e, oc=oc, ubuf=ubuf, dst=dst_ap):
                e.tensor_scalar_mul(ub[:, ubuf, 126:128], ub[:, ubuf, 126:128], hmsk[:, 0:1])
                e.tensor_scalar(ucb[:, 2, :], ub[:, ubuf, 126:126 + TPC],
                                bsw[:, oc, 0:1], bsb[:, oc:oc + 1], ALU.mult, ALU.add)
                e.scalar_tensor_tensor(ucb[:, 2, :], ub[:, ubuf, 127:127 + TPC],
                                       bsw[:, oc, 1:2], ucb[:, 2, :], ALU.mult, ALU.add)
                return e.scalar_tensor_tensor(dst, ub[:, ubuf, 128:128 + TPC],
                                              bsw[:, oc, 2:3], ucb[:, 2, :], ALU.mult, ALU.add)
            P.run("vector", sc, inc=SEM_VV)
            ub_last_vv[ubuf] = P.val(SEM_VV)
            return P.val(SEM_VV)

        P.wait("gpsimd", "hph", 1)
        P.run("gpsimd", lambda e: e.iota(ramp[:, :], [[1, TPC]], channel_multiplier=0, allow_small_or_imprecise_dtypes=True),
              inc="rp", inc_amt=1)


        for g in range(8):
            evs = []
            for j, oc in enumerate((8 + g, 16 + g)):
                ubuf = j
                ev_u = do_win_chunk(oc, ubuf)
                P.wait("vector", SEM_EV, ev_u)
                vv_uc = do_shortconv(oc, ubuf, ucb[:, j, :])
                evs.append(vv_uc)
            if g > 0:
                P.wait("vector", SEM_EV, ev_E)
                P.wait("scalar", SEM_VV, vv_gate)
            def envve(e, g=g):
                e.tensor_scalar_mul(small[:, 70:71], dts[:, g:g + 1], -1.0 / (L - 1))
                return e.tensor_mul(small[:, 71:72], small[:, 70:71], toffs[:, 0:1])
            P.run("vector", envve, inc=SEM_VV)
            P.wait("scalar", SEM_VV, P.val(SEM_VV))
            P.wait("scalar", "rp", 1)
            P.run("scalar", lambda e: e.activation(out=Etile[:, :], in_=ramp[:, :], func=ACT.Exp,
                                                   scale=small[:, 70:71], bias=small[:, 71:72]),
                  inc=SEM_EV)
            ev_E = P.val(SEM_EV)
            wbuf = g % 2
            if g >= 2:
                P.wait("vector", f"wv{wbuf}", 16 * (g // 2))
            P.wait("vector", SEM_EV, ev_E)
            def gate(e, g=g, wbuf=wbuf):
                e.tensor_mul(gsc[:, :], ucb[:, 0, :], ucb[:, 1, :])
                dst = bt.AP(tensor=_th(wvb), offset=wbuf * TPC,
                            ap=[[2 * TPC, 128], [1, TPC // 2], [TPC // 2, 2]])
                return e.tensor_mul(dst, gsc[:, :], Etile[:, :])
            P.run("vector", gate, inc=SEM_VV)
            vv_gate = P.val(SEM_VV)
            P.wait("gpsimd", SEM_VV, P.val(SEM_VV))
            P.run("gpsimd", lambda e, g=g, wbuf=wbuf: e.dma_start(
                out=vb_in[g * 128:(g + 1) * 128, :], in_=wvb[:, wbuf, :]),
                  inc=f"wv{wbuf}", inc_amt=16)
        P.wait("gpsimd", "wv0", P.val("wv0"))
        P.wait("gpsimd", "wv1", P.val("wv1"))
        def cc_v_fn(e):
            with allow_ici_all_to_all():
                return e.collective_compute(
                    "AllToAll", ALU.bypass, replica_groups=[list(range(NC))],
                    ins=[vb_in.opt()], outs=[vb_out.opt()])
        P.run("gpsimd", cc_v_fn, inc=SEM_CC)
        cc_v = P.val(SEM_CC)

        for g in range(8):
            sl = g % 2
            if g >= 2:
                P.wait("vector", f"x0s{sl}", 16 * (g // 2))
            ev_u = do_win_chunk(g, 0)
            P.wait("vector", SEM_EV, ev_u)
            x0dst = bt.AP(tensor=_th(x0bb), offset=sl * TPC,
                          ap=[[2 * TPC, 128], [1, TPC // 2], [TPC // 2, 2]])
            vv_sc = do_shortconv(g, 0, x0dst)
            P.wait("gpsimd", SEM_VV, vv_sc)
            P.run("gpsimd", lambda e, g=g, sl=sl: e.dma_start(
                out=x0sp[g, :, :], in_=x0bb[:, sl, :]), inc=f"x0s{sl}", inc_amt=16)
        vv_x0 = P.val(SEM_VV)
        P.wait("gpsimd", "x0s0", 64)
        P.wait("gpsimd", "x0s1", 64)

        if stop_after == "phaseA":
            P.wait("gpsimd", SEM_CC, cc_v)
            for i in range(TPC // 128):
                P.run("gpsimd", lambda e, i=i: e.dma_start(
                    out=out[i * 128:(i + 1) * 128, :], in_=xt2[:, 0, :]),
                      inc=SEM_GD, inc_amt=16)
            P.wait("gpsimd", SEM_GD, P.val(SEM_GD))
            P.build()
            return nc, dbg_t
        QB = at("QB", (128, 2, 132, 128), BF16, 0)
        Qt = at("Qt", (128, 2, 132, 128), BF16, 67584)
        Tb = at("Tb", (128, 2, 132, 128), BF16, 0)
        Ttb = at("Ttb", (128, 2, 132, 128), BF16, 67584)
        zsend = at("zsend", (128, 16384), BF16, 0)
        cbuf2 = sb("cbuf2", (128, 4, 16, 128), BF16)

        class Banks:

            def __init__(self, ids):
                self.ids = list(ids)
                self.k = 0
                self.ev = {}

            def get(self, n=1):
                out = []
                for _ in range(n):
                    b = self.ids[self.k % len(self.ids)]
                    self.k += 1
                    if b in self.ev:
                        P.wait("tensor", SEM_EV, self.ev[b])
                    out.append(b)
                return out if n > 1 else out[0]

            def done(self, bs):
                if isinstance(bs, int):
                    bs = [bs]
                for b in bs:
                    self.ev[b] = P.val(SEM_EV)

        P.wait("sync", SEM_CC, cc_v)
        P.wait("tensor", "hld", n_hld)

        ev_s1q = {}
        t_s1q = {}
        for q in range(32):
            b = q % 2
            c0 = 4 * q
            if q >= 2:
                P.wait("sync", SEM_T, t_s1q[q - 2])
            for j in range(4):
                for p_ in range(2):
                    def zld(e, c=c0 + j, j=j, p_=p_, b=b):
                        dst = bt.AP(tensor=_th(zqb),
                                    offset=(p_ * 64) * 1024 + b * 512 + j * 128,
                                    ap=[[1024, 64], [1, 128]])
                        v = bt.AP(tensor=vb_out.tensor,
                                  offset=c * TPC + p_ * (TPC // 2),
                                  ap=[[128 * TPC, 8], [128, 8], [1, 128]])
                        return e.dma_start(out=dst, in_=v)
                    P.run("sync", zld, inc=f"zq{b}", inc_amt=16)
            P.wait("tensor", f"zq{b}", 128 * (q // 2 + 1))
            if q >= 2:
                P.wait("tensor", SEM_VV, ev_s1q[q - 2][0])
                P.wait("tensor", SEM_EV, ev_s1q[q - 2][1])
            bk = (0, 1) if b == 0 else (2, 3)
            def s1(e, b=b, bk=bk):
                for j in range(4):
                    e.matmul(pm[bk[0]][:, j * 128:(j + 1) * 128], zqb[:, b, j, :],
                             Wsb[:, 0, :], start=True, stop=True)
                    r = e.matmul(pm[bk[1]][:, j * 128:(j + 1) * 128], zqb[:, b, j, :],
                                 Wsb[:, 1, :], start=True, stop=True)
                return r
            P.run("tensor", s1, inc=SEM_T)
            t_s1q[q] = P.val(SEM_T)
            P.wait("vector", SEM_T, t_s1q[q])
            P.wait("scalar", SEM_T, t_s1q[q])
            P.run("vector", lambda e, c0=c0, bk=bk: e.tensor_copy(
                out=Abuf[:, 0, :, c0:c0 + 4],
                in_=pm[bk[0]][:, 0:512].rearrange("p (j k) -> p k j", j=4)), inc=SEM_VV)
            P.run("scalar", lambda e, c0=c0, bk=bk: e.copy(
                out=Abuf[:, 1, :, c0:c0 + 4],
                in_=pm[bk[1]][:, 0:512].rearrange("p (j k) -> p k j", j=4)), inc=SEM_EV)
            ev_s1q[q] = (P.val(SEM_VV), P.val(SEM_EV))
        P.wait("tensor", SEM_VV, P.val(SEM_VV))
        P.wait("tensor", SEM_EV, P.val(SEM_EV))

        if stop_after == "s1":
            P.wait("gpsimd", SEM_VV, P.val(SEM_VV))
            P.wait("gpsimd", SEM_EV, P.val(SEM_EV))
            for i in range(TPC // 128):
                P.run("gpsimd", lambda e, i=i: e.dma_start(
                    out=out[i * 128:(i + 1) * 128, :], in_=xt2[:, 0, :]),
                      inc=SEM_GD, inc_amt=16)
            P.wait("gpsimd", SEM_GD, P.val(SEM_GD))
            P.build()
            return nc, dbg_t
        sbk = Banks([0, 1, 2, 3])
        P.wait("tensor", SEM_VV, P.val(SEM_VV))
        P.wait("tensor", SEM_EV, P.val(SEM_EV))
        for vid in range(130):
            b = vid % 2
            if vid >= 2:
                P.wait("sync", SEM_T, t_s2[vid - 2])
            ld(Bvb[:, b, :, :], h_Bv[vid], inc=f"bv{b}")
            P.wait("tensor", f"bv{b}", 16 * (vid // 2 + 1))
            bk = sbk.get(2)
            def s2f(e, vid=vid, b=b, bk=bk):
                Xr = pm[bk[0]][:, 0:128]
                Xi = pm[bk[1]][:, 0:128]
                e.matmul(Xr, Abuf[:, 0, vid if False else 0, :], Bvb[:, b, 0, :], start=True, stop=False) if False else None
                k1 = VL[vid][0]
                e.matmul(Xr, Abuf[:, 0, k1, :], Bvb[:, b, 0, :], start=True, stop=False)
                e.matmul(Xr, Abuf[:, 1, k1, :], Bvb[:, b, 2, :], start=False, stop=True)
                e.matmul(Xi, Abuf[:, 0, k1, :], Bvb[:, b, 1, :], start=True, stop=False)
                return e.matmul(Xi, Abuf[:, 1, k1, :], Bvb[:, b, 0, :], start=False, stop=True)
            P.run("tensor", s2f, inc=SEM_T)
            if vid == 0:
                t_s2 = {}
            t_s2[vid] = P.val(SEM_T)
            P.wait("vector", SEM_T, P.val(SEM_T))
            P.wait("scalar", SEM_T, P.val(SEM_T))
            P.run("vector", lambda e, vid=vid, bk=bk: e.tensor_copy(out=XQ[:, 0, vid, :],
                                                                    in_=pm[bk[0]][:, 0:128]), inc=SEM_VV)
            P.run("scalar", lambda e, vid=vid, bk=bk: e.copy(out=XQ[:, 1, vid, :],
                                                             in_=pm[bk[1]][:, 0:128]), inc=SEM_EV)
            P.wait("tensor", SEM_VV, P.val(SEM_VV))
            sbk.done(bk)

        P.wait("tensor", SEM_CC, cc_g)
        P.wait("tensor", "w3l", 16)
        P.wait("tensor", SEM_VV, P.val(SEM_VV))
        P.wait("tensor", SEM_EV, P.val(SEM_EV))
        cgk = Banks([0, 1, 2, 3, 4, 5])

        def col_ci(col):
            ci = col // 12 if col < 120 else 10
            return ci, col - (12 * ci if col < 120 else 116)

        def emit_cgen(cols, cslots):
            for col, sl in zip(cols, cslots):
                gb = cg_n[0] % 2
                cg_n[0] += 1
                ci, colsub = col_ci(col)
                if gb in cg_t:
                    P.wait("sync", SEM_T, cg_t[gb])
                for g in range(4):
                    def gld(e, col=col, ci=ci, colsub=colsub, g=g, gb=gb):
                        dst = bt.AP(tensor=_th(Gstb), offset=gb * (4 * 128) + g * 128,
                                    ap=[[1024, 80], [1, 128]])
                        v = bt.AP(tensor=gb_out.tensor,
                                  offset=(colsub * 10) * 5632 + g * 11 * 128 + ci * 128,
                                  ap=[[120 * 5632, 8], [5632, 10], [1, 128]])
                        return e.dma_start(out=dst, in_=v)
                    P.run("sync", gld, inc=f"gs{gb}", inc_amt=16)
                gst_ev[gb] = gst_ev.get(gb, 0) + 64
                P.wait("tensor", f"gs{gb}", gst_ev[gb])
                bk = cgk.get()
                def cg(e, gb=gb, bk=bk):
                    ps_ = pm[bk]
                    e.matmul(ps_[:, 0:128], w3b[0:80, :], Gstb[0:80, gb, 0, :], start=True, stop=True)
                    e.matmul(ps_[:, 128:256], w3b[0:80, :], Gstb[0:80, gb, 1, :], start=True, stop=True)
                    e.matmul(ps_[:, 256:384], w3b[0:80, :], Gstb[0:80, gb, 2, :], start=True, stop=True)
                    return e.matmul(ps_[:, 384:512], w3b[0:80, :], Gstb[0:80, gb, 3, :], start=True, stop=True)
                P.run("tensor", cg, inc=SEM_T)
                cg_t[gb] = P.val(SEM_T)
                P.wait("scalar", SEM_T, P.val(SEM_T))
                def cgev(e, sl=sl, bk=bk):
                    return e.copy(out=cbuf2[:, :, sl, :],
                                  in_=pm[bk][:, 0:512].rearrange("c (a k) -> c a k", a=4))
                P.run("scalar", cgev, inc=SEM_EV)
                cgk.done(bk)
            return P.val(SEM_EV)

        gst_ev = {}
        cg_t = {}
        cg_n = [0]

        def spectral_batch(a0, n, vq_last):
            XNr = XQ[:, 0, a0:a0 + n, :]
            XNi = XQ[:, 1, a0:a0 + n, :]
            XPr = XQ[:, 0, 128 - a0:128 - a0 - n:-1, :]
            XPi = XQ[:, 1, 128 - a0:128 - a0 - n:-1, :]
            c1r = cbuf2[:, 0, 0:n, :]; c1i = cbuf2[:, 1, 0:n, :]
            c2r = cbuf2[:, 2, 0:n, :]; c2i = cbuf2[:, 3, 0:n, :]
            d1r = cbuf2[:, 0, 8:8 + n, :]; d1i = cbuf2[:, 1, 8:8 + n, :]
            d2r = cbuf2[:, 2, 8:8 + n, :]; d2i = cbuf2[:, 3, 8:8 + n, :]
            s0 = scr2[:, 0, 0:n, :]; s1_ = scr2[:, 1, 0:n, :]
            g0 = scr2g[:, 0, 0:n, :]; g1_ = scr2g[:, 1, 0:n, :]
            QNr = QB[:, 0, a0:a0 + n, :]
            QNi = QB[:, 1, a0:a0 + n, :]
            QPr = QB[:, 0, 128 - a0:128 - a0 - n:-1, :]
            QPi = QB[:, 1, 128 - a0:128 - a0 - n:-1, :]
            def vre(e):
                e.tensor_mul(s0, d1r, XPr)
                e.tensor_mul(s1_, d1i, XPi)
                e.tensor_add(s0, s0, s1_)
                e.tensor_mul(s1_, d2r, XNr)
                e.tensor_add(s0, s0, s1_)
                e.tensor_mul(s1_, d2i, XNi)
                e.tensor_sub(QPr, s0, s1_)
                e.tensor_mul(s0, c1r, XNr)
                e.tensor_mul(s1_, c1i, XNi)
                e.tensor_add(s0, s0, s1_)
                e.tensor_mul(s1_, c2r, XPr)
                e.tensor_add(s0, s0, s1_)
                e.tensor_mul(s1_, c2i, XPi)
                return e.tensor_sub(QNr, s0, s1_)
            def vim(e):
                e.tensor_mul(g0, d1i, XPr)
                e.tensor_mul(g1_, d1r, XPi)
                e.tensor_sub(g0, g0, g1_)
                e.tensor_mul(g1_, d2r, XNi)
                e.tensor_add(g0, g0, g1_)
                e.tensor_mul(g1_, d2i, XNr)
                e.tensor_add(QPi, g0, g1_)
                e.tensor_mul(g0, c1i, XNr)
                e.tensor_mul(g1_, c1r, XNi)
                e.tensor_sub(g0, g0, g1_)
                e.tensor_mul(g1_, c2r, XPi)
                e.tensor_add(g0, g0, g1_)
                e.tensor_mul(g1_, c2i, XPr)
                return e.tensor_add(QNi, g0, g1_)
            P.wait("vector", SEM_EV, vq_last)
            P.wait("gpsimd", SEM_EV, vq_last)
            P.run("vector", vre, inc=SEM_VV)
            P.run("gpsimd", vim, inc="gg")
            return P.val(SEM_VV), P.val("gg")

        scr2 = sb("scr2", (128, 2, 8, 128), BF16)
        scr2g = sb("scr2g", (128, 2, 8, 128), BF16)
        sp_done = []
        for bi in range(8):
            a0 = 1 + 8 * bi
            cols = list(range(a0, a0 + 8)) + [128 - a for a in range(a0, a0 + 8)]
            cslots = list(range(8)) + list(range(8, 16))
            if bi > 0:
                P.wait("scalar", SEM_VV, sp_done[-1][0])
                P.wait("scalar", "gg", sp_done[-1][1])
            ev = emit_cgen(cols, cslots)
            sp_done.append(spectral_batch(a0, 8, ev))
        P.wait("scalar", SEM_VV, sp_done[-1][0])
        P.wait("scalar", "gg", sp_done[-1][1])
        ev = emit_cgen([0], [0])
        def sp0(e):
            XNr = XQ[:, 0, 0, :]; XNi = XQ[:, 1, 0, :]
            XPr = XQ[:, 0, 129, :]; XPi = XQ[:, 1, 129, :]
            c1r = cbuf2[:, 0, 0, :]; c1i = cbuf2[:, 1, 0, :]
            c2r = cbuf2[:, 2, 0, :]; c2i = cbuf2[:, 3, 0, :]
            s0 = scr2[:, 0, 0, :]; s1_ = scr2[:, 1, 0, :]
            e.tensor_mul(s0, c1r, XNr)
            e.tensor_mul(s1_, c1i, XNi)
            e.tensor_add(s0, s0, s1_)
            e.tensor_mul(s1_, c2r, XPr)
            e.tensor_add(s0, s0, s1_)
            e.tensor_mul(s1_, c2i, XPi)
            e.tensor_sub(QB[:, 0, 0, :], s0, s1_)
            e.tensor_mul(s0, c1i, XNr)
            e.tensor_mul(s1_, c1r, XNi)
            e.tensor_sub(s0, s0, s1_)
            e.tensor_mul(s1_, c2r, XPi)
            e.tensor_add(s0, s0, s1_)
            e.tensor_mul(s1_, c2i, XPr)
            return e.tensor_add(QB[:, 1, 0, :], s0, s1_)
        P.wait("vector", SEM_EV, ev)
        P.run("vector", sp0, inc=SEM_VV)
        vv_spec = P.val(SEM_VV)
        gg_spec = P.val("gg")

        P.wait("tensor", SEM_VV, vv_spec)
        P.wait("tensor", "gg", gg_spec)
        P.wait("tensor", SEM_EV, P.val(SEM_EV))
        qt_ev = {}
        for grp in range(64):
            pl, s0 = divmod(grp, 32)
            s0 *= 4
            key = grp % 2
            if key in qt_ev:
                P.wait("tensor", SEM_EV, qt_ev[key])
            def qtr4(e, pl=pl, s0=s0, key=key):
                for j in range(4):
                    r = e.matmul(pm[4 + key][:, j * 128:(j + 1) * 128],
                                 QB[:, pl, s0 + j, :], idenB[:, :],
                                 start=True, stop=True)
                return r
            P.run("tensor", qtr4, inc=SEM_T)
            P.wait("scalar", SEM_T, P.val(SEM_T))
            P.run("scalar", lambda e, pl=pl, s0=s0, key=key: e.copy(
                out=Qt[:, pl, s0:s0 + 4, :],
                in_=pm[4 + key][:, 0:512].rearrange("p (j k) -> p j k", j=4)), inc=SEM_EV)
            qt_ev[key] = P.val(SEM_EV)
        ev_qt = P.val(SEM_EV)

        P.wait("tensor", SEM_EV, ev_qt)
        ibk = Banks([0, 1, 2, 3])
        t_chunks = [(s0_, min(4, 65 - s0_), iWnb) for s0_ in range(0, 65, 4)] + \
                   [(s0_, min(4, 128 - s0_), iWrb) for s0_ in range(65, 128, 4)]
        for (s0_, ns, iwb) in t_chunks:
            bk = ibk.get(2)
            def s1i(e, s0_=s0_, ns=ns, iwb=iwb, bk=bk):
                F = ns * 128
                Qr = Qt[:, 0, s0_:s0_ + ns, :]
                Qi = Qt[:, 1, s0_:s0_ + ns, :]
                Tr = pm[bk[0]][:, 0:F].rearrange("m (s c) -> m s c", s=ns)
                Ti = pm[bk[1]][:, 0:F].rearrange("m (s c) -> m s c", s=ns)
                e.matmul(Tr, iwb[:, 0, :], Qr, start=True, stop=False)
                e.matmul(Tr, iwb[:, 2, :], Qi, start=False, stop=True)
                e.matmul(Ti, iwb[:, 1, :], Qr, start=True, stop=False)
                return e.matmul(Ti, iwb[:, 0, :], Qi, start=False, stop=True)
            P.run("tensor", s1i, inc=SEM_T)
            P.wait("vector", SEM_T, P.val(SEM_T))
            P.wait("scalar", SEM_T, P.val(SEM_T))
            P.run("vector", lambda e, s0_=s0_, ns=ns, bk=bk: e.tensor_copy(
                out=Tb[:, 0, s0_:s0_ + ns, :],
                in_=pm[bk[0]][:, 0:ns * 128].rearrange("m (s c) -> m s c", s=ns)), inc=SEM_VV)
            P.run("scalar", lambda e, s0_=s0_, ns=ns, bk=bk: e.copy(
                out=Tb[:, 1, s0_:s0_ + ns, :],
                in_=pm[bk[1]][:, 0:ns * 128].rearrange("m (s c) -> m s c", s=ns)), inc=SEM_EV)
            P.wait("tensor", SEM_VV, P.val(SEM_VV))
            ibk.done(bk)

        P.wait("tensor", SEM_VV, P.val(SEM_VV))
        P.wait("tensor", SEM_EV, P.val(SEM_EV))
        tt_ev = {}
        for grp in range(64):
            pl, c0 = divmod(grp, 32)
            c0 *= 4
            key = grp % 2
            if key in tt_ev:
                P.wait("tensor", SEM_EV, tt_ev[key])
            def ttr4(e, pl=pl, c0=c0, key=key):
                for j in range(4):
                    r = e.matmul(pm[4 + key][:, j * 128:(j + 1) * 128],
                                 Tb[:, pl, 0:128, c0 + j], idenB[:, :],
                                 start=True, stop=True)
                return r
            P.run("tensor", ttr4, inc=SEM_T)
            P.wait("scalar", SEM_T, P.val(SEM_T))
            P.run("scalar", lambda e, pl=pl, c0=c0, key=key: e.copy(
                out=Ttb[:, pl, c0:c0 + 4, :],
                in_=pm[4 + key][:, 0:512].rearrange("p (j k) -> p j k", j=4)), inc=SEM_EV)
            tt_ev[key] = P.val(SEM_EV)
        ev_tt = P.val(SEM_EV)

        P.wait("tensor", SEM_EV, ev_tt)
        zbk = Banks([0, 1, 2, 3])
        MB = 4
        zbk = Banks([0, 1, 2, 3])
        for m2g in range(0, 128, MB):
            b = (m2g // MB) % 2
            if m2g >= 2 * MB:
                P.wait("sync", SEM_T, t_s2z[m2g - 2 * MB])
            def fold(e, m2g=m2g, b=b):
                return e.dma_start(out=Fob2[:, b, :, :, :], in_=h_Fo[m2g // MB])
            P.run("sync", fold, inc=f"fo{b}", inc_amt=16)
            P.wait("tensor", f"fo{b}", 16 * (m2g // (2 * MB) + 1))
            bk = zbk.get(2)
            for i in range(MB):
                m2p = m2g + i
                m2 = 127 - m2p
                def s2z(e, m2p=m2p, m2=m2, b=b, i=i, bk=bk):
                    Tr = Ttb[:, 0, 0:128, m2]
                    Ti = Ttb[:, 1, 0:128, m2]
                    zr = pm[bk[0]][:, i * 64:(i + 1) * 64]
                    zi = pm[bk[1]][:, i * 64:(i + 1) * 64]
                    e.matmul(zr, Tr, Fob2[:, b, i, 0, :], start=True, stop=False)
                    e.matmul(zr, Ti, Fob2[:, b, i, 1, :], start=False, stop=True)
                    e.matmul(zi, Tr, Fob2[:, b, i, 1, :], start=True, stop=False)
                    return e.matmul(zi, Ti, Fob2[:, b, i, 2, :], start=False, stop=True)
                P.run("tensor", s2z, inc=(SEM_T if i == MB - 1 else None))
            if m2g == 0:
                t_s2z = {}
            t_s2z[m2g] = P.val(SEM_T)
            P.wait("vector", SEM_T, P.val(SEM_T))
            P.wait("scalar", SEM_T, P.val(SEM_T))
            def zev(e, m2g=m2g, bk=bk, pl=0):
                dst = bt.AP(tensor=zsend.tensor, offset=pl * 8192 + m2g,
                            ap=[[16384, 128], [128, 64], [1, MB]])
                srcp = bt.AP(tensor=nc.lookup_mloc("pm%d" % bk[pl]) if False else None, offset=0, ap=[[1, 1]]) if False else None
                return e.tensor_copy(out=dst, in_=pm[bk[pl]][:, 0:MB * 64].rearrange("c (g m) -> c m g", g=MB))
            def zev_s(e, m2g=m2g, bk=bk, pl=1):
                dst = bt.AP(tensor=zsend.tensor, offset=pl * 8192 + m2g,
                            ap=[[16384, 128], [128, 64], [1, MB]])
                return e.copy(out=dst, in_=pm[bk[pl]][:, 0:MB * 64].rearrange("c (g m) -> c m g", g=MB))
            P.run("vector", zev, inc=SEM_VV)
            P.run("scalar", zev_s, inc=SEM_EV)
            P.wait("tensor", SEM_VV, P.val(SEM_VV))
            zbk.done(bk)
        vv_z = P.val(SEM_VV)
        ev_z = P.val(SEM_EV)
        t_fft_end = P.val(SEM_T)

        if stop_after == "s2z":
            P.wait("gpsimd", SEM_VV, vv_z)
            P.wait("gpsimd", SEM_EV, ev_z)
            for i in range(TPC // 128):
                P.run("gpsimd", lambda e, i=i: e.dma_start(
                    out=out[i * 128:(i + 1) * 128, 0:64], in_=small[:, 0:64]),
                      inc=SEM_GD, inc_amt=16)
            P.wait("gpsimd", SEM_GD, P.val(SEM_GD))
            P.wait("sync", SEM_GD, P.val(SEM_GD))
            P.build()
            return nc, dbg_t

        P.wait("gpsimd", SEM_VV, vv_z)
        P.wait("gpsimd", SEM_EV, ev_z)
        for p_ in range(2):
            for R_ in range(8):
                def rbd(e, p_=p_, R_=R_):
                    srcap = bt.AP(tensor=zsend.tensor,
                                  offset=p_ * 8192 + R_ * 8 * 128,
                                  ap=[[16384, 128], [128, 8], [1, 128]])
                    dstap = bt.AP(tensor=rb_in.tensor,
                                  offset=R_ * 128 * TPC + p_ * 1024,
                                  ap=[[TPC, 128], [128, 8], [1, 128]])
                    return e.dma_start(out=dstap, in_=srcap)
                P.run("gpsimd", rbd, inc="rbi", inc_amt=16)
        P.wait("gpsimd", "rbi", P.val("rbi"))
        def cc_r_fn(e):
            with allow_ici_all_to_all():
                return e.collective_compute(
                    "AllToAll", ALU.bypass, replica_groups=[list(range(NC))],
                    ins=[rb_in.opt()], outs=[rb_out.opt()])
        P.run("gpsimd", cc_r_fn, inc=SEM_CC)
        cc_r = P.val(SEM_CC)

        if "hF" in dbg:
            t = dout("hF", (128, 8, TT), BF16)
            P.wait("gpsimd", SEM_EV, ev_hF_done)
            P.run("gpsimd", lambda e, t=t: e.dma_start(out=t[:, :, :], in_=hFb[:, :, :]),
                  inc=SEM_GD, inc_amt=16)
        if "x0" in dbg:
            t = dout("x0", (8, 128, TPC), BF16)
            P.wait("gpsimd", "x0s0", 64)
            P.wait("gpsimd", "x0s1", 64)
            P.run("gpsimd", lambda e, t=t: e.dma_start(out=t[:, :, :], in_=x0sp[:, :, :]),
                  inc=SEM_GD, inc_amt=16)
        if "vre" in dbg:
            t = dout("vre", (D, TPC), BF16)
            P.wait("gpsimd", SEM_CC, cc_v)
            P.run("gpsimd", lambda e, t=t: e.dma_start(out=t[:, :], in_=vb_out[:, :]),
                  inc=SEM_GD, inc_amt=16)
        if "rbo" in dbg:
            t = dout("rbo", (D, TPC), BF16)
            P.wait("gpsimd", SEM_CC, cc_r)
            P.run("gpsimd", lambda e, t=t: e.dma_start(out=t[:, :], in_=rb_out[:, :]),
                  inc="dbg2", inc_amt=16)
        if "gout" in dbg:
            t = dout("gout", (960, 4 * 11 * 128), BF16)
            P.wait("gpsimd", SEM_CC, P.val(SEM_CC))
            P.run("gpsimd", lambda e, t=t: e.dma_start(out=t[:, :], in_=gb_out[:, :]),
                  inc=SEM_GD, inc_amt=16)
        if "mod" in dbg:
            t = dout("mod", (128, 64))
            P.wait("gpsimd", SEM_VV, vv_mod)
            P.run("gpsimd", lambda e, t=t: e.dma_start(out=t[:, :], in_=small[:, 0:64]),
                  inc=SEM_GD, inc_amt=16)

        rbi_n = P.val("rbi")
        LM1 = float(L - 1)

        P.run("gpsimd", lambda e: e.dma_start(out=mspill[:, :], in_=small[:, 0:48]),
              inc="msp", inc_amt=16)

        P.wait("sync", SEM_T, t_fft_end)
        P.wait("vector", SEM_T, t_fft_end)
        P.wait("vector", "rbi", rbi_n)
        P.wait("scalar", SEM_T, t_fft_end)
        P.run("vector", lambda e: e.memset(onesF[0:1, :], 1.0), inc=SEM_VV)
        vv_ones = P.val(SEM_VV)
        AM0, AMX, ARI, ALR = 160, 224, 232, 240
        rwn = {0: 0, 1: 0}
        t_amax = {}
        last_ev = [0]
        for dc in range(8):
            b = dc % 2
            if dc >= 2:
                P.wait("sync", SEM_T, t_amax[dc - 2])
            ld(rwb[:, b, :], h_rhs[dc * 128:(dc + 1) * 128, :], inc=f"rw{b}")
            rwn[b] += 16
            P.wait("tensor", f"rw{b}", rwn[b])
            for jg in range(2):
                if last_ev[0]:
                    P.wait("tensor", SEM_EV, last_ev[0])
                def trq(e, b=b, jg=jg):
                    for jj in range(4):
                        r = e.matmul(pm[6][:, jj * 128:(jj + 1) * 128],
                                     rwb[:, b, (jg * 4 + jj) * 128:(jg * 4 + jj + 1) * 128],
                                     idenF[:, :], start=True, stop=True)
                    return r
                P.run("tensor", trq, inc=SEM_T)
                P.wait("scalar", SEM_T, P.val(SEM_T))
                P.wait("scalar", SEM_VV, P.val(SEM_VV))
                P.run("scalar", lambda e, jg=jg: e.activation(
                    out=h2a[:, 0, jg * 512:jg * 512 + 512], in_=pm[6][:, 0:512],
                    func=ACT.Abs), inc=SEM_EV)
                last_ev[0] = P.val(SEM_EV)
            t_amax[dc] = P.val(SEM_T)
            P.wait("vector", SEM_EV, P.val(SEM_EV))
            def pmax(e, dc=dc):
                for jc in range(8):
                    r = e.reduce_max(small[:, AM0 + jc * 8 + dc:AM0 + jc * 8 + dc + 1],
                                     h2a[:, 0, jc * 128:(jc + 1) * 128],
                                     axis=mybir.AxisListType.X)
                return r
            P.run("vector", pmax, inc=SEM_VV)
        def amx(e):
            for jc in range(8):
                e.reduce_max(small[:, AMX + jc:AMX + jc + 1],
                             small[:, AM0 + jc * 8:AM0 + jc * 8 + 8],
                             axis=mybir.AxisListType.X)
            e.tensor_scalar_max(small[:, AMX:AMX + 8], small[:, AMX:AMX + 8], 1e-12)
            e.reciprocal(small[:, ARI:ARI + 8], small[:, AMX:AMX + 8])
            e.tensor_scalar_mul(small[:, ARI:ARI + 8], small[:, ARI:ARI + 8], 127.0)
            return e.tensor_scalar_mul(small[:, ALR:ALR + 8], small[:, AMX:AMX + 8],
                                       1.0 / 127.0)
        P.run("vector", amx, inc=SEM_VV)
        vv_amax = P.val(SEM_VV)
        P.wait("gpsimd", SEM_VV, vv_amax)
        P.run("gpsimd", lambda e: e.dma_start(out=arsp[:, :], in_=small[:, ARI:ARI + 16]),
              inc="ars", inc_amt=16)

        def bcast(dstT, ev_extra=None):
            P.wait("tensor", "rwl", P.val("rwl"))
            P.wait("tensor", SEM_VV, vv_ones)
            if last_ev[0]:
                P.wait("tensor", SEM_EV, last_ev[0])
            def bmm(e):
                e.matmul(pm[5][:, 0:512], onesF[0:1, :], rowb[0:1, 0:512],
                         start=True, stop=True)
                return e.matmul(pm[6][:, 0:512], onesF[0:1, :], rowb[0:1, 512:1024],
                                start=True, stop=True)
            P.run("tensor", bmm, inc=SEM_T)
            t_bc = P.val(SEM_T)
            P.wait("scalar", SEM_T, t_bc)
            if ev_extra is not None:
                P.wait("scalar", SEM_VV, ev_extra)
            def bev(e, dstT=dstT):
                e.copy(out=dstT[:, 0:512], in_=pm[5][:, 0:512])
                return e.copy(out=dstT[:, 512:1024], in_=pm[6][:, 0:512])
            P.run("scalar", bev, inc=SEM_EV)
            last_ev[0] = P.val(SEM_EV)
            return t_bc, last_ev[0]

        def row_from_mspill(off, wait_t=None, wait_vv=None):
            if wait_t is not None:
                P.wait("sync", SEM_T, wait_t)
            if wait_vv is not None:
                P.wait("sync", SEM_VV, wait_vv)
            P.run("sync", lambda e, off=off: e.dma_start(
                out=rowb[0:1, :],
                in_=bt.AP(tensor=mspill.tensor, offset=off, ap=[[1, 8], [48, 128]]),
                allow_slow_non_contiguous=True), inc="rwl", inc_amt=16)

        P.wait("sync", "msp", 16)
        row_from_mspill(GAC)
        t_bc_ga, ev_bc_ga = bcast(cbT)
        P.wait("sync", SEM_T, t_bc_ga)
        P.run("sync", lambda e: e.dma_start(out=rowb[0:1, :], in_=h_b_out[0:1, :]),
              inc="rwl", inc_amt=16)
        P.wait("vector", "rwl", P.val("rwl"))
        P.wait("vector", SEM_EV, ev_bc_ga)
        P.run("vector", lambda e: e.tensor_mul(gborow[0:1, :], rowb[0:1, :], cbT[0:1, :]),
              inc=SEM_VV)
        vv_gbo = P.val(SEM_VV)
        for dc in range(8):
            b = dc % 2
            P.wait("sync", SEM_VV, P.val(SEM_VV))
            ld(rwb[:, b, :], h_W_out[dc * 128:(dc + 1) * 128, :], inc=f"rw{b}")
            rwn[b] += 16
            P.wait("vector", f"rw{b}", rwn[b])
            P.run("vector", lambda e, b=b, dc=dc: e.tensor_mul(
                woutb[:, dc, :], rwb[:, b, :], cbT[:, :]), inc=SEM_VV)
        vv_wout = P.val(SEM_VV)

        row_from_mspill(CB1, wait_vv=vv_gbo)
        bcast(cbT, ev_extra=vv_wout)
        row_from_mspill(SB_, wait_t=P.val(SEM_T))
        bcast(sbT)
        P.wait("sync", "ars", 16)
        def arrow(e, off=0):
            return e.dma_start(out=rowb[0:1, :],
                               in_=bt.AP(tensor=arsp.tensor, offset=off,
                                         ap=[[1, 8], [16, 128]]),
                               allow_slow_non_contiguous=True)
        P.wait("sync", SEM_T, P.val(SEM_T))
        P.run("sync", arrow, inc="rwl", inc_amt=16)
        bcast(arinvT)
        P.wait("sync", SEM_T, P.val(SEM_T))
        P.run("sync", lambda e: arrow(e, off=8), inc="rwl", inc_amt=16)
        bcast(gbarT)
        row_from_mspill(GBC, wait_t=P.val(SEM_T))
        bcast(x2a[:, 0, :])
        ev_tabs = last_ev[0]

        MAGIC = 12582912.0
        P.wait("vector", SEM_EV, ev_tabs)
        for dc in range(8):
            b = dc % 2
            P.wait("sync", SEM_VV, P.val(SEM_VV))
            ld(rwb[:, b, :], h_rhs[dc * 128:(dc + 1) * 128, :], inc=f"rw{b}")
            rwn[b] += 16
            P.wait("vector", f"rw{b}", rwn[b])
            def qr1(e, b=b, dc=dc):
                e.tensor_mul(rwb[:, b, :], rwb[:, b, :], arinvT[:, :])
                e.tensor_scalar(rwb[:, b, :], rwb[:, b, :], -127.0, 127.0, ALU.max, ALU.min)
                e.tensor_scalar(rwb[:, b, :], rwb[:, b, :], MAGIC, MAGIC, ALU.add, ALU.subtract)
                return e.tensor_copy(out=rhq[:, dc, :], in_=rwb[:, b, :])
            P.run("vector", qr1, inc=SEM_VV)
        vv_qr = P.val(SEM_VV)
        def gbar(e):
            return e.tensor_mul(gbarT[:, :], gbarT[:, :], x2a[:, 0, :])
        P.run("vector", gbar, inc=SEM_VV)

        ld(onesr[0:1, :], h_oz[0:1, 0:128], inc="rwl")
        ld(fbtb[:, :], h_fbt[:, :], inc="rwl")
        P.run("gpsimd", lambda e: e.iota(itile[:, :], [[1, 128]], channel_multiplier=0,
                                         allow_small_or_imprecise_dtypes=True), inc="pbc")
        DSE0, DTL0, EB0, ST0 = 100, 108, 116, 132
        def dsetl(e):
            e.tensor_scalar(small[:, 98:99], toffs[:, 0:1], -LM1, 1.0 / LM1,
                            ALU.add, ALU.mult)
            for dc in range(8):
                e.tensor_scalar_mul(small[:, DSE0 + dc:DSE0 + dc + 1],
                                    dts[:, dc:dc + 1], 2.0 / LM1)
                e.tensor_mul(small[:, DTL0 + dc:DTL0 + dc + 1],
                             dts[:, dc:dc + 1], small[:, 98:99])
            return e.engine_nop()
        P.run("vector", dsetl, inc=SEM_VV)
        vv_prep = P.val(SEM_VV)

        P.wait("sync", SEM_CC, cc_r)
        for dc in range(8):
            P.run("sync", lambda e, dc=dc: e.dma_start(
                out=cvF[:, dc, :], in_=rb_out[dc * 128:(dc + 1) * 128, :]),
                  inc="cvl", inc_amt=16)
        P.wait("vector", "cvl", 128)
        P.wait("scalar", SEM_VV, vv_prep)
        P.wait("scalar", "pbc", P.val("pbc"))
        P.wait("tensor", SEM_VV, P.val(SEM_VV))
        P.wait("tensor", "rwl", P.val("rwl"))
        P.wait("vector", "rwl", P.val("rwl"))

        if stop_after == "ret":
            P.wait("gpsimd", "cvl", 128)
            P.wait("gpsimd", SEM_VV, P.val(SEM_VV))
            for i in range(TPC // 128):
                P.run("gpsimd", lambda e, i=i: e.dma_start(
                    out=out[i * 128:(i + 1) * 128, 0:64], in_=small[:, 0:64]),
                      inc=SEM_GD, inc_amt=16)
            P.wait("gpsimd", SEM_GD, P.val(SEM_GD))
            P.wait("sync", SEM_GD, P.val(SEM_GD))
            P.build()
            return nc, dbg_t

        NT = TPC // 128
        t_wout = {}
        t_tp = {}
        t_ffn = {}
        vv_x2 = {}
        vv_fin = {}
        ev_qf = {}
        ev_sq = {}
        od_n = {0: 0, 1: 0}
        xl_n = {0: 0, 1: 0}
        x0l_n = {0: 0, 1: 0}

        def tile_loads(i):
            sl = i % 2
            p_, mb = divmod(i, 8)
            m0 = mb * 128
            if i >= 2:
                P.wait("sync", SEM_VV, vv_x2.get(i - 2, 0))
                P.wait("sync", SEM_VV, vv_fin.get(i - 2, 0))
            def xld(e, p_=p_, m0=m0, sl=sl):
                src = bt.AP(tensor=h_xs.tensor, offset=(HALO + 2 * m0 + p_) * D,
                            ap=[[2 * D, 128], [1, D]])
                dst = bt.AP(tensor=_th(xt3), offset=sl * D, ap=[[2 * D, 128], [1, D]])
                return e.dma_start(out=dst, in_=src)
            P.run("sync", xld, inc=f"xl{sl}", inc_amt=16)
            xl_n[sl] += 16
            for g_ in range(8):
                def x0ld(e, p_=p_, m0=m0, sl=sl, g_=g_):
                    src = bt.AP(tensor=x0sp.tensor,
                                offset=g_ * 128 * TPC + p_ * (TPC // 2) + m0,
                                ap=[[TPC, 128], [1, 128]])
                    dst = bt.AP(tensor=_th(x0t), offset=sl * 1024 + g_ * 128,
                                ap=[[2048, 128], [1, 128]])
                    return e.dma_start(out=dst, in_=src)
                P.run("sync", x0ld, inc=f"x0l{sl}", inc_amt=16)
                x0l_n[sl] += 16

        def emit_fin(i):
            j = i - 1
            if j < 0:
                return
            slj = j % 2
            s3j = j % 3
            pj, mbj = divmod(j, 8)
            m0j = mbj * 128
            STBj = ST0 + slj * 6
            P.wait("vector", SEM_T, t_ffn[j])
            P.wait("vector", f"od{slj}", od_n[slj])
            def fin(e, sl=slj, s3=s3j, STB=STBj):
                for half in range(2):
                    c0 = half * 512
                    e.scalar_tensor_tensor(da[:, sl, c0:c0 + 512], pm[4 + half][:, 0:512],
                                           small[:, STB + 5:STB + 6],
                                           gbarT[:, c0:c0 + 512], ALU.mult, ALU.mult)
                    r = e.tensor_add(da[:, sl, c0:c0 + 512], da[:, sl, c0:c0 + 512],
                                     x2a[:, s3, c0:c0 + 512])
                return r
            P.run("vector", fin, inc=SEM_VV)
            vv_fin[j] = P.val(SEM_VV)
            P.wait("gpsimd", SEM_VV, vv_fin[j])
            def odma(e, sl=slj, p_=pj, m0=m0j):
                dst = bt.AP(tensor=out.tensor, offset=(2 * m0 + p_) * D,
                            ap=[[2 * D, 128], [1, D]])
                return e.dma_start(out=dst, in_=da[:, sl, :])
            P.run("gpsimd", odma, inc=f"od{slj}", inc_amt=16)
            od_n[slj] += 16

        tile_loads(0)
        tile_loads(1)
        for i in range(NT):
            sl = i % 2
            s3 = i % 3
            p_, mb = divmod(i, 8)
            m0 = mb * 128
            wbk = (0, 1) if sl == 0 else (2, 3)
            ebcol = EB0 + sl * 8

            P.wait("vector", f"x0l{sl}", 128 * (i // 2 + 1))
            if i >= 2:
                P.wait("vector", SEM_T, t_wout[i - 2])
            def ebias8(e, m0=m0, p_=p_, ebcol=ebcol):
                for dc in range(8):
                    r = e.scalar_tensor_tensor(
                        small[:, ebcol + dc:ebcol + dc + 1], dts[:, dc:dc + 1],
                        (2 * m0 + p_) / LM1, small[:, DTL0 + dc:DTL0 + dc + 1],
                        ALU.mult, ALU.add)
                return r
            P.run("vector", ebias8, inc=SEM_VV)
            P.wait("scalar", SEM_VV, P.val(SEM_VV))
            if i >= 1:
                P.wait("scalar", SEM_VV, vv_erv_used)
            def exp8(e, ebcol=ebcol):
                for dc in range(8):
                    r = e.activation(
                        out=ervb[:, dc, :], in_=itile[:, :], func=ACT.Exp,
                        scale=small[:, DSE0 + dc:DSE0 + dc + 1],
                        bias=small[:, ebcol + dc:ebcol + dc + 1])
                return r
            P.run("scalar", exp8, inc=SEM_EV)
            P.wait("vector", SEM_EV, P.val(SEM_EV))
            if i >= 2:
                P.wait("vector", f"od{sl}", od_n[sl])
            def gate8(e, sl=sl, p_=p_, m0=m0):
                for dc in range(8):
                    e.tensor_mul(da[:, sl, dc * 128:dc * 128 + 128],
                                 cvF[:, dc, p_ * 1024 + m0:p_ * 1024 + m0 + 128],
                                 ervb[:, dc, :])
                    r = e.scalar_tensor_tensor(
                        ygb[:, sl, dc, :], da[:, sl, dc * 128:dc * 128 + 128],
                        fbtb[:, dc:dc + 1], x0t[:, sl, dc, :], ALU.add, ALU.mult)
                return r
            P.run("vector", gate8, inc=SEM_VV)
            vv_erv_used = P.val(SEM_VV)
            vv_gate = P.val(SEM_VV)

            P.wait("tensor", SEM_VV, vv_gate)
            if i >= 2:
                P.wait("tensor", SEM_VV, vv_x2.get(i - 2, 0))
            def wmm(e, sl=sl, wbk=wbk):
                for half in range(2):
                    ps_ = pm[wbk[half]][:, 0:512]
                    e.matmul(ps_, onesr[0:1, :], gborow[0:1, half * 512:half * 512 + 512],
                             start=True, stop=False)
                    for dc in range(8):
                        r = e.matmul(ps_, ygb[:, sl, dc, :],
                                     woutb[:, dc, half * 512:half * 512 + 512],
                                     start=False, stop=(dc == 7))
                return r
            P.run("tensor", wmm, inc=SEM_T)
            t_wout[i] = P.val(SEM_T)
            emit_fin(i)

            P.wait("vector", SEM_T, t_wout[i])
            P.wait("vector", f"xl{sl}", 16 * (i // 2 + 1))
            if i >= 3:
                P.wait("vector", SEM_VV, vv_fin.get(i - 3, 0))
            if i >= 2:
                P.wait("vector", SEM_VV, vv_ql.get(i - 2, 0))
            STB = ST0 + sl * 6
            def x2st(e, sl=sl, s3=s3, wbk=wbk, STB=STB):
                e.tensor_add(x2a[:, s3, 0:512], pm[wbk[0]][:, 0:512], xt3[:, sl, 0:512])
                e.tensor_add(x2a[:, s3, 512:1024], pm[wbk[1]][:, 0:512], xt3[:, sl, 512:1024])
                e.reduce_sum(small[:, STB:STB + 1], x2a[:, s3, :],
                             axis=mybir.AxisListType.X)
                e.tensor_mul(h2a[:, sl, :], x2a[:, s3, :], x2a[:, s3, :])
                e.reduce_sum(small[:, STB + 1:STB + 2], h2a[:, sl, :],
                             axis=mybir.AxisListType.X)
                e.tensor_scalar_mul(small[:, STB:STB + 1], small[:, STB:STB + 1], -1.0 / D)
                e.tensor_scalar_mul(small[:, STB + 1:STB + 2], small[:, STB + 1:STB + 2], 1.0 / D)
                e.tensor_mul(small[:, STB + 2:STB + 3], small[:, STB:STB + 1], small[:, STB:STB + 1])
                e.tensor_sub(small[:, STB + 1:STB + 2], small[:, STB + 1:STB + 2], small[:, STB + 2:STB + 3])
                return e.tensor_scalar_add(small[:, STB + 1:STB + 2], small[:, STB + 1:STB + 2], 1e-5)
            P.run("vector", x2st, inc=SEM_VV)
            vv_x2[i] = P.val(SEM_VV)
            P.wait("scalar", SEM_VV, vv_x2[i])
            P.run("scalar", lambda e, STB=STB: e.activation(
                out=small[:, STB + 2:STB + 3], in_=small[:, STB + 1:STB + 2],
                func=ACT.Sqrt), inc=SEM_EV)
            P.wait("vector", SEM_EV, P.val(SEM_EV))
            def lnq1(e, sl=sl, s3=s3, STB=STB):
                e.reciprocal(small[:, STB + 2:STB + 3], small[:, STB + 2:STB + 3])
                e.tensor_mul(small[:, STB + 3:STB + 4], small[:, STB:STB + 1],
                             small[:, STB + 2:STB + 3])
                return e.tensor_scalar(h2a[:, sl, :], x2a[:, s3, :],
                                       small[:, STB + 2:STB + 3],
                                       small[:, STB + 3:STB + 4], ALU.mult, ALU.add)
            P.run("vector", lnq1, inc=SEM_VV)
            P.wait("gpsimd", SEM_VV, P.val(SEM_VV))
            def lngp(e, sl=sl):
                e.tensor_tensor(out=h2a[:, sl, :], in0=h2a[:, sl, :], in1=cbT[:, :],
                                op=ALU.mult)
                return e.tensor_tensor(out=h2a[:, sl, :], in0=h2a[:, sl, :],
                                       in1=sbT[:, :], op=ALU.add)
            P.run("gpsimd", lngp, inc="gg")
            P.wait("vector", "gg", P.val("gg"))
            def lnq2(e, sl=sl, STB=STB):
                e.reduce_max(small[:, STB + 4:STB + 5], h2a[:, sl, :],
                             axis=mybir.AxisListType.X, apply_absolute_value=True)
                e.tensor_scalar_max(small[:, STB + 4:STB + 5], small[:, STB + 4:STB + 5], 1e-12)
                e.tensor_scalar_mul(small[:, STB + 5:STB + 6], small[:, STB + 4:STB + 5], 1.0 / 127.0)
                e.reciprocal(small[:, STB + 4:STB + 5], small[:, STB + 4:STB + 5])
                e.tensor_scalar_mul(small[:, STB + 4:STB + 5], small[:, STB + 4:STB + 5], 127.0)
                e.tensor_scalar(h2a[:, sl, :], h2a[:, sl, :], small[:, STB + 4:STB + 5],
                                -127.0, ALU.mult, ALU.max)
                e.tensor_scalar(h2a[:, sl, :], h2a[:, sl, :], 127.0, MAGIC, ALU.min, ALU.add)
                return e.tensor_scalar_sub(qlb[:, sl, :], h2a[:, sl, :], MAGIC)
            P.run("vector", lnq2, inc=SEM_VV)
            if i == 0:
                vv_ql = {}
            vv_ql[i] = P.val(SEM_VV)

            P.wait("tensor", SEM_VV, vv_ql[i])
            if i >= 1:
                P.wait("tensor", SEM_EV, ev_qf[i - 1])
            def qtp(e, sl=sl):
                for dc in range(8):
                    r = e.transpose(pmB[:, dc * 128:(dc + 1) * 128],
                                    qlb[:, sl, dc * 128:(dc + 1) * 128], idenB[:, :])
                return r
            P.run("tensor", qtp, inc=SEM_T)
            t_tp[i] = P.val(SEM_T)
            P.wait("scalar", SEM_T, t_tp[i])
            if i >= 2:
                P.wait("scalar", SEM_T, t_ffn.get(i - 2, 0))
            P.run("scalar", lambda e, sl=sl: e.copy(
                out=qFa[:, sl, :, :], in_=pmB[:, :].rearrange("c (g t) -> c g t", g=8)),
                inc=SEM_EV)
            ev_qf[i] = P.val(SEM_EV)

            P.wait("tensor", SEM_EV, ev_qf[i])
            if i >= 1:
                P.wait("tensor", SEM_VV, vv_fin.get(i - 1, 0))
            def fmm(e, sl=sl):
                for half in range(2):
                    ps_ = pm[4 + half][:, 0:512]
                    for dc in range(8):
                        r = e.matmul(ps_, qFa[:, sl, dc, :],
                                     rhq[:, dc, half * 512:half * 512 + 512],
                                     start=(dc == 0), stop=(dc == 7))
                return r
            P.run("tensor", fmm, inc=SEM_T)
            t_ffn[i] = P.val(SEM_T)

            if i + 2 < NT:
                tile_loads(i + 2)
        emit_fin(NT)

        P.wait("gpsimd", "od0", od_n[0])
        P.wait("gpsimd", "od1", od_n[1])
        P.wait("sync", "od0", od_n[0])
        P.wait("sync", "od1", od_n[1])

        P.build()
    return nc, dbg_t


def _core_inputs(inputs):
    x = np.asarray(inputs["L0_attn_x"])[0]
    W3 = np.asarray(inputs["L0_attn_W3"])
    b3 = np.asarray(inputs["L0_attn_b3"])
    per = []
    for i in range(NC):
        d = {}
        lo = i * TPC - HALO
        if lo < 0:
            xs = np.concatenate([np.zeros((-lo, D), np.float32), x[0:(i + 1) * TPC]], 0)
        else:
            xs = np.asarray(x[lo:(i + 1) * TPC])
        d["xs"] = _f32(xs)
        d["halo_mask"] = np.full((128, 1), 0.0 if i == 0 else 1.0, np.float32)
        d["W2s"] = _f32(np.asarray(inputs["L0_attn_W2"])[:, 8 * i:8 * i + 8])
        d["b2f2"] = _f32(np.stack([np.asarray(inputs["L0_attn_b2"])[8 * i:8 * i + 8],
                                   np.asarray(inputs["L0_attn_f2"])[8 * i:8 * i + 8]], 1))
        W3p = np.zeros((80, 128), np.float32)
        csl = slice(128 * i, 128 * (i + 1))
        for s in range(8):
            W3p[10 * s:10 * s + 8] = W3[8 * s:8 * s + 8, csl]
        W3p[8] = b3[csl]
        d["W3p"] = _f32(W3p)
        d["toff"] = np.full((128, 1), float(i * TPC), np.float32)
        wm = np.asarray(inputs["L0_wmod"])[:, i * 768:(i + 1) * 768]
        d["wms"] = _f32(wm.reshape(8, 128, 6, 128).transpose(2, 0, 1, 3))
        per.append(d)
    return per


def _in_map(inputs, core, tab):
    return {
        "xs": core["xs"],
        "W_in": _f32(inputs["L0_attn_W_in"]),
        "b_in": _f32(np.asarray(inputs["L0_attn_b_in"]).reshape(24, 128).T),
        "short_w": _f32(np.asarray(inputs["L0_attn_short_w"]).reshape(24, 128, 3).transpose(1, 0, 2)),
        "short_b": _f32(np.asarray(inputs["L0_attn_short_b"]).reshape(24, 128).T),
        "W1": _f32(inputs["L0_attn_W1"]),
        "b1f1": _f32(np.stack([np.asarray(inputs["L0_attn_b1"]),
                               np.asarray(inputs["L0_attn_f1"])], 1)),
        "W2s": core["W2s"],
        "b2f2": core["b2f2"],
        "W3p": core["W3p"],
        "fb": _f32(inputs["L0_attn_filter_bias"]).reshape(1, D),
        "fbt": _f32(np.asarray(inputs["L0_attn_filter_bias"]).reshape(8, 128).T),
        "W_out": _f32(inputs["L0_attn_W_out"]),
        "b_out": _f32(inputs["L0_attn_b_out"]).reshape(1, D),
        "rhs": _f32(inputs["L0_ffn_rhs"]),
        "cvec": _f32(np.asarray(inputs["L0_c"]).reshape(8, 128).T),
        "wms": core["wms"],
        "halo_mask": core["halo_mask"],
        "toff": core["toff"],
        "peT": _f32(tab["peT"]),
        "deltas": _f32(tab["deltas"].reshape(8, 128).T),
        "Ws": _b16(np.stack([tab["Ws_re"], tab["Ws_im"]], 0)),
        "F1f": _b16(np.stack([tab["F1f_re"], tab["F1f_im"]], 0)),
        "Bv": _b16(np.asarray(tab["Bv"]).transpose(0, 2, 1, 3)),
        "Htw": _b16(tab["Htw"]),
        "H2st": _b16(tab["H2st"]),
        "iWn": _b16(tab["iWn"]),
        "iWr": _b16(tab["iWr"]),
        "Fo": _b16(np.asarray(tab["Fo"]).reshape(32, 4, 3, 128, 64).transpose(0, 3, 1, 2, 4)),
        "Atabs": _b16(np.asarray(tab["Atabs"]).transpose(1, 0, 2)),
        "idenF": _f32(tab["iden"]),
        "ozrow": _b16(np.stack([np.ones(4096, np.float32), np.zeros(4096, np.float32)], 0)),
        "idenB": _b16(tab["iden"]),
    }


def kernel(**inputs):
    tab = _tables()
    per = _core_inputs(inputs)
    nc, _ = build_nc()
    in_maps = [_in_map(inputs, per[i], tab) for i in range(NC)]
    res = run_bass_kernel_spmd(nc, in_maps, core_ids=list(range(NC)))
    return np.concatenate([r["out"] for r in res.results], 0)[None]


# version 09
# speedup vs baseline: 3.9561x; 1.0231x over previous
import math
import os

import numpy as np
import ml_dtypes

os.environ.setdefault("BASS_INTERP_NUM_WORKERS", "8")

import concourse.bass as bass
import concourse.bass_types as bt
import concourse.mybir as mybir
from concourse.bass_utils import run_bass_kernel_spmd
from concourse.aot_env import allow_ici_all_to_all
from concourse.bass_isa import ReduceOp
from concourse.library_config import attn as LIB_ATTN, standard as LIB_STD

F32 = mybir.dt.float32
BF16 = mybir.dt.bfloat16
I32 = mybir.dt.int32
ACT = mybir.ActivationFunctionType
ALU = mybir.AluOpType
_SHIM = True

L = 16384
D = 1024
N = 2 * L
M = N // 2
NC = 8
TPC = L // NC
HALO = 128
TT = TPC + HALO
WIDTH = 3 * D

bf16 = ml_dtypes.bfloat16
KREV = 127 - np.arange(128)
KWRAP = (128 - np.arange(128)) % 128


def _f32(a):
    return np.ascontiguousarray(a, dtype=np.float32)


def _b16(a):
    return np.ascontiguousarray(np.asarray(a, dtype=np.float32).astype(bf16))


def _host_tables():
    t = {}
    wN = np.exp(-2j * np.pi * np.arange(N) / N)
    a_f = (1 - 1j * wN) / 2
    b_f = (1 + 1j * wN) / 2
    p_f = (1 + 1j * np.conj(wN[:M])) / 2
    q_f = (1 - 1j * np.conj(wN[:M])) / 2
    A1 = p_f * np.conj(a_f[:M]) * a_f[:M] + q_f * np.conj(a_f[M:]) * a_f[M:]
    A2 = p_f * np.conj(a_f[:M]) * b_f[:M] + q_f * np.conj(a_f[M:]) * b_f[M:]
    B1 = p_f * np.conj(b_f[:M]) * a_f[:M] + q_f * np.conj(b_f[M:]) * a_f[M:]
    B2 = p_f * np.conj(b_f[:M]) * b_f[:M] + q_f * np.conj(b_f[M:]) * b_f[M:]
    k2g = np.arange(128)[:, None]
    sg = np.arange(128)[None, :]
    f_nat = k2g * 128 + sg
    f_rev = (127 - k2g) * 128 + sg
    for nm, arr in (("A1", A1), ("A2", A2), ("B1", B1), ("B2", B2)):
        t[nm + "n"] = arr[f_nat]
        t[nm + "r"] = arr[f_rev]
    w128 = np.exp(-2j * np.pi * np.outer(np.arange(128), np.arange(128)) / 128)
    F1 = w128[:64]
    t["Ws_re"] = np.concatenate([F1.real, -F1.imag], 0)
    t["Ws_im"] = np.concatenate([F1.imag, F1.real], 0)
    t["F1f_re"] = np.ascontiguousarray(w128.real)
    t["F1f_im"] = np.ascontiguousarray(w128.imag)
    m2 = np.arange(128)
    vlist = [(a, "nat") for a in range(0, 65)] + [(b, "rev") for b in range(65, 128)]
    vlist += [(64, "rev"), (0, "wrap")]
    t["vlist"] = vlist
    Bv = np.empty((len(vlist), 3, 128, 128), dtype=np.float32)
    for vid, (k1, vm) in enumerate(vlist):
        B = np.exp(-2j * np.pi * (m2[:, None] * k1) / M) * w128
        if vm == "rev":
            B = B[:, KREV]
        elif vm == "wrap":
            B = B[:, KWRAP]
        Bv[vid, 0] = B.real
        Bv[vid, 1] = B.imag
        Bv[vid, 2] = -B.imag
    t["Bv"] = Bv
    tw = np.exp(-2j * np.pi * np.outer(m2, np.arange(128)) / M)
    t["Htw"] = np.stack([tw.real, tw.imag], 0)
    t["H2st"] = np.stack([
        np.stack([w128.real, w128.imag, -w128.imag], 0),
        np.stack([w128[:, KREV].real, w128[:, KREV].imag, -w128[:, KREV].imag], 0),
        np.stack([w128[:, KWRAP].real, w128[:, KWRAP].imag, -w128[:, KWRAP].imag], 0)], 0)
    iW = np.exp(+2j * np.pi * np.outer(np.arange(128), np.arange(128)) / 128)
    iWr = np.exp(+2j * np.pi * np.outer(127 - np.arange(128), np.arange(128)) / 128)
    t["iWn"] = np.stack([iW.real, iW.imag, -iW.imag], 0)
    t["iWr"] = np.stack([iWr.real, iWr.imag, -iWr.imag], 0)
    Fo = np.empty((128, 3, 128, 64), dtype=np.float32)
    m1p = np.arange(64)
    for m2p in range(128):
        m2v = 127 - m2p
        Ft = np.exp(+2j * np.pi * (np.outer(np.arange(128), np.arange(64)) / 128
                                   + (np.arange(128) * m2v / M)[:, None])) / M
        Fo[m2p, 0] = Ft.imag[:, 63 - m1p]
        Fo[m2p, 1] = Ft.real[:, 63 - m1p]
        Fo[m2p, 2] = -Ft.imag[:, 63 - m1p]
    t["Fo"] = Fo
    tg = np.arange(L, dtype=np.float64) / (L - 1)
    ang = 2.0 * math.pi * np.arange(L, dtype=np.float64) / L
    ph = 0.0001 * ang
    t["peT"] = np.stack([tg, np.cos(ph), -np.sin(ph)], 0).astype(np.float32)
    dmax, dmin = (math.log(0.01) / 0.3, math.log(0.01) / 1.5)
    t["deltas"] = np.abs(np.linspace(dmax, dmin, D)).astype(np.float32)
    t["iden"] = np.eye(128, dtype=np.float32)
    t["Atabs"] = np.stack(
        [x for nm in ("A1", "A2", "B1", "B2") for x in (t[nm + "n"].real, t[nm + "n"].imag)]
        + [x for nm in ("A1", "A2", "B1", "B2") for x in (t[nm + "r"].real, t[nm + "r"].imag)], 0)
    tp = 2 * math.pi
    c1 = np.float32(tp); c2 = np.float32(tp - float(c1)); c3 = np.float32(tp - float(c1) - float(c2))
    t["cw"] = (float(c1), float(c2), float(c3))
    return t


_TAB = None


def _tables():
    global _TAB
    if _TAB is None:
        _TAB = _host_tables()
    return _TAB


class Prog:
    ENGINES = ("sync", "gpsimd", "tensor", "vector", "scalar")

    def __init__(self, nc):
        self.nc = nc
        self.st = {e: [] for e in self.ENGINES}
        self.sems = {}
        self.cnt = {}

    def sem(self, name):
        if name not in self.sems:
            self.sems[name] = self.nc.alloc_semaphore(name)
            self.cnt[name] = 0
        return self.sems[name]

    def val(self, name):
        self.sem(name)
        return self.cnt[name]

    def wait(self, eng, name, val=None):
        s = self.sem(name)
        v = self.cnt[name] if val is None else val
        self.st[eng].append(lambda e, s=s, v=v: e.wait_ge(s, v))

    def run(self, eng, fn, inc=None, inc_amt=1, direct=None):
        if inc is None:
            self.st[eng].append(fn)
            return
        s = self.sem(inc)
        self.cnt[inc] += inc_amt
        if direct is None:
            direct = eng in ("sync", "tensor") or inc_amt >= 16 or inc == "cc"
        if direct:
            self.st[eng].append(lambda e, fn=fn, s=s, a=inc_amt: fn(e).then_inc(s, a))
        else:
            def _go(e, fn=fn, s=s, a=inc_amt):
                fn(e)
                e.drain().then_inc(s, a)
            self.st[eng].append(_go)

    def build(self):
        nc = self.nc

        class Shim:
            PASS = ("wait_ge", "engine_nop", "dma_start", "dma_start_transpose",
                    "collective_compute", "sem_inc", "drain", "nop", "wait_op")

            def __init__(self, eng, sem, state):
                self._eng = eng
                self._sem = sem
                self._state = state

            def __getattr__(self, nm):
                f = getattr(self._eng, nm)
                if nm in Shim.PASS or not callable(f):
                    return f

                def wrapped(*a, **k):
                    if self._state[0] > 0:
                        self._eng.wait_ge(self._sem, self._state[0])
                    inst = f(*a, **k)
                    inst.then_inc(self._sem, 1)
                    self._state[0] += 1
                    return inst
                return wrapped

        if not _SHIM:
            class Shim:
                def __new__(cls, eng, sem, state):
                    return eng
        selfsems = {e: (self.sem("self_" + e), [0]) for e in ("vector", "scalar", "gpsimd")}
        with nc.Block() as block:
            @block.sync
            def _(e):
                for fn in self.st["sync"]:
                    fn(e)

            @block.gpsimd
            def _(e):
                sh = Shim(e, *selfsems["gpsimd"])
                for fn in self.st["gpsimd"]:
                    fn(sh)

            @block.tensor
            def _(e):
                for fn in self.st["tensor"]:
                    fn(e)

            @block.vector
            def _(e):
                sh = Shim(e, *selfsems["vector"])
                for fn in self.st["vector"]:
                    fn(sh)

            @block.scalar
            def _(e):
                sh = Shim(e, *selfsems["scalar"])
                for fn in self.st["scalar"]:
                    fn(sh)


def build_nc(dbg=(), stop_after=None):
    tab = _tables()
    dbg = set(dbg)
    nc = bass.Bass("TRN2", target_bir_lowering=False, num_devices=NC)

    def din(name, shape, dt=F32):
        return nc.dram_tensor(name, list(shape), dt, kind="ExternalInput").ap()

    h_xs = din("xs", (TT, D))
    h_W_in = din("W_in", (D, WIDTH))
    h_b_in = din("b_in", (128, 24))
    h_short_w = din("short_w", (128, 24, 3))
    h_short_b = din("short_b", (128, 24))
    h_W1 = din("W1", (3, 64))
    h_b1f1 = din("b1f1", (64, 2))
    h_W2s = din("W2s", (64, 8))
    h_b2f2 = din("b2f2", (8, 2))
    h_W3p = din("W3p", (80, 128))
    h_fb = din("fb", (1, D))
    h_fbt = din("fbt", (128, 8))
    h_W_out = din("W_out", (D, D))
    h_b_out = din("b_out", (1, D))
    h_rhs = din("rhs", (D, D))
    h_cvec = din("cvec", (128, 8))
    h_wms = din("wms", (6, 8, 128, 128))
    h_hmask = din("halo_mask", (128, 1))
    h_toff = din("toff", (128, 1))
    h_peT = din("peT", (3, L))
    h_deltas = din("deltas", (128, 8))
    h_Ws = din("Ws", (2, 128, 128), BF16)
    h_F1f = din("F1f", (2, 128, 128), BF16)
    h_Bv = din("Bv", (130, 128, 3, 128), BF16)
    h_Htw = din("Htw", (2, 128, 128), BF16)
    h_H2st = din("H2st", (3, 3, 128, 128), BF16)
    h_iWn = din("iWn", (3, 128, 128), BF16)
    h_iWr = din("iWr", (3, 128, 128), BF16)
    h_Fo = din("Fo", (32, 128, 4, 3, 64), BF16)
    h_Atabs = din("Atabs", (128, 16, 128), BF16)
    h_idenF = din("idenF", (128, 128))
    h_idenB = din("idenB", (128, 128), BF16)

    out = nc.dram_tensor("out", [TPC, D], F32, kind="ExternalOutput").ap()

    vb_in = nc.dram_tensor("vb_in", [D, TPC], BF16).ap()
    vb_out = nc.dram_tensor("vb_out", [D, TPC], BF16).ap()
    gb_in = nc.dram_tensor("gb_in", [120, 4 * 11 * 128], BF16).ap()
    gb_out = nc.dram_tensor("gb_out", [960, 4 * 11 * 128], BF16).ap()
    rb_in = nc.dram_tensor("rb_in", [D, TPC], BF16).ap()
    rb_out = nc.dram_tensor("rb_out", [D, TPC], BF16).ap()
    zscr = nc.dram_tensor("zscr", [128, 4096], BF16).ap()
    x0sp = nc.dram_tensor("x0sp", [8, 128, TPC], BF16).ap()
    mspill = nc.dram_tensor("mspill", [128, 48], F32).ap()
    arsp = nc.dram_tensor("arsp", [128, 16], F32).ap()
    msin = nc.dram_tensor("msin", [128, 6], F32).ap()
    msout = nc.dram_tensor("msout", [1024, 6], F32).ap()
    h_oz = din("ozrow", (2, 4096), BF16)

    dbg_t = {}

    def dout(name, shape, dt=F32):
        if name in dbg:
            dbg_t[name] = nc.dram_tensor("dbg_" + name, list(shape), dt,
                                         kind="ExternalOutput").ap()
        return dbg_t.get(name)

    P = Prog(nc)
    VL = _tables()["vlist"]

    def _th(x):
        return x.tensor if hasattr(x, "tensor") else x
    from contextlib import ExitStack
    es = ExitStack()

    def sb(name, shape, dt):
        return es.enter_context(nc.sbuf_tensor(name, list(shape), dt))

    def ps(name, cols=512):
        return es.enter_context(nc.psum_tensor(name, [128, cols], F32))

    with es:
        small = sb("small", (128, 256), F32)
        modv = sb("modv", (128, 48), F32)
        silb = sb("silb", (128, 8), F32)
        wmb = sb("wmb", (128, 2, 128), F32)
        bsw = sb("bsw", (128, 24, 3), F32)
        bsb = sb("bsb", (128, 24), F32)
        binb = sb("binb", (128, 24), F32)
        idenF = sb("idenFs", (128, 128), F32)
        hmsk = sb("hmsk", (128, 1), F32)
        toffs = sb("toffs", (128, 1), F32)
        dts = sb("dts", (128, 8), F32)
        x0bb = sb("x0bb", (128, 2, TPC), BF16)
        w1b = sb("w1b", (128, 64), F32)
        b1f1b = sb("b1f1b", (128, 2), F32)
        w2b = sb("w2b", (128, 8), F32)
        b2f2b = sb("b2f2b", (128, 2), F32)
        w3b = sb("w3b", (128, 128), BF16)
        Wsb = sb("Wsb", (128, 2, 128), BF16)
        Htwb = sb("Htwb", (128, 2, 128), BF16)
        H2stb = sb("H2stb", (128, 3, 3, 128), BF16)
        iWnb = sb("iWnb", (128, 3, 128), BF16)
        iWrb = sb("iWrb", (128, 3, 128), BF16)
        Atabsb = sb("Atabsb", (128, 16, 128), BF16)
        idenB = sb("idenBs", (128, 128), BF16)
        Bvb = sb("Bvb", (128, 2, 3, 128), BF16)
        Fob2 = sb("Fob2", (128, 2, 4, 3, 64), BF16)
        Gstb = sb("Gstb", (128, 2, 4, 128), BF16)
        zqb = sb("zqb", (128, 2, 4, 128), BF16)
        rwb = sb("rwb", (128, 2, D), F32)
        fbtb = sb("fbtb", (128, 8), F32)
        rowb = sb("rowb", (1, D), F32)
        gborow = sb("gborow", (1, D), BF16)
        onesr = sb("onesr", (1, 128), BF16)
        onesF = sb("onesF", (1, 128), F32)
        modsh = sb("modsh", (128, 6), F32)

        SB_TOP = 229344
        AR_SZ = 137728
        AR0 = SB_TOP - AR_SZ
        assert nc.sbuf_bytes_remaining >= AR_SZ + 2048, nc.sbuf_bytes_remaining

        def at(name, shape, dt, off):
            nbytes = int(np.prod(shape[1:])) * (2 if dt == BF16 else 4)
            assert off + nbytes <= AR_SZ, (name, off, nbytes)
            flat = int(np.prod(shape[1:]))
            ap = nc.alloc_sbuf_tensor_at(name, [shape[0], flat], dt, offset=AR0 + off).ap()
            if len(shape) == 2:
                return ap
            dims = " ".join(f"d{i}" for i in range(1, len(shape)))
            kw = {f"d{i}": shape[i] for i in range(1, len(shape))}
            return ap.rearrange(f"p ({dims}) -> p {dims}", **kw)

        winb = at("winb", (128, 8, WIDTH), BF16, 0)
        hFb = at("hF", (128, 8, TT), BF16, 49152)
        ub = at("ub", (128, 2, TT), BF16, 83968)
        ucb = at("ucb", (128, 3, TPC), BF16, 92672)
        wvb = at("wvb", (128, 2, TPC), BF16, 104960)
        ramp = at("ramp", (128, TPC), F32, 113152)
        Etile = at("Etile", (128, TPC), BF16, 121344)
        xt2 = at("xt2", (128, 2, D), F32, 125440)
        gsc = at("gsc", (128, TPC), BF16, 125440)
        xhp = at("xhp", (128, D), F32, 133632)
        Abuf = at("Abuf", (128, 2, 128, 128), BF16, 0)
        XQ = at("XQ", (128, 2, 132, 128), BF16, 67584)
        x2a = at("x2a", (128, 3, D), F32, 0)
        da = at("da", (128, 2, D), F32, 12288)
        qFa = at("qFa", (128, 2, 8, 128), BF16, 20480)
        rhq = at("rhq", (128, 8, D), BF16, 24576)
        h2a = at("h2a", (128, 2, D), F32, 40960)
        qlb = at("qlb", (128, 2, 1024), BF16, 49152)
        xt3 = at("xt3", (128, 2, D), F32, 53248)
        cvF = at("cvF", (128, 8, TPC), BF16, 61440)
        cbT = at("cbT", (128, D), F32, 94208)
        sbT = at("sbT", (128, D), F32, 98304)
        x0t = at("x0t", (128, 2, 8, 128), BF16, 102400)
        ygb = at("ygb", (128, 2, 8, 128), BF16, 106496)
        woutb = at("woutb", (128, 8, D), BF16, 110592)
        arinvT = at("arinvT", (128, D), F32, 126976)
        gbarT = at("gbarT", (128, D), F32, 131072)
        ervb = at("ervb", (128, 8, 128), BF16, 135168)
        itile = at("itile", (128, 128), F32, 137216)
        HB = 83968
        peb = at("peb", (128, 2, 512), F32, HB)
        argb = at("argb", (128, 512), F32, HB + 4096)
        rrf = at("rrf", (128, 512), F32, HB + 6144)
        h1cb = at("h1cb", (128, 512), F32, HB + 8192)
        H2buf = at("H2buf", (128, 4096), BF16, HB + 10240)
        zHb = at("zHb", (128, 10, 128), BF16, HB + 18432)
        AHb = at("AHb", (128, 2, 10, 128), BF16, HB + 20992)
        AHtwb = at("AHtwb", (128, 2, 10, 128), BF16, HB + 26112)
        ZHb = at("ZHb", (128, 2, 10, 130), BF16, HB + 31232)
        Gsh = at("Gsh", (128, 4, 128, 10), BF16, HB + 36448)
        Gtsh = at("Gtsh", (128, 11, 4, 128), BF16, HB)
        csc = at("csc", (128, 2, 64, 10), F32, HB + 20992)
        cscg = at("cscg", (128, 2, 64, 10), F32, HB + 26112)

        pm = [ps(f"pm{i}") for i in range(7)]
        pmB = es.enter_context(nc.psum_tensor("pmB", [128, 1024], BF16))
        pmB = pmB.ap() if hasattr(pmB, "ap") else pmB

        SEM_LD = "ld"
        SEM_GD = "gd"
        SEM_T = "t"
        SEM_EV = "ev"
        SEM_VV = "vv"
        SEM_CC = "cc"

        def ld(dst, src, inc=SEM_LD, eng="sync"):
            P.run(eng, lambda e, d=dst, s=src: e.dma_start(out=d, in_=s), inc=inc, inc_amt=16)

        ld(idenF[:, :], h_idenF[:, :])
        ld(hmsk[:, :], h_hmask[:, :])
        ld(toffs[:, :], h_toff[:, :])
        ld(dts[:, :], h_deltas[:, :])
        ld(bsw[:, :, :], h_short_w[:, :, :])
        ld(bsb[:, :], h_short_b[:, :])
        ld(binb[:, :], h_b_in[:, :])
        ld(silb[:, :], h_cvec[:, :])
        P.run("gpsimd", lambda e: e.dma_start(out=winb[:, :, :],
                                              in_=h_W_in.rearrange("(g p) w -> p g w", p=128)),
              inc="winb", inc_amt=16)
        n_ld0 = P.val(SEM_LD)

        P.wait("scalar", SEM_LD, n_ld0)
        P.run("scalar", lambda e: e.activation(out=silb[:, :], in_=silb[:, :], func=ACT.Silu),
              inc=SEM_EV)
        ev_sil = P.val(SEM_EV)
        t_mod_base = P.val(SEM_T)
        k = 0
        for oc in range(6):
            for ci in range(8):
                k += 1
                if k > 2:
                    P.wait("gpsimd", SEM_T, t_mod_base + k - 2)
                P.run("gpsimd",
                      lambda e, b=(k - 1) % 2, o=oc, c=ci: e.dma_start(
                          out=wmb[:, b, :], in_=h_wms[o, c, :, :]),
                      inc=f"wm{(k - 1) % 2}", inc_amt=16)
        P.wait("tensor", SEM_EV, ev_sil)
        k = 0
        for oc in range(6):
            for ci in range(8):
                k += 1
                P.wait("tensor", f"wm{(k - 1) % 2}", 16 * ((k - 1) // 2 + 1))
                P.run("tensor",
                      lambda e, b=(k - 1) % 2, o=oc, c=ci: e.matmul(
                          pm[0][:, o:o + 1], wmb[:, b, :], silb[:, c:c + 1],
                          start=(c == 0), stop=(c == 7)),
                      inc=SEM_T)
        t_mod = P.val(SEM_T)
        P.wait("vector", SEM_T, t_mod)
        P.run("vector", lambda e: e.tensor_copy(out=modsh[:, :], in_=pm[0][:, 0:6]),
              inc=SEM_VV)
        P.wait("gpsimd", SEM_VV, P.val(SEM_VV))
        P.run("gpsimd", lambda e: e.dma_start(out=msin[:, :], in_=modsh[:, :]),
              inc="msi", inc_amt=16)
        P.wait("gpsimd", "msi", 16)
        def cc_mod_fn(e):
            with allow_ici_all_to_all():
                return e.collective_compute(
                    "AllGather", ALU.bypass, replica_groups=[list(range(NC))],
                    ins=[msin.opt()], outs=[msout.opt()])
        P.run("gpsimd", cc_mod_fn, inc=SEM_CC)
        cc_mod = P.val(SEM_CC)
        P.wait("gpsimd", SEM_CC, cc_mod)
        def mgd(e):
            d = bt.AP(tensor=_th(modv), offset=0, ap=[[48, 128], [6, 8], [1, 6]])
            s = bt.AP(tensor=msout.tensor, offset=0, ap=[[6, 128], [768, 8], [1, 6]])
            return e.dma_start(out=d, in_=s, allow_slow_non_contiguous=True)
        P.run("gpsimd", mgd, inc="mgd", inc_amt=16)
        P.wait("vector", "mgd", 16)
        P.wait("scalar", "mgd", 16)
        P.wait("scalar", SEM_VV, P.val(SEM_VV))
        def modsgn(e):
            e.sign(small[:, 48:56], modv[:, 16:24])
            e.sign(small[:, 64:72], modv[:, 40:48])
            e.activation(out=small[:, 56:64], in_=modv[:, 16:24], func=ACT.Abs)
            return e.activation(out=small[:, 72:80], in_=modv[:, 40:48], func=ACT.Abs)
        P.run("scalar", modsgn, inc=SEM_EV)
        P.wait("vector", SEM_EV, P.val(SEM_EV))
        def modpost(e):
            e.tensor_copy(out=small[:, 0:8], in_=modv[:, 0:8])
            e.tensor_scalar_add(small[:, 8:16], modv[:, 8:16], 1.0)
            e.tensor_scalar_add(small[:, 32:40], modv[:, 32:40], 1.0)
            e.tensor_copy(out=small[:, 24:32], in_=modv[:, 24:32])
            for sg, ab, dst in ((48, 56, 16), (64, 72, 40)):
                e.tensor_mul(small[:, 80:88], small[:, sg:sg + 8], small[:, sg:sg + 8])
                e.tensor_sub(small[:, 88:96], small[:, sg:sg + 8], small[:, 80:88])
                e.tensor_scalar_add(small[:, 88:96], small[:, 88:96], 1.0)
                e.tensor_scalar_max(small[:, 80:88], small[:, ab:ab + 8], 0.25)
                e.tensor_mul(small[:, dst:dst + 8], small[:, 88:96], small[:, 80:88])
            return e.engine_nop()
        P.run("vector", modpost, inc=SEM_VV)
        vv_mod = P.val(SEM_VV)
        SA, CA1, GAC, SB_, CB1, GBC = 0, 8, 16, 24, 32, 40

        if stop_after == "mod":
            P.wait("gpsimd", SEM_VV, vv_mod)
            for i in range(TPC // 128):
                P.run("gpsimd", lambda e, i=i: e.dma_start(
                    out=out[i * 128:(i + 1) * 128, 0:64], in_=small[:, 0:64]),
                      inc=SEM_GD, inc_amt=16)
            P.wait("gpsimd", SEM_GD, P.val(SEM_GD))
            P.wait("sync", SEM_GD, P.val(SEM_GD))
            P.build()
            return nc, dbg_t

        ld(w1b[0:3, :], h_W1[:, :], inc="hld")
        ld(b1f1b[0:64, :], h_b1f1[:, :], inc="hld")
        ld(w2b[0:64, 0:8], h_W2s[:, :], inc="hld")
        ld(b2f2b[0:8, :], h_b2f2[:, :], inc="hld")
        for i in range(2):
            ld(Wsb[:, i, :], h_Ws[i, :, :], inc="hld")
            ld(Htwb[:, i, :], h_Htw[i, :, :], inc="hld")
        for c in range(3):
            for i in range(3):
                ld(H2stb[:, c, i, :], h_H2st[c, i, :, :], inc="hld")
        for i in range(3):
            ld(iWnb[:, i, :], h_iWn[i, :, :], inc="hld")
            ld(iWrb[:, i, :], h_iWr[i, :, :], inc="hld")
        ld(Atabsb[:, :, :], h_Atabs[:, :, :], inc="hld")
        ld(idenB[:, :], h_idenB[:, :], inc="hld")
        P.run("gpsimd", lambda e: e.dma_start(out=w3b[0:80, :], in_=h_W3p[:, :]),
              inc="w3l", inc_amt=16)
        n_hld = P.val("hld")
        CW1, CW2, CW3 = tab["cw"]

        P.wait("vector", "hld", n_hld)
        def hpre(e):
            e.tensor_mul(small[0:64, 96:97], b1f1b[0:64, 0:1], b1f1b[0:64, 1:2])
            return e.tensor_mul(small[0:8, 97:98], b2f2b[0:8, 0:1], b2f2b[0:8, 1:2])
        P.run("vector", hpre, inc=SEM_VV)
        vv_hpre = P.val(SEM_VV)
        P.wait("scalar", SEM_VV, vv_hpre)
        P.wait("tensor", "hld", n_hld)

        def rr_sin(rows, dst_ap, ev_arg):
            P.wait("vector", SEM_EV, ev_arg)
            MAGIC = 12582912.0
            def rrv(e):
                e.tensor_scalar_mul(rrf[0:rows, :], argb[0:rows, :], 1.0 / (2 * math.pi))
                e.tensor_scalar(rrf[0:rows, :], rrf[0:rows, :], MAGIC, MAGIC, ALU.add, ALU.subtract)
                e.scalar_tensor_tensor(argb[0:rows, :], rrf[0:rows, :], -CW1,
                                       argb[0:rows, :], ALU.mult, ALU.add)
                e.scalar_tensor_tensor(argb[0:rows, :], rrf[0:rows, :], -CW2,
                                       argb[0:rows, :], ALU.mult, ALU.add)
                e.scalar_tensor_tensor(argb[0:rows, :], rrf[0:rows, :], -CW3,
                                       argb[0:rows, :], ALU.mult, ALU.add)
                return e.tensor_scalar(argb[0:rows, :], argb[0:rows, :],
                                       -math.pi, math.pi, ALU.max, ALU.min)
            P.run("vector", rrv, inc=SEM_VV)
            vvc = P.val(SEM_VV)
            P.wait("scalar", SEM_VV, vvc)
            P.run("scalar", lambda e: e.activation(out=dst_ap, in_=argb[0:rows, :], func=ACT.Sin),
                  inc=SEM_EV)
            return P.val(SEM_EV), vvc

        ev_h1psum = {}
        vv_cody_last = 0
        t_h2mm_last = 0
        for fc in range(32):
            b = fc % 2
            band = fc % 4
            cg = fc // 4
            if fc >= 2:
                P.wait("sync", SEM_T, ev_h1psum[fc - 2])
            ld(peb[0:3, b, :], h_peT[:, fc * 512:(fc + 1) * 512], inc=f"pe{b}")
            P.wait("tensor", f"pe{b}", 16 * (fc // 2 + 1))
            if fc >= 1:
                P.wait("tensor", SEM_EV, ev_arg1[fc - 1])
            P.run("tensor", lambda e, b=b: e.matmul(pm[6][0:64, 0:512], w1b[0:3, 0:64],
                                                    peb[0:3, b, :], start=True, stop=True),
                  inc=SEM_T)
            ev_h1psum[fc] = P.val(SEM_T)
            P.wait("scalar", SEM_T, P.val(SEM_T))
            P.wait("scalar", SEM_VV, vv_cody_last)
            P.run("scalar", lambda e: e.activation(out=argb[0:64, :], in_=pm[6][0:64, 0:512],
                                                   func=ACT.Identity, scale=b1f1b[0:64, 1:2],
                                                   bias=small[0:64, 96:97]), inc=SEM_EV)
            if fc == 0:
                ev_arg1 = {}
            ev_arg1[fc] = P.val(SEM_EV)
            P.wait("scalar", SEM_T, t_h2mm_last)
            ev, vv_cody_last = rr_sin(64, h1cb[0:64, :], ev_arg1[fc])
            P.wait("tensor", SEM_EV, ev)
            P.run("tensor", lambda e: e.matmul(pm[1][0:8, 0:512], w2b[0:64, 0:8],
                                               h1cb[0:64, 0:512], start=True, stop=True),
                  inc=SEM_T)
            t_h2mm_last = P.val(SEM_T)
            P.wait("scalar", SEM_T, P.val(SEM_T))
            P.run("scalar", lambda e: e.activation(out=argb[0:8, :], in_=pm[1][0:8, 0:512],
                                                   func=ACT.Identity, scale=b2f2b[0:8, 1:2],
                                                   bias=small[0:8, 97:98]), inc=SEM_EV)
            h2dst = bt.AP(tensor=_th(H2buf), offset=(band * 32) * 4096 + cg * 512,
                          ap=[[4096, 8], [1, 256], [256, 2]])
            ev, vv_cody_last = rr_sin(8, h2dst, P.val(SEM_EV))
        if stop_after == "hmlp":
            P.wait("gpsimd", SEM_EV, P.val(SEM_EV))
            P.wait("gpsimd", SEM_VV, P.val(SEM_VV))
            for i in range(TPC // 128):
                P.run("gpsimd", lambda e, i=i: e.dma_start(
                    out=out[i * 128:(i + 1) * 128, 0:64], in_=small[:, 0:64]),
                      inc=SEM_GD, inc_amt=16)
            P.wait("gpsimd", SEM_GD, P.val(SEM_GD))
            P.wait("sync", SEM_GD, P.val(SEM_GD))
            P.build()
            return nc, dbg_t

        P.wait("sync", SEM_EV, P.val(SEM_EV))
        P.wait("sync", SEM_VV, P.val(SEM_VV))
        for w_ in range(4):
            P.run("sync", lambda e, w_=w_: e.dma_start(out=zscr[32 * w_:32 * w_ + 8, :],
                                                       in_=H2buf[32 * w_:32 * w_ + 8, :]),
                  inc="zhs", inc_amt=16)
        for w_ in range(4):
            P.run("sync", lambda e, w_=w_: e.dma_start(out=zscr[32 * w_ + 8:32 * w_ + 10, :],
                                                       in_=h_oz[:, :]), inc="zhs", inc_amt=16)
        P.wait("sync", "zhs", P.val("zhs"))
        for p_ in range(2):
            for w_ in range(4):
                for r_ in range(2):
                    for u_ in range(8):
                        def zhdma(e, p_=p_, w_=w_, r_=r_, u_=u_):
                            row = 10 * 128
                            dst = bt.AP(tensor=zHb.tensor,
                                        offset=(p_ * 64 + 8 * u_ + 2 * w_ + r_) * row,
                                        ap=[[row, 1], [128, 10], [1, 128]])
                            v = bt.AP(tensor=zscr.tensor,
                                      offset=(32 * w_) * 4096 + 512 * u_ + 256 * p_ + 128 * r_,
                                      ap=[[4096, 10], [1, 128]])
                            return e.dma_start(out=dst, in_=v)
                        P.run("sync", zhdma, inc="zh", inc_amt=16)
        if stop_after == "zh":
            P.wait("gpsimd", "zh", P.val("zh"))
            for i in range(TPC // 128):
                P.run("gpsimd", lambda e, i=i: e.dma_start(
                    out=out[i * 128:(i + 1) * 128, 0:64], in_=small[:, 0:64]),
                      inc=SEM_GD, inc_amt=16)
            P.wait("gpsimd", SEM_GD, P.val(SEM_GD))
            P.wait("sync", SEM_GD, P.val(SEM_GD))
            P.build()
            return nc, dbg_t

        P.wait("tensor", "zh", P.val("zh"))
        for j in range(10):
            bank = 6 if (j % 2 == 0) else 1
            if j >= 2:
                P.wait("tensor", SEM_EV, ev_ah[j - 2])
            def s1h(e, j=j, bank=bank):
                e.matmul(pm[bank][:, 0:128], zHb[:, j, :], Wsb[:, 0, :], start=True, stop=True)
                return e.matmul(pm[bank][:, 128:256], zHb[:, j, :], Wsb[:, 1, :], start=True, stop=True)
            P.run("tensor", s1h, inc=SEM_T)
            P.wait("scalar", SEM_T, P.val(SEM_T))
            def ahev(e, j=j, bank=bank):
                e.copy(out=AHb[:, 0, j, :], in_=pm[bank][:, 0:128])
                return e.copy(out=AHb[:, 1, j, :], in_=pm[bank][:, 128:256])
            P.run("scalar", ahev, inc=SEM_EV)
            if j == 0:
                ev_ah = {}
            ev_ah[j] = P.val(SEM_EV)
        ev_ah_all = P.val(SEM_EV)
        P.wait("vector", SEM_EV, ev_ah_all)
        def htw(e):
            twr = Htwb[:, 0, :].rearrange("k (o m) -> k o m", o=1).broadcast_to((128, 10, 128))
            twi = Htwb[:, 1, :].rearrange("k (o m) -> k o m", o=1).broadcast_to((128, 10, 128))
            e.tensor_mul(AHtwb[:, 0, :, :], AHb[:, 0, :, :], twr)
            e.tensor_mul(zHb[:, :, :], AHb[:, 1, :, :], twi)
            e.tensor_sub(AHtwb[:, 0, :, :], AHtwb[:, 0, :, :], zHb[:, :, :])
            e.tensor_mul(AHtwb[:, 1, :, :], AHb[:, 0, :, :], twi)
            e.tensor_mul(zHb[:, :, :], AHb[:, 1, :, :], twr)
            return e.tensor_add(AHtwb[:, 1, :, :], AHtwb[:, 1, :, :], zHb[:, :, :])
        P.run("vector", htw, inc=SEM_VV)
        vv_htw = P.val(SEM_VV)
        P.wait("tensor", SEM_VV, vv_htw)
        CLS = ((0, 0, 65, 0), (1, 64, 128, 65), (2, 0, 1, 129))
        hz_k = 0
        hz_banks = {}
        for cls, k1lo, k1hi, slot0 in CLS:
            nk = k1hi - k1lo
            for c0 in range(0, nk, 51):
                cn = min(51, nk - c0)
                bre, bim = ((2, 3), (4, 5))[hz_k % 2]
                hz_k += 1
                if bre in hz_banks:
                    P.wait("tensor", SEM_EV, hz_banks[bre])
                def s2h(e, cls=cls, k1lo=k1lo, c0=c0, cn=cn, bre=bre, bim=bim):
                    mvr = AHtwb[:, 0, :, k1lo + c0:k1lo + c0 + cn]
                    mvi = AHtwb[:, 1, :, k1lo + c0:k1lo + c0 + cn]
                    F = 10 * cn
                    pr = pm[bre][:, 0:F].rearrange("k (j c) -> k j c", j=10)
                    pi = pm[bim][:, 0:F].rearrange("k (j c) -> k j c", j=10)
                    e.matmul(pr, H2stb[:, cls, 0, :], mvr, start=True, stop=False)
                    e.matmul(pr, H2stb[:, cls, 2, :], mvi, start=False, stop=True)
                    e.matmul(pi, H2stb[:, cls, 1, :], mvr, start=True, stop=False)
                    return e.matmul(pi, H2stb[:, cls, 0, :], mvi, start=False, stop=True)
                P.run("tensor", s2h, inc=SEM_T)
                P.wait("scalar", SEM_T, P.val(SEM_T))
                def hzev(e, c0=c0, cn=cn, slot0=slot0, bre=bre, bim=bim):
                    F = 10 * cn
                    e.copy(out=ZHb[:, 0, :, slot0 + c0:slot0 + c0 + cn],
                           in_=pm[bre][:, 0:F].rearrange("k (j c) -> k j c", j=10))
                    return e.copy(out=ZHb[:, 1, :, slot0 + c0:slot0 + c0 + cn],
                                  in_=pm[bim][:, 0:F].rearrange("k (j c) -> k j c", j=10))
                P.run("scalar", hzev, inc=SEM_EV)
                hz_banks[bre] = P.val(SEM_EV)
                ev_hz = P.val(SEM_EV)
        if "zhb" in dbg:
            t = dout("zhb", (128, 2 * 10 * 130), BF16)
            P.wait("gpsimd", SEM_EV, ev_hz)
            P.run("gpsimd", lambda e, t=t: e.dma_start(out=t[:, :], in_=ZHb[:, :, :, :].rearrange("k a b c -> k (a b c)")),
                  inc="dbg1", inc_amt=16)

        if stop_after == "s2h":
            P.wait("gpsimd", SEM_EV, ev_hz)
            for i in range(TPC // 128):
                P.run("gpsimd", lambda e, i=i: e.dma_start(
                    out=out[i * 128:(i + 1) * 128, 0:64], in_=small[:, 0:64]),
                      inc=SEM_GD, inc_amt=16)
            P.wait("gpsimd", SEM_GD, P.val(SEM_GD))
            P.wait("sync", SEM_GD, P.val(SEM_GD))
            P.build()
            return nc, dbg_t

        P.wait("vector", SEM_EV, ev_hz)
        P.wait("gpsimd", SEM_EV, ev_hz)

        def zh_ap(pl, sl0, sl1, step=1):
            base = ZHb[:, pl, :, :]
            if step == 1:
                v = base[:, :, sl0:sl1]
            else:
                v = base[:, :, sl1:sl0:-1] if False else base[:, :, sl0:sl1:step]
            return v.rearrange("k j c -> k c j")

        def tab_ap(t, c0, c1):
            n = c1 - c0
            return Atabsb[:, t, c0:c1].rearrange("k (c o) -> k c o", o=1).broadcast_to((128, n, 10))

        def emit_combine(eng, gplane, outsl, n1ap, pxap, tre, tim, pre, pim, csc0, csc1):
            g, part = gplane
            out = Gsh[:, g, outsl[0]:outsl[1], :]
            Hnr, Hni = n1ap
            Hpr, Hpi = pxap
            if part == "re":
                def f(e):
                    e.tensor_mul(csc0, tre, Hnr)
                    e.tensor_mul(csc1, tim, Hni)
                    e.tensor_sub(csc0, csc0, csc1)
                    e.tensor_mul(csc1, pre, Hpr)
                    e.tensor_add(csc0, csc0, csc1)
                    e.tensor_mul(csc1, pim, Hpi)
                    return e.tensor_add(out, csc0, csc1)
            else:
                def f(e):
                    e.tensor_mul(csc0, tre, Hni)
                    e.tensor_mul(csc1, tim, Hnr)
                    e.tensor_add(csc0, csc0, csc1)
                    e.tensor_mul(csc1, pre, Hpi)
                    e.tensor_sub(csc0, csc0, csc1)
                    e.tensor_mul(csc1, pim, Hpr)
                    return e.tensor_add(out, csc0, csc1)
            P.run(eng, f, inc=(SEM_VV if eng == "vector" else "gg"))

        csc = at("csc", (128, 2, 64, 10), BF16, HB + 11264)
        cscg = at("cscg", (128, 2, 64, 10), BF16, HB + 13824)
        for grp, (c0, c1, nsl, psl, tset) in enumerate((
                (1, 65, (1, 65), (128, 64, -1), 0),
                (65, 128, (66, 129), (63, 0, -1), 8),
                (0, 1, (0, 1), (129, 130), 0))):
            n = c1 - c0
            nr = ZHb[:, 0, :, nsl[0]:nsl[1]].rearrange("k j c -> k c j")
            ni = ZHb[:, 1, :, nsl[0]:nsl[1]].rearrange("k j c -> k c j")
            if len(psl) == 3:
                pr = ZHb[:, 0, :, psl[0]:psl[1]:psl[2]].rearrange("k j c -> k c j")
                pi = ZHb[:, 1, :, psl[0]:psl[1]:psl[2]].rearrange("k j c -> k c j")
            else:
                pr = ZHb[:, 0, :, psl[0]:psl[1]].rearrange("k j c -> k c j")
                pi = ZHb[:, 1, :, psl[0]:psl[1]].rearrange("k j c -> k c j")
            cs0 = csc[:, 0, 0:n, :]
            cs1 = csc[:, 1, 0:n, :]
            for gi, (t1, t2) in enumerate(((0, 2), (4, 6))):
                a1r = tab_ap(tset + t1, c0, c1); a1i = tab_ap(tset + t1 + 1, c0, c1)
                a2r = tab_ap(tset + t2, c0, c1); a2i = tab_ap(tset + t2 + 1, c0, c1)
                emit_combine("vector", (2 * gi, "re"), (c0, c1), (nr, ni), (pr, pi),
                             a1r, a1i, a2r, a2i, cs0, cs1)
                emit_combine("gpsimd", (2 * gi + 1, "im"), (c0, c1), (nr, ni), (pr, pi),
                             a1r, a1i, a2r, a2i, cscg[:, 0, 0:n, :], cscg[:, 1, 0:n, :])
        vv_comb = P.val(SEM_VV)
        gg_comb = P.val("gg")

        if stop_after == "gcomb":
            P.wait("gpsimd", SEM_VV, vv_comb)
            P.wait("gpsimd", "gg", gg_comb)
            for i in range(TPC // 128):
                P.run("gpsimd", lambda e, i=i: e.dma_start(
                    out=out[i * 128:(i + 1) * 128, 0:64], in_=small[:, 0:64]),
                      inc=SEM_GD, inc_amt=16)
            P.wait("gpsimd", SEM_GD, P.val(SEM_GD))
            P.wait("sync", SEM_GD, P.val(SEM_GD))
            P.build()
            return nc, dbg_t

        P.wait("tensor", SEM_VV, vv_comb)
        P.wait("tensor", "gg", gg_comb)
        gt_ev = {}
        for g in range(4):
            for ci in range(11):
                col0 = 12 * ci if ci < 10 else 116
                ncol = 12
                key = (g * 11 + ci) % 2
                if key in gt_ev:
                    P.wait("tensor", "gte", gt_ev[key])
                def gtr(e, g=g, col0=col0, ncol=ncol, key=key):
                    return e.matmul(pm[2 + key][0:ncol * 10, 0:128],
                                    Gsh[:, g, col0:col0 + ncol, :], idenB[:, :],
                                    start=True, stop=True)
                P.run("tensor", gtr, inc=SEM_T)
                P.wait("scalar", SEM_T, P.val(SEM_T))
                def gte(e, g=g, ci=ci, ncol=ncol, key=key):
                    return e.copy(out=Gtsh[0:ncol * 10, ci, g, :],
                                  in_=pm[2 + key][0:ncol * 10, 0:128])
                P.run("scalar", gte, inc="gte")
                gt_ev[key] = P.val("gte")
        P.wait("gpsimd", "gte", P.val("gte"))

        if stop_after == "gtrT":
            for i in range(TPC // 128):
                P.run("gpsimd", lambda e, i=i: e.dma_start(
                    out=out[i * 128:(i + 1) * 128, 0:64], in_=small[:, 0:64]),
                      inc=SEM_GD, inc_amt=16)
            P.wait("gpsimd", SEM_GD, P.val(SEM_GD))
            P.wait("sync", SEM_GD, P.val(SEM_GD))
            P.build()
            return nc, dbg_t

        P.run("gpsimd", lambda e: e.dma_start(out=gb_in[:, :],
                                              in_=Gtsh[0:120, :, :, :].rearrange("p a b c -> p (a b c)")),
              inc="gbi", inc_amt=16)
        P.wait("gpsimd", "gbi", 16)

        if stop_after == "gtr":
            for i in range(TPC // 128):
                P.run("gpsimd", lambda e, i=i: e.dma_start(
                    out=out[i * 128:(i + 1) * 128, 0:64], in_=small[:, 0:64]),
                      inc=SEM_GD, inc_amt=16)
            P.wait("gpsimd", SEM_GD, P.val(SEM_GD))
            P.wait("sync", SEM_GD, P.val(SEM_GD))
            P.build()
            return nc, dbg_t

        def cc_g_fn(e):
            with allow_ici_all_to_all():
                return e.collective_compute(
                    "AllGather", ALU.bypass, replica_groups=[list(range(NC))],
                    ins=[gb_in.opt()], outs=[gb_out.opt()])
        P.run("gpsimd", cc_g_fn, inc=SEM_CC)
        cc_g = P.val(SEM_CC)
        if "zhb" in dbg:
            P.wait("gpsimd", "dbg1", 16)
        P.run("gpsimd", lambda e: e.engine_nop(), inc="hph")
        for engx in ("sync", "scalar", "vector", "tensor"):
            P.wait(engx, "hph", 1)

        if stop_after == "hpipe":
            P.wait("gpsimd", SEM_CC, cc_g)
            for i in range(TPC // 128):
                P.run("gpsimd", lambda e, i=i: e.dma_start(
                    out=out[i * 128:(i + 1) * 128, 0:64], in_=small[:, 0:64]),
                      inc=SEM_GD, inc_amt=16)
            P.wait("gpsimd", SEM_GD, P.val(SEM_GD))
            P.wait("sync", SEM_GD, P.val(SEM_GD))
            P.build()
            return nc, dbg_t

        ev_need_tp = []
        for i in range(TT // 128):
            b = i % 2
            if i >= 2:
                P.wait("sync", SEM_EV, ev_tile[i - 2])
            ld(xt2[:, b, :], h_xs[i * 128:(i + 1) * 128, :], inc=f"xa{b}")
            P.wait("vector", f"xa{b}", 16 * (i // 2 + 1))
            def stats(e, b=b):
                e.reduce_sum(small[:, 64:65], xt2[:, b, :], axis=mybir.AxisListType.X)
                e.tensor_scalar_mul(small[:, 64:65], small[:, 64:65], 1.0 / D)
                return e.engine_nop()
            P.run("vector", stats, inc=SEM_VV)
            P.wait("scalar", SEM_VV, P.val(SEM_VV))
            P.run("scalar", lambda e, b=b: e.square(out=xhp[:, :], in_=xt2[:, b, :]), inc=SEM_EV)
            P.wait("vector", SEM_EV, P.val(SEM_EV))
            def stats2(e, b=b):
                e.reduce_sum(small[:, 65:66], xhp[:, :], axis=mybir.AxisListType.X)
                e.tensor_scalar_mul(small[:, 65:66], small[:, 65:66], 1.0 / D)
                e.tensor_mul(small[:, 66:67], small[:, 64:65], small[:, 64:65])
                e.tensor_sub(small[:, 65:66], small[:, 65:66], small[:, 66:67])
                e.tensor_scalar_add(small[:, 65:66], small[:, 65:66], 1e-5)
                return e.engine_nop()
            P.run("vector", stats2, inc=SEM_VV)
            P.wait("scalar", SEM_VV, P.val(SEM_VV))
            P.run("scalar", lambda e: e.activation(out=small[:, 65:66], in_=small[:, 65:66],
                                                   func=ACT.Sqrt), inc=SEM_EV)
            P.wait("vector", SEM_EV, P.val(SEM_EV))
            def nmr(e):
                e.reciprocal(small[:, 65:66], small[:, 65:66])
                e.tensor_mul(small[:, 66:67], small[:, 64:65], small[:, 65:66])
                e.tensor_scalar_mul(small[:, 66:67], small[:, 66:67], -1.0)
                return e.engine_nop()
            P.run("vector", nmr, inc=SEM_VV)
            P.wait("scalar", SEM_VV, P.val(SEM_VV))
            P.run("scalar", lambda e, b=b: e.activation(out=xhp[:, :], in_=xt2[:, b, :],
                                                        func=ACT.Identity, scale=small[:, 65:66],
                                                        bias=small[:, 66:67]),
                  inc=SEM_EV)
            ev_norm = P.val(SEM_EV)
            if i == 0:
                ev_tile = {}
            ev_tile[i] = ev_norm
            P.wait("tensor", SEM_EV, ev_norm)
            P.wait("tensor", SEM_VV, vv_mod)
            for dc in range(8):
                bank = 2 + (dc % 2)
                if i > 0 or dc >= 2:
                    P.wait("tensor", SEM_EV, ev_tp[(i, dc - 2) if dc >= 2 else (i - 1, 6 + dc)])
                P.run("tensor", lambda e, bank=bank, dc=dc: e.transpose(
                    pm[bank][:, 0:128], xhp[:, dc * 128:(dc + 1) * 128], idenF[:, :]),
                      inc=SEM_T)
                P.wait("scalar", SEM_T, P.val(SEM_T))
                P.run("scalar", lambda e, bank=bank, dc=dc, i=i: e.activation(
                    out=hFb[:, dc, i * 128:(i + 1) * 128], in_=pm[bank][:, 0:128],
                    func=ACT.Identity, scale=small[:, CA1 + dc:CA1 + dc + 1],
                    bias=small[:, SA + dc:SA + dc + 1]), inc=SEM_EV)
                if i == 0 and dc == 0:
                    ev_tp = {}
                ev_tp[(i, dc)] = P.val(SEM_EV)
        ev_hF_done = P.val(SEM_EV)

        FCH = [(0, 512), (512, 1024), (1024, 1536), (1536, 2048), (2048, 2176)]
        P.wait("tensor", "winb", 16)
        P.wait("tensor", SEM_EV, ev_hF_done)

        bank_ev = {}
        ub_last_vv = [0, 0]

        def do_win_chunk(oc, ubuf):
            for fi, (f0, f1) in enumerate(FCH):
                bank = 4 + (fi % 2)
                if bank in bank_ev:
                    P.wait("tensor", SEM_EV, bank_ev[bank])
                for ci in range(8):
                    P.run("tensor", lambda e, bank=bank, ci=ci, oc=oc, f0=f0, f1=f1: e.matmul(
                        pm[bank][:, 0:f1 - f0], winb[:, ci, oc * 128:(oc + 1) * 128],
                        hFb[:, ci, f0:f1], start=(ci == 0), stop=(ci == 7)),
                          inc=(SEM_T if ci == 7 else None))
                P.wait("scalar", SEM_T, P.val(SEM_T))
                if fi == 0:
                    P.wait("scalar", SEM_VV, ub_last_vv[ubuf])
                P.run("scalar", lambda e, bank=bank, oc=oc, f0=f0, f1=f1, ubuf=ubuf: e.activation(
                    out=ub[:, ubuf, f0:f1], in_=pm[bank][:, 0:f1 - f0],
                    func=ACT.Identity, bias=binb[:, oc:oc + 1]), inc=SEM_EV)
                bank_ev[bank] = P.val(SEM_EV)
            return P.val(SEM_EV)

        def do_shortconv(oc, ubuf, dst_ap):
            def sc(e, oc=oc, ubuf=ubuf, dst=dst_ap):
                e.tensor_scalar_mul(ub[:, ubuf, 126:128], ub[:, ubuf, 126:128], hmsk[:, 0:1])
                e.tensor_scalar(ucb[:, 2, :], ub[:, ubuf, 126:126 + TPC],
                                bsw[:, oc, 0:1], bsb[:, oc:oc + 1], ALU.mult, ALU.add)
                e.scalar_tensor_tensor(ucb[:, 2, :], ub[:, ubuf, 127:127 + TPC],
                                       bsw[:, oc, 1:2], ucb[:, 2, :], ALU.mult, ALU.add)
                return e.scalar_tensor_tensor(dst, ub[:, ubuf, 128:128 + TPC],
                                              bsw[:, oc, 2:3], ucb[:, 2, :], ALU.mult, ALU.add)
            P.run("vector", sc, inc=SEM_VV)
            ub_last_vv[ubuf] = P.val(SEM_VV)
            return P.val(SEM_VV)

        P.wait("gpsimd", "hph", 1)
        P.run("gpsimd", lambda e: e.iota(ramp[:, :], [[1, TPC]], channel_multiplier=0, allow_small_or_imprecise_dtypes=True),
              inc="rp", inc_amt=1)


        for g in range(8):
            evs = []
            for j, oc in enumerate((8 + g, 16 + g)):
                ubuf = j
                ev_u = do_win_chunk(oc, ubuf)
                P.wait("vector", SEM_EV, ev_u)
                vv_uc = do_shortconv(oc, ubuf, ucb[:, j, :])
                evs.append(vv_uc)
            if g > 0:
                P.wait("vector", SEM_EV, ev_E)
                P.wait("scalar", SEM_VV, vv_gate)
            def envve(e, g=g):
                e.tensor_scalar_mul(small[:, 70:71], dts[:, g:g + 1], -1.0 / (L - 1))
                return e.tensor_mul(small[:, 71:72], small[:, 70:71], toffs[:, 0:1])
            P.run("vector", envve, inc=SEM_VV)
            P.wait("scalar", SEM_VV, P.val(SEM_VV))
            P.wait("scalar", "rp", 1)
            P.run("scalar", lambda e: e.activation(out=Etile[:, :], in_=ramp[:, :], func=ACT.Exp,
                                                   scale=small[:, 70:71], bias=small[:, 71:72]),
                  inc=SEM_EV)
            ev_E = P.val(SEM_EV)
            wbuf = g % 2
            if g >= 2:
                P.wait("vector", f"wv{wbuf}", 16 * (g // 2))
            P.wait("vector", SEM_EV, ev_E)
            def gate(e, g=g, wbuf=wbuf):
                e.tensor_mul(gsc[:, :], ucb[:, 0, :], ucb[:, 1, :])
                dst = bt.AP(tensor=_th(wvb), offset=wbuf * TPC,
                            ap=[[2 * TPC, 128], [1, TPC // 2], [TPC // 2, 2]])
                return e.tensor_mul(dst, gsc[:, :], Etile[:, :])
            P.run("vector", gate, inc=SEM_VV)
            vv_gate = P.val(SEM_VV)
            P.wait("gpsimd", SEM_VV, P.val(SEM_VV))
            P.run("gpsimd", lambda e, g=g, wbuf=wbuf: e.dma_start(
                out=vb_in[g * 128:(g + 1) * 128, :], in_=wvb[:, wbuf, :]),
                  inc=f"wv{wbuf}", inc_amt=16)
        P.wait("gpsimd", "wv0", P.val("wv0"))
        P.wait("gpsimd", "wv1", P.val("wv1"))
        def cc_v_fn(e):
            with allow_ici_all_to_all():
                return e.collective_compute(
                    "AllToAll", ALU.bypass, replica_groups=[list(range(NC))],
                    ins=[vb_in.opt()], outs=[vb_out.opt()])
        P.run("gpsimd", cc_v_fn, inc=SEM_CC)
        cc_v = P.val(SEM_CC)

        for g in range(8):
            sl = g % 2
            if g >= 2:
                P.wait("vector", f"x0s{sl}", 16 * (g // 2))
            ev_u = do_win_chunk(g, 0)
            P.wait("vector", SEM_EV, ev_u)
            x0dst = bt.AP(tensor=_th(x0bb), offset=sl * TPC,
                          ap=[[2 * TPC, 128], [1, TPC // 2], [TPC // 2, 2]])
            vv_sc = do_shortconv(g, 0, x0dst)
            P.wait("gpsimd", SEM_VV, vv_sc)
            P.run("gpsimd", lambda e, g=g, sl=sl: e.dma_start(
                out=x0sp[g, :, :], in_=x0bb[:, sl, :]), inc=f"x0s{sl}", inc_amt=16)
        vv_x0 = P.val(SEM_VV)
        P.wait("gpsimd", "x0s0", 64)
        P.wait("gpsimd", "x0s1", 64)

        if stop_after == "phaseA":
            P.wait("gpsimd", SEM_CC, cc_v)
            for i in range(TPC // 128):
                P.run("gpsimd", lambda e, i=i: e.dma_start(
                    out=out[i * 128:(i + 1) * 128, :], in_=xt2[:, 0, :]),
                      inc=SEM_GD, inc_amt=16)
            P.wait("gpsimd", SEM_GD, P.val(SEM_GD))
            P.build()
            return nc, dbg_t
        QB = at("QB", (128, 2, 132, 128), BF16, 0)
        Qt = at("Qt", (128, 2, 132, 128), BF16, 67584)
        Tb = at("Tb", (128, 2, 132, 128), BF16, 0)
        Ttb = at("Ttb", (128, 2, 132, 128), BF16, 67584)
        zsend = at("zsend", (128, 16384), BF16, 0)
        cbuf2 = sb("cbuf2", (128, 4, 16, 128), BF16)

        class Banks:

            def __init__(self, ids):
                self.ids = list(ids)
                self.k = 0
                self.ev = {}

            def get(self, n=1):
                out = []
                for _ in range(n):
                    b = self.ids[self.k % len(self.ids)]
                    self.k += 1
                    if b in self.ev:
                        P.wait("tensor", SEM_EV, self.ev[b])
                    out.append(b)
                return out if n > 1 else out[0]

            def done(self, bs):
                if isinstance(bs, int):
                    bs = [bs]
                for b in bs:
                    self.ev[b] = P.val(SEM_EV)

        P.wait("sync", SEM_CC, cc_v)
        P.wait("tensor", "hld", n_hld)

        ev_s1q = {}
        t_s1q = {}
        for q in range(32):
            b = q % 2
            c0 = 4 * q
            if q >= 2:
                P.wait("sync", SEM_T, t_s1q[q - 2])
            for j in range(4):
                for p_ in range(2):
                    def zld(e, c=c0 + j, j=j, p_=p_, b=b):
                        dst = bt.AP(tensor=_th(zqb),
                                    offset=(p_ * 64) * 1024 + b * 512 + j * 128,
                                    ap=[[1024, 64], [1, 128]])
                        v = bt.AP(tensor=vb_out.tensor,
                                  offset=c * TPC + p_ * (TPC // 2),
                                  ap=[[128 * TPC, 8], [128, 8], [1, 128]])
                        return e.dma_start(out=dst, in_=v)
                    P.run("sync", zld, inc=f"zq{b}", inc_amt=16)
            P.wait("tensor", f"zq{b}", 128 * (q // 2 + 1))
            if q >= 2:
                P.wait("tensor", SEM_VV, ev_s1q[q - 2][0])
                P.wait("tensor", SEM_EV, ev_s1q[q - 2][1])
            bk = (0, 1) if b == 0 else (2, 3)
            def s1(e, b=b, bk=bk):
                for j in range(4):
                    e.matmul(pm[bk[0]][:, j * 128:(j + 1) * 128], zqb[:, b, j, :],
                             Wsb[:, 0, :], start=True, stop=True)
                    r = e.matmul(pm[bk[1]][:, j * 128:(j + 1) * 128], zqb[:, b, j, :],
                                 Wsb[:, 1, :], start=True, stop=True)
                return r
            P.run("tensor", s1, inc=SEM_T)
            t_s1q[q] = P.val(SEM_T)
            P.wait("vector", SEM_T, t_s1q[q])
            P.wait("scalar", SEM_T, t_s1q[q])
            P.run("vector", lambda e, c0=c0, bk=bk: e.tensor_copy(
                out=Abuf[:, 0, :, c0:c0 + 4],
                in_=pm[bk[0]][:, 0:512].rearrange("p (j k) -> p k j", j=4)), inc=SEM_VV)
            P.run("scalar", lambda e, c0=c0, bk=bk: e.copy(
                out=Abuf[:, 1, :, c0:c0 + 4],
                in_=pm[bk[1]][:, 0:512].rearrange("p (j k) -> p k j", j=4)), inc=SEM_EV)
            ev_s1q[q] = (P.val(SEM_VV), P.val(SEM_EV))
        P.wait("tensor", SEM_VV, P.val(SEM_VV))
        P.wait("tensor", SEM_EV, P.val(SEM_EV))

        if stop_after == "s1":
            P.wait("gpsimd", SEM_VV, P.val(SEM_VV))
            P.wait("gpsimd", SEM_EV, P.val(SEM_EV))
            for i in range(TPC // 128):
                P.run("gpsimd", lambda e, i=i: e.dma_start(
                    out=out[i * 128:(i + 1) * 128, :], in_=xt2[:, 0, :]),
                      inc=SEM_GD, inc_amt=16)
            P.wait("gpsimd", SEM_GD, P.val(SEM_GD))
            P.build()
            return nc, dbg_t
        sbk = Banks([0, 1, 2, 3])
        P.wait("tensor", SEM_VV, P.val(SEM_VV))
        P.wait("tensor", SEM_EV, P.val(SEM_EV))
        for vid in range(130):
            b = vid % 2
            if vid >= 2:
                P.wait("sync", SEM_T, t_s2[vid - 2])
            ld(Bvb[:, b, :, :], h_Bv[vid], inc=f"bv{b}")
            P.wait("tensor", f"bv{b}", 16 * (vid // 2 + 1))
            bk = sbk.get(2)
            def s2f(e, vid=vid, b=b, bk=bk):
                Xr = pm[bk[0]][:, 0:128]
                Xi = pm[bk[1]][:, 0:128]
                e.matmul(Xr, Abuf[:, 0, vid if False else 0, :], Bvb[:, b, 0, :], start=True, stop=False) if False else None
                k1 = VL[vid][0]
                e.matmul(Xr, Abuf[:, 0, k1, :], Bvb[:, b, 0, :], start=True, stop=False)
                e.matmul(Xr, Abuf[:, 1, k1, :], Bvb[:, b, 2, :], start=False, stop=True)
                e.matmul(Xi, Abuf[:, 0, k1, :], Bvb[:, b, 1, :], start=True, stop=False)
                return e.matmul(Xi, Abuf[:, 1, k1, :], Bvb[:, b, 0, :], start=False, stop=True)
            P.run("tensor", s2f, inc=SEM_T)
            if vid == 0:
                t_s2 = {}
            t_s2[vid] = P.val(SEM_T)
            P.wait("vector", SEM_T, P.val(SEM_T))
            P.wait("scalar", SEM_T, P.val(SEM_T))
            P.run("vector", lambda e, vid=vid, bk=bk: e.tensor_copy(out=XQ[:, 0, vid, :],
                                                                    in_=pm[bk[0]][:, 0:128]), inc=SEM_VV)
            P.run("scalar", lambda e, vid=vid, bk=bk: e.copy(out=XQ[:, 1, vid, :],
                                                             in_=pm[bk[1]][:, 0:128]), inc=SEM_EV)
            P.wait("tensor", SEM_VV, P.val(SEM_VV))
            sbk.done(bk)


        if stop_after == "s2f":
            P.wait("gpsimd", SEM_VV, P.val(SEM_VV))
            P.wait("gpsimd", SEM_EV, P.val(SEM_EV))
            for i in range(TPC // 128):
                P.run("gpsimd", lambda e, i=i: e.dma_start(
                    out=out[i * 128:(i + 1) * 128, 0:64], in_=small[:, 0:64]),
                      inc=SEM_GD, inc_amt=16)
            P.wait("gpsimd", SEM_GD, P.val(SEM_GD))
            P.wait("sync", SEM_GD, P.val(SEM_GD))
            P.build()
            return nc, dbg_t
        P.wait("tensor", SEM_CC, cc_g)
        P.wait("tensor", "w3l", 16)
        P.wait("tensor", SEM_VV, P.val(SEM_VV))
        P.wait("tensor", SEM_EV, P.val(SEM_EV))
        cgk = Banks([0, 1, 2, 3, 4, 5])

        def col_ci(col):
            ci = col // 12 if col < 120 else 10
            return ci, col - (12 * ci if col < 120 else 116)

        def emit_cgen(cols, cslots):
            for col, sl in zip(cols, cslots):
                gb = cg_n[0] % 2
                cg_n[0] += 1
                ci, colsub = col_ci(col)
                if gb in cg_t:
                    P.wait("sync", SEM_T, cg_t[gb])
                def gld(e, col=col, ci=ci, colsub=colsub, gb=gb):
                    dst = bt.AP(tensor=_th(Gstb), offset=gb * (4 * 128),
                                ap=[[1024, 80], [1, 512]])
                    v = bt.AP(tensor=gb_out.tensor,
                              offset=(colsub * 10) * 5632 + ci * 512,
                              ap=[[120 * 5632, 8], [5632, 10], [1, 512]])
                    return e.dma_start(out=dst, in_=v)
                P.run("sync", gld, inc=f"gs{gb}", inc_amt=16)
                gst_ev[gb] = gst_ev.get(gb, 0) + 16
                P.wait("tensor", f"gs{gb}", gst_ev[gb])
                bk = cgk.get()
                def cg(e, gb=gb, bk=bk):
                    ps_ = pm[bk]
                    e.matmul(ps_[:, 0:128], w3b[0:80, :], Gstb[0:80, gb, 0, :], start=True, stop=True)
                    e.matmul(ps_[:, 128:256], w3b[0:80, :], Gstb[0:80, gb, 1, :], start=True, stop=True)
                    e.matmul(ps_[:, 256:384], w3b[0:80, :], Gstb[0:80, gb, 2, :], start=True, stop=True)
                    return e.matmul(ps_[:, 384:512], w3b[0:80, :], Gstb[0:80, gb, 3, :], start=True, stop=True)
                P.run("tensor", cg, inc=SEM_T)
                cg_t[gb] = P.val(SEM_T)
                P.wait("scalar", SEM_T, P.val(SEM_T))
                def cgev(e, sl=sl, bk=bk):
                    return e.copy(out=cbuf2[:, :, sl, :],
                                  in_=pm[bk][:, 0:512].rearrange("c (a k) -> c a k", a=4))
                P.run("scalar", cgev, inc=SEM_EV)
                cgk.done(bk)
            return P.val(SEM_EV)

        gst_ev = {}
        cg_t = {}
        cg_n = [0]

        def spectral_batch(a0, n, vq_last):
            XNr = XQ[:, 0, a0:a0 + n, :]
            XNi = XQ[:, 1, a0:a0 + n, :]
            XPr = XQ[:, 0, 128 - a0:128 - a0 - n:-1, :]
            XPi = XQ[:, 1, 128 - a0:128 - a0 - n:-1, :]
            c1r = cbuf2[:, 0, 0:n, :]; c1i = cbuf2[:, 1, 0:n, :]
            c2r = cbuf2[:, 2, 0:n, :]; c2i = cbuf2[:, 3, 0:n, :]
            d1r = cbuf2[:, 0, 8:8 + n, :]; d1i = cbuf2[:, 1, 8:8 + n, :]
            d2r = cbuf2[:, 2, 8:8 + n, :]; d2i = cbuf2[:, 3, 8:8 + n, :]
            s0 = scr2[:, 0, 0:n, :]; s1_ = scr2[:, 1, 0:n, :]
            g0 = scr2g[:, 0, 0:n, :]; g1_ = scr2g[:, 1, 0:n, :]
            QNr = QB[:, 0, a0:a0 + n, :]
            QNi = QB[:, 1, a0:a0 + n, :]
            QPr = QB[:, 0, 128 - a0:128 - a0 - n:-1, :]
            QPi = QB[:, 1, 128 - a0:128 - a0 - n:-1, :]
            def vre(e):
                e.tensor_mul(s0, d1r, XPr)
                e.tensor_mul(s1_, d1i, XPi)
                e.tensor_add(s0, s0, s1_)
                e.tensor_mul(s1_, d2r, XNr)
                e.tensor_add(s0, s0, s1_)
                e.tensor_mul(s1_, d2i, XNi)
                e.tensor_sub(QPr, s0, s1_)
                e.tensor_mul(s0, c1r, XNr)
                e.tensor_mul(s1_, c1i, XNi)
                e.tensor_add(s0, s0, s1_)
                e.tensor_mul(s1_, c2r, XPr)
                e.tensor_add(s0, s0, s1_)
                e.tensor_mul(s1_, c2i, XPi)
                return e.tensor_sub(QNr, s0, s1_)
            def vim(e):
                e.tensor_mul(g0, d1i, XPr)
                e.tensor_mul(g1_, d1r, XPi)
                e.tensor_sub(g0, g0, g1_)
                e.tensor_mul(g1_, d2r, XNi)
                e.tensor_add(g0, g0, g1_)
                e.tensor_mul(g1_, d2i, XNr)
                e.tensor_add(QPi, g0, g1_)
                e.tensor_mul(g0, c1i, XNr)
                e.tensor_mul(g1_, c1r, XNi)
                e.tensor_sub(g0, g0, g1_)
                e.tensor_mul(g1_, c2r, XPi)
                e.tensor_add(g0, g0, g1_)
                e.tensor_mul(g1_, c2i, XPr)
                return e.tensor_add(QNi, g0, g1_)
            P.wait("vector", SEM_EV, vq_last)
            P.wait("gpsimd", SEM_EV, vq_last)
            P.run("vector", vre, inc=SEM_VV)
            P.run("gpsimd", vim, inc="gg")
            return P.val(SEM_VV), P.val("gg")

        scr2 = sb("scr2", (128, 2, 8, 128), BF16)
        scr2g = sb("scr2g", (128, 2, 8, 128), BF16)
        sp_done = []
        for bi in range(8):
            a0 = 1 + 8 * bi
            cols = list(range(a0, a0 + 8)) + [128 - a for a in range(a0, a0 + 8)]
            cslots = list(range(8)) + list(range(8, 16))
            if bi > 0:
                P.wait("scalar", SEM_VV, sp_done[-1][0])
                P.wait("scalar", "gg", sp_done[-1][1])
            ev = emit_cgen(cols, cslots)
            sp_done.append(spectral_batch(a0, 8, ev))
        P.wait("scalar", SEM_VV, sp_done[-1][0])
        P.wait("scalar", "gg", sp_done[-1][1])
        ev = emit_cgen([0], [0])
        def sp0(e):
            XNr = XQ[:, 0, 0, :]; XNi = XQ[:, 1, 0, :]
            XPr = XQ[:, 0, 129, :]; XPi = XQ[:, 1, 129, :]
            c1r = cbuf2[:, 0, 0, :]; c1i = cbuf2[:, 1, 0, :]
            c2r = cbuf2[:, 2, 0, :]; c2i = cbuf2[:, 3, 0, :]
            s0 = scr2[:, 0, 0, :]; s1_ = scr2[:, 1, 0, :]
            e.tensor_mul(s0, c1r, XNr)
            e.tensor_mul(s1_, c1i, XNi)
            e.tensor_add(s0, s0, s1_)
            e.tensor_mul(s1_, c2r, XPr)
            e.tensor_add(s0, s0, s1_)
            e.tensor_mul(s1_, c2i, XPi)
            e.tensor_sub(QB[:, 0, 0, :], s0, s1_)
            e.tensor_mul(s0, c1i, XNr)
            e.tensor_mul(s1_, c1r, XNi)
            e.tensor_sub(s0, s0, s1_)
            e.tensor_mul(s1_, c2r, XPi)
            e.tensor_add(s0, s0, s1_)
            e.tensor_mul(s1_, c2i, XPr)
            return e.tensor_add(QB[:, 1, 0, :], s0, s1_)
        P.wait("vector", SEM_EV, ev)
        P.run("vector", sp0, inc=SEM_VV)
        vv_spec = P.val(SEM_VV)
        gg_spec = P.val("gg")


        if stop_after == "spec":
            P.wait("gpsimd", SEM_VV, P.val(SEM_VV))
            P.wait("gpsimd", SEM_EV, P.val(SEM_EV))
            for i in range(TPC // 128):
                P.run("gpsimd", lambda e, i=i: e.dma_start(
                    out=out[i * 128:(i + 1) * 128, 0:64], in_=small[:, 0:64]),
                      inc=SEM_GD, inc_amt=16)
            P.wait("gpsimd", SEM_GD, P.val(SEM_GD))
            P.wait("sync", SEM_GD, P.val(SEM_GD))
            P.build()
            return nc, dbg_t
        P.wait("tensor", SEM_VV, vv_spec)
        P.wait("tensor", "gg", gg_spec)
        P.wait("tensor", SEM_EV, P.val(SEM_EV))
        qt_ev = {}
        for grp in range(64):
            pl, s0 = divmod(grp, 32)
            s0 *= 4
            key = grp % 2
            if key in qt_ev:
                P.wait("tensor", SEM_EV, qt_ev[key])
            def qtr4(e, pl=pl, s0=s0, key=key):
                for j in range(4):
                    r = e.matmul(pm[4 + key][:, j * 128:(j + 1) * 128],
                                 QB[:, pl, s0 + j, :], idenB[:, :],
                                 start=True, stop=True)
                return r
            P.run("tensor", qtr4, inc=SEM_T)
            P.wait("scalar", SEM_T, P.val(SEM_T))
            P.run("scalar", lambda e, pl=pl, s0=s0, key=key: e.copy(
                out=Qt[:, pl, s0:s0 + 4, :],
                in_=pm[4 + key][:, 0:512].rearrange("p (j k) -> p j k", j=4)), inc=SEM_EV)
            qt_ev[key] = P.val(SEM_EV)
        ev_qt = P.val(SEM_EV)


        if stop_after == "qt":
            P.wait("gpsimd", SEM_VV, P.val(SEM_VV))
            P.wait("gpsimd", SEM_EV, P.val(SEM_EV))
            for i in range(TPC // 128):
                P.run("gpsimd", lambda e, i=i: e.dma_start(
                    out=out[i * 128:(i + 1) * 128, 0:64], in_=small[:, 0:64]),
                      inc=SEM_GD, inc_amt=16)
            P.wait("gpsimd", SEM_GD, P.val(SEM_GD))
            P.wait("sync", SEM_GD, P.val(SEM_GD))
            P.build()
            return nc, dbg_t
        P.wait("tensor", SEM_EV, ev_qt)
        ibk = Banks([0, 1, 2, 3])
        t_chunks = [(s0_, min(4, 65 - s0_), iWnb) for s0_ in range(0, 65, 4)] + \
                   [(s0_, min(4, 128 - s0_), iWrb) for s0_ in range(65, 128, 4)]
        for (s0_, ns, iwb) in t_chunks:
            bk = ibk.get(2)
            def s1i(e, s0_=s0_, ns=ns, iwb=iwb, bk=bk):
                F = ns * 128
                Qr = Qt[:, 0, s0_:s0_ + ns, :]
                Qi = Qt[:, 1, s0_:s0_ + ns, :]
                Tr = pm[bk[0]][:, 0:F].rearrange("m (s c) -> m s c", s=ns)
                Ti = pm[bk[1]][:, 0:F].rearrange("m (s c) -> m s c", s=ns)
                e.matmul(Tr, iwb[:, 0, :], Qr, start=True, stop=False)
                e.matmul(Tr, iwb[:, 2, :], Qi, start=False, stop=True)
                e.matmul(Ti, iwb[:, 1, :], Qr, start=True, stop=False)
                return e.matmul(Ti, iwb[:, 0, :], Qi, start=False, stop=True)
            P.run("tensor", s1i, inc=SEM_T)
            P.wait("vector", SEM_T, P.val(SEM_T))
            P.wait("scalar", SEM_T, P.val(SEM_T))
            P.run("vector", lambda e, s0_=s0_, ns=ns, bk=bk: e.tensor_copy(
                out=Tb[:, 0, s0_:s0_ + ns, :],
                in_=pm[bk[0]][:, 0:ns * 128].rearrange("m (s c) -> m s c", s=ns)), inc=SEM_VV)
            P.run("scalar", lambda e, s0_=s0_, ns=ns, bk=bk: e.copy(
                out=Tb[:, 1, s0_:s0_ + ns, :],
                in_=pm[bk[1]][:, 0:ns * 128].rearrange("m (s c) -> m s c", s=ns)), inc=SEM_EV)
            P.wait("tensor", SEM_VV, P.val(SEM_VV))
            ibk.done(bk)


        if stop_after == "s1p":
            P.wait("gpsimd", SEM_VV, P.val(SEM_VV))
            P.wait("gpsimd", SEM_EV, P.val(SEM_EV))
            for i in range(TPC // 128):
                P.run("gpsimd", lambda e, i=i: e.dma_start(
                    out=out[i * 128:(i + 1) * 128, 0:64], in_=small[:, 0:64]),
                      inc=SEM_GD, inc_amt=16)
            P.wait("gpsimd", SEM_GD, P.val(SEM_GD))
            P.wait("sync", SEM_GD, P.val(SEM_GD))
            P.build()
            return nc, dbg_t
        P.wait("tensor", SEM_VV, P.val(SEM_VV))
        P.wait("tensor", SEM_EV, P.val(SEM_EV))
        tt_ev = {}
        for grp in range(64):
            pl, c0 = divmod(grp, 32)
            c0 *= 4
            key = grp % 2
            if key in tt_ev:
                P.wait("tensor", SEM_EV, tt_ev[key])
            def ttr4(e, pl=pl, c0=c0, key=key):
                for j in range(4):
                    r = e.matmul(pm[4 + key][:, j * 128:(j + 1) * 128],
                                 Tb[:, pl, 0:128, c0 + j], idenB[:, :],
                                 start=True, stop=True)
                return r
            P.run("tensor", ttr4, inc=SEM_T)
            P.wait("scalar", SEM_T, P.val(SEM_T))
            P.run("scalar", lambda e, pl=pl, c0=c0, key=key: e.copy(
                out=Ttb[:, pl, c0:c0 + 4, :],
                in_=pm[4 + key][:, 0:512].rearrange("p (j k) -> p j k", j=4)), inc=SEM_EV)
            tt_ev[key] = P.val(SEM_EV)
        ev_tt = P.val(SEM_EV)

        P.wait("tensor", SEM_EV, ev_tt)
        zbk = Banks([0, 1, 2, 3])
        MB = 4
        zbk = Banks([0, 1, 2, 3])
        for m2g in range(0, 128, MB):
            b = (m2g // MB) % 2
            if m2g >= 2 * MB:
                P.wait("sync", SEM_T, t_s2z[m2g - 2 * MB])
            def fold(e, m2g=m2g, b=b):
                return e.dma_start(out=Fob2[:, b, :, :, :], in_=h_Fo[m2g // MB])
            P.run("sync", fold, inc=f"fo{b}", inc_amt=16)
            P.wait("tensor", f"fo{b}", 16 * (m2g // (2 * MB) + 1))
            bk = zbk.get(2)
            for i in range(MB):
                m2p = m2g + i
                m2 = 127 - m2p
                def s2z(e, m2p=m2p, m2=m2, b=b, i=i, bk=bk):
                    Tr = Ttb[:, 0, 0:128, m2]
                    Ti = Ttb[:, 1, 0:128, m2]
                    zr = pm[bk[0]][:, i * 64:(i + 1) * 64]
                    zi = pm[bk[1]][:, i * 64:(i + 1) * 64]
                    e.matmul(zr, Tr, Fob2[:, b, i, 0, :], start=True, stop=False)
                    e.matmul(zr, Ti, Fob2[:, b, i, 1, :], start=False, stop=True)
                    e.matmul(zi, Tr, Fob2[:, b, i, 1, :], start=True, stop=False)
                    return e.matmul(zi, Ti, Fob2[:, b, i, 2, :], start=False, stop=True)
                P.run("tensor", s2z, inc=(SEM_T if i == MB - 1 else None))
            if m2g == 0:
                t_s2z = {}
            t_s2z[m2g] = P.val(SEM_T)
            P.wait("vector", SEM_T, P.val(SEM_T))
            P.wait("scalar", SEM_T, P.val(SEM_T))
            def zev(e, m2g=m2g, bk=bk, pl=0):
                dst = bt.AP(tensor=zsend.tensor, offset=pl * 8192 + m2g,
                            ap=[[16384, 128], [128, 64], [1, MB]])
                srcp = bt.AP(tensor=nc.lookup_mloc("pm%d" % bk[pl]) if False else None, offset=0, ap=[[1, 1]]) if False else None
                return e.tensor_copy(out=dst, in_=pm[bk[pl]][:, 0:MB * 64].rearrange("c (g m) -> c m g", g=MB))
            def zev_s(e, m2g=m2g, bk=bk, pl=1):
                dst = bt.AP(tensor=zsend.tensor, offset=pl * 8192 + m2g,
                            ap=[[16384, 128], [128, 64], [1, MB]])
                return e.copy(out=dst, in_=pm[bk[pl]][:, 0:MB * 64].rearrange("c (g m) -> c m g", g=MB))
            P.run("vector", zev, inc=SEM_VV)
            P.run("scalar", zev_s, inc=SEM_EV)
            P.wait("tensor", SEM_VV, P.val(SEM_VV))
            zbk.done(bk)
        vv_z = P.val(SEM_VV)
        ev_z = P.val(SEM_EV)
        t_fft_end = P.val(SEM_T)

        if stop_after == "s2z":
            P.wait("gpsimd", SEM_VV, vv_z)
            P.wait("gpsimd", SEM_EV, ev_z)
            for i in range(TPC // 128):
                P.run("gpsimd", lambda e, i=i: e.dma_start(
                    out=out[i * 128:(i + 1) * 128, 0:64], in_=small[:, 0:64]),
                      inc=SEM_GD, inc_amt=16)
            P.wait("gpsimd", SEM_GD, P.val(SEM_GD))
            P.wait("sync", SEM_GD, P.val(SEM_GD))
            P.build()
            return nc, dbg_t

        P.wait("gpsimd", SEM_VV, vv_z)
        P.wait("gpsimd", SEM_EV, ev_z)
        for p_ in range(2):
            for R_ in range(8):
                def rbd(e, p_=p_, R_=R_):
                    srcap = bt.AP(tensor=zsend.tensor,
                                  offset=p_ * 8192 + R_ * 8 * 128,
                                  ap=[[16384, 128], [128, 8], [1, 128]])
                    dstap = bt.AP(tensor=rb_in.tensor,
                                  offset=R_ * 128 * TPC + p_ * 1024,
                                  ap=[[TPC, 128], [128, 8], [1, 128]])
                    return e.dma_start(out=dstap, in_=srcap)
                P.run("gpsimd", rbd, inc="rbi", inc_amt=16)
        P.wait("gpsimd", "rbi", P.val("rbi"))
        def cc_r_fn(e):
            with allow_ici_all_to_all():
                return e.collective_compute(
                    "AllToAll", ALU.bypass, replica_groups=[list(range(NC))],
                    ins=[rb_in.opt()], outs=[rb_out.opt()])
        P.run("gpsimd", cc_r_fn, inc=SEM_CC)
        cc_r = P.val(SEM_CC)

        if "hF" in dbg:
            t = dout("hF", (128, 8, TT), BF16)
            P.wait("gpsimd", SEM_EV, ev_hF_done)
            P.run("gpsimd", lambda e, t=t: e.dma_start(out=t[:, :, :], in_=hFb[:, :, :]),
                  inc=SEM_GD, inc_amt=16)
        if "x0" in dbg:
            t = dout("x0", (8, 128, TPC), BF16)
            P.wait("gpsimd", "x0s0", 64)
            P.wait("gpsimd", "x0s1", 64)
            P.run("gpsimd", lambda e, t=t: e.dma_start(out=t[:, :, :], in_=x0sp[:, :, :]),
                  inc=SEM_GD, inc_amt=16)
        if "vre" in dbg:
            t = dout("vre", (D, TPC), BF16)
            P.wait("gpsimd", SEM_CC, cc_v)
            P.run("gpsimd", lambda e, t=t: e.dma_start(out=t[:, :], in_=vb_out[:, :]),
                  inc=SEM_GD, inc_amt=16)
        if "rbo" in dbg:
            t = dout("rbo", (D, TPC), BF16)
            P.wait("gpsimd", SEM_CC, cc_r)
            P.run("gpsimd", lambda e, t=t: e.dma_start(out=t[:, :], in_=rb_out[:, :]),
                  inc="dbg2", inc_amt=16)
        if "gout" in dbg:
            t = dout("gout", (960, 4 * 11 * 128), BF16)
            P.wait("gpsimd", SEM_CC, P.val(SEM_CC))
            P.run("gpsimd", lambda e, t=t: e.dma_start(out=t[:, :], in_=gb_out[:, :]),
                  inc=SEM_GD, inc_amt=16)
        if "mod" in dbg:
            t = dout("mod", (128, 64))
            P.wait("gpsimd", SEM_VV, vv_mod)
            P.run("gpsimd", lambda e, t=t: e.dma_start(out=t[:, :], in_=small[:, 0:64]),
                  inc=SEM_GD, inc_amt=16)

        rbi_n = P.val("rbi")
        LM1 = float(L - 1)

        P.run("gpsimd", lambda e: e.dma_start(out=mspill[:, :], in_=small[:, 0:48]),
              inc="msp", inc_amt=16)

        P.wait("sync", SEM_T, t_fft_end)
        P.wait("vector", SEM_T, t_fft_end)
        P.wait("vector", "rbi", rbi_n)
        P.wait("scalar", SEM_T, t_fft_end)
        P.run("vector", lambda e: e.memset(onesF[0:1, :], 1.0), inc=SEM_VV)
        vv_ones = P.val(SEM_VV)
        AM0, AMX, ARI, ALR = 160, 224, 232, 240
        rwn = {0: 0, 1: 0}
        t_amax = {}
        last_ev = [0]
        for dc in range(8):
            b = dc % 2
            if dc >= 2:
                P.wait("sync", SEM_T, t_amax[dc - 2])
            ld(rwb[:, b, :], h_rhs[dc * 128:(dc + 1) * 128, :], inc=f"rw{b}")
            rwn[b] += 16
            P.wait("tensor", f"rw{b}", rwn[b])
            for jg in range(2):
                if last_ev[0]:
                    P.wait("tensor", SEM_EV, last_ev[0])
                def trq(e, b=b, jg=jg):
                    for jj in range(4):
                        r = e.matmul(pm[6][:, jj * 128:(jj + 1) * 128],
                                     rwb[:, b, (jg * 4 + jj) * 128:(jg * 4 + jj + 1) * 128],
                                     idenF[:, :], start=True, stop=True)
                    return r
                P.run("tensor", trq, inc=SEM_T)
                P.wait("scalar", SEM_T, P.val(SEM_T))
                P.wait("scalar", SEM_VV, P.val(SEM_VV))
                P.run("scalar", lambda e, jg=jg: e.activation(
                    out=h2a[:, 0, jg * 512:jg * 512 + 512], in_=pm[6][:, 0:512],
                    func=ACT.Abs), inc=SEM_EV)
                last_ev[0] = P.val(SEM_EV)
            t_amax[dc] = P.val(SEM_T)
            P.wait("vector", SEM_EV, P.val(SEM_EV))
            def pmax(e, dc=dc):
                for jc in range(8):
                    r = e.reduce_max(small[:, AM0 + jc * 8 + dc:AM0 + jc * 8 + dc + 1],
                                     h2a[:, 0, jc * 128:(jc + 1) * 128],
                                     axis=mybir.AxisListType.X)
                return r
            P.run("vector", pmax, inc=SEM_VV)
        def amx(e):
            for jc in range(8):
                e.reduce_max(small[:, AMX + jc:AMX + jc + 1],
                             small[:, AM0 + jc * 8:AM0 + jc * 8 + 8],
                             axis=mybir.AxisListType.X)
            e.tensor_scalar_max(small[:, AMX:AMX + 8], small[:, AMX:AMX + 8], 1e-12)
            e.reciprocal(small[:, ARI:ARI + 8], small[:, AMX:AMX + 8])
            e.tensor_scalar_mul(small[:, ARI:ARI + 8], small[:, ARI:ARI + 8], 127.0)
            return e.tensor_scalar_mul(small[:, ALR:ALR + 8], small[:, AMX:AMX + 8],
                                       1.0 / 127.0)
        P.run("vector", amx, inc=SEM_VV)
        vv_amax = P.val(SEM_VV)
        P.wait("gpsimd", SEM_VV, vv_amax)
        P.run("gpsimd", lambda e: e.dma_start(out=arsp[:, :], in_=small[:, ARI:ARI + 16]),
              inc="ars", inc_amt=16)

        def bcast(dstT, ev_extra=None):
            P.wait("tensor", "rwl", P.val("rwl"))
            P.wait("tensor", SEM_VV, vv_ones)
            if last_ev[0]:
                P.wait("tensor", SEM_EV, last_ev[0])
            def bmm(e):
                e.matmul(pm[5][:, 0:512], onesF[0:1, :], rowb[0:1, 0:512],
                         start=True, stop=True)
                return e.matmul(pm[6][:, 0:512], onesF[0:1, :], rowb[0:1, 512:1024],
                                start=True, stop=True)
            P.run("tensor", bmm, inc=SEM_T)
            t_bc = P.val(SEM_T)
            P.wait("scalar", SEM_T, t_bc)
            if ev_extra is not None:
                P.wait("scalar", SEM_VV, ev_extra)
            def bev(e, dstT=dstT):
                e.copy(out=dstT[:, 0:512], in_=pm[5][:, 0:512])
                return e.copy(out=dstT[:, 512:1024], in_=pm[6][:, 0:512])
            P.run("scalar", bev, inc=SEM_EV)
            last_ev[0] = P.val(SEM_EV)
            return t_bc, last_ev[0]

        def row_from_mspill(off, wait_t=None, wait_vv=None):
            if wait_t is not None:
                P.wait("sync", SEM_T, wait_t)
            if wait_vv is not None:
                P.wait("sync", SEM_VV, wait_vv)
            P.run("sync", lambda e, off=off: e.dma_start(
                out=rowb[0:1, :],
                in_=bt.AP(tensor=mspill.tensor, offset=off, ap=[[1, 8], [48, 128]]),
                allow_slow_non_contiguous=True), inc="rwl", inc_amt=16)

        P.wait("sync", "msp", 16)
        row_from_mspill(GAC)
        t_bc_ga, ev_bc_ga = bcast(cbT)
        P.wait("sync", SEM_T, t_bc_ga)
        P.run("sync", lambda e: e.dma_start(out=rowb[0:1, :], in_=h_b_out[0:1, :]),
              inc="rwl", inc_amt=16)
        P.wait("vector", "rwl", P.val("rwl"))
        P.wait("vector", SEM_EV, ev_bc_ga)
        P.run("vector", lambda e: e.tensor_mul(gborow[0:1, :], rowb[0:1, :], cbT[0:1, :]),
              inc=SEM_VV)
        vv_gbo = P.val(SEM_VV)
        for dc in range(8):
            b = dc % 2
            P.wait("sync", SEM_VV, P.val(SEM_VV))
            ld(rwb[:, b, :], h_W_out[dc * 128:(dc + 1) * 128, :], inc=f"rw{b}")
            rwn[b] += 16
            P.wait("vector", f"rw{b}", rwn[b])
            P.run("vector", lambda e, b=b, dc=dc: e.tensor_mul(
                woutb[:, dc, :], rwb[:, b, :], cbT[:, :]), inc=SEM_VV)
        vv_wout = P.val(SEM_VV)

        row_from_mspill(CB1, wait_vv=vv_gbo)
        bcast(cbT, ev_extra=vv_wout)
        row_from_mspill(SB_, wait_t=P.val(SEM_T))
        bcast(sbT)
        P.wait("sync", "ars", 16)
        def arrow(e, off=0):
            return e.dma_start(out=rowb[0:1, :],
                               in_=bt.AP(tensor=arsp.tensor, offset=off,
                                         ap=[[1, 8], [16, 128]]),
                               allow_slow_non_contiguous=True)
        P.wait("sync", SEM_T, P.val(SEM_T))
        P.run("sync", arrow, inc="rwl", inc_amt=16)
        bcast(arinvT)
        P.wait("sync", SEM_T, P.val(SEM_T))
        P.run("sync", lambda e: arrow(e, off=8), inc="rwl", inc_amt=16)
        bcast(gbarT)
        row_from_mspill(GBC, wait_t=P.val(SEM_T))
        bcast(x2a[:, 0, :])
        ev_tabs = last_ev[0]

        MAGIC = 12582912.0
        P.wait("vector", SEM_EV, ev_tabs)
        for dc in range(8):
            b = dc % 2
            P.wait("sync", SEM_VV, P.val(SEM_VV))
            ld(rwb[:, b, :], h_rhs[dc * 128:(dc + 1) * 128, :], inc=f"rw{b}")
            rwn[b] += 16
            P.wait("vector", f"rw{b}", rwn[b])
            def qr1(e, b=b, dc=dc):
                e.tensor_mul(rwb[:, b, :], rwb[:, b, :], arinvT[:, :])
                e.tensor_scalar(rwb[:, b, :], rwb[:, b, :], -127.0, 127.0, ALU.max, ALU.min)
                e.tensor_scalar(rwb[:, b, :], rwb[:, b, :], MAGIC, MAGIC, ALU.add, ALU.subtract)
                return e.tensor_copy(out=rhq[:, dc, :], in_=rwb[:, b, :])
            P.run("vector", qr1, inc=SEM_VV)
        vv_qr = P.val(SEM_VV)
        def gbar(e):
            return e.tensor_mul(gbarT[:, :], gbarT[:, :], x2a[:, 0, :])
        P.run("vector", gbar, inc=SEM_VV)

        ld(onesr[0:1, :], h_oz[0:1, 0:128], inc="rwl")
        ld(fbtb[:, :], h_fbt[:, :], inc="rwl")
        P.run("gpsimd", lambda e: e.iota(itile[:, :], [[1, 128]], channel_multiplier=0,
                                         allow_small_or_imprecise_dtypes=True), inc="pbc")
        DSE0, DTL0, EB0, ST0 = 100, 108, 116, 132
        def dsetl(e):
            e.tensor_scalar(small[:, 98:99], toffs[:, 0:1], -LM1, 1.0 / LM1,
                            ALU.add, ALU.mult)
            for dc in range(8):
                e.tensor_scalar_mul(small[:, DSE0 + dc:DSE0 + dc + 1],
                                    dts[:, dc:dc + 1], 2.0 / LM1)
                e.tensor_mul(small[:, DTL0 + dc:DTL0 + dc + 1],
                             dts[:, dc:dc + 1], small[:, 98:99])
            return e.engine_nop()
        P.run("vector", dsetl, inc=SEM_VV)
        vv_prep = P.val(SEM_VV)

        P.wait("sync", SEM_CC, cc_r)
        for dc in range(8):
            P.run("sync", lambda e, dc=dc: e.dma_start(
                out=cvF[:, dc, :], in_=rb_out[dc * 128:(dc + 1) * 128, :]),
                  inc="cvl", inc_amt=16)
        P.wait("vector", "cvl", 128)
        P.wait("scalar", SEM_VV, vv_prep)
        P.wait("scalar", "pbc", P.val("pbc"))
        P.wait("tensor", SEM_VV, P.val(SEM_VV))
        P.wait("tensor", "rwl", P.val("rwl"))
        P.wait("vector", "rwl", P.val("rwl"))

        if stop_after == "ret":
            P.wait("gpsimd", "cvl", 128)
            P.wait("gpsimd", SEM_VV, P.val(SEM_VV))
            for i in range(TPC // 128):
                P.run("gpsimd", lambda e, i=i: e.dma_start(
                    out=out[i * 128:(i + 1) * 128, 0:64], in_=small[:, 0:64]),
                      inc=SEM_GD, inc_amt=16)
            P.wait("gpsimd", SEM_GD, P.val(SEM_GD))
            P.wait("sync", SEM_GD, P.val(SEM_GD))
            P.build()
            return nc, dbg_t

        NT = TPC // 128
        t_wout = {}
        t_tp = {}
        t_ffn = {}
        vv_x2 = {}
        vv_fin = {}
        ev_qf = {}
        ev_sq = {}
        od_n = {0: 0, 1: 0}
        xl_n = {0: 0, 1: 0}
        x0l_n = {0: 0, 1: 0}

        def tile_loads(i):
            sl = i % 2
            p_, mb = divmod(i, 8)
            m0 = mb * 128
            if i >= 2:
                P.wait("sync", SEM_VV, vv_x2.get(i - 2, 0))
                P.wait("sync", SEM_VV, vv_fin.get(i - 2, 0))
            def xld(e, p_=p_, m0=m0, sl=sl):
                src = bt.AP(tensor=h_xs.tensor, offset=(HALO + 2 * m0 + p_) * D,
                            ap=[[2 * D, 128], [1, D]])
                dst = bt.AP(tensor=_th(xt3), offset=sl * D, ap=[[2 * D, 128], [1, D]])
                return e.dma_start(out=dst, in_=src)
            P.run("sync", xld, inc=f"xl{sl}", inc_amt=16)
            xl_n[sl] += 16
            for g_ in range(8):
                def x0ld(e, p_=p_, m0=m0, sl=sl, g_=g_):
                    src = bt.AP(tensor=x0sp.tensor,
                                offset=g_ * 128 * TPC + p_ * (TPC // 2) + m0,
                                ap=[[TPC, 128], [1, 128]])
                    dst = bt.AP(tensor=_th(x0t), offset=sl * 1024 + g_ * 128,
                                ap=[[2048, 128], [1, 128]])
                    return e.dma_start(out=dst, in_=src)
                P.run("sync", x0ld, inc=f"x0l{sl}", inc_amt=16)
                x0l_n[sl] += 16

        def emit_fin(i):
            j = i - 1
            if j < 0:
                return
            slj = j % 2
            s3j = j % 3
            pj, mbj = divmod(j, 8)
            m0j = mbj * 128
            STBj = ST0 + slj * 6
            P.wait("vector", SEM_T, t_ffn[j])
            P.wait("vector", f"od{slj}", od_n[slj])
            def fin(e, sl=slj, s3=s3j, STB=STBj):
                for half in range(2):
                    c0 = half * 512
                    e.scalar_tensor_tensor(da[:, sl, c0:c0 + 512], pm[4 + half][:, 0:512],
                                           small[:, STB + 5:STB + 6],
                                           gbarT[:, c0:c0 + 512], ALU.mult, ALU.mult)
                    r = e.tensor_add(da[:, sl, c0:c0 + 512], da[:, sl, c0:c0 + 512],
                                     x2a[:, s3, c0:c0 + 512])
                return r
            P.run("vector", fin, inc=SEM_VV)
            vv_fin[j] = P.val(SEM_VV)
            P.wait("gpsimd", SEM_VV, vv_fin[j])
            def odma(e, sl=slj, p_=pj, m0=m0j):
                dst = bt.AP(tensor=out.tensor, offset=(2 * m0 + p_) * D,
                            ap=[[2 * D, 128], [1, D]])
                return e.dma_start(out=dst, in_=da[:, sl, :])
            P.run("gpsimd", odma, inc=f"od{slj}", inc_amt=16)
            od_n[slj] += 16

        tile_loads(0)
        tile_loads(1)
        for i in range(NT):
            sl = i % 2
            s3 = i % 3
            p_, mb = divmod(i, 8)
            m0 = mb * 128
            wbk = (0, 1) if sl == 0 else (2, 3)
            ebcol = EB0 + sl * 8

            P.wait("vector", f"x0l{sl}", 128 * (i // 2 + 1))
            if i >= 2:
                P.wait("vector", SEM_T, t_wout[i - 2])
            def ebias8(e, m0=m0, p_=p_, ebcol=ebcol):
                for dc in range(8):
                    r = e.scalar_tensor_tensor(
                        small[:, ebcol + dc:ebcol + dc + 1], dts[:, dc:dc + 1],
                        (2 * m0 + p_) / LM1, small[:, DTL0 + dc:DTL0 + dc + 1],
                        ALU.mult, ALU.add)
                return r
            P.run("vector", ebias8, inc=SEM_VV)
            P.wait("scalar", SEM_VV, P.val(SEM_VV))
            if i >= 1:
                P.wait("scalar", SEM_VV, vv_erv_used)
            def exp8(e, ebcol=ebcol):
                for dc in range(8):
                    r = e.activation(
                        out=ervb[:, dc, :], in_=itile[:, :], func=ACT.Exp,
                        scale=small[:, DSE0 + dc:DSE0 + dc + 1],
                        bias=small[:, ebcol + dc:ebcol + dc + 1])
                return r
            P.run("scalar", exp8, inc=SEM_EV)
            P.wait("vector", SEM_EV, P.val(SEM_EV))
            if i >= 2:
                P.wait("vector", f"od{sl}", od_n[sl])
            def gate8(e, sl=sl, p_=p_, m0=m0):
                for dc in range(8):
                    e.tensor_mul(da[:, sl, dc * 128:dc * 128 + 128],
                                 cvF[:, dc, p_ * 1024 + m0:p_ * 1024 + m0 + 128],
                                 ervb[:, dc, :])
                    r = e.scalar_tensor_tensor(
                        ygb[:, sl, dc, :], da[:, sl, dc * 128:dc * 128 + 128],
                        fbtb[:, dc:dc + 1], x0t[:, sl, dc, :], ALU.add, ALU.mult)
                return r
            P.run("vector", gate8, inc=SEM_VV)
            vv_erv_used = P.val(SEM_VV)
            vv_gate = P.val(SEM_VV)

            P.wait("tensor", SEM_VV, vv_gate)
            if i >= 2:
                P.wait("tensor", SEM_VV, vv_x2.get(i - 2, 0))
            def wmm(e, sl=sl, wbk=wbk):
                for half in range(2):
                    ps_ = pm[wbk[half]][:, 0:512]
                    e.matmul(ps_, onesr[0:1, :], gborow[0:1, half * 512:half * 512 + 512],
                             start=True, stop=False)
                    for dc in range(8):
                        r = e.matmul(ps_, ygb[:, sl, dc, :],
                                     woutb[:, dc, half * 512:half * 512 + 512],
                                     start=False, stop=(dc == 7))
                return r
            P.run("tensor", wmm, inc=SEM_T)
            t_wout[i] = P.val(SEM_T)
            emit_fin(i)

            P.wait("vector", SEM_T, t_wout[i])
            P.wait("vector", f"xl{sl}", 16 * (i // 2 + 1))
            if i >= 3:
                P.wait("vector", SEM_VV, vv_fin.get(i - 3, 0))
            if i >= 2:
                P.wait("vector", SEM_VV, vv_ql.get(i - 2, 0))
            STB = ST0 + sl * 6
            def x2st(e, sl=sl, s3=s3, wbk=wbk, STB=STB):
                e.tensor_add(x2a[:, s3, 0:512], pm[wbk[0]][:, 0:512], xt3[:, sl, 0:512])
                e.tensor_add(x2a[:, s3, 512:1024], pm[wbk[1]][:, 0:512], xt3[:, sl, 512:1024])
                e.reduce_sum(small[:, STB:STB + 1], x2a[:, s3, :],
                             axis=mybir.AxisListType.X)
                e.tensor_mul(h2a[:, sl, :], x2a[:, s3, :], x2a[:, s3, :])
                e.reduce_sum(small[:, STB + 1:STB + 2], h2a[:, sl, :],
                             axis=mybir.AxisListType.X)
                e.tensor_scalar_mul(small[:, STB:STB + 1], small[:, STB:STB + 1], -1.0 / D)
                e.tensor_scalar_mul(small[:, STB + 1:STB + 2], small[:, STB + 1:STB + 2], 1.0 / D)
                e.tensor_mul(small[:, STB + 2:STB + 3], small[:, STB:STB + 1], small[:, STB:STB + 1])
                e.tensor_sub(small[:, STB + 1:STB + 2], small[:, STB + 1:STB + 2], small[:, STB + 2:STB + 3])
                return e.tensor_scalar_add(small[:, STB + 1:STB + 2], small[:, STB + 1:STB + 2], 1e-5)
            P.run("vector", x2st, inc=SEM_VV)
            vv_x2[i] = P.val(SEM_VV)
            P.wait("scalar", SEM_VV, vv_x2[i])
            P.run("scalar", lambda e, STB=STB: e.activation(
                out=small[:, STB + 2:STB + 3], in_=small[:, STB + 1:STB + 2],
                func=ACT.Sqrt), inc=SEM_EV)
            P.wait("vector", SEM_EV, P.val(SEM_EV))
            def lnq1(e, sl=sl, s3=s3, STB=STB):
                e.reciprocal(small[:, STB + 2:STB + 3], small[:, STB + 2:STB + 3])
                e.tensor_mul(small[:, STB + 3:STB + 4], small[:, STB:STB + 1],
                             small[:, STB + 2:STB + 3])
                return e.tensor_scalar(h2a[:, sl, :], x2a[:, s3, :],
                                       small[:, STB + 2:STB + 3],
                                       small[:, STB + 3:STB + 4], ALU.mult, ALU.add)
            P.run("vector", lnq1, inc=SEM_VV)
            P.wait("gpsimd", SEM_VV, P.val(SEM_VV))
            def lngp(e, sl=sl):
                e.tensor_tensor(out=h2a[:, sl, :], in0=h2a[:, sl, :], in1=cbT[:, :],
                                op=ALU.mult)
                return e.tensor_tensor(out=h2a[:, sl, :], in0=h2a[:, sl, :],
                                       in1=sbT[:, :], op=ALU.add)
            P.run("gpsimd", lngp, inc="gg")
            P.wait("vector", "gg", P.val("gg"))
            def lnq2(e, sl=sl, STB=STB):
                e.reduce_max(small[:, STB + 4:STB + 5], h2a[:, sl, :],
                             axis=mybir.AxisListType.X, apply_absolute_value=True)
                e.tensor_scalar_max(small[:, STB + 4:STB + 5], small[:, STB + 4:STB + 5], 1e-12)
                e.tensor_scalar_mul(small[:, STB + 5:STB + 6], small[:, STB + 4:STB + 5], 1.0 / 127.0)
                e.reciprocal(small[:, STB + 4:STB + 5], small[:, STB + 4:STB + 5])
                e.tensor_scalar_mul(small[:, STB + 4:STB + 5], small[:, STB + 4:STB + 5], 127.0)
                e.tensor_scalar(h2a[:, sl, :], h2a[:, sl, :], small[:, STB + 4:STB + 5],
                                -127.0, ALU.mult, ALU.max)
                e.tensor_scalar(h2a[:, sl, :], h2a[:, sl, :], 127.0, MAGIC, ALU.min, ALU.add)
                return e.tensor_scalar_sub(qlb[:, sl, :], h2a[:, sl, :], MAGIC)
            P.run("vector", lnq2, inc=SEM_VV)
            if i == 0:
                vv_ql = {}
            vv_ql[i] = P.val(SEM_VV)

            P.wait("tensor", SEM_VV, vv_ql[i])
            if i >= 1:
                P.wait("tensor", SEM_EV, ev_qf[i - 1])
            def qtp(e, sl=sl):
                for dc in range(8):
                    r = e.transpose(pmB[:, dc * 128:(dc + 1) * 128],
                                    qlb[:, sl, dc * 128:(dc + 1) * 128], idenB[:, :])
                return r
            P.run("tensor", qtp, inc=SEM_T)
            t_tp[i] = P.val(SEM_T)
            P.wait("scalar", SEM_T, t_tp[i])
            if i >= 2:
                P.wait("scalar", SEM_T, t_ffn.get(i - 2, 0))
            P.run("scalar", lambda e, sl=sl: e.copy(
                out=qFa[:, sl, :, :], in_=pmB[:, :].rearrange("c (g t) -> c g t", g=8)),
                inc=SEM_EV)
            ev_qf[i] = P.val(SEM_EV)

            P.wait("tensor", SEM_EV, ev_qf[i])
            if i >= 1:
                P.wait("tensor", SEM_VV, vv_fin.get(i - 1, 0))
            def fmm(e, sl=sl):
                for half in range(2):
                    ps_ = pm[4 + half][:, 0:512]
                    for dc in range(8):
                        r = e.matmul(ps_, qFa[:, sl, dc, :],
                                     rhq[:, dc, half * 512:half * 512 + 512],
                                     start=(dc == 0), stop=(dc == 7))
                return r
            P.run("tensor", fmm, inc=SEM_T)
            t_ffn[i] = P.val(SEM_T)

            if i + 2 < NT:
                tile_loads(i + 2)
        emit_fin(NT)

        P.wait("gpsimd", "od0", od_n[0])
        P.wait("gpsimd", "od1", od_n[1])
        P.wait("sync", "od0", od_n[0])
        P.wait("sync", "od1", od_n[1])

        P.build()
    return nc, dbg_t


def _core_inputs(inputs):
    x = np.asarray(inputs["L0_attn_x"])[0]
    W3 = np.asarray(inputs["L0_attn_W3"])
    b3 = np.asarray(inputs["L0_attn_b3"])
    per = []
    for i in range(NC):
        d = {}
        lo = i * TPC - HALO
        if lo < 0:
            xs = np.concatenate([np.zeros((-lo, D), np.float32), x[0:(i + 1) * TPC]], 0)
        else:
            xs = np.asarray(x[lo:(i + 1) * TPC])
        d["xs"] = _f32(xs)
        d["halo_mask"] = np.full((128, 1), 0.0 if i == 0 else 1.0, np.float32)
        d["W2s"] = _f32(np.asarray(inputs["L0_attn_W2"])[:, 8 * i:8 * i + 8])
        d["b2f2"] = _f32(np.stack([np.asarray(inputs["L0_attn_b2"])[8 * i:8 * i + 8],
                                   np.asarray(inputs["L0_attn_f2"])[8 * i:8 * i + 8]], 1))
        W3p = np.zeros((80, 128), np.float32)
        csl = slice(128 * i, 128 * (i + 1))
        for s in range(8):
            W3p[10 * s:10 * s + 8] = W3[8 * s:8 * s + 8, csl]
        W3p[8] = b3[csl]
        d["W3p"] = _f32(W3p)
        d["toff"] = np.full((128, 1), float(i * TPC), np.float32)
        wm = np.asarray(inputs["L0_wmod"])[:, i * 768:(i + 1) * 768]
        d["wms"] = _f32(wm.reshape(8, 128, 6, 128).transpose(2, 0, 1, 3))
        per.append(d)
    return per


def _in_map(inputs, core, tab):
    return {
        "xs": core["xs"],
        "W_in": _f32(inputs["L0_attn_W_in"]),
        "b_in": _f32(np.asarray(inputs["L0_attn_b_in"]).reshape(24, 128).T),
        "short_w": _f32(np.asarray(inputs["L0_attn_short_w"]).reshape(24, 128, 3).transpose(1, 0, 2)),
        "short_b": _f32(np.asarray(inputs["L0_attn_short_b"]).reshape(24, 128).T),
        "W1": _f32(inputs["L0_attn_W1"]),
        "b1f1": _f32(np.stack([np.asarray(inputs["L0_attn_b1"]),
                               np.asarray(inputs["L0_attn_f1"])], 1)),
        "W2s": core["W2s"],
        "b2f2": core["b2f2"],
        "W3p": core["W3p"],
        "fb": _f32(inputs["L0_attn_filter_bias"]).reshape(1, D),
        "fbt": _f32(np.asarray(inputs["L0_attn_filter_bias"]).reshape(8, 128).T),
        "W_out": _f32(inputs["L0_attn_W_out"]),
        "b_out": _f32(inputs["L0_attn_b_out"]).reshape(1, D),
        "rhs": _f32(inputs["L0_ffn_rhs"]),
        "cvec": _f32(np.asarray(inputs["L0_c"]).reshape(8, 128).T),
        "wms": core["wms"],
        "halo_mask": core["halo_mask"],
        "toff": core["toff"],
        "peT": _f32(tab["peT"]),
        "deltas": _f32(tab["deltas"].reshape(8, 128).T),
        "Ws": _b16(np.stack([tab["Ws_re"], tab["Ws_im"]], 0)),
        "F1f": _b16(np.stack([tab["F1f_re"], tab["F1f_im"]], 0)),
        "Bv": _b16(np.asarray(tab["Bv"]).transpose(0, 2, 1, 3)),
        "Htw": _b16(tab["Htw"]),
        "H2st": _b16(tab["H2st"]),
        "iWn": _b16(tab["iWn"]),
        "iWr": _b16(tab["iWr"]),
        "Fo": _b16(np.asarray(tab["Fo"]).reshape(32, 4, 3, 128, 64).transpose(0, 3, 1, 2, 4)),
        "Atabs": _b16(np.asarray(tab["Atabs"]).transpose(1, 0, 2)),
        "idenF": _f32(tab["iden"]),
        "ozrow": _b16(np.stack([np.ones(4096, np.float32), np.zeros(4096, np.float32)], 0)),
        "idenB": _b16(tab["iden"]),
    }


def kernel(**inputs):
    tab = _tables()
    per = _core_inputs(inputs)
    nc, _ = build_nc()
    in_maps = [_in_map(inputs, per[i], tab) for i in range(NC)]
    res = run_bass_kernel_spmd(nc, in_maps, core_ids=list(range(NC)))
    return np.concatenate([r["out"] for r in res.results], 0)[None]


# version 10
# speedup vs baseline: 4.0336x; 1.0196x over previous
import math
import os

import numpy as np
import ml_dtypes

os.environ.setdefault("BASS_INTERP_NUM_WORKERS", "8")

import concourse.bass as bass
import concourse.bass_types as bt
import concourse.mybir as mybir
from concourse.bass_utils import run_bass_kernel_spmd
from concourse.aot_env import allow_ici_all_to_all
from concourse.bass_isa import ReduceOp
from concourse.library_config import attn as LIB_ATTN, standard as LIB_STD

F32 = mybir.dt.float32
BF16 = mybir.dt.bfloat16
I32 = mybir.dt.int32
ACT = mybir.ActivationFunctionType
ALU = mybir.AluOpType
_SHIM = True

L = 16384
D = 1024
N = 2 * L
M = N // 2
NC = 8
TPC = L // NC
HALO = 128
TT = TPC + HALO
WIDTH = 3 * D

bf16 = ml_dtypes.bfloat16
KREV = 127 - np.arange(128)
KWRAP = (128 - np.arange(128)) % 128


def _f32(a):
    return np.ascontiguousarray(a, dtype=np.float32)


def _b16(a):
    return np.ascontiguousarray(np.asarray(a, dtype=np.float32).astype(bf16))


def _host_tables():
    t = {}
    wN = np.exp(-2j * np.pi * np.arange(N) / N)
    a_f = (1 - 1j * wN) / 2
    b_f = (1 + 1j * wN) / 2
    p_f = (1 + 1j * np.conj(wN[:M])) / 2
    q_f = (1 - 1j * np.conj(wN[:M])) / 2
    A1 = p_f * np.conj(a_f[:M]) * a_f[:M] + q_f * np.conj(a_f[M:]) * a_f[M:]
    A2 = p_f * np.conj(a_f[:M]) * b_f[:M] + q_f * np.conj(a_f[M:]) * b_f[M:]
    B1 = p_f * np.conj(b_f[:M]) * a_f[:M] + q_f * np.conj(b_f[M:]) * a_f[M:]
    B2 = p_f * np.conj(b_f[:M]) * b_f[:M] + q_f * np.conj(b_f[M:]) * b_f[M:]
    k2g = np.arange(128)[:, None]
    sg = np.arange(128)[None, :]
    f_nat = k2g * 128 + sg
    f_rev = (127 - k2g) * 128 + sg
    for nm, arr in (("A1", A1), ("A2", A2), ("B1", B1), ("B2", B2)):
        t[nm + "n"] = arr[f_nat]
        t[nm + "r"] = arr[f_rev]
    w128 = np.exp(-2j * np.pi * np.outer(np.arange(128), np.arange(128)) / 128)
    F1 = w128[:64]
    t["Ws_re"] = np.concatenate([F1.real, -F1.imag], 0)
    t["Ws_im"] = np.concatenate([F1.imag, F1.real], 0)
    t["F1f_re"] = np.ascontiguousarray(w128.real)
    t["F1f_im"] = np.ascontiguousarray(w128.imag)
    m2 = np.arange(128)
    vlist = [(a, "nat") for a in range(0, 65)] + [(b, "rev") for b in range(65, 128)]
    vlist += [(64, "rev"), (0, "wrap")]
    t["vlist"] = vlist
    Bv = np.empty((len(vlist), 3, 128, 128), dtype=np.float32)
    for vid, (k1, vm) in enumerate(vlist):
        B = np.exp(-2j * np.pi * (m2[:, None] * k1) / M) * w128
        if vm == "rev":
            B = B[:, KREV]
        elif vm == "wrap":
            B = B[:, KWRAP]
        Bv[vid, 0] = B.real
        Bv[vid, 1] = B.imag
        Bv[vid, 2] = -B.imag
    t["Bv"] = Bv
    tw = np.exp(-2j * np.pi * np.outer(m2, np.arange(128)) / M)
    t["Htw"] = np.stack([tw.real, tw.imag], 0)
    t["H2st"] = np.stack([
        np.stack([w128.real, w128.imag, -w128.imag], 0),
        np.stack([w128[:, KREV].real, w128[:, KREV].imag, -w128[:, KREV].imag], 0),
        np.stack([w128[:, KWRAP].real, w128[:, KWRAP].imag, -w128[:, KWRAP].imag], 0)], 0)
    iW = np.exp(+2j * np.pi * np.outer(np.arange(128), np.arange(128)) / 128)
    iWr = np.exp(+2j * np.pi * np.outer(127 - np.arange(128), np.arange(128)) / 128)
    t["iWn"] = np.stack([iW.real, iW.imag, -iW.imag], 0)
    t["iWr"] = np.stack([iWr.real, iWr.imag, -iWr.imag], 0)
    Fo = np.empty((128, 3, 128, 64), dtype=np.float32)
    m1p = np.arange(64)
    for m2p in range(128):
        m2v = 127 - m2p
        Ft = np.exp(+2j * np.pi * (np.outer(np.arange(128), np.arange(64)) / 128
                                   + (np.arange(128) * m2v / M)[:, None])) / M
        Fo[m2p, 0] = Ft.imag[:, 63 - m1p]
        Fo[m2p, 1] = Ft.real[:, 63 - m1p]
        Fo[m2p, 2] = -Ft.imag[:, 63 - m1p]
    t["Fo"] = Fo
    tg = np.arange(L, dtype=np.float64) / (L - 1)
    ang = 2.0 * math.pi * np.arange(L, dtype=np.float64) / L
    ph = 0.0001 * ang
    t["peT"] = np.stack([tg, np.cos(ph), -np.sin(ph)], 0).astype(np.float32)
    dmax, dmin = (math.log(0.01) / 0.3, math.log(0.01) / 1.5)
    t["deltas"] = np.abs(np.linspace(dmax, dmin, D)).astype(np.float32)
    t["iden"] = np.eye(128, dtype=np.float32)
    t["Atabs"] = np.stack(
        [x for nm in ("A1", "A2", "B1", "B2") for x in (t[nm + "n"].real, t[nm + "n"].imag)]
        + [x for nm in ("A1", "A2", "B1", "B2") for x in (t[nm + "r"].real, t[nm + "r"].imag)], 0)
    tp = 2 * math.pi
    c1 = np.float32(tp); c2 = np.float32(tp - float(c1)); c3 = np.float32(tp - float(c1) - float(c2))
    t["cw"] = (float(c1), float(c2), float(c3))
    return t


_TAB = None


def _tables():
    global _TAB
    if _TAB is None:
        _TAB = _host_tables()
    return _TAB


class Prog:
    ENGINES = ("sync", "gpsimd", "tensor", "vector", "scalar")

    def __init__(self, nc):
        self.nc = nc
        self.st = {e: [] for e in self.ENGINES}
        self.sems = {}
        self.cnt = {}

    def sem(self, name):
        if name not in self.sems:
            self.sems[name] = self.nc.alloc_semaphore(name)
            self.cnt[name] = 0
        return self.sems[name]

    def val(self, name):
        self.sem(name)
        return self.cnt[name]

    def wait(self, eng, name, val=None):
        s = self.sem(name)
        v = self.cnt[name] if val is None else val
        self.st[eng].append(lambda e, s=s, v=v: e.wait_ge(s, v))

    def run(self, eng, fn, inc=None, inc_amt=1, direct=None):
        if inc is None:
            self.st[eng].append(fn)
            return
        s = self.sem(inc)
        self.cnt[inc] += inc_amt
        if direct is None:
            direct = eng in ("sync", "tensor") or inc_amt >= 16 or inc == "cc"
        if direct:
            self.st[eng].append(lambda e, fn=fn, s=s, a=inc_amt: fn(e).then_inc(s, a))
        else:
            def _go(e, fn=fn, s=s, a=inc_amt):
                fn(e)
                e.drain().then_inc(s, a)
            self.st[eng].append(_go)

    def build(self):
        nc = self.nc

        class Shim:
            PASS = ("wait_ge", "engine_nop", "dma_start", "dma_start_transpose",
                    "collective_compute", "sem_inc", "drain", "nop", "wait_op")

            def __init__(self, eng, sem, state):
                self._eng = eng
                self._sem = sem
                self._state = state

            def __getattr__(self, nm):
                f = getattr(self._eng, nm)
                if nm in Shim.PASS or not callable(f):
                    return f

                def wrapped(*a, **k):
                    if self._state[0] > 0:
                        self._eng.wait_ge(self._sem, self._state[0])
                    inst = f(*a, **k)
                    inst.then_inc(self._sem, 1)
                    self._state[0] += 1
                    return inst
                return wrapped

        if not _SHIM:
            class Shim:
                def __new__(cls, eng, sem, state):
                    return eng
        selfsems = {e: (self.sem("self_" + e), [0]) for e in ("vector", "scalar", "gpsimd")}
        with nc.Block() as block:
            @block.sync
            def _(e):
                for fn in self.st["sync"]:
                    fn(e)

            @block.gpsimd
            def _(e):
                sh = Shim(e, *selfsems["gpsimd"])
                for fn in self.st["gpsimd"]:
                    fn(sh)

            @block.tensor
            def _(e):
                for fn in self.st["tensor"]:
                    fn(e)

            @block.vector
            def _(e):
                sh = Shim(e, *selfsems["vector"])
                for fn in self.st["vector"]:
                    fn(sh)

            @block.scalar
            def _(e):
                sh = Shim(e, *selfsems["scalar"])
                for fn in self.st["scalar"]:
                    fn(sh)


def build_nc(dbg=(), stop_after=None):
    tab = _tables()
    dbg = set(dbg)
    nc = bass.Bass("TRN2", target_bir_lowering=False, num_devices=NC)

    def din(name, shape, dt=F32):
        return nc.dram_tensor(name, list(shape), dt, kind="ExternalInput").ap()

    h_xs = din("xs", (TT, D))
    h_W_in = din("W_in", (D, WIDTH))
    h_b_in = din("b_in", (128, 24))
    h_short_w = din("short_w", (128, 24, 3))
    h_short_b = din("short_b", (128, 24))
    h_W1 = din("W1", (3, 64))
    h_b1f1 = din("b1f1", (64, 2))
    h_W2s = din("W2s", (64, 8))
    h_b2f2 = din("b2f2", (8, 2))
    h_W3p = din("W3p", (80, 128))
    h_fb = din("fb", (1, D))
    h_fbt = din("fbt", (128, 8))
    h_W_out = din("W_out", (D, D))
    h_b_out = din("b_out", (1, D))
    h_rhs = din("rhs", (D, D))
    h_cvec = din("cvec", (128, 8))
    h_wms = din("wms", (6, 8, 128, 128))
    h_hmask = din("halo_mask", (128, 1))
    h_toff = din("toff", (128, 1))
    h_peT = din("peT", (3, L))
    h_deltas = din("deltas", (128, 8))
    h_Ws = din("Ws", (2, 128, 128), BF16)
    h_F1f = din("F1f", (2, 128, 128), BF16)
    h_Bv = din("Bv", (130, 128, 3, 128), BF16)
    h_Htw = din("Htw", (2, 128, 128), BF16)
    h_H2st = din("H2st", (3, 3, 128, 128), BF16)
    h_iWn = din("iWn", (3, 128, 128), BF16)
    h_iWr = din("iWr", (3, 128, 128), BF16)
    h_Fo = din("Fo", (32, 128, 4, 3, 64), BF16)
    h_Atabs = din("Atabs", (128, 16, 128), BF16)
    h_idenF = din("idenF", (128, 128))
    h_idenB = din("idenB", (128, 128), BF16)

    out = nc.dram_tensor("out", [TPC, D], F32, kind="ExternalOutput").ap()

    vb_in = nc.dram_tensor("vb_in", [D, TPC], BF16).ap()
    vb_out = nc.dram_tensor("vb_out", [D, TPC], BF16).ap()
    gb_in = nc.dram_tensor("gb_in", [120, 4 * 11 * 128], BF16).ap()
    gb_out = nc.dram_tensor("gb_out", [960, 4 * 11 * 128], BF16).ap()
    rb_in = nc.dram_tensor("rb_in", [D, TPC], BF16).ap()
    rb_out = nc.dram_tensor("rb_out", [D, TPC], BF16).ap()
    zscr = nc.dram_tensor("zscr", [128, 4096], BF16).ap()
    x0sp = nc.dram_tensor("x0sp", [8, 128, TPC], BF16).ap()
    mspill = nc.dram_tensor("mspill", [128, 48], F32).ap()
    arsp = nc.dram_tensor("arsp", [128, 16], F32).ap()
    msin = nc.dram_tensor("msin", [128, 6], F32).ap()
    msout = nc.dram_tensor("msout", [1024, 6], F32).ap()
    h_oz = din("ozrow", (2, 4096), BF16)

    dbg_t = {}

    def dout(name, shape, dt=F32):
        if name in dbg:
            dbg_t[name] = nc.dram_tensor("dbg_" + name, list(shape), dt,
                                         kind="ExternalOutput").ap()
        return dbg_t.get(name)

    P = Prog(nc)
    VL = _tables()["vlist"]

    def _th(x):
        return x.tensor if hasattr(x, "tensor") else x
    from contextlib import ExitStack
    es = ExitStack()

    def sb(name, shape, dt):
        return es.enter_context(nc.sbuf_tensor(name, list(shape), dt))

    def ps(name, cols=512):
        return es.enter_context(nc.psum_tensor(name, [128, cols], F32))

    with es:
        small = sb("small", (128, 256), F32)
        modv = sb("modv", (128, 48), F32)
        silb = sb("silb", (128, 8), F32)
        wmb = sb("wmb", (128, 2, 128), F32)
        bsw = sb("bsw", (128, 24, 3), F32)
        bsb = sb("bsb", (128, 24), F32)
        binb = sb("binb", (128, 24), F32)
        idenF = sb("idenFs", (128, 128), F32)
        hmsk = sb("hmsk", (128, 1), F32)
        toffs = sb("toffs", (128, 1), F32)
        dts = sb("dts", (128, 8), F32)
        x0bb = sb("x0bb", (128, 2, TPC), BF16)
        w1b = sb("w1b", (128, 64), F32)
        b1f1b = sb("b1f1b", (128, 2), F32)
        w2b = sb("w2b", (128, 8), F32)
        b2f2b = sb("b2f2b", (128, 2), F32)
        w3b = sb("w3b", (128, 128), BF16)
        Wsb = sb("Wsb", (128, 2, 128), BF16)
        Htwb = sb("Htwb", (128, 2, 128), BF16)
        H2stb = sb("H2stb", (128, 3, 3, 128), BF16)
        iWnb = sb("iWnb", (128, 3, 128), BF16)
        iWrb = sb("iWrb", (128, 3, 128), BF16)
        Atabsb = sb("Atabsb", (128, 16, 128), BF16)
        idenB = sb("idenBs", (128, 128), BF16)
        Bvb = sb("Bvb", (128, 2, 3, 128), BF16)
        Fob2 = sb("Fob2", (128, 2, 4, 3, 64), BF16)
        Gstb = sb("Gstb", (128, 2, 4, 128), BF16)
        zqb = sb("zqb", (128, 2, 4, 128), BF16)
        rwb = sb("rwb", (128, 2, D), F32)
        fbtb = sb("fbtb", (128, 8), F32)
        rowb = sb("rowb", (1, D), F32)
        gborow = sb("gborow", (1, D), BF16)
        onesr = sb("onesr", (1, 128), BF16)
        onesF = sb("onesF", (1, 128), F32)
        modsh = sb("modsh", (128, 6), F32)

        SB_TOP = 229344
        AR_SZ = 137728
        AR0 = SB_TOP - AR_SZ
        assert nc.sbuf_bytes_remaining >= AR_SZ + 2048, nc.sbuf_bytes_remaining

        def at(name, shape, dt, off):
            nbytes = int(np.prod(shape[1:])) * (2 if dt == BF16 else 4)
            assert off + nbytes <= AR_SZ, (name, off, nbytes)
            flat = int(np.prod(shape[1:]))
            ap = nc.alloc_sbuf_tensor_at(name, [shape[0], flat], dt, offset=AR0 + off).ap()
            if len(shape) == 2:
                return ap
            dims = " ".join(f"d{i}" for i in range(1, len(shape)))
            kw = {f"d{i}": shape[i] for i in range(1, len(shape))}
            return ap.rearrange(f"p ({dims}) -> p {dims}", **kw)

        winb = at("winb", (128, 8, WIDTH), BF16, 0)
        hFb = at("hF", (128, 8, TT), BF16, 49152)
        ub = at("ub", (128, 2, TT), BF16, 83968)
        ucb = at("ucb", (128, 3, TPC), BF16, 92672)
        wvb = at("wvb", (128, 2, TPC), BF16, 104960)
        ramp = at("ramp", (128, TPC), F32, 113152)
        Etile = at("Etile", (128, TPC), BF16, 121344)
        xt2 = at("xt2", (128, 2, D), F32, 125440)
        gsc = at("gsc", (128, TPC), BF16, 125440)
        xhp = at("xhp", (128, D), F32, 133632)
        Abuf = at("Abuf", (128, 2, 128, 128), BF16, 0)
        XQ = at("XQ", (128, 2, 132, 128), BF16, 67584)
        x2a = at("x2a", (128, 3, D), F32, 0)
        da = at("da", (128, 2, D), F32, 12288)
        qFa = at("qFa", (128, 2, 8, 128), BF16, 20480)
        rhq = at("rhq", (128, 8, D), BF16, 24576)
        h2a = at("h2a", (128, 2, D), F32, 40960)
        qlb = at("qlb", (128, 2, 1024), BF16, 49152)
        xt3 = at("xt3", (128, 2, D), F32, 53248)
        cvF = at("cvF", (128, 8, TPC), BF16, 61440)
        cbT = at("cbT", (128, D), F32, 94208)
        sbT = at("sbT", (128, D), F32, 98304)
        x0t = at("x0t", (128, 2, 8, 128), BF16, 102400)
        ygb = at("ygb", (128, 2, 8, 128), BF16, 106496)
        woutb = at("woutb", (128, 8, D), BF16, 110592)
        arinvT = at("arinvT", (128, D), F32, 126976)
        gbarT = at("gbarT", (128, D), F32, 131072)
        ervb = at("ervb", (128, 8, 128), BF16, 135168)
        itile = at("itile", (128, 128), F32, 137216)
        HB = 83968
        peb = at("peb", (128, 2, 512), F32, HB)
        argb = at("argb", (128, 512), F32, HB + 4096)
        rrf = at("rrf", (128, 512), F32, HB + 6144)
        h1cb = at("h1cb", (128, 512), F32, HB + 8192)
        H2buf = at("H2buf", (128, 4096), BF16, HB + 10240)
        zHb = at("zHb", (128, 10, 128), BF16, HB + 18432)
        AHb = at("AHb", (128, 2, 10, 128), BF16, HB + 20992)
        AHtwb = at("AHtwb", (128, 2, 10, 128), BF16, HB + 26112)
        ZHb = at("ZHb", (128, 2, 10, 130), BF16, HB + 31232)
        Gsh = at("Gsh", (128, 4, 128, 10), BF16, HB + 36448)
        Gtsh = at("Gtsh", (128, 11, 4, 128), BF16, HB)
        csc = at("csc", (128, 2, 64, 10), F32, HB + 20992)
        cscg = at("cscg", (128, 2, 64, 10), F32, HB + 26112)

        pm = [ps(f"pm{i}") for i in range(7)]
        pmB = es.enter_context(nc.psum_tensor("pmB", [128, 1024], BF16))
        pmB = pmB.ap() if hasattr(pmB, "ap") else pmB

        SEM_LD = "ld"
        SEM_GD = "gd"
        SEM_T = "t"
        SEM_EV = "ev"
        SEM_VV = "vv"
        SEM_CC = "cc"

        def ld(dst, src, inc=SEM_LD, eng="sync"):
            P.run(eng, lambda e, d=dst, s=src: e.dma_start(out=d, in_=s), inc=inc, inc_amt=16)

        ld(idenF[:, :], h_idenF[:, :])
        ld(hmsk[:, :], h_hmask[:, :])
        ld(toffs[:, :], h_toff[:, :])
        ld(dts[:, :], h_deltas[:, :])
        ld(bsw[:, :, :], h_short_w[:, :, :])
        ld(bsb[:, :], h_short_b[:, :])
        ld(binb[:, :], h_b_in[:, :])
        ld(silb[:, :], h_cvec[:, :])
        P.run("gpsimd", lambda e: e.dma_start(out=winb[:, :, :],
                                              in_=h_W_in.rearrange("(g p) w -> p g w", p=128)),
              inc="winb", inc_amt=16)
        n_ld0 = P.val(SEM_LD)

        P.wait("scalar", SEM_LD, n_ld0)
        P.run("scalar", lambda e: e.activation(out=silb[:, :], in_=silb[:, :], func=ACT.Silu),
              inc=SEM_EV)
        ev_sil = P.val(SEM_EV)
        t_mod_base = P.val(SEM_T)
        k = 0
        for oc in range(6):
            for ci in range(8):
                k += 1
                if k > 2:
                    P.wait("gpsimd", SEM_T, t_mod_base + k - 2)
                P.run("gpsimd",
                      lambda e, b=(k - 1) % 2, o=oc, c=ci: e.dma_start(
                          out=wmb[:, b, :], in_=h_wms[o, c, :, :]),
                      inc=f"wm{(k - 1) % 2}", inc_amt=16)
        P.wait("tensor", SEM_EV, ev_sil)
        k = 0
        for oc in range(6):
            for ci in range(8):
                k += 1
                P.wait("tensor", f"wm{(k - 1) % 2}", 16 * ((k - 1) // 2 + 1))
                P.run("tensor",
                      lambda e, b=(k - 1) % 2, o=oc, c=ci: e.matmul(
                          pm[0][:, o:o + 1], wmb[:, b, :], silb[:, c:c + 1],
                          start=(c == 0), stop=(c == 7)),
                      inc=SEM_T)
        t_mod = P.val(SEM_T)
        P.wait("vector", SEM_T, t_mod)
        P.run("vector", lambda e: e.tensor_copy(out=modsh[:, :], in_=pm[0][:, 0:6]),
              inc=SEM_VV)
        P.wait("gpsimd", SEM_VV, P.val(SEM_VV))
        P.run("gpsimd", lambda e: e.dma_start(out=msin[:, :], in_=modsh[:, :]),
              inc="msi", inc_amt=16)
        P.wait("gpsimd", "msi", 16)
        def cc_mod_fn(e):
            with allow_ici_all_to_all():
                return e.collective_compute(
                    "AllGather", ALU.bypass, replica_groups=[list(range(NC))],
                    ins=[msin.opt()], outs=[msout.opt()])
        P.run("gpsimd", cc_mod_fn, inc=SEM_CC)
        cc_mod = P.val(SEM_CC)
        P.wait("gpsimd", SEM_CC, cc_mod)
        def mgd(e):
            d = bt.AP(tensor=_th(modv), offset=0, ap=[[48, 128], [6, 8], [1, 6]])
            s = bt.AP(tensor=msout.tensor, offset=0, ap=[[6, 128], [768, 8], [1, 6]])
            return e.dma_start(out=d, in_=s, allow_slow_non_contiguous=True)
        P.run("gpsimd", mgd, inc="mgd", inc_amt=16)
        P.wait("vector", "mgd", 16)
        P.wait("scalar", "mgd", 16)
        P.wait("scalar", SEM_VV, P.val(SEM_VV))
        def modsgn(e):
            e.sign(small[:, 48:56], modv[:, 16:24])
            e.sign(small[:, 64:72], modv[:, 40:48])
            e.activation(out=small[:, 56:64], in_=modv[:, 16:24], func=ACT.Abs)
            return e.activation(out=small[:, 72:80], in_=modv[:, 40:48], func=ACT.Abs)
        P.run("scalar", modsgn, inc=SEM_EV)
        P.wait("vector", SEM_EV, P.val(SEM_EV))
        def modpost(e):
            e.tensor_copy(out=small[:, 0:8], in_=modv[:, 0:8])
            e.tensor_scalar_add(small[:, 8:16], modv[:, 8:16], 1.0)
            e.tensor_scalar_add(small[:, 32:40], modv[:, 32:40], 1.0)
            e.tensor_copy(out=small[:, 24:32], in_=modv[:, 24:32])
            for sg, ab, dst in ((48, 56, 16), (64, 72, 40)):
                e.tensor_mul(small[:, 80:88], small[:, sg:sg + 8], small[:, sg:sg + 8])
                e.tensor_sub(small[:, 88:96], small[:, sg:sg + 8], small[:, 80:88])
                e.tensor_scalar_add(small[:, 88:96], small[:, 88:96], 1.0)
                e.tensor_scalar_max(small[:, 80:88], small[:, ab:ab + 8], 0.25)
                e.tensor_mul(small[:, dst:dst + 8], small[:, 88:96], small[:, 80:88])
            return e.engine_nop()
        P.run("vector", modpost, inc=SEM_VV)
        vv_mod = P.val(SEM_VV)
        SA, CA1, GAC, SB_, CB1, GBC = 0, 8, 16, 24, 32, 40

        if stop_after == "mod":
            P.wait("gpsimd", SEM_VV, vv_mod)
            for i in range(TPC // 128):
                P.run("gpsimd", lambda e, i=i: e.dma_start(
                    out=out[i * 128:(i + 1) * 128, 0:64], in_=small[:, 0:64]),
                      inc=SEM_GD, inc_amt=16)
            P.wait("gpsimd", SEM_GD, P.val(SEM_GD))
            P.wait("sync", SEM_GD, P.val(SEM_GD))
            P.build()
            return nc, dbg_t

        ld(w1b[0:3, :], h_W1[:, :], inc="hld")
        ld(b1f1b[0:64, :], h_b1f1[:, :], inc="hld")
        ld(w2b[0:64, 0:8], h_W2s[:, :], inc="hld")
        ld(b2f2b[0:8, :], h_b2f2[:, :], inc="hld")
        for i in range(2):
            ld(Wsb[:, i, :], h_Ws[i, :, :], inc="hld")
            ld(Htwb[:, i, :], h_Htw[i, :, :], inc="hld")
        for c in range(3):
            for i in range(3):
                ld(H2stb[:, c, i, :], h_H2st[c, i, :, :], inc="hld")
        for i in range(3):
            ld(iWnb[:, i, :], h_iWn[i, :, :], inc="hld")
            ld(iWrb[:, i, :], h_iWr[i, :, :], inc="hld")
        ld(Atabsb[:, :, :], h_Atabs[:, :, :], inc="hld")
        ld(idenB[:, :], h_idenB[:, :], inc="hld")
        P.run("gpsimd", lambda e: e.dma_start(out=w3b[0:80, :], in_=h_W3p[:, :]),
              inc="w3l", inc_amt=16)
        n_hld = P.val("hld")
        CW1, CW2, CW3 = tab["cw"]

        P.wait("vector", "hld", n_hld)
        def hpre(e):
            e.tensor_mul(small[0:64, 96:97], b1f1b[0:64, 0:1], b1f1b[0:64, 1:2])
            return e.tensor_mul(small[0:8, 97:98], b2f2b[0:8, 0:1], b2f2b[0:8, 1:2])
        P.run("vector", hpre, inc=SEM_VV)
        vv_hpre = P.val(SEM_VV)
        P.wait("scalar", SEM_VV, vv_hpre)
        P.wait("tensor", "hld", n_hld)

        def rr_sin(rows, dst_ap, ev_arg):
            P.wait("vector", SEM_EV, ev_arg)
            MAGIC = 12582912.0
            def rrv(e):
                e.tensor_scalar_mul(rrf[0:rows, :], argb[0:rows, :], 1.0 / (2 * math.pi))
                e.tensor_scalar(rrf[0:rows, :], rrf[0:rows, :], MAGIC, MAGIC, ALU.add, ALU.subtract)
                e.scalar_tensor_tensor(argb[0:rows, :], rrf[0:rows, :], -CW1,
                                       argb[0:rows, :], ALU.mult, ALU.add)
                e.scalar_tensor_tensor(argb[0:rows, :], rrf[0:rows, :], -CW2,
                                       argb[0:rows, :], ALU.mult, ALU.add)
                e.scalar_tensor_tensor(argb[0:rows, :], rrf[0:rows, :], -CW3,
                                       argb[0:rows, :], ALU.mult, ALU.add)
                return e.tensor_scalar(argb[0:rows, :], argb[0:rows, :],
                                       -math.pi, math.pi, ALU.max, ALU.min)
            P.run("vector", rrv, inc=SEM_VV)
            vvc = P.val(SEM_VV)
            P.wait("scalar", SEM_VV, vvc)
            P.run("scalar", lambda e: e.activation(out=dst_ap, in_=argb[0:rows, :], func=ACT.Sin),
                  inc=SEM_EV)
            return P.val(SEM_EV), vvc

        ev_h1psum = {}
        vv_cody_last = 0
        t_h2mm_last = 0
        for fc in range(32):
            b = fc % 2
            band = fc % 4
            cg = fc // 4
            if fc >= 2:
                P.wait("sync", SEM_T, ev_h1psum[fc - 2])
            ld(peb[0:3, b, :], h_peT[:, fc * 512:(fc + 1) * 512], inc=f"pe{b}")
            P.wait("tensor", f"pe{b}", 16 * (fc // 2 + 1))
            if fc >= 1:
                P.wait("tensor", SEM_EV, ev_arg1[fc - 1])
            P.run("tensor", lambda e, b=b: e.matmul(pm[6][0:64, 0:512], w1b[0:3, 0:64],
                                                    peb[0:3, b, :], start=True, stop=True),
                  inc=SEM_T)
            ev_h1psum[fc] = P.val(SEM_T)
            P.wait("scalar", SEM_T, P.val(SEM_T))
            P.wait("scalar", SEM_VV, vv_cody_last)
            P.run("scalar", lambda e: e.activation(out=argb[0:64, :], in_=pm[6][0:64, 0:512],
                                                   func=ACT.Identity, scale=b1f1b[0:64, 1:2],
                                                   bias=small[0:64, 96:97]), inc=SEM_EV)
            if fc == 0:
                ev_arg1 = {}
            ev_arg1[fc] = P.val(SEM_EV)
            P.wait("scalar", SEM_T, t_h2mm_last)
            ev, vv_cody_last = rr_sin(64, h1cb[0:64, :], ev_arg1[fc])
            P.wait("tensor", SEM_EV, ev)
            P.run("tensor", lambda e: e.matmul(pm[1][0:8, 0:512], w2b[0:64, 0:8],
                                               h1cb[0:64, 0:512], start=True, stop=True),
                  inc=SEM_T)
            t_h2mm_last = P.val(SEM_T)
            P.wait("scalar", SEM_T, P.val(SEM_T))
            P.run("scalar", lambda e: e.activation(out=argb[0:8, :], in_=pm[1][0:8, 0:512],
                                                   func=ACT.Identity, scale=b2f2b[0:8, 1:2],
                                                   bias=small[0:8, 97:98]), inc=SEM_EV)
            h2dst = bt.AP(tensor=_th(H2buf), offset=(band * 32) * 4096 + cg * 512,
                          ap=[[4096, 8], [1, 256], [256, 2]])
            ev, vv_cody_last = rr_sin(8, h2dst, P.val(SEM_EV))
        if stop_after == "hmlp":
            P.wait("gpsimd", SEM_EV, P.val(SEM_EV))
            P.wait("gpsimd", SEM_VV, P.val(SEM_VV))
            for i in range(TPC // 128):
                P.run("gpsimd", lambda e, i=i: e.dma_start(
                    out=out[i * 128:(i + 1) * 128, 0:64], in_=small[:, 0:64]),
                      inc=SEM_GD, inc_amt=16)
            P.wait("gpsimd", SEM_GD, P.val(SEM_GD))
            P.wait("sync", SEM_GD, P.val(SEM_GD))
            P.build()
            return nc, dbg_t

        P.wait("sync", SEM_EV, P.val(SEM_EV))
        P.wait("sync", SEM_VV, P.val(SEM_VV))
        for w_ in range(4):
            P.run("sync", lambda e, w_=w_: e.dma_start(out=zscr[32 * w_:32 * w_ + 8, :],
                                                       in_=H2buf[32 * w_:32 * w_ + 8, :]),
                  inc="zhs", inc_amt=16)
        for w_ in range(4):
            P.run("sync", lambda e, w_=w_: e.dma_start(out=zscr[32 * w_ + 8:32 * w_ + 10, :],
                                                       in_=h_oz[:, :]), inc="zhs", inc_amt=16)
        P.wait("sync", "zhs", P.val("zhs"))
        for p_ in range(2):
            for w_ in range(4):
                for r_ in range(2):
                    for u_ in range(8):
                        def zhdma(e, p_=p_, w_=w_, r_=r_, u_=u_):
                            row = 10 * 128
                            dst = bt.AP(tensor=zHb.tensor,
                                        offset=(p_ * 64 + 8 * u_ + 2 * w_ + r_) * row,
                                        ap=[[row, 1], [128, 10], [1, 128]])
                            v = bt.AP(tensor=zscr.tensor,
                                      offset=(32 * w_) * 4096 + 512 * u_ + 256 * p_ + 128 * r_,
                                      ap=[[4096, 10], [1, 128]])
                            return e.dma_start(out=dst, in_=v)
                        P.run("sync", zhdma, inc="zh", inc_amt=16)
        if stop_after == "zh":
            P.wait("gpsimd", "zh", P.val("zh"))
            for i in range(TPC // 128):
                P.run("gpsimd", lambda e, i=i: e.dma_start(
                    out=out[i * 128:(i + 1) * 128, 0:64], in_=small[:, 0:64]),
                      inc=SEM_GD, inc_amt=16)
            P.wait("gpsimd", SEM_GD, P.val(SEM_GD))
            P.wait("sync", SEM_GD, P.val(SEM_GD))
            P.build()
            return nc, dbg_t

        P.wait("tensor", "zh", P.val("zh"))
        for j in range(10):
            bank = 6 if (j % 2 == 0) else 1
            if j >= 2:
                P.wait("tensor", SEM_EV, ev_ah[j - 2])
            def s1h(e, j=j, bank=bank):
                e.matmul(pm[bank][:, 0:128], zHb[:, j, :], Wsb[:, 0, :], start=True, stop=True)
                return e.matmul(pm[bank][:, 128:256], zHb[:, j, :], Wsb[:, 1, :], start=True, stop=True)
            P.run("tensor", s1h, inc=SEM_T)
            P.wait("scalar", SEM_T, P.val(SEM_T))
            def ahev(e, j=j, bank=bank):
                e.copy(out=AHb[:, 0, j, :], in_=pm[bank][:, 0:128])
                return e.copy(out=AHb[:, 1, j, :], in_=pm[bank][:, 128:256])
            P.run("scalar", ahev, inc=SEM_EV)
            if j == 0:
                ev_ah = {}
            ev_ah[j] = P.val(SEM_EV)
        ev_ah_all = P.val(SEM_EV)
        P.wait("vector", SEM_EV, ev_ah_all)
        def htw(e):
            twr = Htwb[:, 0, :].rearrange("k (o m) -> k o m", o=1).broadcast_to((128, 10, 128))
            twi = Htwb[:, 1, :].rearrange("k (o m) -> k o m", o=1).broadcast_to((128, 10, 128))
            e.tensor_mul(AHtwb[:, 0, :, :], AHb[:, 0, :, :], twr)
            e.tensor_mul(zHb[:, :, :], AHb[:, 1, :, :], twi)
            e.tensor_sub(AHtwb[:, 0, :, :], AHtwb[:, 0, :, :], zHb[:, :, :])
            e.tensor_mul(AHtwb[:, 1, :, :], AHb[:, 0, :, :], twi)
            e.tensor_mul(zHb[:, :, :], AHb[:, 1, :, :], twr)
            return e.tensor_add(AHtwb[:, 1, :, :], AHtwb[:, 1, :, :], zHb[:, :, :])
        P.run("vector", htw, inc=SEM_VV)
        vv_htw = P.val(SEM_VV)
        P.wait("tensor", SEM_VV, vv_htw)
        CLS = ((0, 0, 65, 0), (1, 64, 128, 65), (2, 0, 1, 129))
        hz_k = 0
        hz_banks = {}
        for cls, k1lo, k1hi, slot0 in CLS:
            nk = k1hi - k1lo
            for c0 in range(0, nk, 51):
                cn = min(51, nk - c0)
                bre, bim = ((2, 3), (4, 5))[hz_k % 2]
                hz_k += 1
                if bre in hz_banks:
                    P.wait("tensor", SEM_EV, hz_banks[bre])
                def s2h(e, cls=cls, k1lo=k1lo, c0=c0, cn=cn, bre=bre, bim=bim):
                    mvr = AHtwb[:, 0, :, k1lo + c0:k1lo + c0 + cn]
                    mvi = AHtwb[:, 1, :, k1lo + c0:k1lo + c0 + cn]
                    F = 10 * cn
                    pr = pm[bre][:, 0:F].rearrange("k (j c) -> k j c", j=10)
                    pi = pm[bim][:, 0:F].rearrange("k (j c) -> k j c", j=10)
                    e.matmul(pr, H2stb[:, cls, 0, :], mvr, start=True, stop=False)
                    e.matmul(pr, H2stb[:, cls, 2, :], mvi, start=False, stop=True)
                    e.matmul(pi, H2stb[:, cls, 1, :], mvr, start=True, stop=False)
                    return e.matmul(pi, H2stb[:, cls, 0, :], mvi, start=False, stop=True)
                P.run("tensor", s2h, inc=SEM_T)
                P.wait("scalar", SEM_T, P.val(SEM_T))
                def hzev(e, c0=c0, cn=cn, slot0=slot0, bre=bre, bim=bim):
                    F = 10 * cn
                    e.copy(out=ZHb[:, 0, :, slot0 + c0:slot0 + c0 + cn],
                           in_=pm[bre][:, 0:F].rearrange("k (j c) -> k j c", j=10))
                    return e.copy(out=ZHb[:, 1, :, slot0 + c0:slot0 + c0 + cn],
                                  in_=pm[bim][:, 0:F].rearrange("k (j c) -> k j c", j=10))
                P.run("scalar", hzev, inc=SEM_EV)
                hz_banks[bre] = P.val(SEM_EV)
                ev_hz = P.val(SEM_EV)
        if "zhb" in dbg:
            t = dout("zhb", (128, 2 * 10 * 130), BF16)
            P.wait("gpsimd", SEM_EV, ev_hz)
            P.run("gpsimd", lambda e, t=t: e.dma_start(out=t[:, :], in_=ZHb[:, :, :, :].rearrange("k a b c -> k (a b c)")),
                  inc="dbg1", inc_amt=16)

        if stop_after == "s2h":
            P.wait("gpsimd", SEM_EV, ev_hz)
            for i in range(TPC // 128):
                P.run("gpsimd", lambda e, i=i: e.dma_start(
                    out=out[i * 128:(i + 1) * 128, 0:64], in_=small[:, 0:64]),
                      inc=SEM_GD, inc_amt=16)
            P.wait("gpsimd", SEM_GD, P.val(SEM_GD))
            P.wait("sync", SEM_GD, P.val(SEM_GD))
            P.build()
            return nc, dbg_t

        P.wait("vector", SEM_EV, ev_hz)
        P.wait("gpsimd", SEM_EV, ev_hz)

        def zh_ap(pl, sl0, sl1, step=1):
            base = ZHb[:, pl, :, :]
            if step == 1:
                v = base[:, :, sl0:sl1]
            else:
                v = base[:, :, sl1:sl0:-1] if False else base[:, :, sl0:sl1:step]
            return v.rearrange("k j c -> k c j")

        def tab_ap(t, c0, c1):
            n = c1 - c0
            return Atabsb[:, t, c0:c1].rearrange("k (c o) -> k c o", o=1).broadcast_to((128, n, 10))

        def emit_combine(eng, gplane, outsl, n1ap, pxap, tre, tim, pre, pim, csc0, csc1):
            g, part = gplane
            out = Gsh[:, g, outsl[0]:outsl[1], :]
            Hnr, Hni = n1ap
            Hpr, Hpi = pxap
            if part == "re":
                def f(e):
                    e.tensor_mul(csc0, tre, Hnr)
                    e.tensor_mul(csc1, tim, Hni)
                    e.tensor_sub(csc0, csc0, csc1)
                    e.tensor_mul(csc1, pre, Hpr)
                    e.tensor_add(csc0, csc0, csc1)
                    e.tensor_mul(csc1, pim, Hpi)
                    return e.tensor_add(out, csc0, csc1)
            else:
                def f(e):
                    e.tensor_mul(csc0, tre, Hni)
                    e.tensor_mul(csc1, tim, Hnr)
                    e.tensor_add(csc0, csc0, csc1)
                    e.tensor_mul(csc1, pre, Hpi)
                    e.tensor_sub(csc0, csc0, csc1)
                    e.tensor_mul(csc1, pim, Hpr)
                    return e.tensor_add(out, csc0, csc1)
            P.run(eng, f, inc=(SEM_VV if eng == "vector" else "gg"))

        csc = at("csc", (128, 2, 64, 10), BF16, HB + 11264)
        cscg = at("cscg", (128, 2, 64, 10), BF16, HB + 13824)
        for grp, (c0, c1, nsl, psl, tset) in enumerate((
                (1, 65, (1, 65), (128, 64, -1), 0),
                (65, 128, (66, 129), (63, 0, -1), 8),
                (0, 1, (0, 1), (129, 130), 0))):
            n = c1 - c0
            nr = ZHb[:, 0, :, nsl[0]:nsl[1]].rearrange("k j c -> k c j")
            ni = ZHb[:, 1, :, nsl[0]:nsl[1]].rearrange("k j c -> k c j")
            if len(psl) == 3:
                pr = ZHb[:, 0, :, psl[0]:psl[1]:psl[2]].rearrange("k j c -> k c j")
                pi = ZHb[:, 1, :, psl[0]:psl[1]:psl[2]].rearrange("k j c -> k c j")
            else:
                pr = ZHb[:, 0, :, psl[0]:psl[1]].rearrange("k j c -> k c j")
                pi = ZHb[:, 1, :, psl[0]:psl[1]].rearrange("k j c -> k c j")
            cs0 = csc[:, 0, 0:n, :]
            cs1 = csc[:, 1, 0:n, :]
            for gi, (t1, t2) in enumerate(((0, 2), (4, 6))):
                a1r = tab_ap(tset + t1, c0, c1); a1i = tab_ap(tset + t1 + 1, c0, c1)
                a2r = tab_ap(tset + t2, c0, c1); a2i = tab_ap(tset + t2 + 1, c0, c1)
                emit_combine("vector", (2 * gi, "re"), (c0, c1), (nr, ni), (pr, pi),
                             a1r, a1i, a2r, a2i, cs0, cs1)
                emit_combine("gpsimd", (2 * gi + 1, "im"), (c0, c1), (nr, ni), (pr, pi),
                             a1r, a1i, a2r, a2i, cscg[:, 0, 0:n, :], cscg[:, 1, 0:n, :])
        vv_comb = P.val(SEM_VV)
        gg_comb = P.val("gg")

        if stop_after == "gcomb":
            P.wait("gpsimd", SEM_VV, vv_comb)
            P.wait("gpsimd", "gg", gg_comb)
            for i in range(TPC // 128):
                P.run("gpsimd", lambda e, i=i: e.dma_start(
                    out=out[i * 128:(i + 1) * 128, 0:64], in_=small[:, 0:64]),
                      inc=SEM_GD, inc_amt=16)
            P.wait("gpsimd", SEM_GD, P.val(SEM_GD))
            P.wait("sync", SEM_GD, P.val(SEM_GD))
            P.build()
            return nc, dbg_t

        P.wait("tensor", SEM_VV, vv_comb)
        P.wait("tensor", "gg", gg_comb)
        gt_ev = {}
        for g in range(4):
            for ci in range(11):
                col0 = 12 * ci if ci < 10 else 116
                ncol = 12
                key = (g * 11 + ci) % 2
                if key in gt_ev:
                    P.wait("tensor", "gte", gt_ev[key])
                def gtr(e, g=g, col0=col0, ncol=ncol, key=key):
                    return e.matmul(pm[2 + key][0:ncol * 10, 0:128],
                                    Gsh[:, g, col0:col0 + ncol, :], idenB[:, :],
                                    start=True, stop=True)
                P.run("tensor", gtr, inc=SEM_T)
                P.wait("scalar", SEM_T, P.val(SEM_T))
                def gte(e, g=g, ci=ci, ncol=ncol, key=key):
                    return e.copy(out=Gtsh[0:ncol * 10, ci, g, :],
                                  in_=pm[2 + key][0:ncol * 10, 0:128])
                P.run("scalar", gte, inc="gte")
                gt_ev[key] = P.val("gte")
        P.wait("gpsimd", "gte", P.val("gte"))

        if stop_after == "gtrT":
            for i in range(TPC // 128):
                P.run("gpsimd", lambda e, i=i: e.dma_start(
                    out=out[i * 128:(i + 1) * 128, 0:64], in_=small[:, 0:64]),
                      inc=SEM_GD, inc_amt=16)
            P.wait("gpsimd", SEM_GD, P.val(SEM_GD))
            P.wait("sync", SEM_GD, P.val(SEM_GD))
            P.build()
            return nc, dbg_t

        P.run("gpsimd", lambda e: e.dma_start(out=gb_in[:, :],
                                              in_=Gtsh[0:120, :, :, :].rearrange("p a b c -> p (a b c)")),
              inc="gbi", inc_amt=16)
        P.wait("gpsimd", "gbi", 16)

        if stop_after == "gtr":
            for i in range(TPC // 128):
                P.run("gpsimd", lambda e, i=i: e.dma_start(
                    out=out[i * 128:(i + 1) * 128, 0:64], in_=small[:, 0:64]),
                      inc=SEM_GD, inc_amt=16)
            P.wait("gpsimd", SEM_GD, P.val(SEM_GD))
            P.wait("sync", SEM_GD, P.val(SEM_GD))
            P.build()
            return nc, dbg_t

        def cc_g_fn(e):
            with allow_ici_all_to_all():
                return e.collective_compute(
                    "AllGather", ALU.bypass, replica_groups=[list(range(NC))],
                    ins=[gb_in.opt()], outs=[gb_out.opt()])
        P.run("gpsimd", cc_g_fn, inc=SEM_CC)
        cc_g = P.val(SEM_CC)
        if "zhb" in dbg:
            P.wait("gpsimd", "dbg1", 16)
        P.run("gpsimd", lambda e: e.engine_nop(), inc="hph")
        for engx in ("sync", "scalar", "vector", "tensor"):
            P.wait(engx, "hph", 1)

        if stop_after == "hpipe":
            P.wait("gpsimd", SEM_CC, cc_g)
            for i in range(TPC // 128):
                P.run("gpsimd", lambda e, i=i: e.dma_start(
                    out=out[i * 128:(i + 1) * 128, 0:64], in_=small[:, 0:64]),
                      inc=SEM_GD, inc_amt=16)
            P.wait("gpsimd", SEM_GD, P.val(SEM_GD))
            P.wait("sync", SEM_GD, P.val(SEM_GD))
            P.build()
            return nc, dbg_t

        ev_need_tp = []
        for i in range(TT // 128):
            b = i % 2
            if i >= 2:
                P.wait("sync", SEM_EV, ev_tile[i - 2])
            ld(xt2[:, b, :], h_xs[i * 128:(i + 1) * 128, :], inc=f"xa{b}")
            P.wait("vector", f"xa{b}", 16 * (i // 2 + 1))
            if i > 0:
                P.wait("vector", SEM_T, t_tp_last)
            def stats(e, b=b):
                e.reduce_sum(small[:, 64:65], xt2[:, b, :], axis=mybir.AxisListType.X)
                e.tensor_scalar_mul(small[:, 64:65], small[:, 64:65], 1.0 / D)
                e.tensor_mul(xhp[:, :], xt2[:, b, :], xt2[:, b, :])
                e.reduce_sum(small[:, 65:66], xhp[:, :], axis=mybir.AxisListType.X)
                e.tensor_scalar_mul(small[:, 65:66], small[:, 65:66], 1.0 / D)
                e.tensor_mul(small[:, 66:67], small[:, 64:65], small[:, 64:65])
                e.tensor_sub(small[:, 65:66], small[:, 65:66], small[:, 66:67])
                return e.tensor_scalar_add(small[:, 65:66], small[:, 65:66], 1e-5)
            P.run("vector", stats, inc=SEM_VV)
            P.wait("scalar", SEM_VV, P.val(SEM_VV))
            P.run("scalar", lambda e: e.activation(out=small[:, 65:66], in_=small[:, 65:66],
                                                   func=ACT.Sqrt), inc=SEM_EV)
            P.wait("vector", SEM_EV, P.val(SEM_EV))
            def nmr(e):
                e.reciprocal(small[:, 65:66], small[:, 65:66])
                e.tensor_mul(small[:, 66:67], small[:, 64:65], small[:, 65:66])
                return e.tensor_scalar_mul(small[:, 66:67], small[:, 66:67], -1.0)
            P.run("vector", nmr, inc=SEM_VV)
            P.wait("scalar", SEM_VV, P.val(SEM_VV))
            P.run("scalar", lambda e, b=b: e.activation(out=xhp[:, :], in_=xt2[:, b, :],
                                                        func=ACT.Identity, scale=small[:, 65:66],
                                                        bias=small[:, 66:67]),
                  inc=SEM_EV)
            ev_norm = P.val(SEM_EV)
            if i == 0:
                ev_tile = {}
                ev_tp = {}
            ev_tile[i] = ev_norm
            P.wait("tensor", SEM_EV, ev_norm)
            P.wait("tensor", SEM_VV, vv_mod)
            for r in range(2):
                bank = 2 + r
                if bank in ev_tp:
                    P.wait("tensor", SEM_EV, ev_tp[bank])
                def tr4(e, bank=bank, r=r):
                    for j in range(4):
                        q = e.transpose(pm[bank][:, j * 128:(j + 1) * 128],
                                        xhp[:, (r * 4 + j) * 128:(r * 4 + j + 1) * 128],
                                        idenF[:, :])
                    return q
                P.run("tensor", tr4, inc=SEM_T)
                P.wait("scalar", SEM_T, P.val(SEM_T))
                def ev4(e, bank=bank, r=r, i=i):
                    for j in range(4):
                        dc = r * 4 + j
                        q = e.activation(
                            out=hFb[:, dc, i * 128:(i + 1) * 128],
                            in_=pm[bank][:, j * 128:(j + 1) * 128],
                            func=ACT.Identity, scale=small[:, CA1 + dc:CA1 + dc + 1],
                            bias=small[:, SA + dc:SA + dc + 1])
                    return q
                P.run("scalar", ev4, inc=SEM_EV)
                ev_tp[bank] = P.val(SEM_EV)
            t_tp_last = P.val(SEM_T)
        ev_hF_done = P.val(SEM_EV)

        FCH = [(0, 512), (512, 1024), (1024, 1536), (1536, 2048), (2048, 2176)]
        P.wait("tensor", "winb", 16)
        P.wait("tensor", SEM_EV, ev_hF_done)

        bank_ev = {}
        ub_last_vv = [0, 0]

        def do_win_chunk(oc, ubuf):
            for fi, (f0, f1) in enumerate(FCH):
                bank = 4 + (fi % 2)
                if bank in bank_ev:
                    P.wait("tensor", SEM_EV, bank_ev[bank])
                for ci in range(8):
                    P.run("tensor", lambda e, bank=bank, ci=ci, oc=oc, f0=f0, f1=f1: e.matmul(
                        pm[bank][:, 0:f1 - f0], winb[:, ci, oc * 128:(oc + 1) * 128],
                        hFb[:, ci, f0:f1], start=(ci == 0), stop=(ci == 7)),
                          inc=(SEM_T if ci == 7 else None))
                P.wait("scalar", SEM_T, P.val(SEM_T))
                if fi == 0:
                    P.wait("scalar", SEM_VV, ub_last_vv[ubuf])
                P.run("scalar", lambda e, bank=bank, oc=oc, f0=f0, f1=f1, ubuf=ubuf: e.activation(
                    out=ub[:, ubuf, f0:f1], in_=pm[bank][:, 0:f1 - f0],
                    func=ACT.Identity, bias=binb[:, oc:oc + 1]), inc=SEM_EV)
                bank_ev[bank] = P.val(SEM_EV)
            return P.val(SEM_EV)

        def do_shortconv(oc, ubuf, dst_ap):
            def sc(e, oc=oc, ubuf=ubuf, dst=dst_ap):
                e.tensor_scalar_mul(ub[:, ubuf, 126:128], ub[:, ubuf, 126:128], hmsk[:, 0:1])
                e.tensor_scalar(ucb[:, 2, :], ub[:, ubuf, 126:126 + TPC],
                                bsw[:, oc, 0:1], bsb[:, oc:oc + 1], ALU.mult, ALU.add)
                e.scalar_tensor_tensor(ucb[:, 2, :], ub[:, ubuf, 127:127 + TPC],
                                       bsw[:, oc, 1:2], ucb[:, 2, :], ALU.mult, ALU.add)
                return e.scalar_tensor_tensor(dst, ub[:, ubuf, 128:128 + TPC],
                                              bsw[:, oc, 2:3], ucb[:, 2, :], ALU.mult, ALU.add)
            P.run("vector", sc, inc=SEM_VV)
            ub_last_vv[ubuf] = P.val(SEM_VV)
            return P.val(SEM_VV)

        P.wait("gpsimd", "hph", 1)
        P.run("gpsimd", lambda e: e.iota(ramp[:, :], [[1, TPC]], channel_multiplier=0, allow_small_or_imprecise_dtypes=True),
              inc="rp", inc_amt=1)


        for g in range(8):
            evs = []
            for j, oc in enumerate((8 + g, 16 + g)):
                ubuf = j
                ev_u = do_win_chunk(oc, ubuf)
                P.wait("vector", SEM_EV, ev_u)
                vv_uc = do_shortconv(oc, ubuf, ucb[:, j, :])
                evs.append(vv_uc)
            if g > 0:
                P.wait("vector", SEM_EV, ev_E)
                P.wait("scalar", SEM_VV, vv_gate)
            def envve(e, g=g):
                e.tensor_scalar_mul(small[:, 70:71], dts[:, g:g + 1], -1.0 / (L - 1))
                return e.tensor_mul(small[:, 71:72], small[:, 70:71], toffs[:, 0:1])
            P.run("vector", envve, inc=SEM_VV)
            P.wait("scalar", SEM_VV, P.val(SEM_VV))
            P.wait("scalar", "rp", 1)
            P.run("scalar", lambda e: e.activation(out=Etile[:, :], in_=ramp[:, :], func=ACT.Exp,
                                                   scale=small[:, 70:71], bias=small[:, 71:72]),
                  inc=SEM_EV)
            ev_E = P.val(SEM_EV)
            wbuf = g % 2
            if g >= 2:
                P.wait("vector", f"wv{wbuf}", 16 * (g // 2))
            P.wait("vector", SEM_EV, ev_E)
            def gate(e, g=g, wbuf=wbuf):
                e.tensor_mul(gsc[:, :], ucb[:, 0, :], ucb[:, 1, :])
                dst = bt.AP(tensor=_th(wvb), offset=wbuf * TPC,
                            ap=[[2 * TPC, 128], [1, TPC // 2], [TPC // 2, 2]])
                return e.tensor_mul(dst, gsc[:, :], Etile[:, :])
            P.run("vector", gate, inc=SEM_VV)
            vv_gate = P.val(SEM_VV)
            P.wait("gpsimd", SEM_VV, P.val(SEM_VV))
            P.run("gpsimd", lambda e, g=g, wbuf=wbuf: e.dma_start(
                out=vb_in[g * 128:(g + 1) * 128, :], in_=wvb[:, wbuf, :]),
                  inc=f"wv{wbuf}", inc_amt=16)
        P.wait("gpsimd", "wv0", P.val("wv0"))
        P.wait("gpsimd", "wv1", P.val("wv1"))
        def cc_v_fn(e):
            with allow_ici_all_to_all():
                return e.collective_compute(
                    "AllToAll", ALU.bypass, replica_groups=[list(range(NC))],
                    ins=[vb_in.opt()], outs=[vb_out.opt()])
        P.run("gpsimd", cc_v_fn, inc=SEM_CC)
        cc_v = P.val(SEM_CC)

        for g in range(8):
            sl = g % 2
            if g >= 2:
                P.wait("vector", f"x0s{sl}", 16 * (g // 2))
            ev_u = do_win_chunk(g, 0)
            P.wait("vector", SEM_EV, ev_u)
            x0dst = bt.AP(tensor=_th(x0bb), offset=sl * TPC,
                          ap=[[2 * TPC, 128], [1, TPC // 2], [TPC // 2, 2]])
            vv_sc = do_shortconv(g, 0, x0dst)
            P.wait("gpsimd", SEM_VV, vv_sc)
            P.run("gpsimd", lambda e, g=g, sl=sl: e.dma_start(
                out=x0sp[g, :, :], in_=x0bb[:, sl, :]), inc=f"x0s{sl}", inc_amt=16)
        vv_x0 = P.val(SEM_VV)
        P.wait("gpsimd", "x0s0", 64)
        P.wait("gpsimd", "x0s1", 64)

        if stop_after == "phaseA":
            P.wait("gpsimd", SEM_CC, cc_v)
            for i in range(TPC // 128):
                P.run("gpsimd", lambda e, i=i: e.dma_start(
                    out=out[i * 128:(i + 1) * 128, :], in_=xt2[:, 0, :]),
                      inc=SEM_GD, inc_amt=16)
            P.wait("gpsimd", SEM_GD, P.val(SEM_GD))
            P.build()
            return nc, dbg_t
        QB = at("QB", (128, 2, 132, 128), BF16, 0)
        Qt = at("Qt", (128, 2, 132, 128), BF16, 67584)
        Tb = at("Tb", (128, 2, 132, 128), BF16, 0)
        Ttb = at("Ttb", (128, 2, 132, 128), BF16, 67584)
        zsend = at("zsend", (128, 16384), BF16, 0)
        cbuf2 = sb("cbuf2", (128, 4, 16, 128), BF16)

        class Banks:

            def __init__(self, ids):
                self.ids = list(ids)
                self.k = 0
                self.ev = {}

            def get(self, n=1):
                out = []
                for _ in range(n):
                    b = self.ids[self.k % len(self.ids)]
                    self.k += 1
                    if b in self.ev:
                        P.wait("tensor", SEM_EV, self.ev[b])
                    out.append(b)
                return out if n > 1 else out[0]

            def done(self, bs):
                if isinstance(bs, int):
                    bs = [bs]
                for b in bs:
                    self.ev[b] = P.val(SEM_EV)

        P.wait("sync", SEM_CC, cc_v)
        P.wait("tensor", "hld", n_hld)

        ev_s1q = {}
        t_s1q = {}
        for q in range(32):
            b = q % 2
            c0 = 4 * q
            if q >= 2:
                P.wait("sync", SEM_T, t_s1q[q - 2])
            for j in range(4):
                for p_ in range(2):
                    def zld(e, c=c0 + j, j=j, p_=p_, b=b):
                        dst = bt.AP(tensor=_th(zqb),
                                    offset=(p_ * 64) * 1024 + b * 512 + j * 128,
                                    ap=[[1024, 64], [1, 128]])
                        v = bt.AP(tensor=vb_out.tensor,
                                  offset=c * TPC + p_ * (TPC // 2),
                                  ap=[[128 * TPC, 8], [128, 8], [1, 128]])
                        return e.dma_start(out=dst, in_=v)
                    P.run("sync", zld, inc=f"zq{b}", inc_amt=16)
            P.wait("tensor", f"zq{b}", 128 * (q // 2 + 1))
            if q >= 2:
                P.wait("tensor", SEM_VV, ev_s1q[q - 2][0])
                P.wait("tensor", SEM_EV, ev_s1q[q - 2][1])
            bk = (0, 1) if b == 0 else (2, 3)
            def s1(e, b=b, bk=bk):
                for j in range(4):
                    e.matmul(pm[bk[0]][:, j * 128:(j + 1) * 128], zqb[:, b, j, :],
                             Wsb[:, 0, :], start=True, stop=True)
                    r = e.matmul(pm[bk[1]][:, j * 128:(j + 1) * 128], zqb[:, b, j, :],
                                 Wsb[:, 1, :], start=True, stop=True)
                return r
            P.run("tensor", s1, inc=SEM_T)
            t_s1q[q] = P.val(SEM_T)
            P.wait("vector", SEM_T, t_s1q[q])
            P.wait("scalar", SEM_T, t_s1q[q])
            P.run("vector", lambda e, c0=c0, bk=bk: e.tensor_copy(
                out=Abuf[:, 0, :, c0:c0 + 4],
                in_=pm[bk[0]][:, 0:512].rearrange("p (j k) -> p k j", j=4)), inc=SEM_VV)
            P.run("scalar", lambda e, c0=c0, bk=bk: e.copy(
                out=Abuf[:, 1, :, c0:c0 + 4],
                in_=pm[bk[1]][:, 0:512].rearrange("p (j k) -> p k j", j=4)), inc=SEM_EV)
            ev_s1q[q] = (P.val(SEM_VV), P.val(SEM_EV))
        P.wait("tensor", SEM_VV, P.val(SEM_VV))
        P.wait("tensor", SEM_EV, P.val(SEM_EV))

        if stop_after == "s1":
            P.wait("gpsimd", SEM_VV, P.val(SEM_VV))
            P.wait("gpsimd", SEM_EV, P.val(SEM_EV))
            for i in range(TPC // 128):
                P.run("gpsimd", lambda e, i=i: e.dma_start(
                    out=out[i * 128:(i + 1) * 128, :], in_=xt2[:, 0, :]),
                      inc=SEM_GD, inc_amt=16)
            P.wait("gpsimd", SEM_GD, P.val(SEM_GD))
            P.build()
            return nc, dbg_t
        sbk = Banks([0, 1, 2, 3])
        P.wait("tensor", SEM_VV, P.val(SEM_VV))
        P.wait("tensor", SEM_EV, P.val(SEM_EV))
        for vid in range(130):
            b = vid % 2
            if vid >= 2:
                P.wait("sync", SEM_T, t_s2[vid - 2])
            ld(Bvb[:, b, :, :], h_Bv[vid], inc=f"bv{b}")
            P.wait("tensor", f"bv{b}", 16 * (vid // 2 + 1))
            bk = sbk.get(2)
            def s2f(e, vid=vid, b=b, bk=bk):
                Xr = pm[bk[0]][:, 0:128]
                Xi = pm[bk[1]][:, 0:128]
                e.matmul(Xr, Abuf[:, 0, vid if False else 0, :], Bvb[:, b, 0, :], start=True, stop=False) if False else None
                k1 = VL[vid][0]
                e.matmul(Xr, Abuf[:, 0, k1, :], Bvb[:, b, 0, :], start=True, stop=False)
                e.matmul(Xr, Abuf[:, 1, k1, :], Bvb[:, b, 2, :], start=False, stop=True)
                e.matmul(Xi, Abuf[:, 0, k1, :], Bvb[:, b, 1, :], start=True, stop=False)
                return e.matmul(Xi, Abuf[:, 1, k1, :], Bvb[:, b, 0, :], start=False, stop=True)
            P.run("tensor", s2f, inc=SEM_T)
            if vid == 0:
                t_s2 = {}
            t_s2[vid] = P.val(SEM_T)
            P.wait("vector", SEM_T, P.val(SEM_T))
            P.wait("scalar", SEM_T, P.val(SEM_T))
            P.run("vector", lambda e, vid=vid, bk=bk: e.tensor_copy(out=XQ[:, 0, vid, :],
                                                                    in_=pm[bk[0]][:, 0:128]), inc=SEM_VV)
            P.run("scalar", lambda e, vid=vid, bk=bk: e.copy(out=XQ[:, 1, vid, :],
                                                             in_=pm[bk[1]][:, 0:128]), inc=SEM_EV)
            P.wait("tensor", SEM_VV, P.val(SEM_VV))
            sbk.done(bk)


        if stop_after == "s2f":
            P.wait("gpsimd", SEM_VV, P.val(SEM_VV))
            P.wait("gpsimd", SEM_EV, P.val(SEM_EV))
            for i in range(TPC // 128):
                P.run("gpsimd", lambda e, i=i: e.dma_start(
                    out=out[i * 128:(i + 1) * 128, 0:64], in_=small[:, 0:64]),
                      inc=SEM_GD, inc_amt=16)
            P.wait("gpsimd", SEM_GD, P.val(SEM_GD))
            P.wait("sync", SEM_GD, P.val(SEM_GD))
            P.build()
            return nc, dbg_t
        P.wait("tensor", SEM_CC, cc_g)
        P.wait("tensor", "w3l", 16)
        P.wait("tensor", SEM_VV, P.val(SEM_VV))
        P.wait("tensor", SEM_EV, P.val(SEM_EV))
        cgk = Banks([0, 1, 2, 3, 4, 5])

        def col_ci(col):
            ci = col // 12 if col < 120 else 10
            return ci, col - (12 * ci if col < 120 else 116)

        def emit_cgen(cols, cslots):
            for col, sl in zip(cols, cslots):
                gb = cg_n[0] % 2
                cg_n[0] += 1
                ci, colsub = col_ci(col)
                if gb in cg_t:
                    P.wait("sync", SEM_T, cg_t[gb])
                def gld(e, col=col, ci=ci, colsub=colsub, gb=gb):
                    dst = bt.AP(tensor=_th(Gstb), offset=gb * (4 * 128),
                                ap=[[1024, 80], [1, 512]])
                    v = bt.AP(tensor=gb_out.tensor,
                              offset=(colsub * 10) * 5632 + ci * 512,
                              ap=[[120 * 5632, 8], [5632, 10], [1, 512]])
                    return e.dma_start(out=dst, in_=v)
                P.run("sync", gld, inc=f"gs{gb}", inc_amt=16)
                gst_ev[gb] = gst_ev.get(gb, 0) + 16
                P.wait("tensor", f"gs{gb}", gst_ev[gb])
                bk = cgk.get()
                def cg(e, gb=gb, bk=bk):
                    ps_ = pm[bk]
                    e.matmul(ps_[:, 0:128], w3b[0:80, :], Gstb[0:80, gb, 0, :], start=True, stop=True)
                    e.matmul(ps_[:, 128:256], w3b[0:80, :], Gstb[0:80, gb, 1, :], start=True, stop=True)
                    e.matmul(ps_[:, 256:384], w3b[0:80, :], Gstb[0:80, gb, 2, :], start=True, stop=True)
                    return e.matmul(ps_[:, 384:512], w3b[0:80, :], Gstb[0:80, gb, 3, :], start=True, stop=True)
                P.run("tensor", cg, inc=SEM_T)
                cg_t[gb] = P.val(SEM_T)
                P.wait("scalar", SEM_T, P.val(SEM_T))
                def cgev(e, sl=sl, bk=bk):
                    return e.copy(out=cbuf2[:, :, sl, :],
                                  in_=pm[bk][:, 0:512].rearrange("c (a k) -> c a k", a=4))
                P.run("scalar", cgev, inc=SEM_EV)
                cgk.done(bk)
            return P.val(SEM_EV)

        gst_ev = {}
        cg_t = {}
        cg_n = [0]

        def spectral_batch(a0, n, vq_last, trim=False):
            XNr = XQ[:, 0, a0:a0 + n, :]
            XNi = XQ[:, 1, a0:a0 + n, :]
            XPr = XQ[:, 0, 128 - a0:128 - a0 - n:-1, :]
            XPi = XQ[:, 1, 128 - a0:128 - a0 - n:-1, :]
            c1r = cbuf2[:, 0, 0:n, :]; c1i = cbuf2[:, 1, 0:n, :]
            c2r = cbuf2[:, 2, 0:n, :]; c2i = cbuf2[:, 3, 0:n, :]
            d1r = cbuf2[:, 0, 8:8 + n, :]; d1i = cbuf2[:, 1, 8:8 + n, :]
            d2r = cbuf2[:, 2, 8:8 + n, :]; d2i = cbuf2[:, 3, 8:8 + n, :]
            s0 = scr2[:, 0, 0:n, :]; s1_ = scr2[:, 1, 0:n, :]
            g0 = scr2g[:, 0, 0:n, :]; g1_ = scr2g[:, 1, 0:n, :]
            QNr = QB[:, 0, a0:a0 + n, :]
            QNi = QB[:, 1, a0:a0 + n, :]
            QPr = QB[:, 0, 128 - a0:128 - a0 - n:-1, :]
            QPi = QB[:, 1, 128 - a0:128 - a0 - n:-1, :]
            def vre(e):
                e.tensor_mul(s0, d1r, XPr)
                e.tensor_mul(s1_, d1i, XPi)
                e.tensor_add(s0, s0, s1_)
                e.tensor_mul(s1_, d2r, XNr)
                e.tensor_add(s0, s0, s1_)
                e.tensor_mul(s1_, d2i, XNi)
                e.tensor_sub(QPr, s0, s1_)
                e.tensor_mul(s0, c1r, XNr)
                e.tensor_mul(s1_, c1i, XNi)
                e.tensor_add(s0, s0, s1_)
                e.tensor_mul(s1_, c2r, XPr)
                e.tensor_add(s0, s0, s1_)
                e.tensor_mul(s1_, c2i, XPi)
                return e.tensor_sub(QNr, s0, s1_)
            def vim(e):
                e.tensor_mul(g0, d1i, XPr)
                e.tensor_mul(g1_, d1r, XPi)
                e.tensor_sub(g0, g0, g1_)
                e.tensor_mul(g1_, d2r, XNi)
                e.tensor_add(g0, g0, g1_)
                e.tensor_mul(g1_, d2i, XNr)
                if not trim:
                    return e.tensor_add(QPi, g0, g1_)
                e.tensor_add(g0, g0, g1_)
                return e.tensor_copy(
                    out=QB[:, 1, 128 - a0:128 - a0 - (n - 1):-1, :],
                    in_=scr2g[:, 0, 0:n - 1, :])
            def vimN(e):
                e.tensor_mul(s0, c1i, XNr)
                e.tensor_mul(s1_, c1r, XNi)
                e.tensor_sub(s0, s0, s1_)
                e.tensor_mul(s1_, c2r, XPi)
                e.tensor_add(s0, s0, s1_)
                e.tensor_mul(s1_, c2i, XPr)
                return e.tensor_add(QNi, s0, s1_)
            P.wait("vector", SEM_EV, vq_last)
            P.wait("gpsimd", SEM_EV, vq_last)
            P.run("vector", vre, inc=SEM_VV)
            P.run("gpsimd", vim, inc="gg")
            P.run("vector", vimN, inc=SEM_VV)
            return P.val(SEM_VV), P.val("gg")

        scr2 = sb("scr2", (128, 2, 8, 128), BF16)
        scr2g = sb("scr2g", (128, 2, 8, 128), BF16)
        sp_done = []
        for bi in range(8):
            a0 = 1 + 8 * bi
            cols = list(range(a0, a0 + 8)) + [128 - a for a in range(a0, a0 + 8)]
            cslots = list(range(8)) + list(range(8, 16))
            if bi > 0:
                P.wait("scalar", SEM_VV, sp_done[-1][0])
                P.wait("scalar", "gg", sp_done[-1][1])
            ev = emit_cgen(cols, cslots)
            sp_done.append(spectral_batch(a0, 8, ev, trim=(bi == 7)))
        P.wait("scalar", SEM_VV, sp_done[-1][0])
        P.wait("scalar", "gg", sp_done[-1][1])
        ev = emit_cgen([0], [0])
        def sp0(e):
            XNr = XQ[:, 0, 0, :]; XNi = XQ[:, 1, 0, :]
            XPr = XQ[:, 0, 129, :]; XPi = XQ[:, 1, 129, :]
            c1r = cbuf2[:, 0, 0, :]; c1i = cbuf2[:, 1, 0, :]
            c2r = cbuf2[:, 2, 0, :]; c2i = cbuf2[:, 3, 0, :]
            s0 = scr2[:, 0, 0, :]; s1_ = scr2[:, 1, 0, :]
            e.tensor_mul(s0, c1r, XNr)
            e.tensor_mul(s1_, c1i, XNi)
            e.tensor_add(s0, s0, s1_)
            e.tensor_mul(s1_, c2r, XPr)
            e.tensor_add(s0, s0, s1_)
            e.tensor_mul(s1_, c2i, XPi)
            e.tensor_sub(QB[:, 0, 0, :], s0, s1_)
            e.tensor_mul(s0, c1i, XNr)
            e.tensor_mul(s1_, c1r, XNi)
            e.tensor_sub(s0, s0, s1_)
            e.tensor_mul(s1_, c2r, XPi)
            e.tensor_add(s0, s0, s1_)
            e.tensor_mul(s1_, c2i, XPr)
            return e.tensor_add(QB[:, 1, 0, :], s0, s1_)
        P.wait("vector", SEM_EV, ev)
        P.run("vector", sp0, inc=SEM_VV)
        vv_spec = P.val(SEM_VV)
        gg_spec = P.val("gg")


        if stop_after == "spec":
            P.wait("gpsimd", SEM_VV, P.val(SEM_VV))
            P.wait("gpsimd", SEM_EV, P.val(SEM_EV))
            for i in range(TPC // 128):
                P.run("gpsimd", lambda e, i=i: e.dma_start(
                    out=out[i * 128:(i + 1) * 128, 0:64], in_=small[:, 0:64]),
                      inc=SEM_GD, inc_amt=16)
            P.wait("gpsimd", SEM_GD, P.val(SEM_GD))
            P.wait("sync", SEM_GD, P.val(SEM_GD))
            P.build()
            return nc, dbg_t
        P.wait("tensor", SEM_VV, vv_spec)
        P.wait("tensor", "gg", gg_spec)
        P.wait("tensor", SEM_EV, P.val(SEM_EV))
        qt_ev = {}
        for grp in range(64):
            pl, s0 = divmod(grp, 32)
            s0 *= 4
            key = grp % 2
            if key in qt_ev:
                P.wait("tensor", SEM_EV, qt_ev[key])
            def qtr4(e, pl=pl, s0=s0, key=key):
                for j in range(4):
                    r = e.matmul(pm[4 + key][:, j * 128:(j + 1) * 128],
                                 QB[:, pl, s0 + j, :], idenB[:, :],
                                 start=True, stop=True)
                return r
            P.run("tensor", qtr4, inc=SEM_T)
            P.wait("scalar", SEM_T, P.val(SEM_T))
            P.run("scalar", lambda e, pl=pl, s0=s0, key=key: e.copy(
                out=Qt[:, pl, s0:s0 + 4, :],
                in_=pm[4 + key][:, 0:512].rearrange("p (j k) -> p j k", j=4)), inc=SEM_EV)
            qt_ev[key] = P.val(SEM_EV)
        ev_qt = P.val(SEM_EV)


        if stop_after == "qt":
            P.wait("gpsimd", SEM_VV, P.val(SEM_VV))
            P.wait("gpsimd", SEM_EV, P.val(SEM_EV))
            for i in range(TPC // 128):
                P.run("gpsimd", lambda e, i=i: e.dma_start(
                    out=out[i * 128:(i + 1) * 128, 0:64], in_=small[:, 0:64]),
                      inc=SEM_GD, inc_amt=16)
            P.wait("gpsimd", SEM_GD, P.val(SEM_GD))
            P.wait("sync", SEM_GD, P.val(SEM_GD))
            P.build()
            return nc, dbg_t
        P.wait("tensor", SEM_EV, ev_qt)
        ibk = Banks([0, 1, 2, 3])
        t_chunks = [(s0_, min(4, 65 - s0_), iWnb) for s0_ in range(0, 65, 4)] + \
                   [(s0_, min(4, 128 - s0_), iWrb) for s0_ in range(65, 128, 4)]
        for (s0_, ns, iwb) in t_chunks:
            bk = ibk.get(2)
            def s1i(e, s0_=s0_, ns=ns, iwb=iwb, bk=bk):
                F = ns * 128
                Qr = Qt[:, 0, s0_:s0_ + ns, :]
                Qi = Qt[:, 1, s0_:s0_ + ns, :]
                Tr = pm[bk[0]][:, 0:F].rearrange("m (s c) -> m s c", s=ns)
                Ti = pm[bk[1]][:, 0:F].rearrange("m (s c) -> m s c", s=ns)
                e.matmul(Tr, iwb[:, 0, :], Qr, start=True, stop=False)
                e.matmul(Tr, iwb[:, 2, :], Qi, start=False, stop=True)
                e.matmul(Ti, iwb[:, 1, :], Qr, start=True, stop=False)
                return e.matmul(Ti, iwb[:, 0, :], Qi, start=False, stop=True)
            P.run("tensor", s1i, inc=SEM_T)
            P.wait("vector", SEM_T, P.val(SEM_T))
            P.wait("scalar", SEM_T, P.val(SEM_T))
            P.run("vector", lambda e, s0_=s0_, ns=ns, bk=bk: e.tensor_copy(
                out=Tb[:, 0, s0_:s0_ + ns, :],
                in_=pm[bk[0]][:, 0:ns * 128].rearrange("m (s c) -> m s c", s=ns)), inc=SEM_VV)
            P.run("scalar", lambda e, s0_=s0_, ns=ns, bk=bk: e.copy(
                out=Tb[:, 1, s0_:s0_ + ns, :],
                in_=pm[bk[1]][:, 0:ns * 128].rearrange("m (s c) -> m s c", s=ns)), inc=SEM_EV)
            P.wait("tensor", SEM_VV, P.val(SEM_VV))
            ibk.done(bk)


        if stop_after == "s1p":
            P.wait("gpsimd", SEM_VV, P.val(SEM_VV))
            P.wait("gpsimd", SEM_EV, P.val(SEM_EV))
            for i in range(TPC // 128):
                P.run("gpsimd", lambda e, i=i: e.dma_start(
                    out=out[i * 128:(i + 1) * 128, 0:64], in_=small[:, 0:64]),
                      inc=SEM_GD, inc_amt=16)
            P.wait("gpsimd", SEM_GD, P.val(SEM_GD))
            P.wait("sync", SEM_GD, P.val(SEM_GD))
            P.build()
            return nc, dbg_t
        P.wait("tensor", SEM_VV, P.val(SEM_VV))
        P.wait("tensor", SEM_EV, P.val(SEM_EV))
        tt_ev = {}
        for grp in range(64):
            pl, c0 = divmod(grp, 32)
            c0 *= 4
            key = grp % 2
            if key in tt_ev:
                P.wait("tensor", SEM_EV, tt_ev[key])
            def ttr4(e, pl=pl, c0=c0, key=key):
                for j in range(4):
                    r = e.matmul(pm[4 + key][:, j * 128:(j + 1) * 128],
                                 Tb[:, pl, 0:128, c0 + j], idenB[:, :],
                                 start=True, stop=True)
                return r
            P.run("tensor", ttr4, inc=SEM_T)
            P.wait("scalar", SEM_T, P.val(SEM_T))
            P.run("scalar", lambda e, pl=pl, c0=c0, key=key: e.copy(
                out=Ttb[:, pl, c0:c0 + 4, :],
                in_=pm[4 + key][:, 0:512].rearrange("p (j k) -> p j k", j=4)), inc=SEM_EV)
            tt_ev[key] = P.val(SEM_EV)
        ev_tt = P.val(SEM_EV)

        P.wait("tensor", SEM_EV, ev_tt)
        zbk = Banks([0, 1, 2, 3])
        MB = 4
        zbk = Banks([0, 1, 2, 3])
        for m2g in range(0, 128, MB):
            b = (m2g // MB) % 2
            if m2g >= 2 * MB:
                P.wait("sync", SEM_T, t_s2z[m2g - 2 * MB])
            def fold(e, m2g=m2g, b=b):
                return e.dma_start(out=Fob2[:, b, :, :, :], in_=h_Fo[m2g // MB])
            P.run("sync", fold, inc=f"fo{b}", inc_amt=16)
            P.wait("tensor", f"fo{b}", 16 * (m2g // (2 * MB) + 1))
            bk = zbk.get(2)
            for i in range(MB):
                m2p = m2g + i
                m2 = 127 - m2p
                def s2z(e, m2p=m2p, m2=m2, b=b, i=i, bk=bk):
                    Tr = Ttb[:, 0, 0:128, m2]
                    Ti = Ttb[:, 1, 0:128, m2]
                    zr = pm[bk[0]][:, i * 64:(i + 1) * 64]
                    zi = pm[bk[1]][:, i * 64:(i + 1) * 64]
                    e.matmul(zr, Tr, Fob2[:, b, i, 0, :], start=True, stop=False)
                    e.matmul(zr, Ti, Fob2[:, b, i, 1, :], start=False, stop=True)
                    e.matmul(zi, Tr, Fob2[:, b, i, 1, :], start=True, stop=False)
                    return e.matmul(zi, Ti, Fob2[:, b, i, 2, :], start=False, stop=True)
                P.run("tensor", s2z, inc=(SEM_T if i == MB - 1 else None))
            if m2g == 0:
                t_s2z = {}
            t_s2z[m2g] = P.val(SEM_T)
            P.wait("vector", SEM_T, P.val(SEM_T))
            P.wait("scalar", SEM_T, P.val(SEM_T))
            def zev(e, m2g=m2g, bk=bk, pl=0):
                dst = bt.AP(tensor=zsend.tensor, offset=pl * 8192 + m2g,
                            ap=[[16384, 128], [128, 64], [1, MB]])
                srcp = bt.AP(tensor=nc.lookup_mloc("pm%d" % bk[pl]) if False else None, offset=0, ap=[[1, 1]]) if False else None
                return e.tensor_copy(out=dst, in_=pm[bk[pl]][:, 0:MB * 64].rearrange("c (g m) -> c m g", g=MB))
            def zev_s(e, m2g=m2g, bk=bk, pl=1):
                dst = bt.AP(tensor=zsend.tensor, offset=pl * 8192 + m2g,
                            ap=[[16384, 128], [128, 64], [1, MB]])
                return e.copy(out=dst, in_=pm[bk[pl]][:, 0:MB * 64].rearrange("c (g m) -> c m g", g=MB))
            P.run("vector", zev, inc=SEM_VV)
            P.run("scalar", zev_s, inc=SEM_EV)
            P.wait("tensor", SEM_VV, P.val(SEM_VV))
            zbk.done(bk)
        vv_z = P.val(SEM_VV)
        ev_z = P.val(SEM_EV)
        t_fft_end = P.val(SEM_T)

        if stop_after == "s2z":
            P.wait("gpsimd", SEM_VV, vv_z)
            P.wait("gpsimd", SEM_EV, ev_z)
            for i in range(TPC // 128):
                P.run("gpsimd", lambda e, i=i: e.dma_start(
                    out=out[i * 128:(i + 1) * 128, 0:64], in_=small[:, 0:64]),
                      inc=SEM_GD, inc_amt=16)
            P.wait("gpsimd", SEM_GD, P.val(SEM_GD))
            P.wait("sync", SEM_GD, P.val(SEM_GD))
            P.build()
            return nc, dbg_t

        P.wait("gpsimd", SEM_VV, vv_z)
        P.wait("gpsimd", SEM_EV, ev_z)
        for p_ in range(2):
            for R_ in range(8):
                def rbd(e, p_=p_, R_=R_):
                    srcap = bt.AP(tensor=zsend.tensor,
                                  offset=p_ * 8192 + R_ * 8 * 128,
                                  ap=[[16384, 128], [128, 8], [1, 128]])
                    dstap = bt.AP(tensor=rb_in.tensor,
                                  offset=R_ * 128 * TPC + p_ * 1024,
                                  ap=[[TPC, 128], [128, 8], [1, 128]])
                    return e.dma_start(out=dstap, in_=srcap)
                P.run("gpsimd", rbd, inc="rbi", inc_amt=16)
        P.wait("gpsimd", "rbi", P.val("rbi"))
        def cc_r_fn(e):
            with allow_ici_all_to_all():
                return e.collective_compute(
                    "AllToAll", ALU.bypass, replica_groups=[list(range(NC))],
                    ins=[rb_in.opt()], outs=[rb_out.opt()])
        P.run("gpsimd", cc_r_fn, inc=SEM_CC)
        cc_r = P.val(SEM_CC)

        if "hF" in dbg:
            t = dout("hF", (128, 8, TT), BF16)
            P.wait("gpsimd", SEM_EV, ev_hF_done)
            P.run("gpsimd", lambda e, t=t: e.dma_start(out=t[:, :, :], in_=hFb[:, :, :]),
                  inc=SEM_GD, inc_amt=16)
        if "x0" in dbg:
            t = dout("x0", (8, 128, TPC), BF16)
            P.wait("gpsimd", "x0s0", 64)
            P.wait("gpsimd", "x0s1", 64)
            P.run("gpsimd", lambda e, t=t: e.dma_start(out=t[:, :, :], in_=x0sp[:, :, :]),
                  inc=SEM_GD, inc_amt=16)
        if "vre" in dbg:
            t = dout("vre", (D, TPC), BF16)
            P.wait("gpsimd", SEM_CC, cc_v)
            P.run("gpsimd", lambda e, t=t: e.dma_start(out=t[:, :], in_=vb_out[:, :]),
                  inc=SEM_GD, inc_amt=16)
        if "rbo" in dbg:
            t = dout("rbo", (D, TPC), BF16)
            P.wait("gpsimd", SEM_CC, cc_r)
            P.run("gpsimd", lambda e, t=t: e.dma_start(out=t[:, :], in_=rb_out[:, :]),
                  inc="dbg2", inc_amt=16)
        if "gout" in dbg:
            t = dout("gout", (960, 4 * 11 * 128), BF16)
            P.wait("gpsimd", SEM_CC, P.val(SEM_CC))
            P.run("gpsimd", lambda e, t=t: e.dma_start(out=t[:, :], in_=gb_out[:, :]),
                  inc=SEM_GD, inc_amt=16)
        if "mod" in dbg:
            t = dout("mod", (128, 64))
            P.wait("gpsimd", SEM_VV, vv_mod)
            P.run("gpsimd", lambda e, t=t: e.dma_start(out=t[:, :], in_=small[:, 0:64]),
                  inc=SEM_GD, inc_amt=16)

        rbi_n = P.val("rbi")
        LM1 = float(L - 1)

        P.run("gpsimd", lambda e: e.dma_start(out=mspill[:, :], in_=small[:, 0:48]),
              inc="msp", inc_amt=16)

        P.wait("sync", SEM_T, t_fft_end)
        P.wait("vector", SEM_T, t_fft_end)
        P.wait("vector", "rbi", rbi_n)
        P.wait("scalar", SEM_T, t_fft_end)
        P.run("vector", lambda e: e.memset(onesF[0:1, :], 1.0), inc=SEM_VV)
        vv_ones = P.val(SEM_VV)
        AM0, AMX, ARI, ALR = 160, 224, 232, 240
        rwn = {0: 0, 1: 0}
        t_amax = {}
        last_ev = [0]
        for dc in range(8):
            b = dc % 2
            if dc >= 2:
                P.wait("sync", SEM_T, t_amax[dc - 2])
            ld(rwb[:, b, :], h_rhs[dc * 128:(dc + 1) * 128, :], inc=f"rw{b}")
            rwn[b] += 16
            P.wait("tensor", f"rw{b}", rwn[b])
            for jg in range(2):
                if last_ev[0]:
                    P.wait("tensor", SEM_EV, last_ev[0])
                def trq(e, b=b, jg=jg):
                    for jj in range(4):
                        r = e.matmul(pm[6][:, jj * 128:(jj + 1) * 128],
                                     rwb[:, b, (jg * 4 + jj) * 128:(jg * 4 + jj + 1) * 128],
                                     idenF[:, :], start=True, stop=True)
                    return r
                P.run("tensor", trq, inc=SEM_T)
                P.wait("scalar", SEM_T, P.val(SEM_T))
                P.wait("scalar", SEM_VV, P.val(SEM_VV))
                P.run("scalar", lambda e, jg=jg: e.activation(
                    out=h2a[:, 0, jg * 512:jg * 512 + 512], in_=pm[6][:, 0:512],
                    func=ACT.Abs), inc=SEM_EV)
                last_ev[0] = P.val(SEM_EV)
            t_amax[dc] = P.val(SEM_T)
            P.wait("vector", SEM_EV, P.val(SEM_EV))
            def pmax(e, dc=dc):
                for jc in range(8):
                    r = e.reduce_max(small[:, AM0 + jc * 8 + dc:AM0 + jc * 8 + dc + 1],
                                     h2a[:, 0, jc * 128:(jc + 1) * 128],
                                     axis=mybir.AxisListType.X)
                return r
            P.run("vector", pmax, inc=SEM_VV)
        def amx(e):
            for jc in range(8):
                e.reduce_max(small[:, AMX + jc:AMX + jc + 1],
                             small[:, AM0 + jc * 8:AM0 + jc * 8 + 8],
                             axis=mybir.AxisListType.X)
            e.tensor_scalar_max(small[:, AMX:AMX + 8], small[:, AMX:AMX + 8], 1e-12)
            e.reciprocal(small[:, ARI:ARI + 8], small[:, AMX:AMX + 8])
            e.tensor_scalar_mul(small[:, ARI:ARI + 8], small[:, ARI:ARI + 8], 127.0)
            return e.tensor_scalar_mul(small[:, ALR:ALR + 8], small[:, AMX:AMX + 8],
                                       1.0 / 127.0)
        P.run("vector", amx, inc=SEM_VV)
        vv_amax = P.val(SEM_VV)
        P.wait("gpsimd", SEM_VV, vv_amax)
        P.run("gpsimd", lambda e: e.dma_start(out=arsp[:, :], in_=small[:, ARI:ARI + 16]),
              inc="ars", inc_amt=16)

        def bcast(dstT, ev_extra=None):
            P.wait("tensor", "rwl", P.val("rwl"))
            P.wait("tensor", SEM_VV, vv_ones)
            if last_ev[0]:
                P.wait("tensor", SEM_EV, last_ev[0])
            def bmm(e):
                e.matmul(pm[5][:, 0:512], onesF[0:1, :], rowb[0:1, 0:512],
                         start=True, stop=True)
                return e.matmul(pm[6][:, 0:512], onesF[0:1, :], rowb[0:1, 512:1024],
                                start=True, stop=True)
            P.run("tensor", bmm, inc=SEM_T)
            t_bc = P.val(SEM_T)
            P.wait("scalar", SEM_T, t_bc)
            if ev_extra is not None:
                P.wait("scalar", SEM_VV, ev_extra)
            def bev(e, dstT=dstT):
                e.copy(out=dstT[:, 0:512], in_=pm[5][:, 0:512])
                return e.copy(out=dstT[:, 512:1024], in_=pm[6][:, 0:512])
            P.run("scalar", bev, inc=SEM_EV)
            last_ev[0] = P.val(SEM_EV)
            return t_bc, last_ev[0]

        def row_from_mspill(off, wait_t=None, wait_vv=None):
            if wait_t is not None:
                P.wait("sync", SEM_T, wait_t)
            if wait_vv is not None:
                P.wait("sync", SEM_VV, wait_vv)
            P.run("sync", lambda e, off=off: e.dma_start(
                out=rowb[0:1, :],
                in_=bt.AP(tensor=mspill.tensor, offset=off, ap=[[1, 8], [48, 128]]),
                allow_slow_non_contiguous=True), inc="rwl", inc_amt=16)

        P.wait("sync", "msp", 16)
        row_from_mspill(GAC)
        t_bc_ga, ev_bc_ga = bcast(cbT)
        P.wait("sync", SEM_T, t_bc_ga)
        P.run("sync", lambda e: e.dma_start(out=rowb[0:1, :], in_=h_b_out[0:1, :]),
              inc="rwl", inc_amt=16)
        P.wait("vector", "rwl", P.val("rwl"))
        P.wait("vector", SEM_EV, ev_bc_ga)
        P.run("vector", lambda e: e.tensor_mul(gborow[0:1, :], rowb[0:1, :], cbT[0:1, :]),
              inc=SEM_VV)
        vv_gbo = P.val(SEM_VV)
        for dc in range(8):
            b = dc % 2
            P.wait("sync", SEM_VV, P.val(SEM_VV))
            ld(rwb[:, b, :], h_W_out[dc * 128:(dc + 1) * 128, :], inc=f"rw{b}")
            rwn[b] += 16
            P.wait("vector", f"rw{b}", rwn[b])
            P.run("vector", lambda e, b=b, dc=dc: e.tensor_mul(
                woutb[:, dc, :], rwb[:, b, :], cbT[:, :]), inc=SEM_VV)
        vv_wout = P.val(SEM_VV)

        row_from_mspill(CB1, wait_vv=vv_gbo)
        bcast(cbT, ev_extra=vv_wout)
        row_from_mspill(SB_, wait_t=P.val(SEM_T))
        bcast(sbT)
        P.wait("sync", "ars", 16)
        def arrow(e, off=0):
            return e.dma_start(out=rowb[0:1, :],
                               in_=bt.AP(tensor=arsp.tensor, offset=off,
                                         ap=[[1, 8], [16, 128]]),
                               allow_slow_non_contiguous=True)
        P.wait("sync", SEM_T, P.val(SEM_T))
        P.run("sync", arrow, inc="rwl", inc_amt=16)
        bcast(arinvT)
        P.wait("sync", SEM_T, P.val(SEM_T))
        P.run("sync", lambda e: arrow(e, off=8), inc="rwl", inc_amt=16)
        bcast(gbarT)
        row_from_mspill(GBC, wait_t=P.val(SEM_T))
        bcast(x2a[:, 0, :])
        ev_tabs = last_ev[0]

        MAGIC = 12582912.0
        P.wait("vector", SEM_EV, ev_tabs)
        for dc in range(8):
            b = dc % 2
            P.wait("sync", SEM_VV, P.val(SEM_VV))
            ld(rwb[:, b, :], h_rhs[dc * 128:(dc + 1) * 128, :], inc=f"rw{b}")
            rwn[b] += 16
            P.wait("vector", f"rw{b}", rwn[b])
            def qr1(e, b=b, dc=dc):
                e.tensor_mul(rwb[:, b, :], rwb[:, b, :], arinvT[:, :])
                e.tensor_scalar(rwb[:, b, :], rwb[:, b, :], -127.0, 127.0, ALU.max, ALU.min)
                e.tensor_scalar(rwb[:, b, :], rwb[:, b, :], MAGIC, MAGIC, ALU.add, ALU.subtract)
                return e.tensor_copy(out=rhq[:, dc, :], in_=rwb[:, b, :])
            P.run("vector", qr1, inc=SEM_VV)
        vv_qr = P.val(SEM_VV)
        def gbar(e):
            return e.tensor_mul(gbarT[:, :], gbarT[:, :], x2a[:, 0, :])
        P.run("vector", gbar, inc=SEM_VV)

        ld(onesr[0:1, :], h_oz[0:1, 0:128], inc="rwl")
        ld(fbtb[:, :], h_fbt[:, :], inc="rwl")
        P.run("gpsimd", lambda e: e.iota(itile[:, :], [[1, 128]], channel_multiplier=0,
                                         allow_small_or_imprecise_dtypes=True), inc="pbc")
        DSE0, DTL0, EB0, ST0 = 100, 108, 116, 132
        def dsetl(e):
            e.tensor_scalar(small[:, 98:99], toffs[:, 0:1], -LM1, 1.0 / LM1,
                            ALU.add, ALU.mult)
            for dc in range(8):
                e.tensor_scalar_mul(small[:, DSE0 + dc:DSE0 + dc + 1],
                                    dts[:, dc:dc + 1], 2.0 / LM1)
                e.tensor_mul(small[:, DTL0 + dc:DTL0 + dc + 1],
                             dts[:, dc:dc + 1], small[:, 98:99])
            return e.engine_nop()
        P.run("vector", dsetl, inc=SEM_VV)
        vv_prep = P.val(SEM_VV)

        P.wait("sync", SEM_CC, cc_r)
        for dc in range(8):
            P.run("sync", lambda e, dc=dc: e.dma_start(
                out=cvF[:, dc, :], in_=rb_out[dc * 128:(dc + 1) * 128, :]),
                  inc="cvl", inc_amt=16)
        P.wait("vector", "cvl", 128)
        P.wait("scalar", SEM_VV, vv_prep)
        P.wait("scalar", "pbc", P.val("pbc"))
        P.wait("tensor", SEM_VV, P.val(SEM_VV))
        P.wait("tensor", "rwl", P.val("rwl"))
        P.wait("vector", "rwl", P.val("rwl"))

        if stop_after == "ret":
            P.wait("gpsimd", "cvl", 128)
            P.wait("gpsimd", SEM_VV, P.val(SEM_VV))
            for i in range(TPC // 128):
                P.run("gpsimd", lambda e, i=i: e.dma_start(
                    out=out[i * 128:(i + 1) * 128, 0:64], in_=small[:, 0:64]),
                      inc=SEM_GD, inc_amt=16)
            P.wait("gpsimd", SEM_GD, P.val(SEM_GD))
            P.wait("sync", SEM_GD, P.val(SEM_GD))
            P.build()
            return nc, dbg_t

        NT = TPC // 128
        t_wout = {}
        t_tp = {}
        t_ffn = {}
        vv_x2 = {}
        vv_fin = {}
        ev_qf = {}
        ev_sq = {}
        od_n = {0: 0, 1: 0}
        xl_n = {0: 0, 1: 0}
        x0l_n = {0: 0, 1: 0}

        def tile_loads(i):
            sl = i % 2
            p_, mb = divmod(i, 8)
            m0 = mb * 128
            if i >= 2:
                P.wait("sync", SEM_VV, vv_x2.get(i - 2, 0))
                P.wait("sync", SEM_VV, vv_fin.get(i - 2, 0))
            def xld(e, p_=p_, m0=m0, sl=sl):
                src = bt.AP(tensor=h_xs.tensor, offset=(HALO + 2 * m0 + p_) * D,
                            ap=[[2 * D, 128], [1, D]])
                dst = bt.AP(tensor=_th(xt3), offset=sl * D, ap=[[2 * D, 128], [1, D]])
                return e.dma_start(out=dst, in_=src)
            P.run("sync", xld, inc=f"xl{sl}", inc_amt=16)
            xl_n[sl] += 16
            for g_ in range(8):
                def x0ld(e, p_=p_, m0=m0, sl=sl, g_=g_):
                    src = bt.AP(tensor=x0sp.tensor,
                                offset=g_ * 128 * TPC + p_ * (TPC // 2) + m0,
                                ap=[[TPC, 128], [1, 128]])
                    dst = bt.AP(tensor=_th(x0t), offset=sl * 1024 + g_ * 128,
                                ap=[[2048, 128], [1, 128]])
                    return e.dma_start(out=dst, in_=src)
                P.run("sync", x0ld, inc=f"x0l{sl}", inc_amt=16)
                x0l_n[sl] += 16

        def emit_fin(i):
            j = i - 1
            if j < 0:
                return
            slj = j % 2
            s3j = j % 3
            pj, mbj = divmod(j, 8)
            m0j = mbj * 128
            STBj = ST0 + slj * 6
            P.wait("vector", SEM_T, t_ffn[j])
            P.wait("vector", f"od{slj}", od_n[slj])
            def fin(e, sl=slj, s3=s3j, STB=STBj):
                for half in range(2):
                    c0 = half * 512
                    e.scalar_tensor_tensor(da[:, sl, c0:c0 + 512], pm[4 + half][:, 0:512],
                                           small[:, STB + 5:STB + 6],
                                           gbarT[:, c0:c0 + 512], ALU.mult, ALU.mult)
                    r = e.tensor_add(da[:, sl, c0:c0 + 512], da[:, sl, c0:c0 + 512],
                                     x2a[:, s3, c0:c0 + 512])
                return r
            P.run("vector", fin, inc=SEM_VV)
            vv_fin[j] = P.val(SEM_VV)
            P.wait("gpsimd", SEM_VV, vv_fin[j])
            def odma(e, sl=slj, p_=pj, m0=m0j):
                dst = bt.AP(tensor=out.tensor, offset=(2 * m0 + p_) * D,
                            ap=[[2 * D, 128], [1, D]])
                return e.dma_start(out=dst, in_=da[:, sl, :])
            P.run("gpsimd", odma, inc=f"od{slj}", inc_amt=16)
            od_n[slj] += 16

        tile_loads(0)
        tile_loads(1)
        for i in range(NT):
            sl = i % 2
            s3 = i % 3
            p_, mb = divmod(i, 8)
            m0 = mb * 128
            wbk = (0, 1) if sl == 0 else (2, 3)
            ebcol = EB0 + sl * 8

            P.wait("vector", f"x0l{sl}", 128 * (i // 2 + 1))
            if i >= 2:
                P.wait("vector", SEM_T, t_wout[i - 2])
            def ebias8(e, m0=m0, p_=p_, ebcol=ebcol):
                for dc in range(8):
                    r = e.scalar_tensor_tensor(
                        small[:, ebcol + dc:ebcol + dc + 1], dts[:, dc:dc + 1],
                        (2 * m0 + p_) / LM1, small[:, DTL0 + dc:DTL0 + dc + 1],
                        ALU.mult, ALU.add)
                return r
            P.run("vector", ebias8, inc=SEM_VV)
            P.wait("scalar", SEM_VV, P.val(SEM_VV))
            if i >= 1:
                P.wait("scalar", SEM_VV, vv_erv_used)
            def exp8(e, ebcol=ebcol):
                for dc in range(8):
                    r = e.activation(
                        out=ervb[:, dc, :], in_=itile[:, :], func=ACT.Exp,
                        scale=small[:, DSE0 + dc:DSE0 + dc + 1],
                        bias=small[:, ebcol + dc:ebcol + dc + 1])
                return r
            P.run("scalar", exp8, inc=SEM_EV)
            P.wait("vector", SEM_EV, P.val(SEM_EV))
            if i >= 2:
                P.wait("vector", f"od{sl}", od_n[sl])
            def gate8(e, sl=sl, p_=p_, m0=m0):
                for dc in range(8):
                    e.tensor_mul(da[:, sl, dc * 128:dc * 128 + 128],
                                 cvF[:, dc, p_ * 1024 + m0:p_ * 1024 + m0 + 128],
                                 ervb[:, dc, :])
                    r = e.scalar_tensor_tensor(
                        ygb[:, sl, dc, :], da[:, sl, dc * 128:dc * 128 + 128],
                        fbtb[:, dc:dc + 1], x0t[:, sl, dc, :], ALU.add, ALU.mult)
                return r
            P.run("vector", gate8, inc=SEM_VV)
            vv_erv_used = P.val(SEM_VV)
            vv_gate = P.val(SEM_VV)

            P.wait("tensor", SEM_VV, vv_gate)
            if i >= 2:
                P.wait("tensor", SEM_VV, vv_x2.get(i - 2, 0))
            def wmm(e, sl=sl, wbk=wbk):
                for half in range(2):
                    ps_ = pm[wbk[half]][:, 0:512]
                    e.matmul(ps_, onesr[0:1, :], gborow[0:1, half * 512:half * 512 + 512],
                             start=True, stop=False)
                    for dc in range(8):
                        r = e.matmul(ps_, ygb[:, sl, dc, :],
                                     woutb[:, dc, half * 512:half * 512 + 512],
                                     start=False, stop=(dc == 7))
                return r
            P.run("tensor", wmm, inc=SEM_T)
            t_wout[i] = P.val(SEM_T)
            emit_fin(i)

            P.wait("vector", SEM_T, t_wout[i])
            P.wait("vector", f"xl{sl}", 16 * (i // 2 + 1))
            if i >= 3:
                P.wait("vector", SEM_VV, vv_fin.get(i - 3, 0))
            if i >= 2:
                P.wait("vector", SEM_VV, vv_ql.get(i - 2, 0))
            STB = ST0 + sl * 6
            def x2st(e, sl=sl, s3=s3, wbk=wbk, STB=STB):
                e.tensor_add(x2a[:, s3, 0:512], pm[wbk[0]][:, 0:512], xt3[:, sl, 0:512])
                e.tensor_add(x2a[:, s3, 512:1024], pm[wbk[1]][:, 0:512], xt3[:, sl, 512:1024])
                e.reduce_sum(small[:, STB:STB + 1], x2a[:, s3, :],
                             axis=mybir.AxisListType.X)
                e.tensor_mul(h2a[:, sl, :], x2a[:, s3, :], x2a[:, s3, :])
                e.reduce_sum(small[:, STB + 1:STB + 2], h2a[:, sl, :],
                             axis=mybir.AxisListType.X)
                e.tensor_scalar_mul(small[:, STB:STB + 1], small[:, STB:STB + 1], -1.0 / D)
                e.tensor_scalar_mul(small[:, STB + 1:STB + 2], small[:, STB + 1:STB + 2], 1.0 / D)
                e.tensor_mul(small[:, STB + 2:STB + 3], small[:, STB:STB + 1], small[:, STB:STB + 1])
                e.tensor_sub(small[:, STB + 1:STB + 2], small[:, STB + 1:STB + 2], small[:, STB + 2:STB + 3])
                return e.tensor_scalar_add(small[:, STB + 1:STB + 2], small[:, STB + 1:STB + 2], 1e-5)
            P.run("vector", x2st, inc=SEM_VV)
            vv_x2[i] = P.val(SEM_VV)
            P.wait("scalar", SEM_VV, vv_x2[i])
            P.run("scalar", lambda e, STB=STB: e.activation(
                out=small[:, STB + 2:STB + 3], in_=small[:, STB + 1:STB + 2],
                func=ACT.Sqrt), inc=SEM_EV)
            P.wait("vector", SEM_EV, P.val(SEM_EV))
            def lnq1(e, sl=sl, s3=s3, STB=STB):
                e.reciprocal(small[:, STB + 2:STB + 3], small[:, STB + 2:STB + 3])
                e.tensor_mul(small[:, STB + 3:STB + 4], small[:, STB:STB + 1],
                             small[:, STB + 2:STB + 3])
                return e.tensor_scalar(h2a[:, sl, :], x2a[:, s3, :],
                                       small[:, STB + 2:STB + 3],
                                       small[:, STB + 3:STB + 4], ALU.mult, ALU.add)
            P.run("vector", lnq1, inc=SEM_VV)
            P.wait("gpsimd", SEM_VV, P.val(SEM_VV))
            def lngp(e, sl=sl):
                e.tensor_tensor(out=h2a[:, sl, :], in0=h2a[:, sl, :], in1=cbT[:, :],
                                op=ALU.mult)
                return e.tensor_tensor(out=h2a[:, sl, :], in0=h2a[:, sl, :],
                                       in1=sbT[:, :], op=ALU.add)
            P.run("gpsimd", lngp, inc="gg")
            P.wait("vector", "gg", P.val("gg"))
            def lnq2(e, sl=sl, STB=STB):
                e.reduce_max(small[:, STB + 4:STB + 5], h2a[:, sl, :],
                             axis=mybir.AxisListType.X, apply_absolute_value=True)
                e.tensor_scalar_max(small[:, STB + 4:STB + 5], small[:, STB + 4:STB + 5], 1e-12)
                e.tensor_scalar_mul(small[:, STB + 5:STB + 6], small[:, STB + 4:STB + 5], 1.0 / 127.0)
                e.reciprocal(small[:, STB + 4:STB + 5], small[:, STB + 4:STB + 5])
                e.tensor_scalar_mul(small[:, STB + 4:STB + 5], small[:, STB + 4:STB + 5], 127.0)
                e.tensor_scalar(h2a[:, sl, :], h2a[:, sl, :], small[:, STB + 4:STB + 5],
                                -127.0, ALU.mult, ALU.max)
                e.tensor_scalar(h2a[:, sl, :], h2a[:, sl, :], 127.0, MAGIC, ALU.min, ALU.add)
                return e.tensor_scalar_sub(qlb[:, sl, :], h2a[:, sl, :], MAGIC)
            P.run("vector", lnq2, inc=SEM_VV)
            if i == 0:
                vv_ql = {}
            vv_ql[i] = P.val(SEM_VV)

            P.wait("tensor", SEM_VV, vv_ql[i])
            if i >= 1:
                P.wait("tensor", SEM_EV, ev_qf[i - 1])
            def qtp(e, sl=sl):
                for dc in range(8):
                    r = e.transpose(pmB[:, dc * 128:(dc + 1) * 128],
                                    qlb[:, sl, dc * 128:(dc + 1) * 128], idenB[:, :])
                return r
            P.run("tensor", qtp, inc=SEM_T)
            t_tp[i] = P.val(SEM_T)
            P.wait("scalar", SEM_T, t_tp[i])
            if i >= 2:
                P.wait("scalar", SEM_T, t_ffn.get(i - 2, 0))
            P.run("scalar", lambda e, sl=sl: e.copy(
                out=qFa[:, sl, :, :], in_=pmB[:, :].rearrange("c (g t) -> c g t", g=8)),
                inc=SEM_EV)
            ev_qf[i] = P.val(SEM_EV)

            P.wait("tensor", SEM_EV, ev_qf[i])
            if i >= 1:
                P.wait("tensor", SEM_VV, vv_fin.get(i - 1, 0))
            def fmm(e, sl=sl):
                for half in range(2):
                    ps_ = pm[4 + half][:, 0:512]
                    for dc in range(8):
                        r = e.matmul(ps_, qFa[:, sl, dc, :],
                                     rhq[:, dc, half * 512:half * 512 + 512],
                                     start=(dc == 0), stop=(dc == 7))
                return r
            P.run("tensor", fmm, inc=SEM_T)
            t_ffn[i] = P.val(SEM_T)

            if i + 2 < NT:
                tile_loads(i + 2)
        emit_fin(NT)

        P.wait("gpsimd", "od0", od_n[0])
        P.wait("gpsimd", "od1", od_n[1])
        P.wait("sync", "od0", od_n[0])
        P.wait("sync", "od1", od_n[1])

        P.build()
    return nc, dbg_t


def _core_inputs(inputs):
    x = np.asarray(inputs["L0_attn_x"])[0]
    W3 = np.asarray(inputs["L0_attn_W3"])
    b3 = np.asarray(inputs["L0_attn_b3"])
    per = []
    for i in range(NC):
        d = {}
        lo = i * TPC - HALO
        if lo < 0:
            xs = np.concatenate([np.zeros((-lo, D), np.float32), x[0:(i + 1) * TPC]], 0)
        else:
            xs = np.asarray(x[lo:(i + 1) * TPC])
        d["xs"] = _f32(xs)
        d["halo_mask"] = np.full((128, 1), 0.0 if i == 0 else 1.0, np.float32)
        d["W2s"] = _f32(np.asarray(inputs["L0_attn_W2"])[:, 8 * i:8 * i + 8])
        d["b2f2"] = _f32(np.stack([np.asarray(inputs["L0_attn_b2"])[8 * i:8 * i + 8],
                                   np.asarray(inputs["L0_attn_f2"])[8 * i:8 * i + 8]], 1))
        W3p = np.zeros((80, 128), np.float32)
        csl = slice(128 * i, 128 * (i + 1))
        for s in range(8):
            W3p[10 * s:10 * s + 8] = W3[8 * s:8 * s + 8, csl]
        W3p[8] = b3[csl]
        d["W3p"] = _f32(W3p)
        d["toff"] = np.full((128, 1), float(i * TPC), np.float32)
        wm = np.asarray(inputs["L0_wmod"])[:, i * 768:(i + 1) * 768]
        d["wms"] = _f32(wm.reshape(8, 128, 6, 128).transpose(2, 0, 1, 3))
        per.append(d)
    return per


def _in_map(inputs, core, tab):
    return {
        "xs": core["xs"],
        "W_in": _f32(inputs["L0_attn_W_in"]),
        "b_in": _f32(np.asarray(inputs["L0_attn_b_in"]).reshape(24, 128).T),
        "short_w": _f32(np.asarray(inputs["L0_attn_short_w"]).reshape(24, 128, 3).transpose(1, 0, 2)),
        "short_b": _f32(np.asarray(inputs["L0_attn_short_b"]).reshape(24, 128).T),
        "W1": _f32(inputs["L0_attn_W1"]),
        "b1f1": _f32(np.stack([np.asarray(inputs["L0_attn_b1"]),
                               np.asarray(inputs["L0_attn_f1"])], 1)),
        "W2s": core["W2s"],
        "b2f2": core["b2f2"],
        "W3p": core["W3p"],
        "fb": _f32(inputs["L0_attn_filter_bias"]).reshape(1, D),
        "fbt": _f32(np.asarray(inputs["L0_attn_filter_bias"]).reshape(8, 128).T),
        "W_out": _f32(inputs["L0_attn_W_out"]),
        "b_out": _f32(inputs["L0_attn_b_out"]).reshape(1, D),
        "rhs": _f32(inputs["L0_ffn_rhs"]),
        "cvec": _f32(np.asarray(inputs["L0_c"]).reshape(8, 128).T),
        "wms": core["wms"],
        "halo_mask": core["halo_mask"],
        "toff": core["toff"],
        "peT": _f32(tab["peT"]),
        "deltas": _f32(tab["deltas"].reshape(8, 128).T),
        "Ws": _b16(np.stack([tab["Ws_re"], tab["Ws_im"]], 0)),
        "F1f": _b16(np.stack([tab["F1f_re"], tab["F1f_im"]], 0)),
        "Bv": _b16(np.asarray(tab["Bv"]).transpose(0, 2, 1, 3)),
        "Htw": _b16(tab["Htw"]),
        "H2st": _b16(tab["H2st"]),
        "iWn": _b16(tab["iWn"]),
        "iWr": _b16(tab["iWr"]),
        "Fo": _b16(np.asarray(tab["Fo"]).reshape(32, 4, 3, 128, 64).transpose(0, 3, 1, 2, 4)),
        "Atabs": _b16(np.asarray(tab["Atabs"]).transpose(1, 0, 2)),
        "idenF": _f32(tab["iden"]),
        "ozrow": _b16(np.stack([np.ones(4096, np.float32), np.zeros(4096, np.float32)], 0)),
        "idenB": _b16(tab["iden"]),
    }


def kernel(**inputs):
    tab = _tables()
    per = _core_inputs(inputs)
    nc, _ = build_nc()
    in_maps = [_in_map(inputs, per[i], tab) for i in range(NC)]
    res = run_bass_kernel_spmd(nc, in_maps, core_ids=list(range(NC)))
    return np.concatenate([r["out"] for r in res.results], 0)[None]


# version 12
# speedup vs baseline: 4.0950x; 1.0152x over previous
import math
import os

import numpy as np
import ml_dtypes

os.environ.setdefault("BASS_INTERP_NUM_WORKERS", "8")

import concourse.bass as bass
import concourse.bass_types as bt
import concourse.mybir as mybir
from concourse.bass_utils import run_bass_kernel_spmd
from concourse.aot_env import allow_ici_all_to_all
from concourse.bass_isa import ReduceOp
from concourse.library_config import attn as LIB_ATTN, standard as LIB_STD

F32 = mybir.dt.float32
BF16 = mybir.dt.bfloat16
I32 = mybir.dt.int32
ACT = mybir.ActivationFunctionType
ALU = mybir.AluOpType
_SHIM = True

L = 16384
D = 1024
N = 2 * L
M = N // 2
NC = 8
TPC = L // NC
HALO = 128
TT = TPC + HALO
WIDTH = 3 * D

bf16 = ml_dtypes.bfloat16
KREV = 127 - np.arange(128)
KWRAP = (128 - np.arange(128)) % 128


def _f32(a):
    return np.ascontiguousarray(a, dtype=np.float32)


def _b16(a):
    return np.ascontiguousarray(np.asarray(a, dtype=np.float32).astype(bf16))


def _host_tables():
    t = {}
    wN = np.exp(-2j * np.pi * np.arange(N) / N)
    a_f = (1 - 1j * wN) / 2
    b_f = (1 + 1j * wN) / 2
    p_f = (1 + 1j * np.conj(wN[:M])) / 2
    q_f = (1 - 1j * np.conj(wN[:M])) / 2
    A1 = p_f * np.conj(a_f[:M]) * a_f[:M] + q_f * np.conj(a_f[M:]) * a_f[M:]
    A2 = p_f * np.conj(a_f[:M]) * b_f[:M] + q_f * np.conj(a_f[M:]) * b_f[M:]
    B1 = p_f * np.conj(b_f[:M]) * a_f[:M] + q_f * np.conj(b_f[M:]) * a_f[M:]
    B2 = p_f * np.conj(b_f[:M]) * b_f[:M] + q_f * np.conj(b_f[M:]) * b_f[M:]
    k2g = np.arange(128)[:, None]
    sg = np.arange(128)[None, :]
    f_nat = k2g * 128 + sg
    f_rev = (127 - k2g) * 128 + sg
    for nm, arr in (("A1", A1), ("A2", A2), ("B1", B1), ("B2", B2)):
        t[nm + "n"] = arr[f_nat]
        t[nm + "r"] = arr[f_rev]
    w128 = np.exp(-2j * np.pi * np.outer(np.arange(128), np.arange(128)) / 128)
    F1 = w128[:64]
    t["Ws_re"] = np.concatenate([F1.real, -F1.imag], 0)
    t["Ws_im"] = np.concatenate([F1.imag, F1.real], 0)
    t["F1f_re"] = np.ascontiguousarray(w128.real)
    t["F1f_im"] = np.ascontiguousarray(w128.imag)
    m2 = np.arange(128)
    vlist = [(a, "nat") for a in range(0, 65)] + [(b, "rev") for b in range(65, 128)]
    vlist += [(64, "rev"), (0, "wrap")]
    t["vlist"] = vlist
    Bv = np.empty((len(vlist), 3, 128, 128), dtype=np.float32)
    for vid, (k1, vm) in enumerate(vlist):
        B = np.exp(-2j * np.pi * (m2[:, None] * k1) / M) * w128
        if vm == "rev":
            B = B[:, KREV]
        elif vm == "wrap":
            B = B[:, KWRAP]
        Bv[vid, 0] = B.real
        Bv[vid, 1] = B.imag
        Bv[vid, 2] = -B.imag
    t["Bv"] = Bv
    tw = np.exp(-2j * np.pi * np.outer(m2, np.arange(128)) / M)
    t["Htw"] = np.stack([tw.real, tw.imag], 0)
    t["H2st"] = np.stack([
        np.stack([w128.real, w128.imag, -w128.imag], 0),
        np.stack([w128[:, KREV].real, w128[:, KREV].imag, -w128[:, KREV].imag], 0),
        np.stack([w128[:, KWRAP].real, w128[:, KWRAP].imag, -w128[:, KWRAP].imag], 0)], 0)
    iW = np.exp(+2j * np.pi * np.outer(np.arange(128), np.arange(128)) / 128)
    iWr = np.exp(+2j * np.pi * np.outer(127 - np.arange(128), np.arange(128)) / 128)
    t["iWn"] = np.stack([iW.real, iW.imag, -iW.imag], 0)
    t["iWr"] = np.stack([iWr.real, iWr.imag, -iWr.imag], 0)
    Fo = np.empty((128, 3, 128, 64), dtype=np.float32)
    m1p = np.arange(64)
    for m2p in range(128):
        m2v = 127 - m2p
        Ft = np.exp(+2j * np.pi * (np.outer(np.arange(128), np.arange(64)) / 128
                                   + (np.arange(128) * m2v / M)[:, None])) / M
        Fo[m2p, 0] = Ft.imag[:, 63 - m1p]
        Fo[m2p, 1] = Ft.real[:, 63 - m1p]
        Fo[m2p, 2] = -Ft.imag[:, 63 - m1p]
    t["Fo"] = Fo
    tg = np.arange(L, dtype=np.float64) / (L - 1)
    ang = 2.0 * math.pi * np.arange(L, dtype=np.float64) / L
    ph = 0.0001 * ang
    t["peT"] = np.stack([tg, np.cos(ph), -np.sin(ph)], 0).astype(np.float32)
    dmax, dmin = (math.log(0.01) / 0.3, math.log(0.01) / 1.5)
    t["deltas"] = np.abs(np.linspace(dmax, dmin, D)).astype(np.float32)
    t["iden"] = np.eye(128, dtype=np.float32)
    t["Atabs"] = np.stack(
        [x for nm in ("A1", "A2", "B1", "B2") for x in (t[nm + "n"].real, t[nm + "n"].imag)]
        + [x for nm in ("A1", "A2", "B1", "B2") for x in (t[nm + "r"].real, t[nm + "r"].imag)], 0)
    tp = 2 * math.pi
    c1 = np.float32(tp); c2 = np.float32(tp - float(c1)); c3 = np.float32(tp - float(c1) - float(c2))
    t["cw"] = (float(c1), float(c2), float(c3))
    return t


_TAB = None


def _tables():
    global _TAB
    if _TAB is None:
        _TAB = _host_tables()
    return _TAB


class Prog:
    ENGINES = ("sync", "gpsimd", "tensor", "vector", "scalar")

    def __init__(self, nc):
        self.nc = nc
        self.st = {e: [] for e in self.ENGINES}
        self.sems = {}
        self.cnt = {}

    def sem(self, name):
        if name not in self.sems:
            self.sems[name] = self.nc.alloc_semaphore(name)
            self.cnt[name] = 0
        return self.sems[name]

    def val(self, name):
        self.sem(name)
        return self.cnt[name]

    def wait(self, eng, name, val=None):
        s = self.sem(name)
        v = self.cnt[name] if val is None else val
        self.st[eng].append(lambda e, s=s, v=v: e.wait_ge(s, v))

    def run(self, eng, fn, inc=None, inc_amt=1, direct=None):
        if inc is None:
            self.st[eng].append(fn)
            return
        s = self.sem(inc)
        self.cnt[inc] += inc_amt
        if direct is None:
            direct = eng in ("sync", "tensor") or inc_amt >= 16 or inc == "cc"
        if direct:
            self.st[eng].append(lambda e, fn=fn, s=s, a=inc_amt: fn(e).then_inc(s, a))
        else:
            def _go(e, fn=fn, s=s, a=inc_amt):
                fn(e)
                e.drain().then_inc(s, a)
            self.st[eng].append(_go)

    def build(self):
        nc = self.nc

        class Shim:
            PASS = ("wait_ge", "engine_nop", "dma_start", "dma_start_transpose",
                    "collective_compute", "sem_inc", "drain", "nop", "wait_op")

            def __init__(self, eng, sem, state):
                self._eng = eng
                self._sem = sem
                self._state = state

            def __getattr__(self, nm):
                f = getattr(self._eng, nm)
                if nm in Shim.PASS or not callable(f):
                    return f

                def wrapped(*a, **k):
                    if self._state[0] > 0:
                        self._eng.wait_ge(self._sem, self._state[0])
                    inst = f(*a, **k)
                    inst.then_inc(self._sem, 1)
                    self._state[0] += 1
                    return inst
                return wrapped

        if not _SHIM:
            class Shim:
                def __new__(cls, eng, sem, state):
                    return eng
        selfsems = {e: (self.sem("self_" + e), [0]) for e in ("vector", "scalar", "gpsimd")}
        with nc.Block() as block:
            @block.sync
            def _(e):
                for fn in self.st["sync"]:
                    fn(e)

            @block.gpsimd
            def _(e):
                sh = Shim(e, *selfsems["gpsimd"])
                for fn in self.st["gpsimd"]:
                    fn(sh)

            @block.tensor
            def _(e):
                for fn in self.st["tensor"]:
                    fn(e)

            @block.vector
            def _(e):
                sh = Shim(e, *selfsems["vector"])
                for fn in self.st["vector"]:
                    fn(sh)

            @block.scalar
            def _(e):
                sh = Shim(e, *selfsems["scalar"])
                for fn in self.st["scalar"]:
                    fn(sh)


def build_nc(dbg=(), stop_after=None):
    tab = _tables()
    dbg = set(dbg)
    nc = bass.Bass("TRN2", target_bir_lowering=False, num_devices=NC)

    def din(name, shape, dt=F32):
        return nc.dram_tensor(name, list(shape), dt, kind="ExternalInput").ap()

    h_xs = din("xs", (TT, D))
    h_W_in = din("W_in", (D, WIDTH))
    h_b_in = din("b_in", (128, 24))
    h_short_w = din("short_w", (128, 24, 3))
    h_short_b = din("short_b", (128, 24))
    h_W1 = din("W1", (3, 64))
    h_b1f1 = din("b1f1", (64, 2))
    h_W2s = din("W2s", (64, 8))
    h_b2f2 = din("b2f2", (8, 2))
    h_W3p = din("W3p", (80, 128))
    h_fb = din("fb", (1, D))
    h_fbt = din("fbt", (128, 8))
    h_W_out = din("W_out", (D, D))
    h_b_out = din("b_out", (1, D))
    h_rhs = din("rhs", (D, D))
    h_cvec = din("cvec", (128, 8))
    h_wms = din("wms", (6, 8, 128, 128))
    h_hmask = din("halo_mask", (128, 1))
    h_toff = din("toff", (128, 1))
    h_peT = din("peT", (3, L))
    h_deltas = din("deltas", (128, 8))
    h_Ws = din("Ws", (2, 128, 128), BF16)
    h_F1f = din("F1f", (2, 128, 128), BF16)
    h_Bv = din("Bv", (130, 128, 3, 128), BF16)
    h_Htw = din("Htw", (2, 128, 128), BF16)
    h_H2st = din("H2st", (3, 3, 128, 128), BF16)
    h_iWn = din("iWn", (3, 128, 128), BF16)
    h_iWr = din("iWr", (3, 128, 128), BF16)
    h_Fo = din("Fo", (32, 128, 4, 3, 64), BF16)
    h_Atabs = din("Atabs", (128, 16, 128), BF16)
    h_idenF = din("idenF", (128, 128))
    h_idenB = din("idenB", (128, 128), BF16)

    out = nc.dram_tensor("out", [TPC, D], F32, kind="ExternalOutput").ap()

    vb_in = nc.dram_tensor("vb_in", [D, TPC], BF16).ap()
    vb_out = nc.dram_tensor("vb_out", [D, TPC], BF16).ap()
    gb_in = nc.dram_tensor("gb_in", [120, 4 * 11 * 128], BF16).ap()
    gb_out = nc.dram_tensor("gb_out", [960, 4 * 11 * 128], BF16).ap()
    rb_in = nc.dram_tensor("rb_in", [D, TPC], BF16).ap()
    rb_out = nc.dram_tensor("rb_out", [D, TPC], BF16).ap()
    zscr = nc.dram_tensor("zscr", [128, 4096], BF16).ap()
    x0sp = nc.dram_tensor("x0sp", [8, 128, TPC], BF16).ap()
    mspill = nc.dram_tensor("mspill", [128, 48], F32).ap()
    arsp = nc.dram_tensor("arsp", [128, 16], F32).ap()
    msin = nc.dram_tensor("msin", [128, 6], F32).ap()
    msout = nc.dram_tensor("msout", [1024, 6], F32).ap()
    h_oz = din("ozrow", (2, 4096), BF16)

    dbg_t = {}

    def dout(name, shape, dt=F32):
        if name in dbg:
            dbg_t[name] = nc.dram_tensor("dbg_" + name, list(shape), dt,
                                         kind="ExternalOutput").ap()
        return dbg_t.get(name)

    P = Prog(nc)
    VL = _tables()["vlist"]

    def _th(x):
        return x.tensor if hasattr(x, "tensor") else x
    from contextlib import ExitStack
    es = ExitStack()

    def sb(name, shape, dt):
        return es.enter_context(nc.sbuf_tensor(name, list(shape), dt))

    def ps(name, cols=512):
        return es.enter_context(nc.psum_tensor(name, [128, cols], F32))

    with es:
        small = sb("small", (128, 256), F32)
        modv = sb("modv", (128, 48), F32)
        silb = sb("silb", (128, 8), F32)
        wmb = sb("wmb", (128, 2, 128), F32)
        bsw = sb("bsw", (128, 24, 3), F32)
        bsb = sb("bsb", (128, 24), F32)
        binb = sb("binb", (128, 24), F32)
        idenF = sb("idenFs", (128, 128), F32)
        hmsk = sb("hmsk", (128, 1), F32)
        toffs = sb("toffs", (128, 1), F32)
        dts = sb("dts", (128, 8), F32)
        x0bb = sb("x0bb", (128, 2, TPC), BF16)
        w1b = sb("w1b", (128, 64), F32)
        b1f1b = sb("b1f1b", (128, 2), F32)
        w2b = sb("w2b", (128, 8), F32)
        b2f2b = sb("b2f2b", (128, 2), F32)
        w3b = sb("w3b", (128, 128), BF16)
        Wsb = sb("Wsb", (128, 2, 128), BF16)
        Htwb = sb("Htwb", (128, 2, 128), BF16)
        H2stb = sb("H2stb", (128, 3, 3, 128), BF16)
        iWnb = sb("iWnb", (128, 3, 128), BF16)
        iWrb = sb("iWrb", (128, 3, 128), BF16)
        Atabsb = sb("Atabsb", (128, 16, 128), BF16)
        idenB = sb("idenBs", (128, 128), BF16)
        Bvb = sb("Bvb", (128, 4, 3, 128), BF16)
        Fob2 = sb("Fob2", (128, 2, 4, 3, 64), BF16)
        Gstb = sb("Gstb", (128, 2, 4, 128), BF16)
        zqb = sb("zqb", (128, 2, 4, 128), BF16)
        rwb = sb("rwb", (128, 2, D), F32)
        fbtb = sb("fbtb", (128, 8), F32)
        rowb = sb("rowb", (1, D), F32)
        gborow = sb("gborow", (1, D), BF16)
        onesr = sb("onesr", (1, 128), BF16)
        onesF = sb("onesF", (1, 128), F32)
        modsh = sb("modsh", (128, 6), F32)

        SB_TOP = 229344
        AR_SZ = 137728
        AR0 = SB_TOP - AR_SZ
        assert nc.sbuf_bytes_remaining >= AR_SZ + 2048, nc.sbuf_bytes_remaining

        def at(name, shape, dt, off):
            nbytes = int(np.prod(shape[1:])) * (2 if dt == BF16 else 4)
            assert off + nbytes <= AR_SZ, (name, off, nbytes)
            flat = int(np.prod(shape[1:]))
            ap = nc.alloc_sbuf_tensor_at(name, [shape[0], flat], dt, offset=AR0 + off).ap()
            if len(shape) == 2:
                return ap
            dims = " ".join(f"d{i}" for i in range(1, len(shape)))
            kw = {f"d{i}": shape[i] for i in range(1, len(shape))}
            return ap.rearrange(f"p ({dims}) -> p {dims}", **kw)

        winb = at("winb", (128, 8, WIDTH), BF16, 0)
        hFb = at("hF", (128, 8, TT), BF16, 49152)
        ub = at("ub", (128, 2, TT), BF16, 83968)
        ucb = at("ucb", (128, 3, TPC), BF16, 92672)
        wvb = at("wvb", (128, 2, TPC), BF16, 104960)
        ramp = at("ramp", (128, TPC), F32, 113152)
        Etile = at("Etile", (128, TPC), BF16, 121344)
        xt2 = at("xt2", (128, 2, D), F32, 125440)
        gsc = at("gsc", (128, TPC), BF16, 125440)
        xhp = at("xhp", (128, D), F32, 133632)
        Abuf = at("Abuf", (128, 2, 128, 128), BF16, 0)
        XQ = at("XQ", (128, 2, 132, 128), BF16, 67584)
        x2a = at("x2a", (128, 3, D), F32, 0)
        da = at("da", (128, 2, D), F32, 12288)
        qFa = at("qFa", (128, 2, 8, 128), BF16, 20480)
        rhq = at("rhq", (128, 8, D), BF16, 24576)
        h2a = at("h2a", (128, 2, D), F32, 40960)
        qlb = at("qlb", (128, 2, 1024), BF16, 49152)
        xt3 = at("xt3", (128, 2, D), F32, 53248)
        cvF = at("cvF", (128, 8, TPC), BF16, 61440)
        cbT = at("cbT", (128, D), F32, 94208)
        sbT = at("sbT", (128, D), F32, 98304)
        x0t = at("x0t", (128, 2, 8, 128), BF16, 102400)
        ygb = at("ygb", (128, 2, 8, 128), BF16, 106496)
        woutb = at("woutb", (128, 8, D), BF16, 110592)
        arinvT = at("arinvT", (128, D), F32, 126976)
        gbarT = at("gbarT", (128, D), F32, 131072)
        ervb = at("ervb", (128, 8, 128), BF16, 135168)
        itile = at("itile", (128, 128), F32, 137216)
        HB = 83968
        peb = at("peb", (128, 2, 512), F32, HB)
        argb = at("argb", (128, 512), F32, HB + 4096)
        rrf = at("rrf", (128, 512), F32, HB + 6144)
        h1cb = at("h1cb", (128, 512), F32, HB + 8192)
        H2buf = at("H2buf", (128, 4096), BF16, HB + 10240)
        zHb = at("zHb", (128, 10, 128), BF16, HB + 18432)
        argbB = at("argbB", (128, 512), F32, HB + 18432)
        rrfB = at("rrfB", (128, 512), F32, HB + 20480)
        h1cbB = at("h1cbB", (128, 512), F32, HB + 22528)
        AHb = at("AHb", (128, 2, 10, 128), BF16, HB + 20992)
        AHtwb = at("AHtwb", (128, 2, 10, 128), BF16, HB + 26112)
        ZHb = at("ZHb", (128, 2, 10, 130), BF16, HB + 31232)
        Gsh = at("Gsh", (128, 4, 128, 10), BF16, HB + 36448)
        Gtsh = at("Gtsh", (128, 11, 4, 128), BF16, HB)
        csc = at("csc", (128, 2, 64, 10), F32, HB + 20992)
        cscg = at("cscg", (128, 2, 64, 10), F32, HB + 26112)

        pm = [ps(f"pm{i}") for i in range(7)]
        pmB = es.enter_context(nc.psum_tensor("pmB", [128, 1024], BF16))
        pmB = pmB.ap() if hasattr(pmB, "ap") else pmB

        SEM_LD = "ld"
        SEM_GD = "gd"
        SEM_T = "t"
        SEM_EV = "ev"
        SEM_VV = "vv"
        SEM_CC = "cc"

        def ld(dst, src, inc=SEM_LD, eng="sync"):
            P.run(eng, lambda e, d=dst, s=src: e.dma_start(out=d, in_=s), inc=inc, inc_amt=16)

        ld(idenF[:, :], h_idenF[:, :])
        ld(hmsk[:, :], h_hmask[:, :])
        ld(toffs[:, :], h_toff[:, :])
        ld(dts[:, :], h_deltas[:, :])
        ld(bsw[:, :, :], h_short_w[:, :, :])
        ld(bsb[:, :], h_short_b[:, :])
        ld(binb[:, :], h_b_in[:, :])
        ld(silb[:, :], h_cvec[:, :])
        P.run("gpsimd", lambda e: e.dma_start(out=winb[:, :, :],
                                              in_=h_W_in.rearrange("(g p) w -> p g w", p=128)),
              inc="winb", inc_amt=16)
        n_ld0 = P.val(SEM_LD)

        P.wait("scalar", SEM_LD, n_ld0)
        P.run("scalar", lambda e: e.activation(out=silb[:, :], in_=silb[:, :], func=ACT.Silu),
              inc=SEM_EV)
        ev_sil = P.val(SEM_EV)
        t_mod_base = P.val(SEM_T)
        k = 0
        for oc in range(6):
            for ci in range(8):
                k += 1
                if k > 2:
                    P.wait("gpsimd", SEM_T, t_mod_base + k - 2)
                P.run("gpsimd",
                      lambda e, b=(k - 1) % 2, o=oc, c=ci: e.dma_start(
                          out=wmb[:, b, :], in_=h_wms[o, c, :, :]),
                      inc=f"wm{(k - 1) % 2}", inc_amt=16)
        P.wait("tensor", SEM_EV, ev_sil)
        k = 0
        for oc in range(6):
            for ci in range(8):
                k += 1
                P.wait("tensor", f"wm{(k - 1) % 2}", 16 * ((k - 1) // 2 + 1))
                P.run("tensor",
                      lambda e, b=(k - 1) % 2, o=oc, c=ci: e.matmul(
                          pm[0][:, o:o + 1], wmb[:, b, :], silb[:, c:c + 1],
                          start=(c == 0), stop=(c == 7)),
                      inc=SEM_T)
        t_mod = P.val(SEM_T)
        P.wait("vector", SEM_T, t_mod)
        P.run("vector", lambda e: e.tensor_copy(out=modsh[:, :], in_=pm[0][:, 0:6]),
              inc=SEM_VV)
        P.wait("gpsimd", SEM_VV, P.val(SEM_VV))
        P.run("gpsimd", lambda e: e.dma_start(out=msin[:, :], in_=modsh[:, :]),
              inc="msi", inc_amt=16)
        P.wait("gpsimd", "msi", 16)
        def cc_mod_fn(e):
            with allow_ici_all_to_all():
                return e.collective_compute(
                    "AllGather", ALU.bypass, replica_groups=[list(range(NC))],
                    ins=[msin.opt()], outs=[msout.opt()])
        P.run("gpsimd", cc_mod_fn, inc=SEM_CC)
        cc_mod = P.val(SEM_CC)
        P.wait("gpsimd", SEM_CC, cc_mod)
        def mgd(e):
            d = bt.AP(tensor=_th(modv), offset=0, ap=[[48, 128], [6, 8], [1, 6]])
            s = bt.AP(tensor=msout.tensor, offset=0, ap=[[6, 128], [768, 8], [1, 6]])
            return e.dma_start(out=d, in_=s, allow_slow_non_contiguous=True)
        P.run("gpsimd", mgd, inc="mgd", inc_amt=16)
        P.wait("vector", "mgd", 16)
        P.wait("scalar", "mgd", 16)
        P.wait("scalar", SEM_VV, P.val(SEM_VV))
        def modsgn(e):
            e.sign(small[:, 48:56], modv[:, 16:24])
            e.sign(small[:, 64:72], modv[:, 40:48])
            e.activation(out=small[:, 56:64], in_=modv[:, 16:24], func=ACT.Abs)
            return e.activation(out=small[:, 72:80], in_=modv[:, 40:48], func=ACT.Abs)
        P.run("scalar", modsgn, inc=SEM_EV)
        P.wait("vector", SEM_EV, P.val(SEM_EV))
        def modpost(e):
            e.tensor_copy(out=small[:, 0:8], in_=modv[:, 0:8])
            e.tensor_scalar_add(small[:, 8:16], modv[:, 8:16], 1.0)
            e.tensor_scalar_add(small[:, 32:40], modv[:, 32:40], 1.0)
            e.tensor_copy(out=small[:, 24:32], in_=modv[:, 24:32])
            for sg, ab, dst in ((48, 56, 16), (64, 72, 40)):
                e.tensor_mul(small[:, 80:88], small[:, sg:sg + 8], small[:, sg:sg + 8])
                e.tensor_sub(small[:, 88:96], small[:, sg:sg + 8], small[:, 80:88])
                e.tensor_scalar_add(small[:, 88:96], small[:, 88:96], 1.0)
                e.tensor_scalar_max(small[:, 80:88], small[:, ab:ab + 8], 0.25)
                e.tensor_mul(small[:, dst:dst + 8], small[:, 88:96], small[:, 80:88])
            return e.engine_nop()
        P.run("vector", modpost, inc=SEM_VV)
        vv_mod = P.val(SEM_VV)
        SA, CA1, GAC, SB_, CB1, GBC = 0, 8, 16, 24, 32, 40

        if stop_after == "mod":
            P.wait("gpsimd", SEM_VV, vv_mod)
            for i in range(TPC // 128):
                P.run("gpsimd", lambda e, i=i: e.dma_start(
                    out=out[i * 128:(i + 1) * 128, 0:64], in_=small[:, 0:64]),
                      inc=SEM_GD, inc_amt=16)
            P.wait("gpsimd", SEM_GD, P.val(SEM_GD))
            P.wait("sync", SEM_GD, P.val(SEM_GD))
            P.build()
            return nc, dbg_t

        ld(w1b[0:3, :], h_W1[:, :], inc="hld")
        ld(b1f1b[0:64, :], h_b1f1[:, :], inc="hld")
        ld(w2b[0:64, 0:8], h_W2s[:, :], inc="hld")
        ld(b2f2b[0:8, :], h_b2f2[:, :], inc="hld")
        for i in range(2):
            ld(Wsb[:, i, :], h_Ws[i, :, :], inc="hld")
            ld(Htwb[:, i, :], h_Htw[i, :, :], inc="hld")
        for c in range(3):
            for i in range(3):
                ld(H2stb[:, c, i, :], h_H2st[c, i, :, :], inc="hld")
        for i in range(3):
            ld(iWnb[:, i, :], h_iWn[i, :, :], inc="hld")
            ld(iWrb[:, i, :], h_iWr[i, :, :], inc="hld")
        ld(Atabsb[:, :, :], h_Atabs[:, :, :], inc="hld")
        ld(idenB[:, :], h_idenB[:, :], inc="hld")
        P.run("gpsimd", lambda e: e.dma_start(out=w3b[0:80, :], in_=h_W3p[:, :]),
              inc="w3l", inc_amt=16)
        n_hld = P.val("hld")
        CW1, CW2, CW3 = tab["cw"]

        P.wait("vector", "hld", n_hld)
        def hpre(e):
            e.tensor_mul(small[0:64, 96:97], b1f1b[0:64, 0:1], b1f1b[0:64, 1:2])
            return e.tensor_mul(small[0:8, 97:98], b2f2b[0:8, 0:1], b2f2b[0:8, 1:2])
        P.run("vector", hpre, inc=SEM_VV)
        vv_hpre = P.val(SEM_VV)
        P.wait("scalar", SEM_VV, vv_hpre)
        P.wait("tensor", "hld", n_hld)

        MAGIC = 12582912.0

        def rr_sin(rows, dst_ap, ev_arg, ab, rb):
            P.wait("vector", SEM_EV, ev_arg)
            def rrv(e, ab=ab, rb=rb):
                e.tensor_scalar_mul(rb[0:rows, :], ab[0:rows, :], 1.0 / (2 * math.pi))
                e.tensor_scalar(rb[0:rows, :], rb[0:rows, :], MAGIC, MAGIC, ALU.add, ALU.subtract)
                e.scalar_tensor_tensor(ab[0:rows, :], rb[0:rows, :], -CW1,
                                       ab[0:rows, :], ALU.mult, ALU.add)
                e.scalar_tensor_tensor(ab[0:rows, :], rb[0:rows, :], -CW2,
                                       ab[0:rows, :], ALU.mult, ALU.add)
                e.scalar_tensor_tensor(ab[0:rows, :], rb[0:rows, :], -CW3,
                                       ab[0:rows, :], ALU.mult, ALU.add)
                return e.tensor_scalar(ab[0:rows, :], ab[0:rows, :],
                                       -math.pi, math.pi, ALU.max, ALU.min)
            P.run("vector", rrv, inc=SEM_VV)
            vvc = P.val(SEM_VV)
            P.wait("scalar", SEM_VV, vvc)
            P.run("scalar", lambda e, ab=ab: e.activation(out=dst_ap, in_=ab[0:rows, :],
                                                          func=ACT.Sin), inc=SEM_EV)
            return P.val(SEM_EV), vvc

        HMB = ((argb, rrf, h1cb), (argbB, rrfB, h1cbB))
        ev_h1psum = {}
        vv_cody = {0: 0, 1: 0}
        t_h2mm = {0: 0, 1: 0}
        for fc in range(32):
            b = fc % 2
            band = fc % 4
            cg = fc // 4
            ab, rb_, hb = HMB[b]
            if fc >= 2:
                P.wait("sync", SEM_T, ev_h1psum[fc - 2])
            ld(peb[0:3, b, :], h_peT[:, fc * 512:(fc + 1) * 512], inc=f"pe{b}")
            P.wait("tensor", f"pe{b}", 16 * (fc // 2 + 1))
            if fc >= 1:
                P.wait("tensor", SEM_EV, ev_arg1[fc - 1])
            P.run("tensor", lambda e, b=b: e.matmul(pm[6][0:64, 0:512], w1b[0:3, 0:64],
                                                    peb[0:3, b, :], start=True, stop=True),
                  inc=SEM_T)
            ev_h1psum[fc] = P.val(SEM_T)
            P.wait("scalar", SEM_T, P.val(SEM_T))
            P.wait("scalar", SEM_VV, vv_cody[b])
            P.run("scalar", lambda e, ab=ab: e.activation(out=ab[0:64, :], in_=pm[6][0:64, 0:512],
                                                          func=ACT.Identity, scale=b1f1b[0:64, 1:2],
                                                          bias=small[0:64, 96:97]), inc=SEM_EV)
            if fc == 0:
                ev_arg1 = {}
            ev_arg1[fc] = P.val(SEM_EV)
            P.wait("scalar", SEM_T, t_h2mm[b])
            ev, vv_cody[b] = rr_sin(64, hb[0:64, :], ev_arg1[fc], ab, rb_)
            P.wait("tensor", SEM_EV, ev)
            P.run("tensor", lambda e, hb=hb, b=b: e.matmul(
                pm[1 + b * 2][0:8, 0:512], w2b[0:64, 0:8],
                hb[0:64, 0:512], start=True, stop=True), inc=SEM_T)
            t_h2mm[b] = P.val(SEM_T)
            P.wait("scalar", SEM_T, P.val(SEM_T))
            P.run("scalar", lambda e, ab=ab, b=b: e.activation(
                out=ab[0:8, :], in_=pm[1 + b * 2][0:8, 0:512],
                func=ACT.Identity, scale=b2f2b[0:8, 1:2],
                bias=small[0:8, 97:98]), inc=SEM_EV)
            h2dst = bt.AP(tensor=_th(H2buf), offset=(band * 32) * 4096 + cg * 512,
                          ap=[[4096, 8], [1, 256], [256, 2]])
            ev, vv_cody[b] = rr_sin(8, h2dst, P.val(SEM_EV), ab, rb_)
        if stop_after == "hmlp":
            P.wait("gpsimd", SEM_EV, P.val(SEM_EV))
            P.wait("gpsimd", SEM_VV, P.val(SEM_VV))
            for i in range(TPC // 128):
                P.run("gpsimd", lambda e, i=i: e.dma_start(
                    out=out[i * 128:(i + 1) * 128, 0:64], in_=small[:, 0:64]),
                      inc=SEM_GD, inc_amt=16)
            P.wait("gpsimd", SEM_GD, P.val(SEM_GD))
            P.wait("sync", SEM_GD, P.val(SEM_GD))
            P.build()
            return nc, dbg_t

        P.wait("sync", SEM_EV, P.val(SEM_EV))
        P.wait("sync", SEM_VV, P.val(SEM_VV))
        for w_ in range(4):
            P.run("sync", lambda e, w_=w_: e.dma_start(out=zscr[32 * w_:32 * w_ + 8, :],
                                                       in_=H2buf[32 * w_:32 * w_ + 8, :]),
                  inc="zhs", inc_amt=16)
        for w_ in range(4):
            P.run("sync", lambda e, w_=w_: e.dma_start(out=zscr[32 * w_ + 8:32 * w_ + 10, :],
                                                       in_=h_oz[:, :]), inc="zhs", inc_amt=16)
        P.wait("sync", "zhs", P.val("zhs"))
        for p_ in range(2):
            for w_ in range(4):
                for r_ in range(2):
                    for u_ in range(8):
                        def zhdma(e, p_=p_, w_=w_, r_=r_, u_=u_):
                            row = 10 * 128
                            dst = bt.AP(tensor=zHb.tensor,
                                        offset=(p_ * 64 + 8 * u_ + 2 * w_ + r_) * row,
                                        ap=[[row, 1], [128, 10], [1, 128]])
                            v = bt.AP(tensor=zscr.tensor,
                                      offset=(32 * w_) * 4096 + 512 * u_ + 256 * p_ + 128 * r_,
                                      ap=[[4096, 10], [1, 128]])
                            return e.dma_start(out=dst, in_=v)
                        P.run("sync", zhdma, inc="zh", inc_amt=16)
        if stop_after == "zh":
            P.wait("gpsimd", "zh", P.val("zh"))
            for i in range(TPC // 128):
                P.run("gpsimd", lambda e, i=i: e.dma_start(
                    out=out[i * 128:(i + 1) * 128, 0:64], in_=small[:, 0:64]),
                      inc=SEM_GD, inc_amt=16)
            P.wait("gpsimd", SEM_GD, P.val(SEM_GD))
            P.wait("sync", SEM_GD, P.val(SEM_GD))
            P.build()
            return nc, dbg_t

        P.wait("tensor", "zh", P.val("zh"))
        for j in range(10):
            bank = 6 if (j % 2 == 0) else 1
            if j >= 2:
                P.wait("tensor", SEM_EV, ev_ah[j - 2])
            def s1h(e, j=j, bank=bank):
                e.matmul(pm[bank][:, 0:128], zHb[:, j, :], Wsb[:, 0, :], start=True, stop=True)
                return e.matmul(pm[bank][:, 128:256], zHb[:, j, :], Wsb[:, 1, :], start=True, stop=True)
            P.run("tensor", s1h, inc=SEM_T)
            P.wait("scalar", SEM_T, P.val(SEM_T))
            def ahev(e, j=j, bank=bank):
                e.copy(out=AHb[:, 0, j, :], in_=pm[bank][:, 0:128])
                return e.copy(out=AHb[:, 1, j, :], in_=pm[bank][:, 128:256])
            P.run("scalar", ahev, inc=SEM_EV)
            if j == 0:
                ev_ah = {}
            ev_ah[j] = P.val(SEM_EV)
        ev_ah_all = P.val(SEM_EV)
        P.wait("vector", SEM_EV, ev_ah_all)
        def htw(e):
            twr = Htwb[:, 0, :].rearrange("k (o m) -> k o m", o=1).broadcast_to((128, 10, 128))
            twi = Htwb[:, 1, :].rearrange("k (o m) -> k o m", o=1).broadcast_to((128, 10, 128))
            e.tensor_mul(AHtwb[:, 0, :, :], AHb[:, 0, :, :], twr)
            e.tensor_mul(zHb[:, :, :], AHb[:, 1, :, :], twi)
            e.tensor_sub(AHtwb[:, 0, :, :], AHtwb[:, 0, :, :], zHb[:, :, :])
            e.tensor_mul(AHtwb[:, 1, :, :], AHb[:, 0, :, :], twi)
            e.tensor_mul(zHb[:, :, :], AHb[:, 1, :, :], twr)
            return e.tensor_add(AHtwb[:, 1, :, :], AHtwb[:, 1, :, :], zHb[:, :, :])
        P.run("vector", htw, inc=SEM_VV)
        vv_htw = P.val(SEM_VV)
        P.wait("tensor", SEM_VV, vv_htw)
        CLS = ((0, 0, 65, 0), (1, 64, 128, 65), (2, 0, 1, 129))
        hz_k = 0
        hz_banks = {}
        for cls, k1lo, k1hi, slot0 in CLS:
            nk = k1hi - k1lo
            for c0 in range(0, nk, 51):
                cn = min(51, nk - c0)
                bre, bim = ((2, 3), (4, 5))[hz_k % 2]
                hz_k += 1
                if bre in hz_banks:
                    P.wait("tensor", SEM_EV, hz_banks[bre])
                def s2h(e, cls=cls, k1lo=k1lo, c0=c0, cn=cn, bre=bre, bim=bim):
                    mvr = AHtwb[:, 0, :, k1lo + c0:k1lo + c0 + cn]
                    mvi = AHtwb[:, 1, :, k1lo + c0:k1lo + c0 + cn]
                    F = 10 * cn
                    pr = pm[bre][:, 0:F].rearrange("k (j c) -> k j c", j=10)
                    pi = pm[bim][:, 0:F].rearrange("k (j c) -> k j c", j=10)
                    e.matmul(pr, H2stb[:, cls, 0, :], mvr, start=True, stop=False)
                    e.matmul(pr, H2stb[:, cls, 2, :], mvi, start=False, stop=True)
                    e.matmul(pi, H2stb[:, cls, 1, :], mvr, start=True, stop=False)
                    return e.matmul(pi, H2stb[:, cls, 0, :], mvi, start=False, stop=True)
                P.run("tensor", s2h, inc=SEM_T)
                P.wait("scalar", SEM_T, P.val(SEM_T))
                def hzev(e, c0=c0, cn=cn, slot0=slot0, bre=bre, bim=bim):
                    F = 10 * cn
                    e.copy(out=ZHb[:, 0, :, slot0 + c0:slot0 + c0 + cn],
                           in_=pm[bre][:, 0:F].rearrange("k (j c) -> k j c", j=10))
                    return e.copy(out=ZHb[:, 1, :, slot0 + c0:slot0 + c0 + cn],
                                  in_=pm[bim][:, 0:F].rearrange("k (j c) -> k j c", j=10))
                P.run("scalar", hzev, inc=SEM_EV)
                hz_banks[bre] = P.val(SEM_EV)
                ev_hz = P.val(SEM_EV)
        if "zhb" in dbg:
            t = dout("zhb", (128, 2 * 10 * 130), BF16)
            P.wait("gpsimd", SEM_EV, ev_hz)
            P.run("gpsimd", lambda e, t=t: e.dma_start(out=t[:, :], in_=ZHb[:, :, :, :].rearrange("k a b c -> k (a b c)")),
                  inc="dbg1", inc_amt=16)

        if stop_after == "s2h":
            P.wait("gpsimd", SEM_EV, ev_hz)
            for i in range(TPC // 128):
                P.run("gpsimd", lambda e, i=i: e.dma_start(
                    out=out[i * 128:(i + 1) * 128, 0:64], in_=small[:, 0:64]),
                      inc=SEM_GD, inc_amt=16)
            P.wait("gpsimd", SEM_GD, P.val(SEM_GD))
            P.wait("sync", SEM_GD, P.val(SEM_GD))
            P.build()
            return nc, dbg_t

        P.wait("vector", SEM_EV, ev_hz)
        P.wait("gpsimd", SEM_EV, ev_hz)

        def zh_ap(pl, sl0, sl1, step=1):
            base = ZHb[:, pl, :, :]
            if step == 1:
                v = base[:, :, sl0:sl1]
            else:
                v = base[:, :, sl1:sl0:-1] if False else base[:, :, sl0:sl1:step]
            return v.rearrange("k j c -> k c j")

        def tab_ap(t, c0, c1):
            n = c1 - c0
            return Atabsb[:, t, c0:c1].rearrange("k (c o) -> k c o", o=1).broadcast_to((128, n, 10))

        def emit_combine(eng, gplane, outsl, n1ap, pxap, tre, tim, pre, pim, csc0, csc1):
            g, part = gplane
            out = Gsh[:, g, outsl[0]:outsl[1], :]
            Hnr, Hni = n1ap
            Hpr, Hpi = pxap
            if part == "re":
                def f(e):
                    e.tensor_mul(csc0, tre, Hnr)
                    e.tensor_mul(csc1, tim, Hni)
                    e.tensor_sub(csc0, csc0, csc1)
                    e.tensor_mul(csc1, pre, Hpr)
                    e.tensor_add(csc0, csc0, csc1)
                    e.tensor_mul(csc1, pim, Hpi)
                    return e.tensor_add(out, csc0, csc1)
            else:
                def f(e):
                    e.tensor_mul(csc0, tre, Hni)
                    e.tensor_mul(csc1, tim, Hnr)
                    e.tensor_add(csc0, csc0, csc1)
                    e.tensor_mul(csc1, pre, Hpi)
                    e.tensor_sub(csc0, csc0, csc1)
                    e.tensor_mul(csc1, pim, Hpr)
                    return e.tensor_add(out, csc0, csc1)
            P.run(eng, f, inc=(SEM_VV if eng == "vector" else "gg"))

        csc = at("csc", (128, 2, 64, 10), BF16, HB + 11264)
        cscg = at("cscg", (128, 2, 64, 10), BF16, HB + 13824)
        for grp, (c0, c1, nsl, psl, tset) in enumerate((
                (1, 65, (1, 65), (128, 64, -1), 0),
                (65, 128, (66, 129), (63, 0, -1), 8),
                (0, 1, (0, 1), (129, 130), 0))):
            n = c1 - c0
            nr = ZHb[:, 0, :, nsl[0]:nsl[1]].rearrange("k j c -> k c j")
            ni = ZHb[:, 1, :, nsl[0]:nsl[1]].rearrange("k j c -> k c j")
            if len(psl) == 3:
                pr = ZHb[:, 0, :, psl[0]:psl[1]:psl[2]].rearrange("k j c -> k c j")
                pi = ZHb[:, 1, :, psl[0]:psl[1]:psl[2]].rearrange("k j c -> k c j")
            else:
                pr = ZHb[:, 0, :, psl[0]:psl[1]].rearrange("k j c -> k c j")
                pi = ZHb[:, 1, :, psl[0]:psl[1]].rearrange("k j c -> k c j")
            cs0 = csc[:, 0, 0:n, :]
            cs1 = csc[:, 1, 0:n, :]
            for gi, (t1, t2) in enumerate(((0, 2), (4, 6))):
                a1r = tab_ap(tset + t1, c0, c1); a1i = tab_ap(tset + t1 + 1, c0, c1)
                a2r = tab_ap(tset + t2, c0, c1); a2i = tab_ap(tset + t2 + 1, c0, c1)
                emit_combine("vector", (2 * gi, "re"), (c0, c1), (nr, ni), (pr, pi),
                             a1r, a1i, a2r, a2i, cs0, cs1)
                emit_combine("gpsimd", (2 * gi + 1, "im"), (c0, c1), (nr, ni), (pr, pi),
                             a1r, a1i, a2r, a2i, cscg[:, 0, 0:n, :], cscg[:, 1, 0:n, :])
        vv_comb = P.val(SEM_VV)
        gg_comb = P.val("gg")

        if stop_after == "gcomb":
            P.wait("gpsimd", SEM_VV, vv_comb)
            P.wait("gpsimd", "gg", gg_comb)
            for i in range(TPC // 128):
                P.run("gpsimd", lambda e, i=i: e.dma_start(
                    out=out[i * 128:(i + 1) * 128, 0:64], in_=small[:, 0:64]),
                      inc=SEM_GD, inc_amt=16)
            P.wait("gpsimd", SEM_GD, P.val(SEM_GD))
            P.wait("sync", SEM_GD, P.val(SEM_GD))
            P.build()
            return nc, dbg_t

        P.wait("tensor", SEM_VV, vv_comb)
        P.wait("tensor", "gg", gg_comb)
        gt_ev = {}
        for g in range(4):
            for ci in range(11):
                col0 = 12 * ci if ci < 10 else 116
                ncol = 12
                key = (g * 11 + ci) % 2
                if key in gt_ev:
                    P.wait("tensor", "gte", gt_ev[key])
                def gtr(e, g=g, col0=col0, ncol=ncol, key=key):
                    return e.matmul(pm[2 + key][0:ncol * 10, 0:128],
                                    Gsh[:, g, col0:col0 + ncol, :], idenB[:, :],
                                    start=True, stop=True)
                P.run("tensor", gtr, inc=SEM_T)
                P.wait("scalar", SEM_T, P.val(SEM_T))
                def gte(e, g=g, ci=ci, ncol=ncol, key=key):
                    return e.copy(out=Gtsh[0:ncol * 10, ci, g, :],
                                  in_=pm[2 + key][0:ncol * 10, 0:128])
                P.run("scalar", gte, inc="gte")
                gt_ev[key] = P.val("gte")
        P.wait("gpsimd", "gte", P.val("gte"))

        if stop_after == "gtrT":
            for i in range(TPC // 128):
                P.run("gpsimd", lambda e, i=i: e.dma_start(
                    out=out[i * 128:(i + 1) * 128, 0:64], in_=small[:, 0:64]),
                      inc=SEM_GD, inc_amt=16)
            P.wait("gpsimd", SEM_GD, P.val(SEM_GD))
            P.wait("sync", SEM_GD, P.val(SEM_GD))
            P.build()
            return nc, dbg_t

        P.run("gpsimd", lambda e: e.dma_start(out=gb_in[:, :],
                                              in_=Gtsh[0:120, :, :, :].rearrange("p a b c -> p (a b c)")),
              inc="gbi", inc_amt=16)
        P.wait("gpsimd", "gbi", 16)

        if stop_after == "gtr":
            for i in range(TPC // 128):
                P.run("gpsimd", lambda e, i=i: e.dma_start(
                    out=out[i * 128:(i + 1) * 128, 0:64], in_=small[:, 0:64]),
                      inc=SEM_GD, inc_amt=16)
            P.wait("gpsimd", SEM_GD, P.val(SEM_GD))
            P.wait("sync", SEM_GD, P.val(SEM_GD))
            P.build()
            return nc, dbg_t

        def cc_g_fn(e):
            with allow_ici_all_to_all():
                return e.collective_compute(
                    "AllGather", ALU.bypass, replica_groups=[list(range(NC))],
                    ins=[gb_in.opt()], outs=[gb_out.opt()])
        P.run("gpsimd", cc_g_fn, inc=SEM_CC)
        cc_g = P.val(SEM_CC)
        if "zhb" in dbg:
            P.wait("gpsimd", "dbg1", 16)
        P.run("gpsimd", lambda e: e.engine_nop(), inc="hph")
        for engx in ("sync", "scalar", "vector", "tensor"):
            P.wait(engx, "hph", 1)

        if stop_after == "hpipe":
            P.wait("gpsimd", SEM_CC, cc_g)
            for i in range(TPC // 128):
                P.run("gpsimd", lambda e, i=i: e.dma_start(
                    out=out[i * 128:(i + 1) * 128, 0:64], in_=small[:, 0:64]),
                      inc=SEM_GD, inc_amt=16)
            P.wait("gpsimd", SEM_GD, P.val(SEM_GD))
            P.wait("sync", SEM_GD, P.val(SEM_GD))
            P.build()
            return nc, dbg_t

        ev_need_tp = []
        for i in range(TT // 128):
            b = i % 2
            if i >= 2:
                P.wait("sync", SEM_EV, ev_tile[i - 2])
            ld(xt2[:, b, :], h_xs[i * 128:(i + 1) * 128, :], inc=f"xa{b}")
            P.wait("vector", f"xa{b}", 16 * (i // 2 + 1))
            if i > 0:
                P.wait("vector", SEM_T, t_tp_last)
            def stats(e, b=b):
                e.reduce_sum(small[:, 64:65], xt2[:, b, :], axis=mybir.AxisListType.X)
                e.tensor_scalar_mul(small[:, 64:65], small[:, 64:65], 1.0 / D)
                e.tensor_mul(xhp[:, :], xt2[:, b, :], xt2[:, b, :])
                e.reduce_sum(small[:, 65:66], xhp[:, :], axis=mybir.AxisListType.X)
                e.tensor_scalar_mul(small[:, 65:66], small[:, 65:66], 1.0 / D)
                e.tensor_mul(small[:, 66:67], small[:, 64:65], small[:, 64:65])
                e.tensor_sub(small[:, 65:66], small[:, 65:66], small[:, 66:67])
                return e.tensor_scalar_add(small[:, 65:66], small[:, 65:66], 1e-5)
            P.run("vector", stats, inc=SEM_VV)
            P.wait("scalar", SEM_VV, P.val(SEM_VV))
            P.run("scalar", lambda e: e.activation(out=small[:, 65:66], in_=small[:, 65:66],
                                                   func=ACT.Sqrt), inc=SEM_EV)
            P.wait("vector", SEM_EV, P.val(SEM_EV))
            def nmr(e):
                e.reciprocal(small[:, 65:66], small[:, 65:66])
                e.tensor_mul(small[:, 66:67], small[:, 64:65], small[:, 65:66])
                return e.tensor_scalar_mul(small[:, 66:67], small[:, 66:67], -1.0)
            P.run("vector", nmr, inc=SEM_VV)
            P.wait("scalar", SEM_VV, P.val(SEM_VV))
            P.run("scalar", lambda e, b=b: e.activation(out=xhp[:, :], in_=xt2[:, b, :],
                                                        func=ACT.Identity, scale=small[:, 65:66],
                                                        bias=small[:, 66:67]),
                  inc=SEM_EV)
            ev_norm = P.val(SEM_EV)
            if i == 0:
                ev_tile = {}
                ev_tp = {}
            ev_tile[i] = ev_norm
            P.wait("tensor", SEM_EV, ev_norm)
            P.wait("tensor", SEM_VV, vv_mod)
            for r in range(2):
                bank = 2 + r
                if bank in ev_tp:
                    P.wait("tensor", SEM_EV, ev_tp[bank])
                def tr4(e, bank=bank, r=r):
                    for j in range(4):
                        q = e.transpose(pm[bank][:, j * 128:(j + 1) * 128],
                                        xhp[:, (r * 4 + j) * 128:(r * 4 + j + 1) * 128],
                                        idenF[:, :])
                    return q
                P.run("tensor", tr4, inc=SEM_T)
                P.wait("scalar", SEM_T, P.val(SEM_T))
                def ev4(e, bank=bank, r=r, i=i):
                    for j in range(4):
                        dc = r * 4 + j
                        q = e.activation(
                            out=hFb[:, dc, i * 128:(i + 1) * 128],
                            in_=pm[bank][:, j * 128:(j + 1) * 128],
                            func=ACT.Identity, scale=small[:, CA1 + dc:CA1 + dc + 1],
                            bias=small[:, SA + dc:SA + dc + 1])
                    return q
                P.run("scalar", ev4, inc=SEM_EV)
                ev_tp[bank] = P.val(SEM_EV)
            t_tp_last = P.val(SEM_T)
        ev_hF_done = P.val(SEM_EV)

        FCH = [(0, 512), (512, 1024), (1024, 1536), (1536, 2048), (2048, 2176)]
        P.wait("tensor", "winb", 16)
        P.wait("tensor", SEM_EV, ev_hF_done)

        bank_ev = {}
        ub_last_vv = [0, 0]

        def do_win_chunk(oc, ubuf):
            for fi, (f0, f1) in enumerate(FCH):
                bank = 2 + (fi % 4)
                if bank in bank_ev:
                    P.wait("tensor", SEM_EV, bank_ev[bank])
                for ci in range(8):
                    P.run("tensor", lambda e, bank=bank, ci=ci, oc=oc, f0=f0, f1=f1: e.matmul(
                        pm[bank][:, 0:f1 - f0], winb[:, ci, oc * 128:(oc + 1) * 128],
                        hFb[:, ci, f0:f1], start=(ci == 0), stop=(ci == 7)),
                          inc=(SEM_T if ci == 7 else None))
                P.wait("scalar", SEM_T, P.val(SEM_T))
                if fi == 0:
                    P.wait("scalar", SEM_VV, ub_last_vv[ubuf])
                P.run("scalar", lambda e, bank=bank, oc=oc, f0=f0, f1=f1, ubuf=ubuf: e.activation(
                    out=ub[:, ubuf, f0:f1], in_=pm[bank][:, 0:f1 - f0],
                    func=ACT.Identity, bias=binb[:, oc:oc + 1]), inc=SEM_EV)
                bank_ev[bank] = P.val(SEM_EV)
            return P.val(SEM_EV)

        def do_shortconv(oc, ubuf, dst_ap):
            def sc(e, oc=oc, ubuf=ubuf, dst=dst_ap):
                e.tensor_scalar_mul(ub[:, ubuf, 126:128], ub[:, ubuf, 126:128], hmsk[:, 0:1])
                e.tensor_scalar(ucb[:, 2, :], ub[:, ubuf, 126:126 + TPC],
                                bsw[:, oc, 0:1], bsb[:, oc:oc + 1], ALU.mult, ALU.add)
                e.scalar_tensor_tensor(ucb[:, 2, :], ub[:, ubuf, 127:127 + TPC],
                                       bsw[:, oc, 1:2], ucb[:, 2, :], ALU.mult, ALU.add)
                return e.scalar_tensor_tensor(dst, ub[:, ubuf, 128:128 + TPC],
                                              bsw[:, oc, 2:3], ucb[:, 2, :], ALU.mult, ALU.add)
            P.run("vector", sc, inc=SEM_VV)
            ub_last_vv[ubuf] = P.val(SEM_VV)
            return P.val(SEM_VV)

        P.wait("gpsimd", "hph", 1)
        P.run("gpsimd", lambda e: e.iota(ramp[:, :], [[1, TPC]], channel_multiplier=0, allow_small_or_imprecise_dtypes=True),
              inc="rp", inc_amt=1)


        for g in range(8):
            evs = []
            for j, oc in enumerate((8 + g, 16 + g)):
                ubuf = j
                ev_u = do_win_chunk(oc, ubuf)
                P.wait("vector", SEM_EV, ev_u)
                vv_uc = do_shortconv(oc, ubuf, ucb[:, j, :])
                evs.append(vv_uc)
            if g > 0:
                P.wait("vector", SEM_EV, ev_E)
                P.wait("scalar", SEM_VV, vv_gate)
            def envve(e, g=g):
                e.tensor_scalar_mul(small[:, 70:71], dts[:, g:g + 1], -1.0 / (L - 1))
                return e.tensor_mul(small[:, 71:72], small[:, 70:71], toffs[:, 0:1])
            P.run("vector", envve, inc=SEM_VV)
            P.wait("scalar", SEM_VV, P.val(SEM_VV))
            P.wait("scalar", "rp", 1)
            P.run("scalar", lambda e: e.activation(out=Etile[:, :], in_=ramp[:, :], func=ACT.Exp,
                                                   scale=small[:, 70:71], bias=small[:, 71:72]),
                  inc=SEM_EV)
            ev_E = P.val(SEM_EV)
            wbuf = g % 2
            if g >= 2:
                P.wait("vector", f"wv{wbuf}", 16 * (g // 2))
            P.wait("vector", SEM_EV, ev_E)
            def gate(e, g=g, wbuf=wbuf):
                e.tensor_mul(gsc[:, :], ucb[:, 0, :], ucb[:, 1, :])
                dst = bt.AP(tensor=_th(wvb), offset=wbuf * TPC,
                            ap=[[2 * TPC, 128], [1, TPC // 2], [TPC // 2, 2]])
                return e.tensor_mul(dst, gsc[:, :], Etile[:, :])
            P.run("vector", gate, inc=SEM_VV)
            vv_gate = P.val(SEM_VV)
            P.wait("gpsimd", SEM_VV, P.val(SEM_VV))
            P.run("gpsimd", lambda e, g=g, wbuf=wbuf: e.dma_start(
                out=vb_in[g * 128:(g + 1) * 128, :], in_=wvb[:, wbuf, :]),
                  inc=f"wv{wbuf}", inc_amt=16)
        P.wait("gpsimd", "wv0", P.val("wv0"))
        P.wait("gpsimd", "wv1", P.val("wv1"))
        def cc_v_fn(e):
            with allow_ici_all_to_all():
                return e.collective_compute(
                    "AllToAll", ALU.bypass, replica_groups=[list(range(NC))],
                    ins=[vb_in.opt()], outs=[vb_out.opt()])
        P.run("gpsimd", cc_v_fn, inc=SEM_CC)
        cc_v = P.val(SEM_CC)

        for g in range(8):
            sl = g % 2
            if g >= 2:
                P.wait("vector", f"x0s{sl}", 16 * (g // 2))
            ev_u = do_win_chunk(g, 0)
            P.wait("vector", SEM_EV, ev_u)
            x0dst = bt.AP(tensor=_th(x0bb), offset=sl * TPC,
                          ap=[[2 * TPC, 128], [1, TPC // 2], [TPC // 2, 2]])
            vv_sc = do_shortconv(g, 0, x0dst)
            P.wait("gpsimd", SEM_VV, vv_sc)
            P.run("gpsimd", lambda e, g=g, sl=sl: e.dma_start(
                out=x0sp[g, :, :], in_=x0bb[:, sl, :]), inc=f"x0s{sl}", inc_amt=16)
        vv_x0 = P.val(SEM_VV)
        P.wait("gpsimd", "x0s0", 64)
        P.wait("gpsimd", "x0s1", 64)

        if stop_after == "phaseA":
            P.wait("gpsimd", SEM_CC, cc_v)
            for i in range(TPC // 128):
                P.run("gpsimd", lambda e, i=i: e.dma_start(
                    out=out[i * 128:(i + 1) * 128, :], in_=xt2[:, 0, :]),
                      inc=SEM_GD, inc_amt=16)
            P.wait("gpsimd", SEM_GD, P.val(SEM_GD))
            P.build()
            return nc, dbg_t
        QB = at("QB", (128, 2, 132, 128), BF16, 0)
        Qt = at("Qt", (128, 2, 132, 128), BF16, 67584)
        Tb = at("Tb", (128, 2, 132, 128), BF16, 0)
        Ttb = at("Ttb", (128, 2, 132, 128), BF16, 67584)
        zsend = at("zsend", (128, 16384), BF16, 0)
        cbuf2 = sb("cbuf2", (128, 4, 16, 128), BF16)

        class Banks:

            def __init__(self, ids):
                self.ids = list(ids)
                self.k = 0
                self.ev = {}

            def get(self, n=1):
                out = []
                for _ in range(n):
                    b = self.ids[self.k % len(self.ids)]
                    self.k += 1
                    if b in self.ev:
                        P.wait("tensor", SEM_EV, self.ev[b])
                    out.append(b)
                return out if n > 1 else out[0]

            def done(self, bs):
                if isinstance(bs, int):
                    bs = [bs]
                for b in bs:
                    self.ev[b] = P.val(SEM_EV)

        P.wait("sync", SEM_CC, cc_v)
        P.wait("tensor", "hld", n_hld)
        P.wait("tensor", SEM_EV, P.val(SEM_EV))

        ev_s1q = {}
        t_s1q = {}
        for q in range(32):
            b = q % 2
            c0 = 4 * q
            if q >= 2:
                P.wait("sync", SEM_T, t_s1q[q - 2])
            for j in range(4):
                for p_ in range(2):
                    def zld(e, c=c0 + j, j=j, p_=p_, b=b):
                        dst = bt.AP(tensor=_th(zqb),
                                    offset=(p_ * 64) * 1024 + b * 512 + j * 128,
                                    ap=[[1024, 64], [1, 128]])
                        v = bt.AP(tensor=vb_out.tensor,
                                  offset=c * TPC + p_ * (TPC // 2),
                                  ap=[[128 * TPC, 8], [128, 8], [1, 128]])
                        return e.dma_start(out=dst, in_=v)
                    P.run("sync", zld, inc=f"zq{b}", inc_amt=16)
            P.wait("tensor", f"zq{b}", 128 * (q // 2 + 1))
            if q >= 2:
                P.wait("tensor", SEM_VV, ev_s1q[q - 2][0])
                P.wait("tensor", SEM_EV, ev_s1q[q - 2][1])
            bk = (0, 1) if b == 0 else (2, 3)
            def s1(e, b=b, bk=bk):
                for j in range(4):
                    e.matmul(pm[bk[0]][:, j * 128:(j + 1) * 128], zqb[:, b, j, :],
                             Wsb[:, 0, :], start=True, stop=True)
                    r = e.matmul(pm[bk[1]][:, j * 128:(j + 1) * 128], zqb[:, b, j, :],
                                 Wsb[:, 1, :], start=True, stop=True)
                return r
            P.run("tensor", s1, inc=SEM_T)
            t_s1q[q] = P.val(SEM_T)
            P.wait("vector", SEM_T, t_s1q[q])
            P.wait("scalar", SEM_T, t_s1q[q])
            P.run("vector", lambda e, c0=c0, bk=bk: e.tensor_copy(
                out=Abuf[:, 0, :, c0:c0 + 4],
                in_=pm[bk[0]][:, 0:512].rearrange("p (j k) -> p k j", j=4)), inc=SEM_VV)
            P.run("scalar", lambda e, c0=c0, bk=bk: e.copy(
                out=Abuf[:, 1, :, c0:c0 + 4],
                in_=pm[bk[1]][:, 0:512].rearrange("p (j k) -> p k j", j=4)), inc=SEM_EV)
            ev_s1q[q] = (P.val(SEM_VV), P.val(SEM_EV))
        P.wait("tensor", SEM_VV, P.val(SEM_VV))
        P.wait("tensor", SEM_EV, P.val(SEM_EV))

        if stop_after == "s1":
            P.wait("gpsimd", SEM_VV, P.val(SEM_VV))
            P.wait("gpsimd", SEM_EV, P.val(SEM_EV))
            for i in range(TPC // 128):
                P.run("gpsimd", lambda e, i=i: e.dma_start(
                    out=out[i * 128:(i + 1) * 128, :], in_=xt2[:, 0, :]),
                      inc=SEM_GD, inc_amt=16)
            P.wait("gpsimd", SEM_GD, P.val(SEM_GD))
            P.build()
            return nc, dbg_t
        sbk = Banks([0, 1, 2, 3])
        P.wait("tensor", SEM_VV, P.val(SEM_VV))
        P.wait("tensor", SEM_EV, P.val(SEM_EV))
        for vid in range(130):
            b = vid % 4
            if vid >= 4:
                P.wait("sync", SEM_T, t_s2[vid - 4])
            ld(Bvb[:, b, :, :], h_Bv[vid], inc=f"bv{b}")
            P.wait("tensor", f"bv{b}", 16 * (vid // 4 + 1))
            bk = sbk.get(2)
            def s2f(e, vid=vid, b=b, bk=bk):
                Xr = pm[bk[0]][:, 0:128]
                Xi = pm[bk[1]][:, 0:128]
                e.matmul(Xr, Abuf[:, 0, vid if False else 0, :], Bvb[:, b, 0, :], start=True, stop=False) if False else None
                k1 = VL[vid][0]
                e.matmul(Xr, Abuf[:, 0, k1, :], Bvb[:, b, 0, :], start=True, stop=False)
                e.matmul(Xr, Abuf[:, 1, k1, :], Bvb[:, b, 2, :], start=False, stop=True)
                e.matmul(Xi, Abuf[:, 0, k1, :], Bvb[:, b, 1, :], start=True, stop=False)
                return e.matmul(Xi, Abuf[:, 1, k1, :], Bvb[:, b, 0, :], start=False, stop=True)
            P.run("tensor", s2f, inc=SEM_T)
            if vid == 0:
                t_s2 = {}
            t_s2[vid] = P.val(SEM_T)
            P.wait("vector", SEM_T, P.val(SEM_T))
            P.wait("scalar", SEM_T, P.val(SEM_T))
            P.run("vector", lambda e, vid=vid, bk=bk: e.tensor_copy(out=XQ[:, 0, vid, :],
                                                                    in_=pm[bk[0]][:, 0:128]), inc=SEM_VV)
            P.run("scalar", lambda e, vid=vid, bk=bk: e.copy(out=XQ[:, 1, vid, :],
                                                             in_=pm[bk[1]][:, 0:128]), inc=SEM_EV)
            P.wait("tensor", SEM_VV, P.val(SEM_VV))
            sbk.done(bk)


        if stop_after == "s2f":
            P.wait("gpsimd", SEM_VV, P.val(SEM_VV))
            P.wait("gpsimd", SEM_EV, P.val(SEM_EV))
            for i in range(TPC // 128):
                P.run("gpsimd", lambda e, i=i: e.dma_start(
                    out=out[i * 128:(i + 1) * 128, 0:64], in_=small[:, 0:64]),
                      inc=SEM_GD, inc_amt=16)
            P.wait("gpsimd", SEM_GD, P.val(SEM_GD))
            P.wait("sync", SEM_GD, P.val(SEM_GD))
            P.build()
            return nc, dbg_t
        P.wait("tensor", SEM_CC, cc_g)
        P.wait("tensor", "w3l", 16)
        P.wait("tensor", SEM_VV, P.val(SEM_VV))
        P.wait("tensor", SEM_EV, P.val(SEM_EV))
        cgk = Banks([0, 1, 2, 3, 4, 5])

        def col_ci(col):
            ci = col // 12 if col < 120 else 10
            return ci, col - (12 * ci if col < 120 else 116)

        def emit_cgen(cols, cslots):
            for col, sl in zip(cols, cslots):
                gb = cg_n[0] % 2
                cg_n[0] += 1
                ci, colsub = col_ci(col)
                if gb in cg_t:
                    P.wait("sync", SEM_T, cg_t[gb])
                def gld(e, col=col, ci=ci, colsub=colsub, gb=gb):
                    dst = bt.AP(tensor=_th(Gstb), offset=gb * (4 * 128),
                                ap=[[1024, 80], [1, 512]])
                    v = bt.AP(tensor=gb_out.tensor,
                              offset=(colsub * 10) * 5632 + ci * 512,
                              ap=[[120 * 5632, 8], [5632, 10], [1, 512]])
                    return e.dma_start(out=dst, in_=v)
                P.run("sync", gld, inc=f"gs{gb}", inc_amt=16)
                gst_ev[gb] = gst_ev.get(gb, 0) + 16
                P.wait("tensor", f"gs{gb}", gst_ev[gb])
                bk = cgk.get()
                def cg(e, gb=gb, bk=bk):
                    ps_ = pm[bk]
                    e.matmul(ps_[:, 0:128], w3b[0:80, :], Gstb[0:80, gb, 0, :], start=True, stop=True)
                    e.matmul(ps_[:, 128:256], w3b[0:80, :], Gstb[0:80, gb, 1, :], start=True, stop=True)
                    e.matmul(ps_[:, 256:384], w3b[0:80, :], Gstb[0:80, gb, 2, :], start=True, stop=True)
                    return e.matmul(ps_[:, 384:512], w3b[0:80, :], Gstb[0:80, gb, 3, :], start=True, stop=True)
                P.run("tensor", cg, inc=SEM_T)
                cg_t[gb] = P.val(SEM_T)
                P.wait("scalar", SEM_T, P.val(SEM_T))
                def cgev(e, sl=sl, bk=bk):
                    return e.copy(out=cbuf2[:, :, sl, :],
                                  in_=pm[bk][:, 0:512].rearrange("c (a k) -> c a k", a=4))
                P.run("scalar", cgev, inc=SEM_EV)
                cgk.done(bk)
            return P.val(SEM_EV)

        gst_ev = {}
        cg_t = {}
        cg_n = [0]

        def spectral_batch(a0, n, vq_last, trim=False):
            XNr = XQ[:, 0, a0:a0 + n, :]
            XNi = XQ[:, 1, a0:a0 + n, :]
            XPr = XQ[:, 0, 128 - a0:128 - a0 - n:-1, :]
            XPi = XQ[:, 1, 128 - a0:128 - a0 - n:-1, :]
            c1r = cbuf2[:, 0, 0:n, :]; c1i = cbuf2[:, 1, 0:n, :]
            c2r = cbuf2[:, 2, 0:n, :]; c2i = cbuf2[:, 3, 0:n, :]
            d1r = cbuf2[:, 0, 8:8 + n, :]; d1i = cbuf2[:, 1, 8:8 + n, :]
            d2r = cbuf2[:, 2, 8:8 + n, :]; d2i = cbuf2[:, 3, 8:8 + n, :]
            s0 = scr2[:, 0, 0:n, :]; s1_ = scr2[:, 1, 0:n, :]
            g0 = scr2g[:, 0, 0:n, :]; g1_ = scr2g[:, 1, 0:n, :]
            QNr = QB[:, 0, a0:a0 + n, :]
            QNi = QB[:, 1, a0:a0 + n, :]
            QPr = QB[:, 0, 128 - a0:128 - a0 - n:-1, :]
            QPi = QB[:, 1, 128 - a0:128 - a0 - n:-1, :]
            def vre(e):
                e.tensor_mul(s0, d1r, XPr)
                e.tensor_mul(s1_, d1i, XPi)
                e.tensor_add(s0, s0, s1_)
                e.tensor_mul(s1_, d2r, XNr)
                e.tensor_add(s0, s0, s1_)
                e.tensor_mul(s1_, d2i, XNi)
                e.tensor_sub(QPr, s0, s1_)
                e.tensor_mul(s0, c1r, XNr)
                e.tensor_mul(s1_, c1i, XNi)
                e.tensor_add(s0, s0, s1_)
                e.tensor_mul(s1_, c2r, XPr)
                e.tensor_add(s0, s0, s1_)
                e.tensor_mul(s1_, c2i, XPi)
                return e.tensor_sub(QNr, s0, s1_)
            def vim(e):
                e.tensor_mul(g0, d1i, XPr)
                e.tensor_mul(g1_, d1r, XPi)
                e.tensor_sub(g0, g0, g1_)
                e.tensor_mul(g1_, d2r, XNi)
                e.tensor_add(g0, g0, g1_)
                e.tensor_mul(g1_, d2i, XNr)
                if not trim:
                    return e.tensor_add(QPi, g0, g1_)
                e.tensor_add(g0, g0, g1_)
                return e.tensor_copy(
                    out=QB[:, 1, 128 - a0:128 - a0 - (n - 1):-1, :],
                    in_=scr2g[:, 0, 0:n - 1, :])
            def vimN(e):
                e.tensor_mul(s0, c1i, XNr)
                e.tensor_mul(s1_, c1r, XNi)
                e.tensor_sub(s0, s0, s1_)
                e.tensor_mul(s1_, c2r, XPi)
                e.tensor_add(s0, s0, s1_)
                e.tensor_mul(s1_, c2i, XPr)
                return e.tensor_add(QNi, s0, s1_)
            P.wait("vector", SEM_EV, vq_last)
            P.wait("gpsimd", SEM_EV, vq_last)
            P.run("vector", vre, inc=SEM_VV)
            P.run("gpsimd", vim, inc="gg")
            P.run("vector", vimN, inc=SEM_VV)
            return P.val(SEM_VV), P.val("gg")

        scr2 = sb("scr2", (128, 2, 8, 128), BF16)
        scr2g = sb("scr2g", (128, 2, 8, 128), BF16)
        sp_done = []
        for bi in range(8):
            a0 = 1 + 8 * bi
            cols = list(range(a0, a0 + 8)) + [128 - a for a in range(a0, a0 + 8)]
            cslots = list(range(8)) + list(range(8, 16))
            if bi > 0:
                P.wait("scalar", SEM_VV, sp_done[-1][0])
                P.wait("scalar", "gg", sp_done[-1][1])
            ev = emit_cgen(cols, cslots)
            sp_done.append(spectral_batch(a0, 8, ev, trim=(bi == 7)))
        P.wait("scalar", SEM_VV, sp_done[-1][0])
        P.wait("scalar", "gg", sp_done[-1][1])
        ev = emit_cgen([0], [0])
        def sp0(e):
            XNr = XQ[:, 0, 0, :]; XNi = XQ[:, 1, 0, :]
            XPr = XQ[:, 0, 129, :]; XPi = XQ[:, 1, 129, :]
            c1r = cbuf2[:, 0, 0, :]; c1i = cbuf2[:, 1, 0, :]
            c2r = cbuf2[:, 2, 0, :]; c2i = cbuf2[:, 3, 0, :]
            s0 = scr2[:, 0, 0, :]; s1_ = scr2[:, 1, 0, :]
            e.tensor_mul(s0, c1r, XNr)
            e.tensor_mul(s1_, c1i, XNi)
            e.tensor_add(s0, s0, s1_)
            e.tensor_mul(s1_, c2r, XPr)
            e.tensor_add(s0, s0, s1_)
            e.tensor_mul(s1_, c2i, XPi)
            e.tensor_sub(QB[:, 0, 0, :], s0, s1_)
            e.tensor_mul(s0, c1i, XNr)
            e.tensor_mul(s1_, c1r, XNi)
            e.tensor_sub(s0, s0, s1_)
            e.tensor_mul(s1_, c2r, XPi)
            e.tensor_add(s0, s0, s1_)
            e.tensor_mul(s1_, c2i, XPr)
            return e.tensor_add(QB[:, 1, 0, :], s0, s1_)
        P.wait("vector", SEM_EV, ev)
        P.run("vector", sp0, inc=SEM_VV)
        vv_spec = P.val(SEM_VV)
        gg_spec = P.val("gg")


        if stop_after == "spec":
            P.wait("gpsimd", SEM_VV, P.val(SEM_VV))
            P.wait("gpsimd", SEM_EV, P.val(SEM_EV))
            for i in range(TPC // 128):
                P.run("gpsimd", lambda e, i=i: e.dma_start(
                    out=out[i * 128:(i + 1) * 128, 0:64], in_=small[:, 0:64]),
                      inc=SEM_GD, inc_amt=16)
            P.wait("gpsimd", SEM_GD, P.val(SEM_GD))
            P.wait("sync", SEM_GD, P.val(SEM_GD))
            P.build()
            return nc, dbg_t
        P.wait("tensor", SEM_VV, vv_spec)
        P.wait("tensor", "gg", gg_spec)
        P.wait("tensor", SEM_EV, P.val(SEM_EV))
        qt_ev = {}
        for grp in range(64):
            pl, s0 = divmod(grp, 32)
            s0 *= 4
            key = grp % 2
            if key in qt_ev:
                P.wait("tensor", SEM_EV, qt_ev[key])
            def qtr4(e, pl=pl, s0=s0, key=key):
                for j in range(4):
                    r = e.matmul(pm[4 + key][:, j * 128:(j + 1) * 128],
                                 QB[:, pl, s0 + j, :], idenB[:, :],
                                 start=True, stop=True)
                return r
            P.run("tensor", qtr4, inc=SEM_T)
            P.wait("scalar", SEM_T, P.val(SEM_T))
            P.run("scalar", lambda e, pl=pl, s0=s0, key=key: e.copy(
                out=Qt[:, pl, s0:s0 + 4, :],
                in_=pm[4 + key][:, 0:512].rearrange("p (j k) -> p j k", j=4)), inc=SEM_EV)
            qt_ev[key] = P.val(SEM_EV)
        ev_qt = P.val(SEM_EV)


        if stop_after == "qt":
            P.wait("gpsimd", SEM_VV, P.val(SEM_VV))
            P.wait("gpsimd", SEM_EV, P.val(SEM_EV))
            for i in range(TPC // 128):
                P.run("gpsimd", lambda e, i=i: e.dma_start(
                    out=out[i * 128:(i + 1) * 128, 0:64], in_=small[:, 0:64]),
                      inc=SEM_GD, inc_amt=16)
            P.wait("gpsimd", SEM_GD, P.val(SEM_GD))
            P.wait("sync", SEM_GD, P.val(SEM_GD))
            P.build()
            return nc, dbg_t
        P.wait("tensor", SEM_EV, ev_qt)
        ibk = Banks([0, 1, 2, 3])
        t_chunks = [(s0_, min(4, 65 - s0_), iWnb) for s0_ in range(0, 65, 4)] + \
                   [(s0_, min(4, 128 - s0_), iWrb) for s0_ in range(65, 128, 4)]
        for (s0_, ns, iwb) in t_chunks:
            bk = ibk.get(2)
            def s1i(e, s0_=s0_, ns=ns, iwb=iwb, bk=bk):
                F = ns * 128
                Qr = Qt[:, 0, s0_:s0_ + ns, :]
                Qi = Qt[:, 1, s0_:s0_ + ns, :]
                Tr = pm[bk[0]][:, 0:F].rearrange("m (s c) -> m s c", s=ns)
                Ti = pm[bk[1]][:, 0:F].rearrange("m (s c) -> m s c", s=ns)
                e.matmul(Tr, iwb[:, 0, :], Qr, start=True, stop=False)
                e.matmul(Tr, iwb[:, 2, :], Qi, start=False, stop=True)
                e.matmul(Ti, iwb[:, 1, :], Qr, start=True, stop=False)
                return e.matmul(Ti, iwb[:, 0, :], Qi, start=False, stop=True)
            P.run("tensor", s1i, inc=SEM_T)
            P.wait("vector", SEM_T, P.val(SEM_T))
            P.wait("scalar", SEM_T, P.val(SEM_T))
            P.run("vector", lambda e, s0_=s0_, ns=ns, bk=bk: e.tensor_copy(
                out=Tb[:, 0, s0_:s0_ + ns, :],
                in_=pm[bk[0]][:, 0:ns * 128].rearrange("m (s c) -> m s c", s=ns)), inc=SEM_VV)
            P.run("scalar", lambda e, s0_=s0_, ns=ns, bk=bk: e.copy(
                out=Tb[:, 1, s0_:s0_ + ns, :],
                in_=pm[bk[1]][:, 0:ns * 128].rearrange("m (s c) -> m s c", s=ns)), inc=SEM_EV)
            P.wait("tensor", SEM_VV, P.val(SEM_VV))
            ibk.done(bk)


        if stop_after == "s1p":
            P.wait("gpsimd", SEM_VV, P.val(SEM_VV))
            P.wait("gpsimd", SEM_EV, P.val(SEM_EV))
            for i in range(TPC // 128):
                P.run("gpsimd", lambda e, i=i: e.dma_start(
                    out=out[i * 128:(i + 1) * 128, 0:64], in_=small[:, 0:64]),
                      inc=SEM_GD, inc_amt=16)
            P.wait("gpsimd", SEM_GD, P.val(SEM_GD))
            P.wait("sync", SEM_GD, P.val(SEM_GD))
            P.build()
            return nc, dbg_t
        P.wait("tensor", SEM_VV, P.val(SEM_VV))
        P.wait("tensor", SEM_EV, P.val(SEM_EV))
        tt_ev = {}
        for grp in range(64):
            pl, c0 = divmod(grp, 32)
            c0 *= 4
            key = grp % 2
            if key in tt_ev:
                P.wait("tensor", SEM_EV, tt_ev[key])
            def ttr4(e, pl=pl, c0=c0, key=key):
                for j in range(4):
                    r = e.matmul(pm[4 + key][:, j * 128:(j + 1) * 128],
                                 Tb[:, pl, 0:128, c0 + j], idenB[:, :],
                                 start=True, stop=True)
                return r
            P.run("tensor", ttr4, inc=SEM_T)
            P.wait("scalar", SEM_T, P.val(SEM_T))
            P.run("scalar", lambda e, pl=pl, c0=c0, key=key: e.copy(
                out=Ttb[:, pl, c0:c0 + 4, :],
                in_=pm[4 + key][:, 0:512].rearrange("p (j k) -> p j k", j=4)), inc=SEM_EV)
            tt_ev[key] = P.val(SEM_EV)
        ev_tt = P.val(SEM_EV)

        P.wait("tensor", SEM_EV, ev_tt)
        zbk = Banks([0, 1, 2, 3])
        MB = 4
        zbk = Banks([0, 1, 2, 3])
        for m2g in range(0, 128, MB):
            b = (m2g // MB) % 2
            if m2g >= 2 * MB:
                P.wait("sync", SEM_T, t_s2z[m2g - 2 * MB])
            def fold(e, m2g=m2g, b=b):
                return e.dma_start(out=Fob2[:, b, :, :, :], in_=h_Fo[m2g // MB])
            P.run("sync", fold, inc=f"fo{b}", inc_amt=16)
            P.wait("tensor", f"fo{b}", 16 * (m2g // (2 * MB) + 1))
            bk = zbk.get(2)
            for i in range(MB):
                m2p = m2g + i
                m2 = 127 - m2p
                def s2z(e, m2p=m2p, m2=m2, b=b, i=i, bk=bk):
                    Tr = Ttb[:, 0, 0:128, m2]
                    Ti = Ttb[:, 1, 0:128, m2]
                    zr = pm[bk[0]][:, i * 64:(i + 1) * 64]
                    zi = pm[bk[1]][:, i * 64:(i + 1) * 64]
                    e.matmul(zr, Tr, Fob2[:, b, i, 0, :], start=True, stop=False)
                    e.matmul(zr, Ti, Fob2[:, b, i, 1, :], start=False, stop=True)
                    e.matmul(zi, Tr, Fob2[:, b, i, 1, :], start=True, stop=False)
                    return e.matmul(zi, Ti, Fob2[:, b, i, 2, :], start=False, stop=True)
                P.run("tensor", s2z, inc=(SEM_T if i == MB - 1 else None))
            if m2g == 0:
                t_s2z = {}
            t_s2z[m2g] = P.val(SEM_T)
            P.wait("vector", SEM_T, P.val(SEM_T))
            P.wait("scalar", SEM_T, P.val(SEM_T))
            def zev(e, m2g=m2g, bk=bk, pl=0):
                dst = bt.AP(tensor=zsend.tensor, offset=pl * 8192 + m2g,
                            ap=[[16384, 128], [128, 64], [1, MB]])
                srcp = bt.AP(tensor=nc.lookup_mloc("pm%d" % bk[pl]) if False else None, offset=0, ap=[[1, 1]]) if False else None
                return e.tensor_copy(out=dst, in_=pm[bk[pl]][:, 0:MB * 64].rearrange("c (g m) -> c m g", g=MB))
            def zev_s(e, m2g=m2g, bk=bk, pl=1):
                dst = bt.AP(tensor=zsend.tensor, offset=pl * 8192 + m2g,
                            ap=[[16384, 128], [128, 64], [1, MB]])
                return e.copy(out=dst, in_=pm[bk[pl]][:, 0:MB * 64].rearrange("c (g m) -> c m g", g=MB))
            P.run("vector", zev, inc=SEM_VV)
            P.run("scalar", zev_s, inc=SEM_EV)
            P.wait("tensor", SEM_VV, P.val(SEM_VV))
            zbk.done(bk)
        vv_z = P.val(SEM_VV)
        ev_z = P.val(SEM_EV)
        t_fft_end = P.val(SEM_T)

        if stop_after == "s2z":
            P.wait("gpsimd", SEM_VV, vv_z)
            P.wait("gpsimd", SEM_EV, ev_z)
            for i in range(TPC // 128):
                P.run("gpsimd", lambda e, i=i: e.dma_start(
                    out=out[i * 128:(i + 1) * 128, 0:64], in_=small[:, 0:64]),
                      inc=SEM_GD, inc_amt=16)
            P.wait("gpsimd", SEM_GD, P.val(SEM_GD))
            P.wait("sync", SEM_GD, P.val(SEM_GD))
            P.build()
            return nc, dbg_t

        P.wait("gpsimd", SEM_VV, vv_z)
        P.wait("gpsimd", SEM_EV, ev_z)
        for p_ in range(2):
            for R_ in range(8):
                def rbd(e, p_=p_, R_=R_):
                    srcap = bt.AP(tensor=zsend.tensor,
                                  offset=p_ * 8192 + R_ * 8 * 128,
                                  ap=[[16384, 128], [128, 8], [1, 128]])
                    dstap = bt.AP(tensor=rb_in.tensor,
                                  offset=R_ * 128 * TPC + p_ * 1024,
                                  ap=[[TPC, 128], [128, 8], [1, 128]])
                    return e.dma_start(out=dstap, in_=srcap)
                P.run("gpsimd", rbd, inc="rbi", inc_amt=16)
        P.wait("gpsimd", "rbi", P.val("rbi"))
        def cc_r_fn(e):
            with allow_ici_all_to_all():
                return e.collective_compute(
                    "AllToAll", ALU.bypass, replica_groups=[list(range(NC))],
                    ins=[rb_in.opt()], outs=[rb_out.opt()])
        P.run("gpsimd", cc_r_fn, inc=SEM_CC)
        cc_r = P.val(SEM_CC)

        if "hF" in dbg:
            t = dout("hF", (128, 8, TT), BF16)
            P.wait("gpsimd", SEM_EV, ev_hF_done)
            P.run("gpsimd", lambda e, t=t: e.dma_start(out=t[:, :, :], in_=hFb[:, :, :]),
                  inc=SEM_GD, inc_amt=16)
        if "x0" in dbg:
            t = dout("x0", (8, 128, TPC), BF16)
            P.wait("gpsimd", "x0s0", 64)
            P.wait("gpsimd", "x0s1", 64)
            P.run("gpsimd", lambda e, t=t: e.dma_start(out=t[:, :, :], in_=x0sp[:, :, :]),
                  inc=SEM_GD, inc_amt=16)
        if "vre" in dbg:
            t = dout("vre", (D, TPC), BF16)
            P.wait("gpsimd", SEM_CC, cc_v)
            P.run("gpsimd", lambda e, t=t: e.dma_start(out=t[:, :], in_=vb_out[:, :]),
                  inc=SEM_GD, inc_amt=16)
        if "rbo" in dbg:
            t = dout("rbo", (D, TPC), BF16)
            P.wait("gpsimd", SEM_CC, cc_r)
            P.run("gpsimd", lambda e, t=t: e.dma_start(out=t[:, :], in_=rb_out[:, :]),
                  inc="dbg2", inc_amt=16)
        if "gout" in dbg:
            t = dout("gout", (960, 4 * 11 * 128), BF16)
            P.wait("gpsimd", SEM_CC, P.val(SEM_CC))
            P.run("gpsimd", lambda e, t=t: e.dma_start(out=t[:, :], in_=gb_out[:, :]),
                  inc=SEM_GD, inc_amt=16)
        if "mod" in dbg:
            t = dout("mod", (128, 64))
            P.wait("gpsimd", SEM_VV, vv_mod)
            P.run("gpsimd", lambda e, t=t: e.dma_start(out=t[:, :], in_=small[:, 0:64]),
                  inc=SEM_GD, inc_amt=16)

        rbi_n = P.val("rbi")
        LM1 = float(L - 1)

        P.run("gpsimd", lambda e: e.dma_start(out=mspill[:, :], in_=small[:, 0:48]),
              inc="msp", inc_amt=16)

        P.wait("sync", SEM_T, t_fft_end)
        P.wait("vector", SEM_T, t_fft_end)
        P.wait("vector", "rbi", rbi_n)
        P.wait("scalar", SEM_T, t_fft_end)
        P.run("vector", lambda e: e.memset(onesF[0:1, :], 1.0), inc=SEM_VV)
        vv_ones = P.val(SEM_VV)
        AM0, AMX, ARI, ALR = 160, 224, 232, 240
        rwn = {0: 0, 1: 0}
        t_amax = {}
        last_ev = [0]
        for dc in range(8):
            b = dc % 2
            if dc >= 2:
                P.wait("sync", SEM_T, t_amax[dc - 2])
            ld(rwb[:, b, :], h_rhs[dc * 128:(dc + 1) * 128, :], inc=f"rw{b}")
            rwn[b] += 16
            P.wait("tensor", f"rw{b}", rwn[b])
            for jg in range(2):
                if last_ev[0]:
                    P.wait("tensor", SEM_EV, last_ev[0])
                def trq(e, b=b, jg=jg):
                    for jj in range(4):
                        r = e.matmul(pm[6][:, jj * 128:(jj + 1) * 128],
                                     rwb[:, b, (jg * 4 + jj) * 128:(jg * 4 + jj + 1) * 128],
                                     idenF[:, :], start=True, stop=True)
                    return r
                P.run("tensor", trq, inc=SEM_T)
                P.wait("scalar", SEM_T, P.val(SEM_T))
                P.wait("scalar", SEM_VV, P.val(SEM_VV))
                P.run("scalar", lambda e, jg=jg: e.activation(
                    out=h2a[:, 0, jg * 512:jg * 512 + 512], in_=pm[6][:, 0:512],
                    func=ACT.Abs), inc=SEM_EV)
                last_ev[0] = P.val(SEM_EV)
            t_amax[dc] = P.val(SEM_T)
            P.wait("vector", SEM_EV, P.val(SEM_EV))
            def pmax(e, dc=dc):
                for jc in range(8):
                    r = e.reduce_max(small[:, AM0 + jc * 8 + dc:AM0 + jc * 8 + dc + 1],
                                     h2a[:, 0, jc * 128:(jc + 1) * 128],
                                     axis=mybir.AxisListType.X)
                return r
            P.run("vector", pmax, inc=SEM_VV)
        def amx(e):
            for jc in range(8):
                e.reduce_max(small[:, AMX + jc:AMX + jc + 1],
                             small[:, AM0 + jc * 8:AM0 + jc * 8 + 8],
                             axis=mybir.AxisListType.X)
            e.tensor_scalar_max(small[:, AMX:AMX + 8], small[:, AMX:AMX + 8], 1e-12)
            e.reciprocal(small[:, ARI:ARI + 8], small[:, AMX:AMX + 8])
            e.tensor_scalar_mul(small[:, ARI:ARI + 8], small[:, ARI:ARI + 8], 127.0)
            return e.tensor_scalar_mul(small[:, ALR:ALR + 8], small[:, AMX:AMX + 8],
                                       1.0 / 127.0)
        P.run("vector", amx, inc=SEM_VV)
        vv_amax = P.val(SEM_VV)
        P.wait("gpsimd", SEM_VV, vv_amax)
        P.run("gpsimd", lambda e: e.dma_start(out=arsp[:, :], in_=small[:, ARI:ARI + 16]),
              inc="ars", inc_amt=16)

        def bcast(dstT, ev_extra=None):
            P.wait("tensor", "rwl", P.val("rwl"))
            P.wait("tensor", SEM_VV, vv_ones)
            if last_ev[0]:
                P.wait("tensor", SEM_EV, last_ev[0])
            def bmm(e):
                e.matmul(pm[5][:, 0:512], onesF[0:1, :], rowb[0:1, 0:512],
                         start=True, stop=True)
                return e.matmul(pm[6][:, 0:512], onesF[0:1, :], rowb[0:1, 512:1024],
                                start=True, stop=True)
            P.run("tensor", bmm, inc=SEM_T)
            t_bc = P.val(SEM_T)
            P.wait("scalar", SEM_T, t_bc)
            if ev_extra is not None:
                P.wait("scalar", SEM_VV, ev_extra)
            def bev(e, dstT=dstT):
                e.copy(out=dstT[:, 0:512], in_=pm[5][:, 0:512])
                return e.copy(out=dstT[:, 512:1024], in_=pm[6][:, 0:512])
            P.run("scalar", bev, inc=SEM_EV)
            last_ev[0] = P.val(SEM_EV)
            return t_bc, last_ev[0]

        def row_from_mspill(off, wait_t=None, wait_vv=None):
            if wait_t is not None:
                P.wait("sync", SEM_T, wait_t)
            if wait_vv is not None:
                P.wait("sync", SEM_VV, wait_vv)
            P.run("sync", lambda e, off=off: e.dma_start(
                out=rowb[0:1, :],
                in_=bt.AP(tensor=mspill.tensor, offset=off, ap=[[1, 8], [48, 128]]),
                allow_slow_non_contiguous=True), inc="rwl", inc_amt=16)

        P.wait("sync", "msp", 16)
        row_from_mspill(GAC)
        t_bc_ga, ev_bc_ga = bcast(cbT)
        P.wait("sync", SEM_T, t_bc_ga)
        P.run("sync", lambda e: e.dma_start(out=rowb[0:1, :], in_=h_b_out[0:1, :]),
              inc="rwl", inc_amt=16)
        P.wait("vector", "rwl", P.val("rwl"))
        P.wait("vector", SEM_EV, ev_bc_ga)
        P.run("vector", lambda e: e.tensor_mul(gborow[0:1, :], rowb[0:1, :], cbT[0:1, :]),
              inc=SEM_VV)
        vv_gbo = P.val(SEM_VV)
        for dc in range(8):
            b = dc % 2
            P.wait("sync", SEM_VV, P.val(SEM_VV))
            ld(rwb[:, b, :], h_W_out[dc * 128:(dc + 1) * 128, :], inc=f"rw{b}")
            rwn[b] += 16
            P.wait("vector", f"rw{b}", rwn[b])
            P.run("vector", lambda e, b=b, dc=dc: e.tensor_mul(
                woutb[:, dc, :], rwb[:, b, :], cbT[:, :]), inc=SEM_VV)
        vv_wout = P.val(SEM_VV)

        row_from_mspill(CB1, wait_vv=vv_gbo)
        bcast(cbT, ev_extra=vv_wout)
        row_from_mspill(SB_, wait_t=P.val(SEM_T))
        bcast(sbT)
        P.wait("sync", "ars", 16)
        def arrow(e, off=0):
            return e.dma_start(out=rowb[0:1, :],
                               in_=bt.AP(tensor=arsp.tensor, offset=off,
                                         ap=[[1, 8], [16, 128]]),
                               allow_slow_non_contiguous=True)
        P.wait("sync", SEM_T, P.val(SEM_T))
        P.run("sync", arrow, inc="rwl", inc_amt=16)
        bcast(arinvT)
        P.wait("sync", SEM_T, P.val(SEM_T))
        P.run("sync", lambda e: arrow(e, off=8), inc="rwl", inc_amt=16)
        bcast(gbarT)
        row_from_mspill(GBC, wait_t=P.val(SEM_T))
        bcast(x2a[:, 0, :])
        ev_tabs = last_ev[0]

        MAGIC = 12582912.0
        P.wait("vector", SEM_EV, ev_tabs)
        for dc in range(8):
            b = dc % 2
            P.wait("sync", SEM_VV, P.val(SEM_VV))
            ld(rwb[:, b, :], h_rhs[dc * 128:(dc + 1) * 128, :], inc=f"rw{b}")
            rwn[b] += 16
            P.wait("vector", f"rw{b}", rwn[b])
            def qr1(e, b=b, dc=dc):
                e.tensor_mul(rwb[:, b, :], rwb[:, b, :], arinvT[:, :])
                e.tensor_scalar(rwb[:, b, :], rwb[:, b, :], -127.0, 127.0, ALU.max, ALU.min)
                e.tensor_scalar(rwb[:, b, :], rwb[:, b, :], MAGIC, MAGIC, ALU.add, ALU.subtract)
                return e.tensor_copy(out=rhq[:, dc, :], in_=rwb[:, b, :])
            P.run("vector", qr1, inc=SEM_VV)
        vv_qr = P.val(SEM_VV)
        def gbar(e):
            return e.tensor_mul(gbarT[:, :], gbarT[:, :], x2a[:, 0, :])
        P.run("vector", gbar, inc=SEM_VV)

        ld(onesr[0:1, :], h_oz[0:1, 0:128], inc="rwl")
        ld(fbtb[:, :], h_fbt[:, :], inc="rwl")
        P.run("gpsimd", lambda e: e.iota(itile[:, :], [[1, 128]], channel_multiplier=0,
                                         allow_small_or_imprecise_dtypes=True), inc="pbc")
        DSE0, DTL0, EB0, ST0 = 100, 108, 116, 132
        def dsetl(e):
            e.tensor_scalar(small[:, 98:99], toffs[:, 0:1], -LM1, 1.0 / LM1,
                            ALU.add, ALU.mult)
            for dc in range(8):
                e.tensor_scalar_mul(small[:, DSE0 + dc:DSE0 + dc + 1],
                                    dts[:, dc:dc + 1], 2.0 / LM1)
                e.tensor_mul(small[:, DTL0 + dc:DTL0 + dc + 1],
                             dts[:, dc:dc + 1], small[:, 98:99])
            return e.engine_nop()
        P.run("vector", dsetl, inc=SEM_VV)
        vv_prep = P.val(SEM_VV)

        P.wait("sync", SEM_CC, cc_r)
        for dc in range(8):
            P.run("sync", lambda e, dc=dc: e.dma_start(
                out=cvF[:, dc, :], in_=rb_out[dc * 128:(dc + 1) * 128, :]),
                  inc="cvl", inc_amt=16)
        P.wait("vector", "cvl", 128)
        P.wait("scalar", SEM_VV, vv_prep)
        P.wait("scalar", "pbc", P.val("pbc"))
        P.wait("tensor", SEM_VV, P.val(SEM_VV))
        P.wait("tensor", "rwl", P.val("rwl"))
        P.wait("vector", "rwl", P.val("rwl"))

        if stop_after == "ret":
            P.wait("gpsimd", "cvl", 128)
            P.wait("gpsimd", SEM_VV, P.val(SEM_VV))
            for i in range(TPC // 128):
                P.run("gpsimd", lambda e, i=i: e.dma_start(
                    out=out[i * 128:(i + 1) * 128, 0:64], in_=small[:, 0:64]),
                      inc=SEM_GD, inc_amt=16)
            P.wait("gpsimd", SEM_GD, P.val(SEM_GD))
            P.wait("sync", SEM_GD, P.val(SEM_GD))
            P.build()
            return nc, dbg_t

        NT = TPC // 128
        t_wout = {}
        t_tp = {}
        t_ffn = {}
        vv_x2 = {}
        vv_fin = {}
        ev_qf = {}
        ev_sq = {}
        od_n = {0: 0, 1: 0}
        xl_n = {0: 0, 1: 0}
        x0l_n = {0: 0, 1: 0}

        def tile_loads(i):
            sl = i % 2
            p_, mb = divmod(i, 8)
            m0 = mb * 128
            if i >= 2:
                P.wait("sync", SEM_VV, vv_x2.get(i - 2, 0))
                P.wait("sync", SEM_VV, vv_fin.get(i - 2, 0))
            def xld(e, p_=p_, m0=m0, sl=sl):
                src = bt.AP(tensor=h_xs.tensor, offset=(HALO + 2 * m0 + p_) * D,
                            ap=[[2 * D, 128], [1, D]])
                dst = bt.AP(tensor=_th(xt3), offset=sl * D, ap=[[2 * D, 128], [1, D]])
                return e.dma_start(out=dst, in_=src)
            P.run("sync", xld, inc=f"xl{sl}", inc_amt=16)
            xl_n[sl] += 16
            for g_ in range(8):
                def x0ld(e, p_=p_, m0=m0, sl=sl, g_=g_):
                    src = bt.AP(tensor=x0sp.tensor,
                                offset=g_ * 128 * TPC + p_ * (TPC // 2) + m0,
                                ap=[[TPC, 128], [1, 128]])
                    dst = bt.AP(tensor=_th(x0t), offset=sl * 1024 + g_ * 128,
                                ap=[[2048, 128], [1, 128]])
                    return e.dma_start(out=dst, in_=src)
                P.run("sync", x0ld, inc=f"x0l{sl}", inc_amt=16)
                x0l_n[sl] += 16

        def emit_fin(i):
            j = i - 1
            if j < 0:
                return
            slj = j % 2
            s3j = j % 3
            pj, mbj = divmod(j, 8)
            m0j = mbj * 128
            STBj = ST0 + slj * 6
            P.wait("vector", SEM_T, t_ffn[j])
            P.wait("vector", f"od{slj}", od_n[slj])
            def fin(e, sl=slj, s3=s3j, STB=STBj):
                for half in range(2):
                    c0 = half * 512
                    e.scalar_tensor_tensor(da[:, sl, c0:c0 + 512], pm[4 + half][:, 0:512],
                                           small[:, STB + 5:STB + 6],
                                           gbarT[:, c0:c0 + 512], ALU.mult, ALU.mult)
                    r = e.tensor_add(da[:, sl, c0:c0 + 512], da[:, sl, c0:c0 + 512],
                                     x2a[:, s3, c0:c0 + 512])
                return r
            P.run("vector", fin, inc=SEM_VV)
            vv_fin[j] = P.val(SEM_VV)
            P.wait("gpsimd", SEM_VV, vv_fin[j])
            def odma(e, sl=slj, p_=pj, m0=m0j):
                dst = bt.AP(tensor=out.tensor, offset=(2 * m0 + p_) * D,
                            ap=[[2 * D, 128], [1, D]])
                return e.dma_start(out=dst, in_=da[:, sl, :])
            P.run("gpsimd", odma, inc=f"od{slj}", inc_amt=16)
            od_n[slj] += 16

        tile_loads(0)
        tile_loads(1)
        for i in range(NT):
            sl = i % 2
            s3 = i % 3
            p_, mb = divmod(i, 8)
            m0 = mb * 128
            wbk = (0, 1) if sl == 0 else (2, 3)
            ebcol = EB0 + sl * 8

            P.wait("vector", f"x0l{sl}", 128 * (i // 2 + 1))
            if i >= 2:
                P.wait("vector", SEM_T, t_wout[i - 2])
            def ebias8(e, m0=m0, p_=p_, ebcol=ebcol):
                for dc in range(8):
                    r = e.scalar_tensor_tensor(
                        small[:, ebcol + dc:ebcol + dc + 1], dts[:, dc:dc + 1],
                        (2 * m0 + p_) / LM1, small[:, DTL0 + dc:DTL0 + dc + 1],
                        ALU.mult, ALU.add)
                return r
            P.run("vector", ebias8, inc=SEM_VV)
            P.wait("scalar", SEM_VV, P.val(SEM_VV))
            if i >= 1:
                P.wait("scalar", SEM_VV, vv_erv_used)
            def exp8(e, ebcol=ebcol):
                for dc in range(8):
                    r = e.activation(
                        out=ervb[:, dc, :], in_=itile[:, :], func=ACT.Exp,
                        scale=small[:, DSE0 + dc:DSE0 + dc + 1],
                        bias=small[:, ebcol + dc:ebcol + dc + 1])
                return r
            P.run("scalar", exp8, inc=SEM_EV)
            P.wait("vector", SEM_EV, P.val(SEM_EV))
            if i >= 2:
                P.wait("vector", f"od{sl}", od_n[sl])
            def gate8(e, sl=sl, p_=p_, m0=m0):
                for dc in range(8):
                    e.tensor_mul(da[:, sl, dc * 128:dc * 128 + 128],
                                 cvF[:, dc, p_ * 1024 + m0:p_ * 1024 + m0 + 128],
                                 ervb[:, dc, :])
                    r = e.scalar_tensor_tensor(
                        ygb[:, sl, dc, :], da[:, sl, dc * 128:dc * 128 + 128],
                        fbtb[:, dc:dc + 1], x0t[:, sl, dc, :], ALU.add, ALU.mult)
                return r
            P.run("vector", gate8, inc=SEM_VV)
            vv_erv_used = P.val(SEM_VV)
            vv_gate = P.val(SEM_VV)

            P.wait("tensor", SEM_VV, vv_gate)
            if i >= 2:
                P.wait("tensor", SEM_VV, vv_x2.get(i - 2, 0))
            def wmm(e, sl=sl, wbk=wbk):
                for half in range(2):
                    ps_ = pm[wbk[half]][:, 0:512]
                    e.matmul(ps_, onesr[0:1, :], gborow[0:1, half * 512:half * 512 + 512],
                             start=True, stop=False)
                    for dc in range(8):
                        r = e.matmul(ps_, ygb[:, sl, dc, :],
                                     woutb[:, dc, half * 512:half * 512 + 512],
                                     start=False, stop=(dc == 7))
                return r
            P.run("tensor", wmm, inc=SEM_T)
            t_wout[i] = P.val(SEM_T)
            emit_fin(i)

            P.wait("vector", SEM_T, t_wout[i])
            P.wait("vector", f"xl{sl}", 16 * (i // 2 + 1))
            if i >= 3:
                P.wait("vector", SEM_VV, vv_fin.get(i - 3, 0))
            if i >= 2:
                P.wait("vector", SEM_VV, vv_ql.get(i - 2, 0))
            STB = ST0 + sl * 6
            def x2st(e, sl=sl, s3=s3, wbk=wbk, STB=STB):
                e.tensor_add(x2a[:, s3, 0:512], pm[wbk[0]][:, 0:512], xt3[:, sl, 0:512])
                e.tensor_add(x2a[:, s3, 512:1024], pm[wbk[1]][:, 0:512], xt3[:, sl, 512:1024])
                e.reduce_sum(small[:, STB:STB + 1], x2a[:, s3, :],
                             axis=mybir.AxisListType.X)
                e.tensor_mul(h2a[:, sl, :], x2a[:, s3, :], x2a[:, s3, :])
                e.reduce_sum(small[:, STB + 1:STB + 2], h2a[:, sl, :],
                             axis=mybir.AxisListType.X)
                e.tensor_scalar_mul(small[:, STB:STB + 1], small[:, STB:STB + 1], -1.0 / D)
                e.tensor_scalar_mul(small[:, STB + 1:STB + 2], small[:, STB + 1:STB + 2], 1.0 / D)
                e.tensor_mul(small[:, STB + 2:STB + 3], small[:, STB:STB + 1], small[:, STB:STB + 1])
                e.tensor_sub(small[:, STB + 1:STB + 2], small[:, STB + 1:STB + 2], small[:, STB + 2:STB + 3])
                return e.tensor_scalar_add(small[:, STB + 1:STB + 2], small[:, STB + 1:STB + 2], 1e-5)
            P.run("vector", x2st, inc=SEM_VV)
            vv_x2[i] = P.val(SEM_VV)
            P.wait("scalar", SEM_VV, vv_x2[i])
            P.run("scalar", lambda e, STB=STB: e.activation(
                out=small[:, STB + 2:STB + 3], in_=small[:, STB + 1:STB + 2],
                func=ACT.Sqrt), inc=SEM_EV)
            P.wait("vector", SEM_EV, P.val(SEM_EV))
            def lnq1(e, sl=sl, s3=s3, STB=STB):
                e.reciprocal(small[:, STB + 2:STB + 3], small[:, STB + 2:STB + 3])
                e.tensor_mul(small[:, STB + 3:STB + 4], small[:, STB:STB + 1],
                             small[:, STB + 2:STB + 3])
                return e.tensor_scalar(h2a[:, sl, :], x2a[:, s3, :],
                                       small[:, STB + 2:STB + 3],
                                       small[:, STB + 3:STB + 4], ALU.mult, ALU.add)
            P.run("vector", lnq1, inc=SEM_VV)
            P.wait("gpsimd", SEM_VV, P.val(SEM_VV))
            def lngp(e, sl=sl):
                e.tensor_tensor(out=h2a[:, sl, :], in0=h2a[:, sl, :], in1=cbT[:, :],
                                op=ALU.mult)
                return e.tensor_tensor(out=h2a[:, sl, :], in0=h2a[:, sl, :],
                                       in1=sbT[:, :], op=ALU.add)
            P.run("gpsimd", lngp, inc="gg")
            P.wait("vector", "gg", P.val("gg"))
            def lnq2(e, sl=sl, STB=STB):
                e.reduce_max(small[:, STB + 4:STB + 5], h2a[:, sl, :],
                             axis=mybir.AxisListType.X, apply_absolute_value=True)
                e.tensor_scalar_max(small[:, STB + 4:STB + 5], small[:, STB + 4:STB + 5], 1e-12)
                e.tensor_scalar_mul(small[:, STB + 5:STB + 6], small[:, STB + 4:STB + 5], 1.0 / 127.0)
                e.reciprocal(small[:, STB + 4:STB + 5], small[:, STB + 4:STB + 5])
                e.tensor_scalar_mul(small[:, STB + 4:STB + 5], small[:, STB + 4:STB + 5], 127.0)
                e.tensor_scalar(h2a[:, sl, :], h2a[:, sl, :], small[:, STB + 4:STB + 5],
                                -127.0, ALU.mult, ALU.max)
                e.tensor_scalar(h2a[:, sl, :], h2a[:, sl, :], 127.0, MAGIC, ALU.min, ALU.add)
                return e.tensor_scalar_sub(qlb[:, sl, :], h2a[:, sl, :], MAGIC)
            P.run("vector", lnq2, inc=SEM_VV)
            if i == 0:
                vv_ql = {}
            vv_ql[i] = P.val(SEM_VV)

            P.wait("tensor", SEM_VV, vv_ql[i])
            if i >= 1:
                P.wait("tensor", SEM_EV, ev_qf[i - 1])
            def qtp(e, sl=sl):
                for dc in range(8):
                    r = e.transpose(pmB[:, dc * 128:(dc + 1) * 128],
                                    qlb[:, sl, dc * 128:(dc + 1) * 128], idenB[:, :])
                return r
            P.run("tensor", qtp, inc=SEM_T)
            t_tp[i] = P.val(SEM_T)
            P.wait("scalar", SEM_T, t_tp[i])
            if i >= 2:
                P.wait("scalar", SEM_T, t_ffn.get(i - 2, 0))
            P.run("scalar", lambda e, sl=sl: e.copy(
                out=qFa[:, sl, :, :], in_=pmB[:, :].rearrange("c (g t) -> c g t", g=8)),
                inc=SEM_EV)
            ev_qf[i] = P.val(SEM_EV)

            P.wait("tensor", SEM_EV, ev_qf[i])
            if i >= 1:
                P.wait("tensor", SEM_VV, vv_fin.get(i - 1, 0))
            def fmm(e, sl=sl):
                for half in range(2):
                    ps_ = pm[4 + half][:, 0:512]
                    for dc in range(8):
                        r = e.matmul(ps_, qFa[:, sl, dc, :],
                                     rhq[:, dc, half * 512:half * 512 + 512],
                                     start=(dc == 0), stop=(dc == 7))
                return r
            P.run("tensor", fmm, inc=SEM_T)
            t_ffn[i] = P.val(SEM_T)

            if i + 2 < NT:
                tile_loads(i + 2)
        emit_fin(NT)

        P.wait("gpsimd", "od0", od_n[0])
        P.wait("gpsimd", "od1", od_n[1])
        P.wait("sync", "od0", od_n[0])
        P.wait("sync", "od1", od_n[1])

        P.build()
    return nc, dbg_t


def _core_inputs(inputs):
    x = np.asarray(inputs["L0_attn_x"])[0]
    W3 = np.asarray(inputs["L0_attn_W3"])
    b3 = np.asarray(inputs["L0_attn_b3"])
    per = []
    for i in range(NC):
        d = {}
        lo = i * TPC - HALO
        if lo < 0:
            xs = np.concatenate([np.zeros((-lo, D), np.float32), x[0:(i + 1) * TPC]], 0)
        else:
            xs = np.asarray(x[lo:(i + 1) * TPC])
        d["xs"] = _f32(xs)
        d["halo_mask"] = np.full((128, 1), 0.0 if i == 0 else 1.0, np.float32)
        d["W2s"] = _f32(np.asarray(inputs["L0_attn_W2"])[:, 8 * i:8 * i + 8])
        d["b2f2"] = _f32(np.stack([np.asarray(inputs["L0_attn_b2"])[8 * i:8 * i + 8],
                                   np.asarray(inputs["L0_attn_f2"])[8 * i:8 * i + 8]], 1))
        W3p = np.zeros((80, 128), np.float32)
        csl = slice(128 * i, 128 * (i + 1))
        for s in range(8):
            W3p[10 * s:10 * s + 8] = W3[8 * s:8 * s + 8, csl]
        W3p[8] = b3[csl]
        d["W3p"] = _f32(W3p)
        d["toff"] = np.full((128, 1), float(i * TPC), np.float32)
        wm = np.asarray(inputs["L0_wmod"])[:, i * 768:(i + 1) * 768]
        d["wms"] = _f32(wm.reshape(8, 128, 6, 128).transpose(2, 0, 1, 3))
        per.append(d)
    return per


def _in_map(inputs, core, tab):
    return {
        "xs": core["xs"],
        "W_in": _f32(inputs["L0_attn_W_in"]),
        "b_in": _f32(np.asarray(inputs["L0_attn_b_in"]).reshape(24, 128).T),
        "short_w": _f32(np.asarray(inputs["L0_attn_short_w"]).reshape(24, 128, 3).transpose(1, 0, 2)),
        "short_b": _f32(np.asarray(inputs["L0_attn_short_b"]).reshape(24, 128).T),
        "W1": _f32(inputs["L0_attn_W1"]),
        "b1f1": _f32(np.stack([np.asarray(inputs["L0_attn_b1"]),
                               np.asarray(inputs["L0_attn_f1"])], 1)),
        "W2s": core["W2s"],
        "b2f2": core["b2f2"],
        "W3p": core["W3p"],
        "fb": _f32(inputs["L0_attn_filter_bias"]).reshape(1, D),
        "fbt": _f32(np.asarray(inputs["L0_attn_filter_bias"]).reshape(8, 128).T),
        "W_out": _f32(inputs["L0_attn_W_out"]),
        "b_out": _f32(inputs["L0_attn_b_out"]).reshape(1, D),
        "rhs": _f32(inputs["L0_ffn_rhs"]),
        "cvec": _f32(np.asarray(inputs["L0_c"]).reshape(8, 128).T),
        "wms": core["wms"],
        "halo_mask": core["halo_mask"],
        "toff": core["toff"],
        "peT": _f32(tab["peT"]),
        "deltas": _f32(tab["deltas"].reshape(8, 128).T),
        "Ws": _b16(np.stack([tab["Ws_re"], tab["Ws_im"]], 0)),
        "F1f": _b16(np.stack([tab["F1f_re"], tab["F1f_im"]], 0)),
        "Bv": _b16(np.asarray(tab["Bv"]).transpose(0, 2, 1, 3)),
        "Htw": _b16(tab["Htw"]),
        "H2st": _b16(tab["H2st"]),
        "iWn": _b16(tab["iWn"]),
        "iWr": _b16(tab["iWr"]),
        "Fo": _b16(np.asarray(tab["Fo"]).reshape(32, 4, 3, 128, 64).transpose(0, 3, 1, 2, 4)),
        "Atabs": _b16(np.asarray(tab["Atabs"]).transpose(1, 0, 2)),
        "idenF": _f32(tab["iden"]),
        "ozrow": _b16(np.stack([np.ones(4096, np.float32), np.zeros(4096, np.float32)], 0)),
        "idenB": _b16(tab["iden"]),
    }


def kernel(**inputs):
    tab = _tables()
    per = _core_inputs(inputs)
    nc, _ = build_nc()
    in_maps = [_in_map(inputs, per[i], tab) for i in range(NC)]
    res = run_bass_kernel_spmd(nc, in_maps, core_ids=list(range(NC)))
    return np.concatenate([r["out"] for r in res.results], 0)[None]
